# version 05
# baseline (speedup 1.0000x reference)
import math
import numpy as np
import ml_dtypes
from contextlib import ExitStack
import concourse.bass as bass
import concourse.mybir as mybir
from concourse.bass_utils import run_bass_kernel_spmd

F32 = mybir.dt.float32
BF16 = mybir.dt.bfloat16
AF = mybir.ActivationFunctionType
ALU = mybir.AluOpType
AX = mybir.AxisListType

BF = ml_dtypes.bfloat16

NCORES = 8
T = 2048
TH = 2560
TF = 2 * T
D = 1024
NK = D // 128
CT = 16
NSTATE = 16
BLKS = [(i * 512, 512) for i in range(4)]
BLKS_H = [(i * 512, 512) for i in range(5)]

ENG_ATTR = {"pe": "tensor", "act": "scalar", "dve": "vector", "pool": "gpsimd", "sp": "sync"}


class Op:
    __slots__ = ("eng", "clockeng", "build", "tag", "idx",
                 "eng_idx", "waits", "needs_inc", "inc_val", "cc")

    def __init__(self):
        self.waits = []
        self.needs_inc = False
        self.inc_val = None


class Fw:
    def __init__(self, nc):
        self.nc = nc
        self.ops = []
        self.eng_ops = {e: [] for e in ENG_ATTR}
        self.tacc = {}
        self.eng_clock = {e: {} for e in ENG_ATTR}
        self.part_elems = {}
        self.cms = {}
        self.alloc_seq = {}
        self.free_evt = {}
        self.ctx = ExitStack()
        self.n_false_fallback = 0
        self.dma_domains = set()
        self.prefix = ""
        self.input_names = []
        self.onerow = None
        self.aux_ps = None

    def sb(self, name, shape, dtype=F32):
        name = self.prefix + name
        cm = self.nc.sbuf_tensor(name, shape, dtype)
        h = cm.__enter__()
        self.cms[name] = cm
        self.alloc_seq[name] = len(self.alloc_seq)
        self.ctx.callback(self._free_cm, name)
        self.part_elems[name] = int(np.prod(shape[1:]))
        return h

    def _free_cm(self, name):
        cm = self.cms.pop(name, None)
        if cm is not None:
            self.free_evt[name] = len(self.ops)
            cm.__exit__(None, None, None)

    def mark(self):
        return len(self.alloc_seq)

    def free_since(self, mark):
        names = [n for n, i in self.alloc_seq.items() if i >= mark and n in self.cms]
        for name in sorted(names, key=lambda n: -self.alloc_seq[n]):
            self._free_cm(name)

    def free_sb(self, *handles):
        names = [h.name if hasattr(h, "name") else self.prefix + h for h in handles]
        for name in sorted(names, key=lambda n: -self.alloc_seq.get(n, -1)):
            self._free_cm(name)

    def barrier(self, engines=None):
        deps = {}
        for ce, lst in self.eng_ops.items():
            if lst:
                deps[ce] = len(lst) - 1
        for eng in (engines or list(ENG_ATTR)):
            op = self._add(eng, eng, lambda h: h.nop(), [], [], tag="barrier")
            self._force_deps(op, {ce: k for ce, k in deps.items()
                                  if not (ce == eng and k >= op.eng_idx)})

    def ps(self, name, shape, dtype=F32):
        name = self.prefix + name
        h = self.ctx.enter_context(self.nc.psum_tensor(name, shape, dtype))
        self.part_elems[name] = int(np.prod(shape[1:]))
        return h

    def dram(self, name, shape, dtype=F32, kind="Internal"):
        name = self.prefix + name
        t = self.nc.dram_tensor(name, shape, dtype, kind=kind)
        self.part_elems[name] = None
        return t.ap()

    def inp(self, name, shape, dtype=F32):
        ap = self.dram(name, shape, dtype, kind="ExternalInput")
        self.input_names.append(self.prefix + name)
        return ap

    def outp(self, name, shape, dtype=F32):
        return self.dram(name, shape, dtype, kind="ExternalOutput")

    def _region(self, ap):
        th = ap.tensor
        name = th.name
        pe = self.part_elems.get(name, None)
        off = int(ap.offset)
        dims = ap.ap
        if pe is None:
            ext = 0
            for step, count in dims:
                if count > 1:
                    ext += abs(int(step)) * (int(count) - 1)
            return (name, 0, 1, off, off + ext + 1)
        p_lo, b_lo = divmod(off, pe)
        p_ext = 0
        b_ext = 0
        ok = True
        for step, count in dims:
            step = int(step); count = int(count)
            if count <= 1 or step == 0:
                continue
            if abs(step) % pe == 0:
                p_ext += (abs(step) // pe) * (count - 1)
            elif abs(step) < pe:
                b_ext += abs(step) * (count - 1)
            else:
                ok = False
                break
        if not ok or b_lo + b_ext >= pe:
            self.n_false_fallback += 1
            return (name, 0, 1 << 20, 0, 1 << 40)
        return (name, p_lo, p_lo + p_ext + 1, b_lo, b_lo + b_ext + 1)

    def _full_region(self, ap):
        return (ap.tensor.name, 0, 1 << 20, 0, 1 << 40)

    @staticmethod
    def _overlap(r1, r2):
        return (r1[1] < r2[2] and r2[1] < r1[2] and r1[3] < r2[4] and r2[3] < r1[4])

    @staticmethod
    def _covers(r1, r2):
        return r1[1] <= r2[1] and r1[2] >= r2[2] and r1[3] <= r2[3] and r1[4] >= r2[4]

    def comp(self, eng, build, reads=(), writes=(), tag=""):
        rregs = [self._region(ap) for ap in reads]
        wregs = [self._region(ap) for ap in writes]
        return self._add(eng, eng, build, rregs, wregs, tag)

    def dma(self, qeng, out, in_, tag="", **kw):
        def build(h):
            return h.dma_start(out=out, in_=in_, **kw)
        dom = "D." + out.tensor.name
        if dom not in self.eng_ops:
            self.eng_ops[dom] = []
            self.dma_domains.add(dom)
        rregs = [self._region(in_)]
        wregs = [self._full_region(out)]
        op = self._add(qeng, dom, build, rregs, wregs, tag)
        op.needs_inc = True
        return op

    def _add(self, eng, clockeng, build, rregs, wregs, tag):
        op = Op()
        op.eng = eng
        op.clockeng = clockeng
        op.build = build
        op.tag = tag
        op.idx = len(self.ops)
        op.eng_idx = len(self.eng_ops[clockeng])
        self.ops.append(op)
        self.eng_ops[clockeng].append(op)
        deps = {}

        def add_dep(dop):
            if dop is op:
                return
            if dop.clockeng == "pe" and op.clockeng == "pe":
                return
            ce = dop.clockeng
            if deps.get(ce, -1) < dop.eng_idx:
                deps[ce] = dop.eng_idx

        for reg in rregs:
            acc = self.tacc.get(reg[0])
            if acc:
                for dop, wreg in acc["w"]:
                    if self._overlap(reg, wreg):
                        add_dep(dop)
        for reg in wregs:
            acc = self.tacc.get(reg[0])
            if acc:
                for dop, wreg in acc["w"]:
                    if self._overlap(reg, wreg):
                        add_dep(dop)
                for dop, rreg in acc["r"]:
                    if self._overlap(reg, rreg):
                        add_dep(dop)
        self._install(op, deps)
        for reg in wregs:
            acc = self.tacc.setdefault(reg[0], {"w": [], "r": []})
            acc["w"] = [(o, r) for (o, r) in acc["w"] if not self._covers(reg, r)]
            acc["r"] = [(o, r) for (o, r) in acc["r"] if not self._covers(reg, r)]
            acc["w"].append((op, reg))
        for reg in rregs:
            acc = self.tacc.setdefault(reg[0], {"w": [], "r": []})
            acc["r"].append((op, reg))
        return op

    def _add_wait(self, op, ce, k):
        wait_eng = op.eng
        if self.eng_clock[wait_eng].get(ce, -1) >= k:
            return
        target = self.eng_ops[ce][k]
        if ce not in self.dma_domains:
            target.needs_inc = True
        op.waits.append((ce, k))
        clk = self.eng_clock[wait_eng]
        tc = target.cc
        if tc:
            for e2, v2 in tc.items():
                if clk.get(e2, -1) < v2:
                    clk[e2] = v2
        if clk.get(ce, -1) < k:
            clk[ce] = k

    def _install(self, op, deps):
        for ce in sorted(deps):
            self._add_wait(op, ce, deps[ce])
        cc = dict(self.eng_clock[op.eng])
        cc[op.clockeng] = op.eng_idx
        op.cc = cc

    def _force_deps(self, op, deps):
        for ce, k in deps.items():
            if ce == op.clockeng and k <= op.eng_idx:
                continue
            self._add_wait(op, ce, k)

    def finish_outputs(self):
        deps = {}
        for name, pe in self.part_elems.items():
            if pe is not None:
                continue
            acc = self.tacc.get(name)
            if not acc:
                continue
            for dop, _ in acc["w"]:
                ce = dop.clockeng
                if deps.get(ce, -1) < dop.eng_idx:
                    deps[ce] = dop.eng_idx
        op = self._add("sp", "sp", lambda h: h.nop(), [], [], tag="finish")
        self._force_deps(op, deps)

    def finalize(self, emit_stats=True):
        nc = self.nc
        val_of = {}
        for ce, lst in self.eng_ops.items():
            n = 0
            for o in lst:
                if o.needs_inc:
                    n += 1
                    o.inc_val = n
                val_of[(ce, o.eng_idx)] = n
        sems = {}
        base_val = {}
        n_phys = 0
        phys = []
        doms = []
        for ce in self.eng_ops:
            if ce in self.dma_domains and self.eng_ops[ce]:
                first = self.eng_ops[ce][0].idx
                last = self.eng_ops[ce][-1].idx
                tname = ce[2:]
                fidx = self.free_evt.get(tname)
                end = fidx if (fidx is not None and fidx > last) else None
                doms.append((first, end, ce))
        doms.sort()
        for first, end, ce in doms:
            slot = None
            for s in phys:
                if s["end"] is not None and s["end"] <= first:
                    slot = s
                    break
            if slot is None:
                slot = {"end": None, "cum": 0,
                        "handle": self.ctx.enter_context(nc.semaphore(f"dsem{len(phys)}"))}
                phys.append(slot)
            base_val[ce] = slot["cum"]
            slot["cum"] += 16 * len(self.eng_ops[ce])
            slot["end"] = end
            sems[ce] = slot["handle"]
        for ce, lst in self.eng_ops.items():
            if ce in self.dma_domains:
                continue
            if any(o.needs_inc for o in lst):
                sems[ce] = self.ctx.enter_context(nc.semaphore("sem_" + ce))
                base_val[ce] = 0
        per_eng = {e: [] for e in ENG_ATTR}
        for op in self.ops:
            per_eng[op.eng].append(op)
        mult = {ce: (16 if ce in self.dma_domains else 1) for ce in self.eng_ops}
        tmap = {}
        nc._tag_map = tmap
        block = self.ctx.enter_context(nc.Block())

        def make_body(ops):
            def body(h):
                for op in ops:
                    for (ce, k) in op.waits:
                        h.wait_ge(sems[ce], base_val[ce] + val_of[(ce, k)] * mult[ce])
                    inst = op.build(h)
                    try:
                        tmap[inst.ins.name] = op.tag or "?"
                    except Exception:
                        pass
                    if op.needs_inc:
                        inst.then_inc(sems[op.clockeng], mult[op.clockeng])
            return body

        for eng_name, ops in per_eng.items():
            if not ops:
                continue
            getattr(block, ENG_ATTR[eng_name])(make_body(ops))
        if emit_stats:
            n_w = sum(len(o.waits) for o in self.ops)
            n_i = sum(1 for o in self.ops if o.needs_inc)
            print(f"[fw] dsems={len(phys)} ", end="")
            print(f"[fw] ops={len(self.ops)} waits={n_w} incs={n_i} "
                  f"fallbacks={self.n_false_fallback} "
                  f"per_eng={{{', '.join(f'{e}:{len(v)}' for e, v in per_eng.items() if v)}}}",
                  flush=True)
        self.ctx.close()



def mm1(fw, psum_ap, l, r, first, last, tag=""):
    return fw.comp("pe",
                   lambda h: h.matmul(psum_ap, l, r, start=first, stop=last),
                   reads=[l, r], writes=[psum_ap], tag=tag)


def mm(fw, psum_ap, pairs, tag=""):
    n = len(pairs)
    for i, (l, r) in enumerate(pairs):
        mm1(fw, psum_ap, l, r, i == 0, i == n - 1, tag=tag)


def act(fw, out, in_, func, bias=0.0, scale=1.0, tag=""):
    reads = [in_] + [b for b in (bias, scale) if hasattr(b, "tensor")]
    return fw.comp("act", lambda h: h.activation(out=out, in_=in_, func=func, bias=bias, scale=scale),
                   reads=reads, writes=[out], tag=tag)


def tt(fw, eng, out, in0, in1, op, tag=""):
    return fw.comp(eng, lambda h: h.tensor_tensor(out=out, in0=in0, in1=in1, op=op),
                   reads=[in0, in1], writes=[out], tag=tag)


def ts(fw, eng, out, in0, s1, op0, s2=None, op1=None, tag=""):
    reads = [in0] + [s for s in (s1, s2) if hasattr(s, "tensor")]
    if op1 is None:
        return fw.comp(eng, lambda h: h.tensor_single_scalar(out=out, in_=in0, scalar=s1, op=op0),
                       reads=reads, writes=[out], tag=tag)
    return fw.comp(eng, lambda h: h.tensor_scalar(out=out, in0=in0, scalar1=s1, scalar2=s2,
                                                  op0=op0, op1=op1),
                   reads=reads, writes=[out], tag=tag)


def stt(fw, eng, out, in0, s, in1, op0, op1, tag=""):
    reads = [in0, in1] + ([s] if hasattr(s, "tensor") else [])
    return fw.comp(eng, lambda h: h.scalar_tensor_tensor(out=out, in0=in0, scalar=s, in1=in1,
                                                         op0=op0, op1=op1),
                   reads=reads, writes=[out], tag=tag)


def copy(fw, eng, out, in_, tag=""):
    if eng == "act":
        return fw.comp("act", lambda h: h.copy(out=out, in_=in_), reads=[in_], writes=[out], tag=tag)
    return fw.comp(eng, lambda h: h.tensor_copy(out=out, in_=in_), reads=[in_], writes=[out], tag=tag)


def scan(fw, eng, out, d0, d1, initial, tag=""):
    reads = [d0, d1] + ([initial] if hasattr(initial, "tensor") else [])
    return fw.comp(eng, lambda h: h.tensor_tensor_scan(out=out, data0=d0, data1=d1, initial=initial,
                                                       op0=ALU.mult, op1=ALU.add),
                   reads=reads, writes=[out], tag=tag)


def reduce(fw, eng, out, in_, op=ALU.add, axis=AX.X, negate=None, tag=""):
    return fw.comp(eng, lambda h: h.tensor_reduce(out=out, in_=in_, axis=axis, op=op, negate=negate),
                   reads=[in_], writes=[out], tag=tag)


def recip(fw, out, in_, tag=""):
    return fw.comp("dve", lambda h: h.reciprocal(out=out, in_=in_), reads=[in_], writes=[out], tag=tag)


def memset(fw, eng, ap_, v, tag=""):
    return fw.comp(eng, lambda h: h.memset(ap_, v), reads=[], writes=[ap_], tag=tag)


def transpose_pe(fw, out_psum, in_, identity, tag=""):
    return fw.comp("pe", lambda h: h.transpose(out_psum, in_, identity),
                   reads=[in_, identity], writes=[out_psum], tag=tag)


class PsumPool:
    def __init__(self, fw, n=6, prefix="pp"):
        self.tiles = [fw.ps(f"{prefix}{i}", [128, 512], F32) for i in range(n)]
        self.i = 0

    def take(self):
        t = self.tiles[self.i % len(self.tiles)]
        self.i += 1
        return t


class Consts:
    def __init__(self, fw):
        self.fw = fw
        self.arrays = {}
        self.handles = {}
        self.scalar_vals = []
        self.scalar_sb = None

    def scalar(self, val, p=128):
        if self.scalar_sb is None:
            self.scalar_sb = self.fw.sb("cs_scalars", [128, 16], F32)
            d_in = self.fw.inp("c_scalars", [128, 16], F32)
            self.fw.dma("sp", self.scalar_sb[:], d_in)
        if val not in self.scalar_vals:
            assert len(self.scalar_vals) < 16
            self.scalar_vals.append(val)
            arr = np.zeros((128, 16), np.float32)
            arr[:, :len(self.scalar_vals)] = np.asarray(self.scalar_vals, np.float32)
            self.arrays["c_scalars"] = arr
        i = self.scalar_vals.index(val)
        return self.scalar_sb[0:p, i:i + 1]

    def add(self, name, arr, dtype=None, qeng="sp"):
        if name in self.handles:
            return self.handles[name]
        fw = self.fw
        arr = np.asarray(arr)
        dt = dtype or (BF16 if arr.dtype == BF else F32)
        d_in = fw.inp("c_" + name, list(arr.shape), F32 if arr.dtype == np.float32 else dt)
        sb_h = fw.sb("cs_" + name, list(arr.shape), dt)
        fw.dma("pool" if (dt == BF16 and arr.dtype == np.float32) else qeng, sb_h[:], d_in)
        self.arrays["c_" + name] = arr.astype(np.float32) if arr.dtype != BF else arr
        self.handles[name] = sb_h
        return sb_h


def mk_ln_tmp(fw, pre="t"):
    sq = fw.sb(pre + "_sq", [128, 512], F32)
    m = fw.sb(pre + "_ra", [1, 512], F32)
    q = fw.sb(pre + "_qq", [1, 512], F32)
    v = fw.sb(pre + "_vv", [1, 512], F32)
    r = fw.sb(pre + "_rb", [1, 512], F32)
    b = fw.sb(pre + "_bb", [1, 512], F32)
    rr = fw.sb(pre + "_rr", [128, 512], F32)
    br = fw.sb(pre + "_br", [128, 512], F32)
    return (sq, m[:], q[:], v[:], r[:], b[:], rr, br)


def ln_block(fw, P, cst, xtiles, src_off, W, out_tiles, dst_off, tmp, nd=1024, eps=1e-5,
             out_dtype_note="bf16"):
    onecol = cst.handles["onecol"]
    onerow = cst.handles["onerow"]
    sq, row_m, row_q, row_v, row_r, row_b, rrep, brep = tmp
    nk = len(xtiles)
    ps_s = P.take()
    ps_q = P.take()
    for i, x in enumerate(xtiles):
        mm1(fw, ps_s[0:1, :W], onecol[:, 0:1], x[:, src_off:src_off + W], i == 0, i == nk - 1, tag="ln_s")
    for i, x in enumerate(xtiles):
        act(fw, sq[:, :W], x[:, src_off:src_off + W], AF.Square, tag="ln_sq")
        mm1(fw, ps_q[0:1, :W], onecol[:, 0:1], sq[:, :W], i == 0, i == nk - 1, tag="ln_q")
    ts(fw, "dve", row_m[0:1, :W], ps_s[0:1, :W], 1.0 / nd, ALU.mult, tag="ln_m")
    ts(fw, "dve", row_q[0:1, :W], ps_q[0:1, :W], 1.0 / nd, ALU.mult, tag="ln_q2")
    stt(fw, "dve", row_v[0:1, :W], row_m[0:1, :W], -1.0, row_m[0:1, :W], ALU.mult, ALU.mult, tag="ln_mm")
    tt(fw, "dve", row_v[0:1, :W], row_q[0:1, :W], row_v[0:1, :W], ALU.add, tag="ln_v")
    act(fw, row_r[0:1, :W], row_v[0:1, :W], AF.Ln, bias=cst.scalar(eps, 1), tag="ln_ln")
    act(fw, row_r[0:1, :W], row_r[0:1, :W], AF.Exp, scale=cst.scalar(-0.5, 1), tag="ln_rs")
    stt(fw, "dve", row_b[0:1, :W], row_m[0:1, :W], -1.0, row_r[0:1, :W], ALU.mult, ALU.mult, tag="ln_b")
    ps_r = P.take()
    ps_b = P.take()
    mm1(fw, ps_r[:, :W], onerow[0:1, :], row_r[0:1, :W], True, True, tag="ln_repr")
    mm1(fw, ps_b[:, :W], onerow[0:1, :], row_b[0:1, :W], True, True, tag="ln_repb")
    copy(fw, "act", rrep[:, :W], ps_r[:, :W], tag="ln_er")
    copy(fw, "act", brep[:, :W], ps_b[:, :W], tag="ln_eb")
    for x, o in zip(xtiles, out_tiles):
        tt(fw, "dve", sq[:, :W], x[:, src_off:src_off + W], rrep[:, :W], ALU.mult, tag="ln_xr")
        tt(fw, "dve", o[:, dst_off:dst_off + W], sq[:, :W], brep[:, :W], ALU.add, tag="ln_xn")



CONST_SETS = {
    "l0a": {"onecol", "onerow"},
    "l0b_carry": {"sel16"},
    "l0b_full": {"sel16"},
    "l0f": {"onecol", "onerow", "idt", "sel8"},
    "l1": {"onecol", "onecol_bf", "onerow", "idt", "iotaq", "sel8"},
    "l2": {"onecol", "onerow", "ones_row512", "idt", "sel8", "perm"},
    "l3": {"onecol", "onerow", "ones_row512", "idt", "sel6", "perm3"},
}


def setup_consts(fw, cst, only=None):
    def add(name, arr, **kw):
        if only is None or name in only:
            cst.add(name, arr, **kw)
    add("onecol", np.ones((128, 1), np.float32))
    add("onecol_bf", np.ones((128, 1), np.float32), dtype=BF16)
    add("onerow", np.ones((96, 128), np.float32))
    add("ones_row512", np.ones((1, 512), np.float32), dtype=BF16)
    add("idt", np.eye(128, dtype=np.float32))
    add("iotaq", np.broadcast_to(np.arange(512, dtype=np.float32), (128, 512)).copy())
    sel16 = np.zeros((16, 16 * 128), np.float32)
    for n in range(16):
        sel16[n, n * 128:(n + 1) * 128] = 1.0
    add("sel16", sel16, dtype=BF16)
    sel8 = np.zeros((8, 8 * 128), np.float32)
    for e in range(8):
        sel8[e, e * 128:(e + 1) * 128] = 1.0
    add("sel8", sel8, dtype=BF16)
    sel6 = np.zeros((6, 6 * 128), np.float32)
    for e in range(6):
        sel6[e, e * 128:(e + 1) * 128] = 1.0
    add("sel6", sel6, dtype=BF16)
    perm = np.zeros((64, 64), np.float32)
    for i in range(32):
        perm[2 * i + 1, i] = -1.0
    for i in range(32, 64):
        perm[2 * (i - 32), i] = 1.0
    add("perm", perm, dtype=BF16)
    perm3 = np.zeros((64, 64), np.float32)
    for i in range(32):
        perm3[i + 32, i] = -1.0
        perm3[i, i + 32] = 1.0
    add("perm3", perm3, dtype=BF16)
    for v in (1e-5, -0.5, 1.0 / 32.0, 1e-6, 1.0 / 64.0):
        cst.scalar(v, 1)
    fw.onerow = cst.handles.get("onerow")
    if fw.aux_ps is None:
        fw.aux_ps = fw.ps("aux_ps", [1, 512], F32)


IO_SPECS = {
    "l0a": dict(
        ins=dict(xh=([NK, 128, TH], F32), w_in=([D, 4096], F32), convw=([128, CT * 4], F32),
                 convb=([128, CT], F32), wxp=([2048, 96], F32), wdt=([64, 2048], F32),
                 bdt=([128, CT], F32)),
        outs=dict(xn_o=([NK, 128, T], BF16), u_o=([CT, 128, T], BF16),
                  z_o=([CT, 128, T], BF16), delta_o=([CT, 128, T], F32),
                  bmcm_o=([32, T], F32))),
    "l0b_carry": dict(
        ins=dict(delta_i=([CT, 128, T], F32), u_i=([CT, 128, T], BF16),
                 bmcm_i=([32, T], F32), alog=([128, CT * NSTATE], F32)),
        outs=dict(carry_o=([128, CT * NSTATE], F32))),
    "l0b_full": dict(
        ins=dict(delta_i=([CT, 128, T], F32), u_i=([CT, 128, T], BF16),
                 bmcm_i=([32, T], F32), alog=([128, CT * NSTATE], F32),
                 carry_i=([128, CT * NSTATE], F32), z_i=([CT, 128, T], BF16),
                 dskip=([128, CT], F32), wout=([2048, D], F32)),
        outs=dict(s_o=([NK, 128, T], F32))),
    "l0f": dict(
        ins=dict(xn_i=([NK, 128, T], BF16), x_i=([NK, 128, T], F32), s_i=([NK, 128, T], F32),
                 wgred_t=([D, 16], F32), wg_t=([16, 8], F32), w1=([8, D, 4096], F32),
                 b1_t=([128, 8 * 32], F32), w2=([8, 4096, D], F32), b2_t=([128, 8 * 8], F32)),
        outs=dict(x1_o=([NK, 128, T], F32))),
    "l1": dict(
        ins=dict(xf_i=([NK, 128, TF], F32), wq_t=([D, D], F32), wk_t=([D, D], F32),
                 wv_t=([D, D], F32), wo_t=([D, D], F32), colq=([128, 4 * (TF // 128)], F32),
                 gw=([D, 8], F32), gb=([8, 1], F32), ew=([8, D, D], F32), eb=([8, D], F32)),
        outs=dict(x2_o=([NK, 128, T], F32))),
    "l2": dict(
        ins=dict(xf_i=([NK, 128, TF], F32), wq_t=([D, D], F32), bq_c=([128, 8], F32),
                 wk_t=([D, 256], F32), bk_c=([128, 2], F32), wv_t=([D, 256], F32),
                 bv_r=([1, 256], F32), wo_t=([D, D], F32), bo_r=([1, D], F32),
                 cosq=([64, T], F32), sinq=([64, T], F32), cosk=([64, TF], F32),
                 sink=([64, TF], F32), wr_t=([D, 8], F32), br_c=([8, 1], F32),
                 we_t=([8, D, D], F32), be_i=([8, D], F32), fwo_t=([D, D], F32),
                 fbo_r=([1, D], F32)),
        outs=dict(x3_o=([NK, 128, T], F32))),
    "l3": dict(
        ins=dict(xf_i=([NK, 128, TF], F32), wqkv=([D, 3 * D], F32), qs_c=([64, 1], F32),
                 ks_c=([64, 1], F32), wp_t=([D, D], F32), bp_r=([1, D], F32),
                 cosq=([64, T], F32), sinq=([64, T], F32), cosk=([64, TF], F32),
                 sink=([64, TF], F32), bw_t=([D, D], F32), bb_r=([1, D], F32),
                 shA_t=([D, 16], F32), shB_t=([16, D], F32), rA_t=([6, D, 16], F32),
                 rB_t=([6, 16, D], F32), rw_t=([D, 6], F32), rb_c=([6, 1], F32)),
        outs=dict(x4_o=([NK, 128, T], F32))),
}

EMIT_OF = {"l0a": "emit_l0a", "l0b_carry": "emit_l0b", "l0b_full": "emit_l0b",
           "l0f": "emit_l0f", "l1": "emit_l1", "l2": "emit_l2", "l3": "emit_l3"}


def _mk_builder(key):
    def build():
        nc = bass.Bass("TRN2", target_bir_lowering=False)
        fw = Fw(nc)
        cst = Consts(fw)
        setup_consts(fw, cst, CONST_SETS.get(key))
        P = PsumPool(fw, 7)
        spec = IO_SPECS[key]
        IO = {}
        for nm, (shape, dt) in spec["ins"].items():
            IO[nm] = fw.inp(nm, shape, dt)
        for nm, (shape, dt) in spec["outs"].items():
            IO[nm] = fw.outp(nm, shape, dt)
        fn = globals()[EMIT_OF[key]]
        if key.startswith("l0b"):
            fn(fw, cst, P, IO, mode=key.split("_")[1])
        else:
            fn(fw, cst, P, IO)
        fw.finish_outputs()
        fw.finalize()
        return nc, cst
    return build


def emit_l0a(fw, cst, P, IO):

    xh = IO["xh"]
    w_in = IO["w_in"]
    convw = IO["convw"]
    convb = IO["convb"]
    wxp = IO["wxp"]
    wdt = IO["wdt"]
    bdt = IO["bdt"]

    xn_o = IO["xn_o"]
    u_o = IO["u_o"]
    z_o = IO["z_o"]
    delta_o = IO["delta_o"]
    bmcm_o = IO["bmcm_o"]


    win_sb = [fw.sb(f"win{k}", [128, 4096], BF16) for k in range(NK)]
    for k in range(NK):
        fw.dma("pool", win_sb[k][:], w_in[k * 128:(k + 1) * 128, :], tag="ld_win")
    wxp_sb = fw.sb("s_wxp", [128, CT * 96], BF16)
    fw.dma("pool", wxp_sb[:].rearrange("p (c r) -> p c r", c=CT), wxp.rearrange("(c p) r -> p c r", p=128), tag="ld_wxp")
    wdt_sb = fw.sb("s_wdt", [64, 2048], BF16)
    fw.dma("pool", wdt_sb[:], wdt, tag="ld_wdt")
    convw_sb = fw.sb("s_convw", [128, CT * 4], F32)
    fw.dma("sp", convw_sb[:], convw)
    convb_sb = fw.sb("s_convb", [128, CT], F32)
    fw.dma("sp", convb_sb[:], convb)
    bdt_sb = fw.sb("s_bdt", [128, CT], F32)
    fw.dma("sp", bdt_sb[:], bdt)

    xblk = [fw.sb(f"xb{k}", [128, 512], F32) for k in range(NK)]
    xn_sb = [fw.sb(f"xn{k}", [128, TH], BF16) for k in range(NK)]
    ln_tmp = mk_ln_tmp(fw, "t")
    xin_buf = fw.sb("xin", [128, TH], F32)
    cacc_a = fw.sb("cacc_a", [128, T], F32)
    cacc_b = fw.sb("cacc_b", [128, T], F32)
    u_ct = fw.sb("u_ct", [128, T], BF16)
    z_ct = fw.sb("z_ct", [128, T], BF16)
    xdbl = fw.sb("xdbl", [96, T], F32)
    dtlr = fw.sb("dtlr", [64, T], BF16)
    delta_tmp = fw.sb("delta_tmp", [128, T], F32)
    d_abs = fw.sb("d_abs", [128, 512], F32)
    d_m = fw.sb("d_m", [128, 512], F32)

    for (off, W) in BLKS_H:
        for k in range(NK):
            fw.dma("sp", xblk[k][:], xh[k, :, off:off + W], tag="ld_x")
        ln_block(fw, P, cst, xblk, 0, W, xn_sb, off, ln_tmp)

    for k in range(NK):
        fw.dma("sp", xn_o[k], xn_sb[k][:, 512:TH], tag="st_xn")

    for ct in range(CT):
        for (off, W) in BLKS_H:
            p = P.take()
            mm(fw, p[:, :W],
               [(win_sb[k][:, ct * 128:(ct + 1) * 128], xn_sb[k][:, off:off + W]) for k in range(NK)],
               tag="mm_xin")
            copy(fw, "act", xin_buf[:, off:off + W], p[:, :W], tag="ev_xin")
        ts(fw, "dve", cacc_a[:], xin_buf[:, 509:509 + T], convw_sb[:, ct * 4:ct * 4 + 1],
           ALU.mult, tag="conv0")
        stt(fw, "dve", cacc_b[:], xin_buf[:, 510:510 + T], convw_sb[:, ct * 4 + 1:ct * 4 + 2],
            cacc_a[:], ALU.mult, ALU.add, tag="conv1")
        stt(fw, "dve", cacc_a[:], xin_buf[:, 511:511 + T], convw_sb[:, ct * 4 + 2:ct * 4 + 3],
            cacc_b[:], ALU.mult, ALU.add, tag="conv2")
        stt(fw, "dve", cacc_b[:], xin_buf[:, 512:512 + T], convw_sb[:, ct * 4 + 3:ct * 4 + 4],
            cacc_a[:], ALU.mult, ALU.add, tag="conv3")
        act(fw, u_ct[:], cacc_b[:], AF.Silu, bias=convb_sb[:, ct:ct + 1], tag="silu_u")
        fw.dma("sp", u_o[ct], u_ct[:], tag="st_u")
        for (off, W) in BLKS:
            p = P.take()
            mm(fw, p[:, :W],
               [(win_sb[k][:, 2048 + ct * 128:2048 + (ct + 1) * 128],
                 xn_sb[k][:, 512 + off:512 + off + W]) for k in range(NK)],
               tag="mm_z")
            copy(fw, "act", z_ct[:, off:off + W], p[:, :W], tag="ev_z")
        fw.dma("sp", z_o[ct], z_ct[:], tag="st_z")
        for (off, W) in BLKS:
            p = P.take()
            mm1(fw, p[0:96, :W], wxp_sb[:, ct * 96:(ct + 1) * 96], u_ct[:, off:off + W],
                True, True, tag="mm_xdbl")
            if ct == 0:
                copy(fw, "dve", xdbl[:, off:off + W], p[0:96, :W], tag="xdbl_init")
            else:
                tt(fw, "dve", xdbl[:, off:off + W], xdbl[:, off:off + W], p[0:96, :W],
                   ALU.add, tag="xdbl_acc")

    copy(fw, "dve", dtlr[:], xdbl[0:64, :], tag="dtlr")
    fw.dma("sp", bmcm_o, xdbl[64:96, :], tag="st_bmcm")

    for ct in range(CT):
        for (off, W) in BLKS:
            p = P.take()
            mm1(fw, p[:, :W], wdt_sb[:, ct * 128:(ct + 1) * 128], dtlr[:, off:off + W],
                True, True, tag="mm_dt")
            act(fw, d_abs[:, :W], p[:, :W], AF.Abs, bias=bdt_sb[:, ct:ct + 1], tag="dt_abs")
            act(fw, d_abs[:, :W], d_abs[:, :W], AF.Exp, scale=-1.0, tag="dt_exp")
            act(fw, d_abs[:, :W], d_abs[:, :W], AF.Ln, bias=1.0, tag="dt_ln")
            act(fw, d_m[:, :W], p[:, :W], AF.Relu, bias=bdt_sb[:, ct:ct + 1], tag="dt_relu")
            tt(fw, "dve", delta_tmp[:, off:off + W], d_abs[:, :W], d_m[:, :W], ALU.add, tag="dt_add")
        fw.dma("sp", delta_o[ct], delta_tmp[:], tag="st_delta")





def emit_l0b(fw, cst, P, IO, mode):
    full = (mode == "full")

    delta_i = IO["delta_i"]
    u_i = IO["u_i"]
    bmcm_i = IO["bmcm_i"]
    alog = IO["alog"]
    if full:
        carry_i = IO["carry_i"]
        z_i = IO["z_i"]
        dskip = IO["dskip"]
        wout = IO["wout"]
        s_o = IO["s_o"]
    else:
        carry_o = IO["carry_o"]

    sel_sb = cst.handles["sel16"]

    bm_sb = fw.sb("s_bm", [16, T], BF16)
    fw.dma("pool", bm_sb[:], bmcm_i[0:16, :])
    cm_sb = fw.sb("s_cm", [16, T], BF16)
    fw.dma("pool", cm_sb[:], bmcm_i[16:32, :])
    alog_sb = fw.sb("s_alog", [128, CT * NSTATE], F32)
    fw.dma("sp", alog_sb[:], alog)
    A_sb = fw.sb("s_A", [128, CT * NSTATE], F32)
    act(fw, A_sb[:], alog_sb[:], AF.Exp, tag="A_exp")
    ts(fw, "dve", A_sb[:], A_sb[:], -1.0, ALU.mult, tag="A_neg")

    hcarry = fw.sb("hcarry", [128, CT * NSTATE], F32)
    if full:
        fw.dma("sp", hcarry[:], carry_i)
        dskip_sb = fw.sb("s_dskip", [128, CT], F32)
        fw.dma("sp", dskip_sb[:], dskip)
    else:
        memset(fw, "dve", hcarry[:], 0.0)

    NG = 2
    GW = 8 * 512
    brep = fw.sb("brep", [128, NSTATE * 512], BF16)
    crep = fw.sb("crep", [128, NSTATE * 512], BF16) if full else None
    dAb = fw.sb("dAb", [128, GW], F32)
    dBb = fw.sb("dBb", [128, GW], F32)
    hb = fw.sb("hb", [128, GW], F32)
    du = fw.sb("du", [128, 512], F32)
    delta_ct = fw.sb("delta_ct", [128, 512], F32)
    u_blk = fw.sb("u_blk", [128, 512], BF16)
    if full:
        y_all = [fw.sb(f"y{ct}", [128, T], BF16) for ct in range(CT)]
        y_blk = fw.sb("y_blk", [128, 512], F32)
        y_tmp = fw.sb("y_tmp", [128, 512], F32)
        z_blk = fw.sb("z_blk", [128, 512], BF16)
        zs_blk = fw.sb("zs_blk", [128, 512], F32)

    for (off, W) in BLKS:
        for n in range(NSTATE):
            p = P.take()
            mm1(fw, p[:, :W], sel_sb[:, n * 128:(n + 1) * 128], bm_sb[:, off:off + W],
                True, True, tag="rep_b")
            copy(fw, "act", brep[:, n * 512:n * 512 + W], p[:, :W], tag="ev_repb")
            if full:
                p2 = P.take()
                mm1(fw, p2[:, :W], sel_sb[:, n * 128:(n + 1) * 128],
                    cm_sb[:, off:off + W], True, True, tag="rep_c")
                copy(fw, "act", crep[:, n * 512:n * 512 + W], p2[:, :W], tag="ev_repc")
        for ct in range(CT):
            fw.dma("sp", delta_ct[:, :W], delta_i[ct, :, off:off + W], tag="ld_d")
            fw.dma("sp", u_blk[:, :W], u_i[ct, :, off:off + W], tag="ld_u")
            tt(fw, "dve", du[:, :W], delta_ct[:, :W], u_blk[:, :W], ALU.mult, tag="du")
            for g in range(NG):
                for j in range(8):
                    n = g * 8 + j
                    act(fw, dAb[:, j * 512:j * 512 + W], delta_ct[:, :W], AF.Exp,
                        scale=A_sb[:, ct * NSTATE + n:ct * NSTATE + n + 1], tag="dA")
                    tt(fw, "dve", dBb[:, j * 512:j * 512 + W], du[:, :W],
                       brep[:, n * 512:n * 512 + W], ALU.mult, tag="dBu")
                    scan(fw, "dve", hb[:, j * 512:j * 512 + W], dAb[:, j * 512:j * 512 + W],
                         dBb[:, j * 512:j * 512 + W],
                         hcarry[:, ct * NSTATE + n:ct * NSTATE + n + 1], tag="scan")
                hlast = hb[:].rearrange("p (n w) -> p n w", n=8)[:, :, W - 1:W]
                copy(fw, "dve", hcarry[:, ct * NSTATE + g * 8:ct * NSTATE + g * 8 + 8],
                     hlast, tag="carry")
                if full:
                    tt(fw, "pool", dBb[:, :8 * W], hb[:, :8 * W],
                       crep[:, g * GW:g * GW + 8 * W], ALU.mult, tag="hC")
                    red_in = dBb[:].rearrange("p (n w) -> p w n", n=8)[:, :W, :]
                    if g == 0:
                        reduce(fw, "dve", y_blk[:, :W], red_in, tag="yred")
                    else:
                        reduce(fw, "dve", y_tmp[:, :W], red_in, tag="yred2")
                        tt(fw, "pool", y_blk[:, :W], y_blk[:, :W], y_tmp[:, :W], ALU.add, tag="yacc")
            if full:
                fw.dma("sp", z_blk[:, :W], z_i[ct, :, off:off + W], tag="ld_z")
                act(fw, zs_blk[:, :W], z_blk[:, :W], AF.Silu, tag="silu_z")
                stt(fw, "dve", y_blk[:, :W], u_blk[:, :W], dskip_sb[:, ct:ct + 1], y_blk[:, :W],
                    ALU.mult, ALU.add, tag="yD")
                tt(fw, "dve", y_all[ct][:, off:off + W], y_blk[:, :W], zs_blk[:, :W],
                   ALU.mult, tag="ygate")

    if not full:
        fw.dma("sp", carry_o, hcarry[:], tag="st_carry")
    else:
        wv = [dAb[:].bitcast(BF16), dBb[:].bitcast(BF16)]

        def wtile(ct):
            v = wv[ct // 8]
            return v[:, (ct % 8) * D:(ct % 8 + 1) * D]
        s_out_buf = fw.sb("s_out", [128, T], F32)
        for ct in range(CT):
            fw.dma("pool", wtile(ct), wout[ct * 128:(ct + 1) * 128, :], tag="ld_wout")
        for ot in range(NK):
            for (off, W) in BLKS:
                p = P.take()
                mm(fw, p[:, :W],
                   [(wtile(ct)[:, ot * 128:(ot + 1) * 128], y_all[ct][:, off:off + W])
                    for ct in range(CT)], tag="mm_s")
                copy(fw, "act", s_out_buf[:, off:off + W], p[:, :W], tag="ev_s")
            fw.dma("sp", s_o[ot], s_out_buf[:], tag="st_s")





def emit_l0f(fw, cst, P, IO):

    xn_i = IO["xn_i"]
    x_i = IO["x_i"]
    s_i = IO["s_i"]
    wgred_t = IO["wgred_t"]
    wg_t = IO["wg_t"]
    w1 = IO["w1"]
    b1_t = IO["b1_t"]
    w2 = IO["w2"]
    b2_t = IO["b2_t"]
    x1_o = IO["x1_o"]

    onecol = cst.handles["onecol"]
    onerow = cst.handles["onerow"]
    idt = cst.handles["idt"]
    sel_sb = cst.handles["sel8"]

    xn_sb = [fw.sb(f"s_xn{k}", [128, T], BF16) for k in range(NK)]
    for k in range(NK):
        fw.dma("sp", xn_sb[k][:], xn_i[k], tag="ld_xn")
    b1_sb = fw.sb("s_b1", [128, 8 * 32], F32)
    fw.dma("sp", b1_sb[:], b1_t)
    b2_sb = fw.sb("s_b2", [128, 8 * 8], F32)
    fw.dma("sp", b2_sb[:], b2_t)

    wgred_sb = fw.sb("s_wgred", [128, NK * 16], BF16)
    fw.dma("pool", wgred_sb[:].rearrange("p (k g) -> p k g", k=NK),
           wgred_t.rearrange("(k p) g -> p k g", p=128), tag="ld_wgred")
    wgt_sb = fw.sb("s_wgt", [16, 8], F32)
    fw.dma("sp", wgt_sb[:], wg_t)
    xr_sb = fw.sb("s_xr", [16, T], BF16)
    for (off, W) in BLKS:
        p = P.take()
        mm(fw, p[0:16, :W], [(wgred_sb[:, k * 16:(k + 1) * 16], xn_sb[k][:, off:off + W])
                             for k in range(NK)], tag="mm_xr")
        copy(fw, "act", xr_sb[:, off:off + W], p[0:16, :W], tag="ev_xr")
    wg_sq = fw.sb("wg_sq", [16, 8], F32)
    nrm = fw.sb("nrm", [1, 8], F32)
    nrm_rep = fw.sb("nrm_rep", [16, 8], F32)
    wgu_bf = fw.sb("wgu_bf", [16, 8], BF16)
    act(fw, wg_sq[:], wgt_sb[:], AF.Square, tag="wg_sq")
    p = P.take()
    mm1(fw, p[0:1, 0:8], onecol[0:16, 0:1], wg_sq[:], True, True, tag="wg_ss")
    act(fw, nrm[:], p[0:1, 0:8], AF.Sqrt, tag="wg_sqrt")
    ts(fw, "dve", nrm[:], nrm[:], 1e-4, ALU.max, tag="wg_max")
    recip(fw, nrm[:], nrm[:], tag="wg_rec")
    p = P.take()
    mm1(fw, p[0:16, 0:8], onerow[0:1, 0:16], nrm[:], True, True, tag="wg_rep")
    copy(fw, "act", nrm_rep[:], p[0:16, 0:8], tag="wg_evr")
    tt(fw, "dve", wgu_bf[:], wgt_sb[:], nrm_rep[:], ALU.mult, tag="wg_div")
    lg_sb = fw.sb("lg", [8, T], F32)
    for (off, W) in BLKS:
        p = P.take()
        mm1(fw, p[0:8, :W], wgu_bf[:], xr_sb[:, off:off + W], True, True, tag="mm_lg")
        copy(fw, "act", lg_sb[:, off:off + W], p[0:8, :W], tag="ev_lg")
    ms_fm = fw.sb("ms_fm", [8, T], BF16)
    tm_g = fw.sb("tm_g", [128, 8], F32)
    tm_e = fw.sb("tm_e", [128, 8], F32)
    tm_r = fw.sb("tm_r", [128, 1], F32)
    for i in range(T // 128):
        p = P.take()
        transpose_pe(fw, p[:, 0:8], lg_sb[:, i * 128:(i + 1) * 128], idt[0:8, 0:8], tag="tr_lg")
        copy(fw, "dve", tm_g[:], p[:, 0:8], tag="ev_tg")
        reduce(fw, "dve", tm_r[:], tm_g[:], op=ALU.max, negate=True, tag="g_max")
        act(fw, tm_e[:], tm_g[:], AF.Exp, bias=tm_r[:, 0:1], tag="g_exp")
        reduce(fw, "dve", tm_r[:], tm_e[:], op=ALU.add, tag="g_sum")
        recip(fw, tm_r[:], tm_r[:], tag="g_rec")
        ts(fw, "dve", tm_e[:], tm_e[:], tm_r[:, 0:1], ALU.mult, tag="g_nrm")
        reduce(fw, "dve", tm_r[:], tm_e[:], op=ALU.max, tag="g_m")
        ts(fw, "dve", tm_g[:], tm_e[:], tm_r[:, 0:1], ALU.is_equal, tag="g_msk")
        ts(fw, "dve", tm_g[:], tm_g[:], tm_r[:, 0:1], ALU.mult, tag="g_msc")
        p = P.take()
        transpose_pe(fw, p[0:8, 0:128], tm_g[:], idt[:], tag="tr_ms")
        copy(fw, "dve", ms_fm[:, i * 128:(i + 1) * 128], p[0:8, 0:128], tag="ev_ms")

    TG = 512
    msrep = fw.sb("msrep", [128, T], BF16)
    h_buf = [fw.sb(f"hb{s}", [128, 32 * 512], BF16) for s in range(2)]
    out_acc = [fw.sb(f"oacc{o}", [128, T], BF16) for o in range(NK)]
    w1buf = [fw.sb(f"w1b{i}", [128, NK * 128], BF16) for i in range(3)]
    w2buf = [fw.sb(f"w2b{i}", [128, 32 * 128], BF16) for i in range(2)]
    t_eo = [fw.sb(f"t_eo{s}", [128, 512], BF16) for s in range(2)]
    wslot = [0]
    w2slot = [0]

    for e in range(8):
        for (off, W) in BLKS:
            p = P.take()
            mm1(fw, p[:, :W], sel_sb[:, e * 128:(e + 1) * 128], ms_fm[:, off:off + W],
                True, True, tag="rep_ms")
            copy(fw, "act", msrep[:, off:off + W], p[:, :W], tag="ev_msr")
        for g in range(T // TG):
            g0 = g * TG
            hb = h_buf[g % 2]
            for ht in range(32):
                wb = w1buf[wslot[0] % 3]; wslot[0] += 1
                fw.dma("pool", wb[:].rearrange("p (k h) -> p k h", k=NK),
                       w1.rearrange("e (k p) h -> e p k h", p=128)[e, :, :, ht * 128:(ht + 1) * 128],
                       tag="ld_w1")
                p = P.take()
                mm(fw, p[:],
                   [(wb[:, k * 128:(k + 1) * 128], xn_sb[k][:, g0:g0 + 512]) for k in range(NK)],
                   tag="mm_h")
                act(fw, hb[:, ht * 512:(ht + 1) * 512], p[:], AF.Gelu,
                    bias=b1_sb[:, e * 32 + ht:e * 32 + ht + 1], tag="gelu")
            for ot in range(NK):
                wb = w2buf[w2slot[0] % 2]; w2slot[0] += 1
                fw.dma("pool", wb[:].rearrange("p (c o) -> p c o", c=32),
                       w2.rearrange("e (c p) o -> e p c o", p=128)[e, :, :, ot * 128:(ot + 1) * 128],
                       tag="ld_w2")
                p = P.take()
                mm(fw, p[:],
                   [(wb[:, c * 128:(c + 1) * 128], hb[:, c * 512:(c + 1) * 512])
                    for c in range(32)], tag="mm_o")
                if e == 0:
                    stt(fw, "dve", out_acc[ot][:, g0:g0 + 512], p[:],
                        b2_sb[:, ot:ot + 1], msrep[:, g0:g0 + 512],
                        ALU.add, ALU.mult, tag="eo0")
                else:
                    te = t_eo[ot % 2]
                    stt(fw, "dve", te[:], p[:],
                        b2_sb[:, e * 8 + ot:e * 8 + ot + 1], msrep[:, g0:g0 + 512],
                        ALU.add, ALU.mult, tag="eo")
                    tt(fw, "dve", out_acc[ot][:, g0:g0 + 512], out_acc[ot][:, g0:g0 + 512],
                       te[:], ALU.add, tag="eacc")

    g_sl = fw.sb("g_sl", [128, 512], F32)
    x_sl = fw.sb("x_sl", [128, 512], F32)
    s_sl = fw.sb("s_sl", [128, 512], F32)
    x1_blk = fw.sb("x1_blk", [128, 512], F32)
    for ot in range(NK):
        for (off, W) in BLKS:
            act(fw, g_sl[:, :W], out_acc[ot][:, off:off + W], AF.Silu, tag="silu_g")
            fw.dma("sp", x_sl[:, :W], x_i[ot, :, off:off + W], tag="ld_x")
            fw.dma("sp", s_sl[:, :W], s_i[ot, :, off:off + W], tag="ld_s")
            tt(fw, "dve", g_sl[:, :W], s_sl[:, :W], g_sl[:, :W], ALU.mult, tag="sg")
            tt(fw, "dve", x1_blk[:, :W], x_sl[:, :W], g_sl[:, :W], ALU.add, tag="x1")
            fw.dma("sp", x1_o[ot, :, off:off + W], x1_blk[:, :W], tag="st_x1")




def emit_l1(fw, cst, P, IO):

    xf_i = IO["xf_i"]
    wq_t = IO["wq_t"]
    wk_t = IO["wk_t"]
    wv_t = IO["wv_t"]
    wo_t = IO["wo_t"]
    colq = IO["colq"]
    gw = IO["gw"]
    gb = IO["gb"]
    ew = IO["ew"]
    eb = IO["eb"]
    x2_o = IO["x2_o"]

    v_d = fw.dram("v_spill", [TF // 128, 128, D], BF16)
    p_d = fw.dram("p_spill", [4, 128, (TF // 128) * 512], BF16)
    s1_d = fw.dram("s1_spill", [NK, 128, T], F32)

    onecol = cst.handles["onecol"]
    onecol_bf = cst.handles["onecol_bf"]
    onerow = cst.handles["onerow"]
    iotaq = cst.handles["iotaq"]
    NKT = TF // 128

    ln_tmp = mk_ln_tmp(fw, "t")
    xblk = [fw.sb(f"xb{k}", [128, 512], F32) for k in range(NK)]
    xn_blk = [fw.sb(f"xnb{k}", [128, 512], BF16) for k in range(NK)]
    sqt = fw.sb("sqt", [128, 512], F32)
    colq_sb = fw.sb("s_colq", [128, 4 * NKT], F32)
    fw.dma("sp", colq_sb[:], colq)

    def ln_into(cols_off, W=512):
        for k in range(NK):
            fw.dma("sp", xblk[k][:, :W], xf_i[k, :, cols_off:cols_off + W], tag="ld_x")
        ln_block(fw, P, cst, xblk, 0, W, xn_blk, 0, ln_tmp)

    k2max = fw.sb("k2max", [1, 1], F32)
    rden = fw.sb("rden", [1, 4 * 512], F32)

    K_sb = [fw.sb(f"K{i}", [128, TF], BF16) for i in range(NK)]
    wkT = [fw.sb(f"s_wk{k}", [128, D], BF16) for k in range(NK)]
    wvT = [fw.sb(f"s_wv{k}", [128, D], BF16) for k in range(NK)]
    for k in range(NK):
        fw.dma("pool", wkT[k][:], wk_t[k * 128:(k + 1) * 128, :], tag="ld_wk")
        fw.dma("pool", wvT[k][:], wv_t[k * 128:(k + 1) * 128, :], tag="ld_wv")
    v_tile = fw.sb("v_tile", [128, D], BF16)
    k2psum = fw.aux_ps
    k2row = fw.sb("k2row", [1, TF], F32)
    for bi in range(TF // 512):
        off = bi * 512
        ln_into(off)
        for ot in range(NK):
            p = P.take()
            mm(fw, p[:], [(wkT[k][:, ot * 128:(ot + 1) * 128], xn_blk[k][:]) for k in range(NK)],
               tag="mm_k")
            copy(fw, "act", K_sb[ot][:, off:off + 512], p[:], tag="ev_k")
        for st_ in range(4):
            to = st_ * 128
            for half in range(2):
                p = P.take()
                mm(fw, p[:], [(xn_blk[k][:, to:to + 128], wvT[k][:, half * 512:(half + 1) * 512])
                              for k in range(NK)], tag="mm_v")
                copy(fw, "act", v_tile[:, half * 512:(half + 1) * 512], p[:], tag="ev_v")
            fw.dma("sp", v_d[(off + to) // 128], v_tile[:], tag="st_v")
    for bi in range(TF // 512):
        off = bi * 512
        for i in range(NK):
            act(fw, sqt[:], K_sb[i][:, off:off + 512], AF.Square, tag="k2sq")
            mm1(fw, k2psum[:], onecol[:, 0:1], sqt[:], i == 0, i == NK - 1, tag="k2mm")
        copy(fw, "dve", k2row[0:1, off:off + 512], k2psum[:], tag="k2ev")
    reduce(fw, "dve", k2max[:], k2row[0:1, :], op=ALU.max, tag="k2max")

    fw.barrier()
    fw.free_sb(*[f"s_wk{k}" for k in range(NK)], *[f"s_wv{k}" for k in range(NK)],
               "v_tile", "k2row")

    wqT = [fw.sb(f"s_wq{k}", [128, D], BF16) for k in range(NK)]
    for k in range(NK):
        fw.dma("pool", wqT[k][:], wq_t[k * 128:(k + 1) * 128, :], tag="ld_wq")
    q_blk = [fw.sb(f"qt{k}", [128, 512], BF16) for k in range(NK)]
    Psb = fw.sb("Psb", [128, NKT * 512], BF16)
    negc = fw.sb("negc", [1, 512], F32)
    q2 = fw.sb("q2", [1, 512], F32)
    etmp = fw.sb("etmp", [128, 512], BF16)
    mtmp = fw.sb("mtmp", [128, 512], BF16)
    den = fw.sb("den", [1, 4 * 512], F32)
    denp = fw.aux_ps
    for qb in range(4):
        qoff = qb * 512
        ln_into(qoff)
        for k in range(NK):
            p = P.take()
            mm(fw, p[:], [(wqT[i][:, k * 128:(k + 1) * 128], xn_blk[i][:]) for i in range(NK)],
               tag="mm_q")
            act(fw, q_blk[k][:], p[:], AF.Identity, scale=cst.scalar(1.0 / 32.0, 128), tag="ev_q")
        p = P.take()
        for k in range(NK):
            act(fw, sqt[:], q_blk[k][:], AF.Square, tag="q2sq")
            mm1(fw, p[0:1, :], onecol[:, 0:1], sqt[:], k == 0, k == NK - 1, tag="q2mm")
        copy(fw, "dve", q2[:], p[0:1, :], tag="q2ev")
        act(fw, negc[:], q2[:], AF.Sqrt, scale=k2max[0:1, 0:1], tag="csqrt")
        ts(fw, "dve", negc[:], negc[:], -1.0, ALU.mult, tag="cneg")
        for kt in range(NKT):
            p = P.take()
            for i in range(NK):
                mm1(fw, p[:], K_sb[i][:, kt * 128:(kt + 1) * 128], q_blk[i][:],
                    i == 0, False, tag="mm_s")
            mm1(fw, p[:], onerow[0:1, :], negc[0:1, :], False, True, tag="mm_c")
            act(fw, etmp[:], p[:], AF.Exp, tag="p_exp")
            ts(fw, "dve", mtmp[:], iotaq[:], colq_sb[:, (qb * NKT + kt):(qb * NKT + kt) + 1],
               ALU.is_ge, tag="p_msk")
            tt(fw, "dve", Psb[:, kt * 512:(kt + 1) * 512], etmp[:], mtmp[:], ALU.mult, tag="p_m")
            mm1(fw, denp[:], onecol_bf[:, 0:1], Psb[:, kt * 512:(kt + 1) * 512],
                kt == 0, kt == NKT - 1, tag="mm_den")
        copy(fw, "dve", den[0:1, qoff:qoff + 512], denp[:], tag="ev_den")
        fw.dma("sp", p_d[qb], Psb[:], tag="st_p")
    recip(fw, rden[:], den[:], tag="rden")

    fw.barrier()
    fw.free_sb(*[f"K{i}" for i in range(NK)], *[f"s_wq{k}" for k in range(NK)],
               *[f"qt{k}" for k in range(NK)], "negc", "q2", "den", "Psb", "etmp", "mtmp")

    V_sb = [fw.sb(f"V{i}", [128, D], BF16) for i in range(NKT)]
    for i in range(NKT):
        fw.dma("sp", V_sb[i][:], v_d[i], tag="ld_v")
    woT = [fw.sb(f"s_wo{k}", [128, D], BF16) for k in range(NK)]
    for k in range(NK):
        fw.dma("pool", woT[k][:], wo_t[k * 128:(k + 1) * 128, :], tag="ld_wo")
    attn = [fw.sb(f"at{k}", [128, 512], BF16) for k in range(NK)]
    rrep = fw.sb("rrep", [128, 512], F32)
    s1_blk = fw.sb("s1_blk", [128, 512], F32)
    Psb = fw.sb("Psb2", [128, NKT * 512], BF16)
    for qb in range(4):
        fw.dma("sp", Psb[:], p_d[qb], tag="ld_p")
        p = P.take()
        mm1(fw, p[:], onerow[0:1, :], rden[0:1, qb * 512:(qb + 1) * 512], True, True, tag="mm_rr")
        copy(fw, "act", rrep[:], p[:], tag="ev_rr")
        for dt in range(NK):
            p = P.take()
            mm(fw, p[:], [(V_sb[kt][:, dt * 128:(dt + 1) * 128], Psb[:, kt * 512:(kt + 1) * 512])
                          for kt in range(NKT)], tag="mm_pv")
            tt(fw, "dve", attn[dt][:], p[:], rrep[:], ALU.mult, tag="pv_nrm")
        for ot in range(NK):
            p = P.take()
            mm(fw, p[:], [(woT[k][:, ot * 128:(ot + 1) * 128], attn[k][:]) for k in range(NK)],
               tag="mm_wo")
            copy(fw, "act", s1_blk[:], p[:], tag="ev_s1")
            fw.dma("sp", s1_d[ot, :, qb * 512:(qb + 1) * 512], s1_blk[:], tag="st_s1")

    fw.barrier()
    fw.free_sb(*[f"V{i}" for i in range(NKT)], *[f"s_wo{k}" for k in range(NK)],
               *[f"at{k}" for k in range(NK)], "Psb2", "rrep", "s1_blk", "rden")

    idt = cst.handles["idt"]
    sel_sb = cst.handles["sel8"]
    gw_sb = fw.sb("s_gw", [128, NK * 8], BF16)
    fw.dma("pool", gw_sb[:].rearrange("p (k g) -> p k g", k=NK),
           gw.rearrange("(k p) g -> p k g", p=128), tag="ld_gw")
    gb_sb = fw.sb("s_gb", [8, 1], F32)
    fw.dma("sp", gb_sb[:], gb)
    eb_sb = fw.sb("s_eb", [8, D], BF16)
    fw.dma("pool", eb_sb[:], eb)
    lgb = fw.sb("lgb", [8, 512], F32)
    tm_g = fw.sb("tm_g", [128, 8], F32)
    tm_e = fw.sb("tm_e", [128, 8], F32)
    tm_r = fw.sb("tm_r", [128, 1], F32)
    wfm = fw.sb("wfm", [8, 512], BF16)
    wrep_all = fw.sb("wrep_all", [128, 8 * 512], BF16)
    xnw = [fw.sb(f"xnw{k}", [128, 512], BF16) for k in range(NK)]
    wexp = [fw.sb(f"wexp{i}", [128, NK * 128], BF16) for i in range(2)]
    out_acc = [fw.sb(f"oacc{o}", [128, 512], F32) for o in range(NK)]
    g_sl = fw.sb("g_sl", [128, 512], F32)
    x_sl = fw.sb("x_sl", [128, 512], F32)
    s_sl = fw.sb("s_sl", [128, 512], F32)
    x2_blk = fw.sb("x2_blk", [128, 512], F32)
    wslot = [0]
    for qb in range(4):
        qoff = qb * 512
        ln_into(qoff)
        p = P.take()
        mm(fw, p[0:8, :], [(gw_sb[:, k * 8:(k + 1) * 8], xn_blk[k][:]) for k in range(NK)],
           tag="mm_lg")
        act(fw, lgb[:], p[0:8, :], AF.Identity, bias=gb_sb[0:8, 0:1], tag="ev_lg")
        for i in range(4):
            p = P.take()
            transpose_pe(fw, p[:, 0:8], lgb[:, i * 128:(i + 1) * 128], idt[0:8, 0:8], tag="tr_lg")
            copy(fw, "dve", tm_g[:], p[:, 0:8], tag="ev_tg")
            reduce(fw, "dve", tm_r[:], tm_g[:], op=ALU.max, negate=True, tag="f_max")
            act(fw, tm_e[:], tm_g[:], AF.Exp, bias=tm_r[:, 0:1], tag="f_exp")
            reduce(fw, "dve", tm_r[:], tm_e[:], op=ALU.add, tag="f_sum")
            recip(fw, tm_r[:], tm_r[:], tag="f_rec")
            ts(fw, "dve", tm_e[:], tm_e[:], tm_r[:, 0:1], ALU.mult, tag="f_nrm")
            ts(fw, "dve", tm_g[:], tm_e[:], 0.1, ALU.is_ge, tag="f_sel")
            tt(fw, "dve", tm_e[:], tm_e[:], tm_g[:], ALU.mult, tag="f_wm")
            reduce(fw, "dve", tm_r[:], tm_e[:], op=ALU.add, tag="f_ws")
            recip(fw, tm_r[:], tm_r[:], tag="f_wr")
            ts(fw, "dve", tm_e[:], tm_e[:], tm_r[:, 0:1], ALU.mult, tag="f_wn")
            p = P.take()
            transpose_pe(fw, p[0:8, 0:128], tm_e[:], idt[:], tag="tr_w")
            copy(fw, "dve", wfm[:, i * 128:(i + 1) * 128], p[0:8, 0:128], tag="ev_w")
        for e in range(8):
            p = P.take()
            mm1(fw, p[:], sel_sb[:, e * 128:(e + 1) * 128], wfm[:], True, True, tag="mm_wr")
            copy(fw, "act", wrep_all[:, e * 512:(e + 1) * 512], p[:], tag="ev_wr")
        for e in range(8):
            for k in range(NK):
                tt(fw, "dve", xnw[k][:], xn_blk[k][:], wrep_all[:, e * 512:(e + 1) * 512],
                   ALU.mult, tag="xnw")
            for ot in range(NK):
                wb = wexp[wslot[0] % 2]; wslot[0] += 1
                fw.dma("pool", wb[:].rearrange("p (k o) -> p k o", k=NK),
                       ew.rearrange("e (k p) o -> e p k o", p=128)[e, :, :, ot * 128:(ot + 1) * 128],
                       tag="ld_ew")
                p = P.take()
                pairs = [(wb[:, k * 128:(k + 1) * 128], xnw[k][:]) for k in range(NK)]
                for i, (l, r) in enumerate(pairs):
                    mm1(fw, p[:], l, r, i == 0, (e != 0 and i == len(pairs) - 1), tag="mm_ex")
                if e == 0:
                    mm1(fw, p[:], eb_sb[:, ot * 128:(ot + 1) * 128], wfm[:],
                        False, True, tag="mm_eb")
                if e == 0:
                    copy(fw, "dve", out_acc[ot][:], p[:], tag="eo0")
                else:
                    tt(fw, "dve", out_acc[ot][:], out_acc[ot][:], p[:], ALU.add, tag="eo")
        for ot in range(NK):
            act(fw, g_sl[:], out_acc[ot][:], AF.Silu, tag="silu_g")
            fw.dma("sp", x_sl[:], xf_i[ot, :, qoff:qoff + 512], tag="ld_xr")
            fw.dma("sp", s_sl[:], s1_d[ot, :, qoff:qoff + 512], tag="ld_s1")
            tt(fw, "dve", g_sl[:], s_sl[:], g_sl[:], ALU.mult, tag="sg")
            tt(fw, "dve", x2_blk[:], x_sl[:], g_sl[:], ALU.add, tag="x2")
            fw.dma("sp", x2_o[ot, :, qoff:qoff + 512], x2_blk[:], tag="st_x2")





def attn_core(fw, P, q_heads, k_heads, kv_of, V_tm, attn_tiles, Psb, rrep, r_row,
              NKT, n_heads, tag=""):
    for h in range(n_heads):
        hv = kv_of(h)
        kh = k_heads[hv]
        for qb in range(4):
            q_ap = q_heads[h][:, qb * 512:(qb + 1) * 512]
            for kt in range(NKT):
                p = P.take()
                mm1(fw, p[:], kh[:, kt * 128:(kt + 1) * 128], q_ap, True, True, tag=tag + "mm_s")
                act(fw, Psb[:, kt * 512:(kt + 1) * 512], p[:], AF.Exp, tag=tag + "p_exp")
            p = P.take()
            mm(fw, p[0:65, :], [(V_tm[kt][:, hv * 65:(hv + 1) * 65],
                                 Psb[:, kt * 512:(kt + 1) * 512]) for kt in range(NKT)],
               tag=tag + "mm_pv")
            recip(fw, r_row[0:1, :], p[64:65, :], tag=tag + "rden")
            p2 = P.take()
            mm1(fw, p2[0:64, :], fw.onerow[0:1, 0:64], r_row[0:1, :], True, True, tag=tag + "mm_rr")
            copy(fw, "act", rrep[0:64, :], p2[0:64, :], tag=tag + "ev_rr")
            dst = attn_tiles[(h // 2, qb)][(h % 2) * 64:(h % 2) * 64 + 64, :]
            fw.comp("dve", lambda hh, dst=dst, p=p: hh.tensor_tensor(
                out=dst, in0=p[0:64, :], in1=rrep[0:64, :], op=ALU.mult),
                reads=[p[0:64, :], rrep[0:64, :]], writes=[dst], tag=tag + "pv_nrm")



def emit_l2(fw, cst, P, IO):

    xf_i = IO["xf_i"]
    wq_t = IO["wq_t"]
    bq_c = IO["bq_c"]
    wk_t = IO["wk_t"]
    bk_c = IO["bk_c"]
    wv_t = IO["wv_t"]
    bv_r = IO["bv_r"]
    wo_t = IO["wo_t"]
    bo_r = IO["bo_r"]
    cosq = IO["cosq"]
    sinq = IO["sinq"]
    cosk = IO["cosk"]
    sink = IO["sink"]
    wr_t = IO["wr_t"]
    br_c = IO["br_c"]
    we_t = IO["we_t"]
    be_i = IO["be_i"]
    fwo_t = IO["fwo_t"]
    fbo_r = IO["fbo_r"]
    x3_o = IO["x3_o"]

    s2_d = fw.dram("s2_spill", [NK, 128, T], F32)
    onecol = cst.handles["onecol"]
    onerow = cst.handles["onerow"]
    ones512 = cst.handles["ones_row512"]
    perm_sb = cst.handles["perm"]
    NKT = TF // 128

    attn_tiles = {}
    for pair in range(8):
        for qb in range(4):
            attn_tiles[(pair, qb)] = fw.sb(f"atn{pair}_{qb}", [128, 512], BF16)
    k_heads = [fw.sb(f"kh{v}", [65, TF], BF16) for v in range(4)]
    V_tm = [fw.sb(f"vt{i}", [128, 4 * 65], BF16) for i in range(NKT)]
    k2m = fw.sb("k2m", [1, 4], F32)
    q_d = fw.dram("q2_spill", [16, 64, T], BF16)

    ln_tmp = mk_ln_tmp(fw, "t")
    xblk = [fw.sb(f"xb{k}", [128, 512], F32) for k in range(NK)]
    xn_blk = [fw.sb(f"xnb{k}", [128, 512], BF16) for k in range(NK)]
    sqt = ln_tmp[0]

    def ln_into(cols_off, W=512):
        for k in range(NK):
            fw.dma("sp", xblk[k][:, :W], xf_i[k, :, cols_off:cols_off + W], tag="ld_x")
        ln_block(fw, P, cst, xblk, 0, W, xn_blk, 0, ln_tmp)

    wqs = [fw.sb(f"s_wqs{i}", [128, NK * 128], BF16) for i in range(2)]
    wqslot = [0]
    wkT = fw.sb("s_wkT", [128, 8 * 256], BF16)
    wvT = fw.sb("s_wvT", [128, 8 * 256], BF16)
    fw.dma("pool", wkT[:].rearrange("p (k o) -> p k o", k=NK),
           wk_t.rearrange("(k p) o -> p k o", p=128), tag="ld_wk")
    fw.dma("pool", wvT[:].rearrange("p (k o) -> p k o", k=NK),
           wv_t.rearrange("(k p) o -> p k o", p=128), tag="ld_wv")
    bq_sb = fw.sb("s_bq", [128, 8], F32)
    fw.dma("sp", bq_sb[:], bq_c)
    bk_sb = fw.sb("s_bk", [128, 2], F32)
    fw.dma("sp", bk_sb[:], bk_c)
    bv_sb = fw.sb("s_bv", [1, 256], BF16)
    fw.dma("pool", bv_sb[:], bv_r)
    tb_cos = fw.sb("tb_cos", [64, 512], BF16)
    tb_sin = fw.sb("tb_sin", [64, 512], BF16)
    NSET = 4
    rsets = [dict(pre_h=fw.sb(f"pre_h{s}", [64, 512], BF16),
                  rop1=fw.sb(f"rop1{s}", [64, 512], F32),
                  rop2=fw.sb(f"rop2{s}", [64, 512], F32),
                  qout=fw.sb(f"qout{s}", [64, 512], BF16)) for s in range(NSET)]
    rturn = [0]
    vtile = [fw.sb(f"vtile{s}", [128, 256], BF16) for s in range(2)]
    vturn = [0]

    for hv in range(4):
        memset(fw, "dve", k_heads[hv][64:65, :], 1.0, tag="kones")
    for bi in range(NKT // 4):
        off = bi * 512
        own = off < T
        ln_into(off)
        fw.dma("pool", tb_cos[:], cosk[:, off:off + 512], tag="ld_ck")
        fw.dma("pool", tb_sin[:], sink[:, off:off + 512], tag="ld_sk")
        for mt in range(2):
            p = P.take()
            mm(fw, p[:], [(wkT[:, k * 256 + mt * 128:k * 256 + (mt + 1) * 128], xn_blk[k][:])
                          for k in range(NK)], tag="mm_kv")
            for sub in range(2):
                hv = mt * 2 + sub
                B = rsets[rturn[0] % NSET]; rturn[0] += 1
                pre_h, rop1, rop2 = B["pre_h"], B["rop1"], B["rop2"]
                act(fw, pre_h[:], p[sub * 64:(sub + 1) * 64, :], AF.Identity,
                    bias=bk_sb[sub * 64:(sub + 1) * 64, mt:mt + 1], tag="ev_kh")
                p2 = P.take()
                mm1(fw, p2[0:64, :], perm_sb[:], pre_h[:], True, True, tag="mm_rot")
                tt(fw, "dve", rop1[:], pre_h[:], tb_cos[:], ALU.mult, tag="rp1")
                fw.comp("dve", lambda h, p2=p2, rop2=rop2: h.tensor_tensor(
                    out=rop2[:], in0=p2[0:64, :], in1=tb_sin[:], op=ALU.mult),
                    reads=[p2[0:64, :], tb_sin[:]], writes=[rop2[:]], tag="rp2")
                tt(fw, "dve", k_heads[hv][0:64, off:off + 512], rop1[:], rop2[:],
                   ALU.add, tag="rp3")
        for st_ in range(4):
            to = st_ * 128
            p = P.take()
            pairs = [(xn_blk[k][:, to:to + 128], wvT[:, k * 256:(k + 1) * 256]) for k in range(NK)]
            for i, (l, r) in enumerate(pairs):
                mm1(fw, p[:, 0:256], l, r, i == 0, False, tag="mm_v")
            mm1(fw, p[:, 0:256], ones512[0:1, 0:128], bv_sb[0:1, :], False, True, tag="mm_vb")
            vt = vtile[vturn[0] % 2]; vturn[0] += 1
            copy(fw, "act", vt[:], p[:, 0:256], tag="ev_v")
            kt = (off + to) // 128
            for hv in range(4):
                copy(fw, "dve", V_tm[kt][:, hv * 65:hv * 65 + 64], vt[:, hv * 64:(hv + 1) * 64],
                     tag="v_sp")
            memset(fw, "dve", V_tm[kt][:].rearrange("p (g c) -> p g c", g=4)[:, :, 64:65],
                   1.0, tag="v_ones")
        if own:
            fw.dma("pool", tb_cos[:], cosq[:, off:off + 512], tag="ld_cq")
            fw.dma("pool", tb_sin[:], sinq[:, off:off + 512], tag="ld_sq")
            for mt in range(NK):
                wb = wqs[wqslot[0] % 2]; wqslot[0] += 1
                fw.dma("pool", wb[:].rearrange("p (k o) -> p k o", k=NK),
                       wq_t.rearrange("(k p) o -> p k o", p=128)[:, :, mt * 128:(mt + 1) * 128],
                       tag="ld_wq")
                p = P.take()
                mm(fw, p[:], [(wb[:, k * 128:(k + 1) * 128], xn_blk[k][:])
                              for k in range(NK)], tag="mm_q")
                for sub in range(2):
                    h_ = mt * 2 + sub
                    B = rsets[rturn[0] % NSET]; rturn[0] += 1
                    pre_h, rop1, rop2, qout = B["pre_h"], B["rop1"], B["rop2"], B["qout"]
                    act(fw, pre_h[:], p[sub * 64:(sub + 1) * 64, :], AF.Identity,
                        bias=bq_sb[sub * 64:(sub + 1) * 64, mt:mt + 1], tag="ev_qh")
                    p2 = P.take()
                    mm1(fw, p2[0:64, :], perm_sb[:], pre_h[:], True, True, tag="mm_rotq")
                    tt(fw, "dve", rop1[:], pre_h[:], tb_cos[:], ALU.mult, tag="rq1")
                    fw.comp("dve", lambda h, p2=p2, rop2=rop2: h.tensor_tensor(
                        out=rop2[:], in0=p2[0:64, :], in1=tb_sin[:], op=ALU.mult),
                        reads=[p2[0:64, :], tb_sin[:]], writes=[rop2[:]], tag="rq2")
                    tt(fw, "dve", qout[:], rop1[:], rop2[:], ALU.add, tag="rq3")
                    fw.dma("sp", q_d[h_, :, off:off + 512], qout[:], tag="st_q2")
    k2ps = fw.aux_ps
    k2one = fw.sb("k2one", [1, 1], F32)
    for hv in range(4):
        for bi in range(NKT // 4):
            off = bi * 512
            act(fw, sqt[0:64, :], k_heads[hv][0:64, off:off + 512], AF.Square, tag="k2sq")
            mm1(fw, k2ps[:], onecol[0:64, 0:1], sqt[0:64, :], True, True, tag="k2mm")
            if bi == 0:
                reduce(fw, "dve", k2m[0:1, hv:hv + 1], k2ps[:], op=ALU.max, tag="k2max")
            else:
                reduce(fw, "dve", k2one[:], k2ps[:], op=ALU.max, tag="k2max2")
                tt(fw, "dve", k2m[0:1, hv:hv + 1], k2m[0:1, hv:hv + 1], k2one[:],
                   ALU.max, tag="k2acc")
    fw.barrier()
    fw.free_sb("k2one", "vtile0", "vtile1",
               *[f"{nm}{s}" for s in range(4) for nm in ("pre_h", "rop1", "rop2", "qout")],
               "tb_sin", "tb_cos",
               "s_bv", "s_bk", "s_bq", "s_wvT", "s_wkT", "s_wqs1", "s_wqs0",
               *[f"xnb{k}" for k in range(NK)], *[f"xb{k}" for k in range(NK)],
               "t_br", "t_rr", "t_bb", "t_rb", "t_vv", "t_qq", "t_ra", "t_sq")

    Psb = fw.sb("Psb", [128, NKT * 512], BF16)
    rrep = fw.sb("rrep", [128, 512], F32)
    r_row = fw.sb("r_row", [1, 512], F32)
    qbuf = [fw.sb(f"qb2_{i}", [65, T], BF16) for i in range(2)]
    qsq = fw.sb("qsq", [64, 512], F32)
    q2r = fw.sb("q2r", [1, 512], F32)
    onerow = fw.onerow
    for h_ in range(16):
        hv = h_ // 4
        qh = qbuf[h_ % 2]
        fw.dma("sp", qh[0:64, :], q_d[h_], tag="ld_q2")
        for qb in range(4):
            off = qb * 512
            act(fw, qsq[:], qh[0:64, off:off + 512], AF.Square, tag="q2sq")
            p = P.take()
            mm1(fw, p[0:1, :], onecol[0:64, 0:1], qsq[:], True, True, tag="q2mm")
            copy(fw, "dve", q2r[:], p[0:1, :], tag="q2ev")
            act(fw, q2r[:], q2r[:], AF.Sqrt, scale=k2m[0:1, hv:hv + 1], tag="csq")
            fw.comp("dve", lambda h, qh=qh, off=off: h.tensor_single_scalar(
                out=qh[64:65, off:off + 512], in_=q2r[0:1, :], scalar=-1.0, op=ALU.mult),
                reads=[q2r[0:1, :]], writes=[qh[64:65, off:off + 512]], tag="cneg")
        for qb in range(4):
            q_ap = qh[:, qb * 512:(qb + 1) * 512]
            for kt in range(NKT):
                p = P.take()
                mm1(fw, p[:], k_heads[hv][:, kt * 128:(kt + 1) * 128], q_ap, True, True,
                    tag="mm_s2")
                act(fw, Psb[:, kt * 512:(kt + 1) * 512], p[:], AF.Exp, tag="p2_exp")
            p = P.take()
            mm(fw, p[0:65, :], [(V_tm[kt][:, hv * 65:(hv + 1) * 65],
                                 Psb[:, kt * 512:(kt + 1) * 512]) for kt in range(NKT)],
               tag="mm_pv2")
            recip(fw, r_row[0:1, :], p[64:65, :], tag="rden2")
            p2 = P.take()
            mm1(fw, p2[0:64, :], onerow[0:1, 0:64], r_row[0:1, :], True, True, tag="mm_rr2")
            copy(fw, "act", rrep[0:64, :], p2[0:64, :], tag="ev_rr2")
            dst = attn_tiles[(h_ // 2, qb)][(h_ % 2) * 64:(h_ % 2) * 64 + 64, :]
            fw.comp("dve", lambda hh, dst=dst, p=p: hh.tensor_tensor(
                out=dst, in0=p[0:64, :], in1=rrep[0:64, :], op=ALU.mult),
                reads=[p[0:64, :], rrep[0:64, :]], writes=[dst], tag="pv2_nrm")
    fw.barrier()
    fw.free_sb("q2r", "qsq", "qb2_1", "qb2_0", "r_row", "rrep", "Psb", "k2m",
               *[f"kh{v}" for v in range(4)], *[f"vt{i}" for i in range(NKT)])
    woT = [fw.sb(f"s_wo{k}", [128, D], BF16) for k in range(NK)]
    for k in range(NK):
        fw.dma("pool", woT[k][:], wo_t[k * 128:(k + 1) * 128, :], tag="ld_wo")
    bo_sb = fw.sb("s_bo", [1, D], BF16)
    fw.dma("pool", bo_sb[:], bo_r)
    s2_blk = fw.sb("s2_blk", [128, 512], F32)
    for qb in range(4):
        for ot in range(NK):
            p = P.take()
            pairs = [(woT[k][:, ot * 128:(ot + 1) * 128], attn_tiles[(k, qb)][:]) for k in range(NK)]
            for i, (l, r) in enumerate(pairs):
                mm1(fw, p[:], l, r, i == 0, False, tag="mm_wo")
            mm1(fw, p[:], bo_sb[0:1, ot * 128:(ot + 1) * 128], ones512[0:1, :], False, True,
                tag="mm_bo")
            copy(fw, "act", s2_blk[:], p[:], tag="ev_s2")
            fw.dma("sp", s2_d[ot, :, qb * 512:(qb + 1) * 512], s2_blk[:], tag="st_s2")

    fw.barrier()
    fw.free_sb("s2_blk", "s_bo", *[f"s_wo{k}" for k in range(NK)],
               *[f"atn{pr}_{qb}" for pr in range(8) for qb in range(4)])

    idt = cst.handles["idt"]
    sel_sb = cst.handles["sel8"]
    wrT = fw.sb("s_wrT", [128, NK * 8], BF16)
    fw.dma("pool", wrT[:].rearrange("p (k g) -> p k g", k=NK),
           wr_t.rearrange("(k p) g -> p k g", p=128), tag="ld_wr")
    br_sb = fw.sb("s_br", [8, 1], F32)
    fw.dma("sp", br_sb[:], br_c)
    be_sb = fw.sb("s_be", [8, D], BF16)
    fw.dma("pool", be_sb[:], be_i)
    fwoT = [fw.sb(f"s_fwo{k}", [128, D], BF16) for k in range(NK)]
    for k in range(NK):
        fw.dma("pool", fwoT[k][:], fwo_t[k * 128:(k + 1) * 128, :], tag="ld_fwo")
    fbo_sb = fw.sb("s_fbo", [1, D], BF16)
    fw.dma("pool", fbo_sb[:], fbo_r)
    ln_tmp = mk_ln_tmp(fw, "c")
    xblk = [fw.sb(f"cxb{k}", [128, 512], F32) for k in range(NK)]
    xn_blk = [fw.sb(f"cxnb{k}", [128, 512], BF16) for k in range(NK)]

    def ln_into(cols_off, W=512, xblk=xblk, xn_blk=xn_blk, ln_tmp=ln_tmp):
        for k in range(NK):
            fw.dma("sp", xblk[k][:, :W], xf_i[k, :, cols_off:cols_off + W], tag="ld_x")
        ln_block(fw, P, cst, xblk, 0, W, xn_blk, 0, ln_tmp)

    lgb = fw.sb("lgb", [8, 512], F32)
    tm_g = fw.sb("tm_g", [128, 8], F32)
    tm_e = fw.sb("tm_e", [128, 8], F32)
    tm_m = fw.sb("tm_m", [128, 8], F32)
    tm_m2 = fw.sb("tm_m2", [128, 8], F32)
    tm_r = fw.sb("tm_r", [128, 1], F32)
    wfm = fw.sb("wfm", [8, 512], BF16)
    wrep_all = fw.sb("wrep_all", [128, 8 * 512], BF16)
    xnw = [fw.sb(f"xnw{k}", [128, 512], BF16) for k in range(NK)]
    wexp = [fw.sb(f"wexp{i}", [128, NK * 128], BF16) for i in range(2)]
    comb = [fw.sb(f"comb{k}", [128, 512], BF16) for k in range(NK)]
    out_acc = [fw.sb(f"oacc{o}", [128, 512], F32) for o in range(NK)]
    g_sl = fw.sb("g_sl", [128, 512], F32)
    x_sl = fw.sb("x_sl", [128, 512], F32)
    s_sl = fw.sb("s_sl", [128, 512], F32)
    x3_blk = fw.sb("x3_blk", [128, 512], F32)
    wslot = [0]
    for qb in range(4):
        qoff = qb * 512
        ln_into(qoff)
        p = P.take()
        mm(fw, p[0:8, :], [(wrT[:, k * 8:(k + 1) * 8], xn_blk[k][:]) for k in range(NK)],
           tag="mm_lg")
        act(fw, lgb[:], p[0:8, :], AF.Identity, bias=br_sb[0:8, 0:1], tag="ev_lg")
        for i in range(4):
            p = P.take()
            transpose_pe(fw, p[:, 0:8], lgb[:, i * 128:(i + 1) * 128], idt[0:8, 0:8], tag="tr_lg")
            copy(fw, "dve", tm_g[:], p[:, 0:8], tag="ev_tg")
            reduce(fw, "dve", tm_r[:], tm_g[:], op=ALU.max, negate=True, tag="f_max")
            act(fw, tm_e[:], tm_g[:], AF.Exp, bias=tm_r[:, 0:1], tag="f_exp")
            reduce(fw, "dve", tm_r[:], tm_e[:], op=ALU.add, tag="f_sum")
            recip(fw, tm_r[:], tm_r[:], tag="f_rec")
            ts(fw, "dve", tm_e[:], tm_e[:], tm_r[:, 0:1], ALU.mult, tag="f_nrm")
            reduce(fw, "dve", tm_r[:], tm_e[:], op=ALU.max, tag="t_m1")
            ts(fw, "dve", tm_m[:], tm_e[:], tm_r[:, 0:1], ALU.is_equal, tag="t_k1")
            stt(fw, "dve", tm_g[:], tm_m[:], -1e30, tm_e[:], ALU.mult, ALU.add, tag="t_g2")
            reduce(fw, "dve", tm_r[:], tm_g[:], op=ALU.max, tag="t_m2")
            ts(fw, "dve", tm_m2[:], tm_g[:], tm_r[:, 0:1], ALU.is_equal, tag="t_k2")
            tt(fw, "dve", tm_m[:], tm_m[:], tm_m2[:], ALU.add, tag="t_ms")
            tt(fw, "dve", tm_e[:], tm_e[:], tm_m[:], ALU.mult, tag="t_w")
            p = P.take()
            transpose_pe(fw, p[0:8, 0:128], tm_e[:], idt[:], tag="tr_w")
            copy(fw, "dve", wfm[:, i * 128:(i + 1) * 128], p[0:8, 0:128], tag="ev_w")
        for e in range(8):
            p = P.take()
            mm1(fw, p[:], sel_sb[:, e * 128:(e + 1) * 128], wfm[:], True, True, tag="mm_wr")
            copy(fw, "act", wrep_all[:, e * 512:(e + 1) * 512], p[:], tag="ev_wr")
        for e in range(8):
            for k in range(NK):
                tt(fw, "dve", xnw[k][:], xn_blk[k][:], wrep_all[:, e * 512:(e + 1) * 512],
                   ALU.mult, tag="xnw")
            for ot in range(NK):
                wb = wexp[wslot[0] % 2]; wslot[0] += 1
                fw.dma("pool", wb[:].rearrange("p (k o) -> p k o", k=NK),
                       we_t.rearrange("e (k p) o -> e p k o", p=128)[e, :, :, ot * 128:(ot + 1) * 128],
                       tag="ld_we")
                p = P.take()
                pairs = [(wb[:, k * 128:(k + 1) * 128], xnw[k][:]) for k in range(NK)]
                for i, (l, r) in enumerate(pairs):
                    mm1(fw, p[:], l, r, i == 0, (e != 0 and i == len(pairs) - 1), tag="mm_ex")
                if e == 0:
                    mm1(fw, p[:], be_sb[:, ot * 128:(ot + 1) * 128], wfm[:], False, True,
                        tag="mm_be")
                if e == 0:
                    copy(fw, "dve", out_acc[ot][:], p[:], tag="eo0")
                elif e < 7:
                    tt(fw, "dve", out_acc[ot][:], out_acc[ot][:], p[:], ALU.add, tag="eo")
                else:
                    tt(fw, "dve", comb[ot][:], out_acc[ot][:], p[:], ALU.add, tag="comb_c")
        for ot in range(NK):
            p = P.take()
            pairs = [(fwoT[k][:, ot * 128:(ot + 1) * 128], comb[k][:]) for k in range(NK)]
            for i, (l, r) in enumerate(pairs):
                mm1(fw, p[:], l, r, i == 0, False, tag="mm_fwo")
            mm1(fw, p[:], fbo_sb[0:1, ot * 128:(ot + 1) * 128], ones512[0:1, :], False, True,
                tag="mm_fbo")
            act(fw, g_sl[:], p[:], AF.Silu, tag="silu_g")
            fw.dma("sp", x_sl[:], xf_i[ot, :, qoff:qoff + 512], tag="ld_xr")
            fw.dma("sp", s_sl[:], s2_d[ot, :, qoff:qoff + 512], tag="ld_s2")
            tt(fw, "dve", g_sl[:], s_sl[:], g_sl[:], ALU.mult, tag="sg")
            tt(fw, "dve", x3_blk[:], x_sl[:], g_sl[:], ALU.add, tag="x3")
            fw.dma("sp", x3_o[ot, :, qoff:qoff + 512], x3_blk[:], tag="st_x3")





def emit_l3(fw, cst, P, IO):

    xf_i = IO["xf_i"]
    wqkv = IO["wqkv"]
    qs_c = IO["qs_c"]
    ks_c = IO["ks_c"]
    wp_t = IO["wp_t"]
    bp_r = IO["bp_r"]
    cosq = IO["cosq"]
    sinq = IO["sinq"]
    cosk = IO["cosk"]
    sink = IO["sink"]
    bw_t = IO["bw_t"]
    bb_r = IO["bb_r"]
    shA_t = IO["shA_t"]
    shB_t = IO["shB_t"]
    rA_t = IO["rA_t"]
    rB_t = IO["rB_t"]
    rw_t = IO["rw_t"]
    rb_c = IO["rb_c"]
    x4_o = IO["x4_o"]

    k_d = fw.dram("k3_spill", [16, 64, TF], BF16)
    v_d = fw.dram("v3_spill", [16, TF // 128, 128, 65], BF16)
    s3_d = fw.dram("s3_spill", [NK, 128, T], F32)

    onecol = cst.handles["onecol"]
    onerow = cst.handles["onerow"]
    ones512 = cst.handles["ones_row512"]
    perm_sb = cst.handles["perm3"]
    NKT = TF // 128

    k2m = fw.sb("k2m", [1, 16], F32)
    q_d = fw.dram("q3_spill", [16, 64, T], BF16)

    ln_tmp = mk_ln_tmp(fw, "t")
    xblk = [fw.sb(f"xb{k}", [128, 512], F32) for k in range(NK)]
    xn_blk = [fw.sb(f"xnb{k}", [128, 512], BF16) for k in range(NK)]

    def ln_into(cols_off, W=512):
        for k in range(NK):
            fw.dma("sp", xblk[k][:, :W], xf_i[k, :, cols_off:cols_off + W], tag="ld_x")
        ln_block(fw, P, cst, xblk, 0, W, xn_blk, 0, ln_tmp)

    wqkv_sb = [fw.sb(f"s_wqkv{k}", [128, 3 * D], BF16) for k in range(NK)]
    for k in range(NK):
        fw.dma("pool", wqkv_sb[k][:], wqkv[k * 128:(k + 1) * 128, :], tag="ld_wqkv")
    qs_sb = fw.sb("s_qs", [64, 1], F32)
    fw.dma("sp", qs_sb[:], qs_c)
    ks_sb = fw.sb("s_ks", [64, 1], F32)
    fw.dma("sp", ks_sb[:], ks_c)
    tb_cos = fw.sb("tb_cos", [64, 512], BF16)
    tb_sin = fw.sb("tb_sin", [64, 512], BF16)
    NSET = 4
    rsets = []
    for s in range(NSET):
        rsets.append(dict(
            pre_h=fw.sb(f"pre_h{s}", [64, 512], F32),
            sq_h=fw.sb(f"sq_h{s}", [64, 512], F32),
            row_a=fw.sb(f"row_a{s}", [1, 512], F32),
            rrep_h=fw.sb(f"rrep_h{s}", [64, 512], F32),
            qn_h=fw.sb(f"qn_h{s}", [64, 512], BF16),
            rop1=fw.sb(f"rop1{s}", [64, 512], F32),
            rop2=fw.sb(f"rop2{s}", [64, 512], F32),
            kout=fw.sb(f"kout{s}", [64, 512], BF16)))
    rturn = [0]
    vtile = fw.sb("vtile", [128, 512], BF16)
    k2one = fw.sb("k2one", [1, 1], F32)
    k2ps = fw.aux_ps

    def rms_rope(sub_psum, scale_col, cos_ap, sin_ap, out_ap, eps_s, tag):
        B = rsets[rturn[0] % NSET]; rturn[0] += 1
        pre_h, sq_h, row_a = B["pre_h"], B["sq_h"], B["row_a"]
        rrep_h, qn_h, rop1, rop2 = B["rrep_h"], B["qn_h"], B["rop1"], B["rop2"]
        copy(fw, "act", pre_h[:], sub_psum, tag=tag + "pre")
        act(fw, sq_h[:], pre_h[:], AF.Square, tag=tag + "sq")
        p2 = P.take()
        mm1(fw, p2[0:1, :], onecol[0:64, 0:1], sq_h[:], True, True, tag=tag + "ssum")
        ts(fw, "dve", row_a[:], p2[0:1, :], 1.0 / 64.0, ALU.mult, tag=tag + "var")
        act(fw, row_a[:], row_a[:], AF.Ln, bias=eps_s, tag=tag + "ln")
        act(fw, row_a[:], row_a[:], AF.Exp, scale=cst.scalar(-0.5, 1), tag=tag + "rs")
        p2 = P.take()
        mm1(fw, p2[0:64, :], onerow[0:1, 0:64], row_a[0:1, :], True, True, tag=tag + "rep")
        copy(fw, "act", rrep_h[:], p2[0:64, :], tag=tag + "evr")
        stt(fw, "dve", qn_h[:], pre_h[:], scale_col, rrep_h[:], ALU.mult, ALU.mult, tag=tag + "n")
        p2 = P.take()
        mm1(fw, p2[0:64, :], perm_sb[:], qn_h[:], True, True, tag=tag + "rot")
        tt(fw, "dve", rop1[:], qn_h[:], cos_ap, ALU.mult, tag=tag + "r1")
        fw.comp("dve", lambda h, p2=p2, sin_ap=sin_ap, rop2=rop2: h.tensor_tensor(
            out=rop2[:], in0=p2[0:64, :], in1=sin_ap, op=ALU.mult),
            reads=[p2[0:64, :], sin_ap], writes=[rop2[:]], tag=tag + "r2")
        tt(fw, "dve", out_ap if out_ap is not None else B["kout"][:], rop1[:], rop2[:],
           ALU.add, tag=tag + "r3")
        return B

    eps6 = cst.scalar(1e-6, 1)
    for bi in range(NKT // 4):
        off = bi * 512
        own = off < T
        ln_into(off)
        fw.dma("pool", tb_cos[:], cosk[:, off:off + 512], tag="ld_ck")
        fw.dma("pool", tb_sin[:], sink[:, off:off + 512], tag="ld_sk")
        for mt in range(NK):
            p = P.take()
            mm(fw, p[:], [(wqkv_sb[k][:, D + mt * 128:D + (mt + 1) * 128], xn_blk[k][:])
                          for k in range(NK)], tag="mm_k3")
            for sub in range(2):
                h_ = mt * 2 + sub
                B = rms_rope(p[sub * 64:(sub + 1) * 64, :], ks_sb[:, 0:1], tb_cos[:], tb_sin[:],
                             B0ko := None, eps6, tag="k3")
                kout = B["kout"]
                fw.dma("sp", k_d[h_, :, off:off + 512], kout[:], tag="st_k3")
                act(fw, B["sq_h"][:], kout[:], AF.Square, tag="k2sq")
                mm1(fw, k2ps[:], onecol[0:64, 0:1], B["sq_h"][:], True, True, tag="k2mm")
                if bi == 0:
                    reduce(fw, "dve", k2m[0:1, h_:h_ + 1], k2ps[:], op=ALU.max, tag="k2max")
                else:
                    reduce(fw, "dve", k2one[:], k2ps[:], op=ALU.max, tag="k2max2")
                    tt(fw, "dve", k2m[0:1, h_:h_ + 1], k2m[0:1, h_:h_ + 1], k2one[:],
                       ALU.max, tag="k2acc")
        for st_ in range(4):
            to = st_ * 128
            kt = (off + to) // 128
            for half in range(2):
                p = P.take()
                mm(fw, p[:], [(xn_blk[k][:, to:to + 128],
                               wqkv_sb[k][:, 2 * D + half * 512:2 * D + (half + 1) * 512])
                              for k in range(NK)], tag="mm_v3")
                copy(fw, "act", vtile[:], p[:], tag="ev_v3")
                for hv in range(8):
                    h_ = half * 8 + hv
                    fw.dma("sp", v_d[h_, kt, :, 0:64], vtile[:, hv * 64:(hv + 1) * 64],
                           tag="st_v3")
        if own:
            fw.dma("pool", tb_cos[:], cosq[:, off:off + 512], tag="ld_cq")
            fw.dma("pool", tb_sin[:], sinq[:, off:off + 512], tag="ld_sq")
            for mt in range(NK):
                p = P.take()
                mm(fw, p[:], [(wqkv_sb[k][:, mt * 128:(mt + 1) * 128], xn_blk[k][:])
                              for k in range(NK)], tag="mm_q3")
                for sub in range(2):
                    h_ = mt * 2 + sub
                    B = rms_rope(p[sub * 64:(sub + 1) * 64, :], qs_sb[:, 0:1], tb_cos[:],
                                 tb_sin[:], None, eps6, tag="q3")
                    fw.dma("sp", q_d[h_, :, off:off + 512], B["kout"][:], tag="st_q3")
    fw.barrier()
    fw.free_sb("k2one", "vtile",
               *[f"{nm}{s}" for s in range(4) for nm in
                 ("pre_h", "sq_h", "row_a", "rrep_h", "qn_h", "rop1", "rop2", "kout")],
               "tb_sin", "tb_cos",
               "s_ks", "s_qs", *[f"s_wqkv{k}" for k in range(NK)],
               *[f"xnb{k}" for k in range(NK)], *[f"xb{k}" for k in range(NK)],
               "t_br", "t_rr", "t_bb", "t_rb", "t_vv", "t_qq", "t_ra", "t_sq")

    kbuf = [fw.sb(f"kb{i}", [65, TF], BF16) for i in range(2)]
    vbuf_t = [fw.sb(f"vb{i}", [128, NKT * 65], BF16) for i in range(2)]
    vbuf = [[vbuf_t[i][:, kt * 65:(kt + 1) * 65] for kt in range(NKT)] for i in range(2)]
    attn_tiles = {}
    for pair in range(8):
        for qb in range(4):
            attn_tiles[(pair, qb)] = fw.sb(f"atn{pair}_{qb}", [128, 512], BF16)
    Psb = fw.sb("Psb", [128, NKT * 512], BF16)
    rrep = fw.sb("rrep", [128, 512], F32)
    r_row = fw.sb("r_row", [1, 512], F32)
    qbuf = [fw.sb(f"qb3_{i}", [65, T], BF16) for i in range(2)]
    qsq = fw.sb("qsq3", [64, 512], F32)
    q2r = fw.sb("q2r3", [1, 512], F32)
    for h_ in range(16):
        sl = h_ % 2
        kb = kbuf[sl]
        fw.dma("sp", kb[0:64, :], k_d[h_], tag="ld_k3")
        memset(fw, "dve", kb[64:65, :], 1.0, tag="k3ones")
        fw.dma("sp", vbuf_t[sl][:].rearrange("p (kt c) -> p kt c", kt=NKT)[:, :, 0:64],
               v_d[h_, :, :, 0:64].rearrange("kt p c -> p kt c"), tag="ld_v3")
        memset(fw, "dve", vbuf_t[sl][:].rearrange("p (kt c) -> p kt c", kt=NKT)[:, :, 64:65],
               1.0, tag="v3ones")
        qh = qbuf[sl]
        fw.dma("sp", qh[0:64, :], q_d[h_], tag="ld_q3")
        for qb in range(4):
            off = qb * 512
            act(fw, qsq[:], qh[0:64, off:off + 512], AF.Square, tag="q3sq")
            p = P.take()
            mm1(fw, p[0:1, :], onecol[0:64, 0:1], qsq[:], True, True, tag="q3mm")
            copy(fw, "dve", q2r[:], p[0:1, :], tag="q3ev")
            act(fw, q2r[:], q2r[:], AF.Sqrt, scale=k2m[0:1, h_:h_ + 1], tag="csq3")
            fw.comp("dve", lambda h, qh=qh, off=off: h.tensor_single_scalar(
                out=qh[64:65, off:off + 512], in_=q2r[0:1, :], scalar=-1.0, op=ALU.mult),
                reads=[q2r[0:1, :]], writes=[qh[64:65, off:off + 512]], tag="cneg3")
        for qb in range(4):
            q_ap = qh[:, qb * 512:(qb + 1) * 512]
            for kt in range(NKT):
                p = P.take()
                mm1(fw, p[:], kb[:, kt * 128:(kt + 1) * 128], q_ap, True, True, tag="mm_s3")
                act(fw, Psb[:, kt * 512:(kt + 1) * 512], p[:], AF.Exp, tag="p3_exp")
            p = P.take()
            mm(fw, p[0:65, :], [(vbuf[sl][kt], Psb[:, kt * 512:(kt + 1) * 512])
                                for kt in range(NKT)], tag="mm_pv3")
            recip(fw, r_row[0:1, :], p[64:65, :], tag="rden3")
            p2 = P.take()
            mm1(fw, p2[0:64, :], onerow[0:1, 0:64], r_row[0:1, :], True, True, tag="mm_rr3")
            copy(fw, "act", rrep[0:64, :], p2[0:64, :], tag="ev_rr3")
            dst = attn_tiles[(h_ // 2, qb)][(h_ % 2) * 64:(h_ % 2) * 64 + 64, :]
            fw.comp("dve", lambda hh, dst=dst, p=p: hh.tensor_tensor(
                out=dst, in0=p[0:64, :], in1=rrep[0:64, :], op=ALU.mult),
                reads=[p[0:64, :], rrep[0:64, :]], writes=[dst], tag="pv3_nrm")
    wpT = [fw.sb(f"s_wp{k}", [128, D], BF16) for k in range(NK)]
    for k in range(NK):
        fw.dma("pool", wpT[k][:], wp_t[k * 128:(k + 1) * 128, :], tag="ld_wp")
    bp_sb = fw.sb("s_bp", [1, D], BF16)
    fw.dma("pool", bp_sb[:], bp_r)
    s3_blk = fw.sb("s3_blk", [128, 512], F32)
    for qb in range(4):
        for ot in range(NK):
            p = P.take()
            pairs = [(wpT[k][:, ot * 128:(ot + 1) * 128], attn_tiles[(k, qb)][:]) for k in range(NK)]
            for i, (l, r) in enumerate(pairs):
                mm1(fw, p[:], l, r, i == 0, False, tag="mm_wp")
            mm1(fw, p[:], bp_sb[0:1, ot * 128:(ot + 1) * 128], ones512[0:1, :], False, True,
                tag="mm_bp")
            copy(fw, "act", s3_blk[:], p[:], tag="ev_s3")
            fw.dma("sp", s3_d[ot, :, qb * 512:(qb + 1) * 512], s3_blk[:], tag="st_s3")

    fw.barrier()
    fw.free_sb("s3_blk", "s_bp", *[f"s_wp{k}" for k in range(NK)],
               "r_row", "rrep", "Psb", "q2r3", "qsq3", "qb3_1", "qb3_0",
               *[f"atn{pr}_{qb}" for pr in range(8) for qb in range(4)],
               "vb0", "vb1", "kb0", "kb1", "k2m")

    idt = cst.handles["idt"]
    sel_sb = cst.handles["sel6"]
    bwT = [fw.sb(f"s_bw{k}", [128, D], BF16) for k in range(NK)]
    for k in range(NK):
        fw.dma("pool", bwT[k][:], bw_t[k * 128:(k + 1) * 128, :], tag="ld_bw")
    bb_sb = fw.sb("s_bb", [1, D], BF16)
    fw.dma("pool", bb_sb[:], bb_r)
    shA_sb = fw.sb("s_shA", [128, NK * 16], BF16)
    fw.dma("pool", shA_sb[:].rearrange("p (k r) -> p k r", k=NK),
           shA_t.rearrange("(k p) r -> p k r", p=128), tag="ld_shA")
    shB_sb = fw.sb("s_shB", [16, D], BF16)
    fw.dma("pool", shB_sb[:], shB_t)
    rA_sb = fw.sb("s_rA", [128, 6 * NK * 16], BF16)
    fw.dma("pool", rA_sb[:].rearrange("p (e k r) -> p e k r", e=6, k=NK),
           rA_t.rearrange("e (k p) r -> p e k r", p=128), tag="ld_rA")
    rB_sb = fw.sb("s_rB", [16, 6 * D], BF16)
    fw.dma("pool", rB_sb[:].rearrange("p (e d) -> p e d", e=6),
           rB_t.rearrange("e r d -> r e d"), tag="ld_rB")
    rwT = fw.sb("s_rwT", [128, NK * 6], BF16)
    fw.dma("pool", rwT[:].rearrange("p (k g) -> p k g", k=NK),
           rw_t.rearrange("(k p) g -> p k g", p=128), tag="ld_rw")
    rb_sb = fw.sb("s_rb", [6, 1], F32)
    fw.dma("sp", rb_sb[:], rb_c)
    ln_tmp = mk_ln_tmp(fw, "c")
    xblk = [fw.sb(f"cxb{k}", [128, 512], F32) for k in range(NK)]
    xn_blk = [fw.sb(f"cxnb{k}", [128, 512], BF16) for k in range(NK)]

    def ln_into(cols_off, W=512, xblk=xblk, xn_blk=xn_blk, ln_tmp=ln_tmp):
        for k in range(NK):
            fw.dma("sp", xblk[k][:, :W], xf_i[k, :, cols_off:cols_off + W], tag="ld_x")
        ln_block(fw, P, cst, xblk, 0, W, xn_blk, 0, ln_tmp)

    lgb = fw.sb("lgb", [6, 512], F32)
    tm_g = fw.sb("tm_g", [128, 6], F32)
    tm_e = fw.sb("tm_e", [128, 6], F32)
    tm_m = fw.sb("tm_m", [128, 6], F32)
    tm_m2 = fw.sb("tm_m2", [128, 6], F32)
    tm_r = fw.sb("tm_r", [128, 1], F32)
    wfm = fw.sb("wfm", [6, 512], BF16)
    wrep = fw.sb("wrep", [128, 512], BF16)
    xnw = [fw.sb(f"xnw{k}", [128, 512], BF16) for k in range(NK)]
    tsh = fw.sb("tsh", [16, 512], BF16)
    he_sb = fw.sb("he_sb", [16, 6 * 512], BF16)
    g_sl = fw.sb("g_sl", [128, 512], F32)
    x_sl = fw.sb("x_sl", [128, 512], F32)
    s_sl = fw.sb("s_sl", [128, 512], F32)
    x4_blk = fw.sb("x4_blk", [128, 512], F32)
    for qb in range(4):
        qoff = qb * 512
        ln_into(qoff)
        p = P.take()
        mm(fw, p[0:6, :], [(rwT[:, k * 6:(k + 1) * 6], xn_blk[k][:]) for k in range(NK)],
           tag="mm_rt")
        act(fw, lgb[:], p[0:6, :], AF.Identity, bias=rb_sb[0:6, 0:1], tag="ev_rt")
        for i in range(4):
            p = P.take()
            transpose_pe(fw, p[:, 0:6], lgb[:, i * 128:(i + 1) * 128], idt[0:6, 0:6], tag="tr_rt")
            copy(fw, "dve", tm_g[:], p[:, 0:6], tag="ev_tg")
            reduce(fw, "dve", tm_r[:], tm_g[:], op=ALU.max, negate=True, tag="r_max")
            act(fw, tm_e[:], tm_g[:], AF.Exp, bias=tm_r[:, 0:1], tag="r_exp")
            reduce(fw, "dve", tm_r[:], tm_e[:], op=ALU.add, tag="r_sum")
            recip(fw, tm_r[:], tm_r[:], tag="r_rec")
            ts(fw, "dve", tm_e[:], tm_e[:], tm_r[:, 0:1], ALU.mult, tag="r_nrm")
            reduce(fw, "dve", tm_r[:], tm_e[:], op=ALU.max, tag="r_m1")
            ts(fw, "dve", tm_m[:], tm_e[:], tm_r[:, 0:1], ALU.is_equal, tag="r_k1")
            stt(fw, "dve", tm_g[:], tm_m[:], -1e30, tm_e[:], ALU.mult, ALU.add, tag="r_g2")
            reduce(fw, "dve", tm_r[:], tm_g[:], op=ALU.max, tag="r_m2")
            ts(fw, "dve", tm_m2[:], tm_g[:], tm_r[:, 0:1], ALU.is_equal, tag="r_k2")
            tt(fw, "dve", tm_m[:], tm_m[:], tm_m2[:], ALU.add, tag="r_ms")
            tt(fw, "dve", tm_e[:], tm_e[:], tm_m[:], ALU.mult, tag="r_w")
            reduce(fw, "dve", tm_r[:], tm_e[:], op=ALU.add, tag="r_ws")
            ts(fw, "dve", tm_r[:], tm_r[:], 1e-6, ALU.add, tag="r_eps")
            recip(fw, tm_r[:], tm_r[:], tag="r_wr")
            ts(fw, "dve", tm_e[:], tm_e[:], tm_r[:, 0:1], ALU.mult, tag="r_wn")
            p = P.take()
            transpose_pe(fw, p[0:6, 0:128], tm_e[:], idt[:], tag="tr_w")
            copy(fw, "dve", wfm[:, i * 128:(i + 1) * 128], p[0:6, 0:128], tag="ev_w")
        p = P.take()
        mm(fw, p[0:16, :], [(shA_sb[:, k * 16:(k + 1) * 16], xn_blk[k][:]) for k in range(NK)],
           tag="mm_shA")
        copy(fw, "act", tsh[:], p[0:16, :], tag="ev_shA")
        for e in range(6):
            p = P.take()
            mm1(fw, p[:], sel_sb[:, e * 128:(e + 1) * 128], wfm[:], True, True, tag="mm_wr")
            copy(fw, "act", wrep[:], p[:], tag="ev_wr")
            for k in range(NK):
                tt(fw, "dve", xnw[k][:], xn_blk[k][:], wrep[:], ALU.mult, tag="xnw")
            p = P.take()
            mm(fw, p[0:16, :], [(rA_sb[:, (e * NK + k) * 16:(e * NK + k + 1) * 16], xnw[k][:])
                                for k in range(NK)], tag="mm_rA")
            copy(fw, "act", he_sb[:, e * 512:(e + 1) * 512], p[0:16, :], tag="ev_he")
        for ot in range(NK):
            p = P.take()
            for i, k in enumerate(range(NK)):
                mm1(fw, p[:], bwT[k][:, ot * 128:(ot + 1) * 128], xn_blk[k][:],
                    i == 0, False, tag="mm_base")
            mm1(fw, p[:], bb_sb[0:1, ot * 128:(ot + 1) * 128], ones512[0:1, :], False, False,
                tag="mm_bb")
            mm1(fw, p[:], shB_sb[:, ot * 128:(ot + 1) * 128], tsh[:], False, False, tag="mm_shB")
            for e in range(6):
                mm1(fw, p[:], rB_sb[:, e * D + ot * 128:e * D + (ot + 1) * 128],
                    he_sb[:, e * 512:(e + 1) * 512], False, e == 5, tag="mm_rB")
            act(fw, g_sl[:], p[:], AF.Silu, tag="silu_g")
            fw.dma("sp", x_sl[:], xf_i[ot, :, qoff:qoff + 512], tag="ld_xr")
            fw.dma("sp", s_sl[:], s3_d[ot, :, qoff:qoff + 512], tag="ld_s3")
            tt(fw, "dve", g_sl[:], s_sl[:], g_sl[:], ALU.mult, tag="sg")
            tt(fw, "dve", x4_blk[:], x_sl[:], g_sl[:], ALU.add, tag="x4")
            fw.dma("sp", x4_o[ot, :, qoff:qoff + 512], x4_blk[:], tag="st_x4")




RG_PAIRS = [[0, 1], [2, 3], [4, 5], [6, 7]]


def emit_collective(fw, kind, src_ap, dst_ap, tag="cc"):
    def build(h):
        return h.collective_compute(kind, ALU.bypass, replica_groups=RG_PAIRS,
                                    ins=[src_ap.opt()], outs=[dst_ap.opt()])
    return fw.comp("pool", build, reads=[src_ap], writes=[dst_ap], tag=tag)


def emit_rotate(fw, P, gathered_ap, xf_ap, mrot_sb, bufs, tag="rot"):
    bufA, bufB, bufT = bufs
    g3 = gathered_ap.rearrange("(r k p) t -> r k p t", r=2, p=128)
    for k in range(NK):
        for half in range(2):
            ma = mrot_sb[:, half:half + 1]
            mb = mrot_sb[:, 1 - half:2 - half]
            for (off, W) in BLKS:
                fw.dma("sp", bufA[:, :W], g3[0, k, :, off:off + W], tag=tag + "g0")
                fw.dma("sp", bufB[:, :W], g3[1, k, :, off:off + W], tag=tag + "g1")
                ts(fw, "dve", bufT[:, :W], bufA[:, :W], ma, ALU.mult, tag=tag + "m0")
                stt(fw, "dve", bufA[:, :W], bufB[:, :W], mb, bufT[:, :W],
                    ALU.mult, ALU.add, tag=tag + "m1")
                fw.dma("sp", xf_ap[k, :, half * T + off:half * T + off + W], bufA[:, :W],
                       tag=tag + "st")


def build_fused():
    nc = bass.Bass("TRN2", target_bir_lowering=False)
    fw = Fw(nc)
    cst = Consts(fw)
    setup_consts(fw, cst, only={"onecol", "onerow"})
    P = PsumPool(fw, 7)

    def ext(pre, names, spec):
        fw.prefix = pre
        out = {}
        for nm in names:
            shape, dt = spec["ins"][nm]
            out[nm] = fw.inp(nm, shape, dt)
        fw.prefix = ""
        return out

    E0a = ext("l0a_", ["xh", "w_in", "convw", "convb", "wxp", "wdt", "bdt"], IO_SPECS["l0a"])
    E0b = ext("l0b_", ["alog", "dskip", "wout"], IO_SPECS["l0b_full"])
    E0f = ext("l0f_", ["wgred_t", "wg_t", "w1", "b1_t", "w2", "b2_t"], IO_SPECS["l0f"])
    E1 = ext("l1_", ["wq_t", "wk_t", "wv_t", "wo_t", "colq", "gw", "gb", "ew", "eb"],
             IO_SPECS["l1"])
    E2 = ext("l2_", ["wq_t", "bq_c", "wk_t", "bk_c", "wv_t", "bv_r", "wo_t", "bo_r",
                     "cosq", "sinq", "cosk", "sink", "wr_t", "br_c", "we_t", "be_i",
                     "fwo_t", "fbo_r"], IO_SPECS["l2"])
    E3 = ext("l3_", ["wqkv", "qs_c", "ks_c", "wp_t", "bp_r", "cosq", "sinq", "cosk",
                     "sink", "bw_t", "bb_r", "shA_t", "shB_t", "rA_t", "rB_t", "rw_t",
                     "rb_c"], IO_SPECS["l3"])
    mrot_i = fw.inp("mrot", [128, 2], F32)
    x4_ext = fw.outp("x4", [NK, 128, T], F32)

    ia = {}
    for nm, (shape, dt) in IO_SPECS["l0a"]["outs"].items():
        ia[nm] = fw.dram("ia_" + nm, shape, dt)
    carry_own = fw.dram("carry_own", [128, CT * NSTATE], F32)
    carry_g = fw.dram("carry_g", [2 * 128, CT * NSTATE], F32)
    carry_in = fw.dram("carry_in", [128, CT * NSTATE], F32)
    s0_d = fw.dram("s0_d", [NK, 128, T], F32)
    ccx = [fw.dram(f"ccx{l}_src", [NK * 128, T], F32) for l in (1, 2, 3)]
    ccg = [fw.dram(f"ccx{l}_dst", [2 * NK * 128, T], F32) for l in (1, 2, 3)]
    xf = [fw.dram(f"xf{l}", [NK, 128, TF], F32) for l in (1, 2, 3)]

    fw.prefix = "l0a_"
    m = fw.mark()
    IO = dict(E0a)
    IO.update({k: ia[k] for k in ("xn_o", "u_o", "z_o", "delta_o", "bmcm_o")})
    emit_l0a(fw, cst, P, IO)
    fw.barrier(); fw.free_since(m); fw.prefix = ""

    setup_consts(fw, cst)
    mrot_sb = fw.sb("g_mrot", [128, 2], F32)
    fw.dma("sp", mrot_sb[:], mrot_i)
    rot_bufs = (fw.sb("g_rA", [128, 512], F32), fw.sb("g_rB", [128, 512], F32),
                fw.sb("g_rT", [128, 512], F32))
    cbuf = fw.sb("g_cb", [128, CT * NSTATE], F32)

    fw.prefix = "l0c_"
    m = fw.mark()
    IO = dict(delta_i=ia["delta_o"], u_i=ia["u_o"], bmcm_i=ia["bmcm_o"],
              alog=E0b["alog"], carry_o=carry_own)
    emit_l0b(fw, cst, P, IO, mode="carry")
    fw.barrier(); fw.free_since(m); fw.prefix = ""

    emit_collective(fw, "AllGather", carry_own, carry_g, tag="cc_carry")
    fw.dma("sp", cbuf[:], carry_g[0:128, :], tag="cc_ld")
    ts(fw, "dve", cbuf[:], cbuf[:], mrot_sb[:, 1:2], ALU.mult, tag="cc_sel")
    fw.dma("sp", carry_in, cbuf[:], tag="cc_st")

    fw.prefix = "l0d_"
    m = fw.mark()
    IO = dict(delta_i=ia["delta_o"], u_i=ia["u_o"], bmcm_i=ia["bmcm_o"],
              alog=E0b["alog"], carry_i=carry_in, z_i=ia["z_o"], dskip=E0b["dskip"],
              wout=E0b["wout"], s_o=s0_d)
    emit_l0b(fw, cst, P, IO, mode="full")
    fw.barrier(); fw.free_since(m); fw.prefix = ""

    fw.prefix = "l0f_"
    m = fw.mark()
    IO = dict(E0f)
    IO.update(xn_i=ia["xn_o"], x_i=E0a["xh"][:, :, 512:TH], s_i=s0_d,
              x1_o=ccx[0].rearrange("(k p) t -> k p t", p=128))
    emit_l0f(fw, cst, P, IO)
    fw.barrier(); fw.free_since(m); fw.prefix = ""

    emits = [emit_l1, emit_l2, emit_l3]
    exts = [E1, E2, E3]
    outs_nm = ["x2_o", "x3_o", "x4_o"]
    for l in range(3):
        emit_collective(fw, "AllGather", ccx[l], ccg[l], tag=f"cc_x{l + 1}")
        fw.prefix = f"g{l + 1}_"
        emit_rotate(fw, P, ccg[l], xf[l], mrot_sb, rot_bufs, tag=f"rot{l + 1}")
        fw.prefix = ""
        fw.barrier()
        fw.prefix = f"l{l + 1}_"
        m = fw.mark()
        IO = dict(exts[l])
        IO["xf_i"] = xf[l]
        if l < 2:
            IO[outs_nm[l]] = ccx[l + 1].rearrange("(k p) t -> k p t", p=128)
        else:
            IO[outs_nm[l]] = x4_ext
        emits[l](fw, cst, P, IO)
        fw.barrier(); fw.free_since(m); fw.prefix = ""

    fw.finish_outputs()
    fw.finalize()
    return nc, cst


def host_fused_maps(inputs):
    g = lambda k: np.asarray(inputs[k], np.float32)
    x = np.asarray(inputs["L0_attn_x"])
    shared = {}
    conv_w = g("L0_attn_conv_w").reshape(2048, 4)
    shared.update({
        "l0a_w_in": np.ascontiguousarray(g("L0_attn_W_in")),
        "l0a_convw": np.ascontiguousarray(conv_w.reshape(CT, 128, 4).transpose(1, 0, 2)
                                          .reshape(128, CT * 4)),
        "l0a_convb": np.ascontiguousarray(g("L0_attn_conv_b").reshape(CT, 128).T),
        "l0a_wxp": np.ascontiguousarray(g("L0_attn_W_xproj")),
        "l0a_wdt": np.ascontiguousarray(g("L0_attn_W_dt")),
        "l0a_bdt": np.ascontiguousarray(g("L0_attn_b_dt").reshape(CT, 128).T),
        "l0b_alog": np.ascontiguousarray(g("L0_attn_A_log").reshape(CT, 128, NSTATE)
                                         .transpose(1, 0, 2).reshape(128, CT * NSTATE)),
        "l0b_dskip": np.ascontiguousarray(g("L0_attn_D_skip").reshape(CT, 128).T),
        "l0b_wout": np.ascontiguousarray(g("L0_attn_W_out")),
        "l0f_wgred_t": np.ascontiguousarray(g("L0_ffn_wg_red").T),
        "l0f_wg_t": np.ascontiguousarray(g("L0_ffn_wg").T),
        "l0f_w1": np.ascontiguousarray(g("L0_ffn_w1")),
        "l0f_b1_t": np.ascontiguousarray(g("L0_ffn_b1").reshape(8, 32, 128)
                                         .transpose(2, 0, 1).reshape(128, 256)),
        "l0f_w2": np.ascontiguousarray(g("L0_ffn_w2")),
        "l0f_b2_t": np.ascontiguousarray(g("L0_ffn_b2").reshape(8, 8, 128)
                                         .transpose(2, 0, 1).reshape(128, 64)),
        "l1_wq_t": np.ascontiguousarray(g("L1_attn_Wq").T),
        "l1_wk_t": np.ascontiguousarray(g("L1_attn_Wk").T),
        "l1_wv_t": np.ascontiguousarray(g("L1_attn_Wv").T),
        "l1_wo_t": np.ascontiguousarray(g("L1_attn_Wo").T),
        "l1_gw": np.ascontiguousarray(g("L1_ffn_gate_w")),
        "l1_gb": np.ascontiguousarray(g("L1_ffn_gate_b").reshape(8, 1)),
        "l1_ew": np.ascontiguousarray(g("L1_ffn_expert_w")),
        "l1_eb": np.ascontiguousarray(g("L1_ffn_expert_b")),
        "l2_wq_t": np.ascontiguousarray(g("L2_attn_Wq").T),
        "l2_bq_c": np.ascontiguousarray(g("L2_attn_bq").reshape(8, 128).T),
        "l2_wk_t": np.ascontiguousarray(g("L2_attn_Wk").T),
        "l2_bk_c": np.ascontiguousarray(g("L2_attn_bk").reshape(2, 128).T),
        "l2_wv_t": np.ascontiguousarray(g("L2_attn_Wv").T),
        "l2_bv_r": np.ascontiguousarray(g("L2_attn_bv").reshape(1, 256)),
        "l2_wo_t": np.ascontiguousarray(g("L2_attn_Wo").T),
        "l2_bo_r": np.ascontiguousarray(g("L2_attn_bo").reshape(1, D)),
        "l2_wr_t": np.ascontiguousarray(g("L2_ffn_Wr").T),
        "l2_br_c": np.ascontiguousarray(g("L2_ffn_br").reshape(8, 1)),
        "l2_we_t": np.ascontiguousarray(g("L2_ffn_We").transpose(0, 2, 1)),
        "l2_be_i": np.ascontiguousarray(g("L2_ffn_be")),
        "l2_fwo_t": np.ascontiguousarray(g("L2_ffn_Wo").T),
        "l2_fbo_r": np.ascontiguousarray(g("L2_ffn_bo").reshape(1, D)),
        "l3_wqkv": np.ascontiguousarray(g("L3_attn_w_qkv")),
        "l3_qs_c": np.ascontiguousarray(g("L3_attn_q_scale").reshape(64, 1)),
        "l3_ks_c": np.ascontiguousarray(g("L3_attn_k_scale").reshape(64, 1)),
        "l3_wp_t": np.ascontiguousarray(g("L3_attn_w_proj")),
        "l3_bp_r": np.ascontiguousarray(g("L3_attn_b_proj").reshape(1, D)),
        "l3_bw_t": np.ascontiguousarray(g("L3_ffn_base_W").T),
        "l3_bb_r": np.ascontiguousarray(g("L3_ffn_base_b").reshape(1, D)),
        "l3_shA_t": np.ascontiguousarray(g("L3_ffn_shared_A").T),
        "l3_shB_t": np.ascontiguousarray(g("L3_ffn_shared_B").T),
        "l3_rA_t": np.ascontiguousarray(g("L3_ffn_routed_A").transpose(0, 2, 1)),
        "l3_rB_t": np.ascontiguousarray(g("L3_ffn_routed_B").transpose(0, 2, 1)),
        "l3_rw_t": np.ascontiguousarray(g("L3_ffn_router_W").T),
        "l3_rb_c": np.ascontiguousarray(g("L3_ffn_router_b").reshape(6, 1)),
    })
    maps = []
    for c in range(NCORES):
        b, h = c // 2, c % 2
        rows = np.zeros((TH, D), np.float32)
        rows[512:] = x[b, h * T:(h + 1) * T]
        if h == 1:
            rows[:512] = x[b, T - 512:T]
        mrot = np.zeros((128, 2), np.float32)
        mrot[:, h] = 1.0
        own_pos = np.arange(h * T, (h + 1) * T, dtype=np.float64)
        rot_pos = np.concatenate([own_pos, np.arange((1 - h) * T, (2 - h) * T,
                                                     dtype=np.float64)])
        cosq, sinq = _rope_tables(own_pos)
        cosk, sink = _rope_tables(rot_pos)
        maps.append(dict(shared, l0a_xh=_fm(rows), mrot=mrot,
                         l1_colq=host_l1_colq(c),
                         l2_cosq=cosq * 0.125, l2_sinq=sinq * 0.125,
                         l2_cosk=cosk, l2_sink=sink,
                         l3_cosq=cosq * 0.125, l3_sinq=sinq * 0.125,
                         l3_cosk=cosk, l3_sink=sink))
    return maps


def kernel_fused(inputs):
    nc, cst = _get_prog("fused", build_fused)
    maps = host_fused_maps(inputs)
    in_maps = [dict(m, **cst.arrays) for m in maps]
    res = run_bass_kernel_spmd(nc, in_maps, core_ids=list(range(NCORES)))
    out = np.empty((4, 4096, D), np.float32)
    for c in range(NCORES):
        b, h = c // 2, c % 2
        out[b, h * T:(h + 1) * T] = _unfm(np.asarray(res.results[c]["x4"], np.float32))
    return out



def _fm(a2d):
    Tt, D2 = a2d.shape
    return np.ascontiguousarray(a2d.T.reshape(D2 // 128, 128, Tt))


def _unfm(a3d):
    nt, p, Tt = a3d.shape
    return np.ascontiguousarray(a3d.reshape(nt * p, Tt).T)


def _percore_rows(x, c):
    b, h = c // 2, c % 2
    return x[b, h * T:(h + 1) * T]


def host_l0a_inputs(inputs):
    x = np.asarray(inputs["L0_attn_x"])
    conv_w = np.asarray(inputs["L0_attn_conv_w"]).reshape(2048, 4)
    per_core = []
    shared = {
        "w_in": np.ascontiguousarray(inputs["L0_attn_W_in"]),
        "convw": np.ascontiguousarray(conv_w.reshape(CT, 128, 4).transpose(1, 0, 2).reshape(128, CT * 4)),
        "convb": np.ascontiguousarray(np.asarray(inputs["L0_attn_conv_b"]).reshape(CT, 128).T),
        "wxp": np.ascontiguousarray(inputs["L0_attn_W_xproj"]),
        "wdt": np.ascontiguousarray(inputs["L0_attn_W_dt"]),
        "bdt": np.ascontiguousarray(np.asarray(inputs["L0_attn_b_dt"]).reshape(CT, 128).T),
    }
    for c in range(NCORES):
        b, h = c // 2, c % 2
        rows = np.zeros((TH, D), np.float32)
        rows[512:] = x[b, h * T:(h + 1) * T]
        if h == 1:
            rows[:512] = x[b, T - 512:T]
        per_core.append(dict(shared, xh=_fm(rows)))
    return per_core


_BUILt = {}


def _get_prog(name, builder):
    if name not in _BUILt:
        _BUILt[name] = builder()
    return _BUILt[name]


def _spmd(name, builder, in_maps_fn):
    nc, cst = _get_prog(name, builder)
    in_maps = [dict(pc, **cst.arrays) for pc in in_maps_fn()]
    res = run_bass_kernel_spmd(nc, in_maps, core_ids=list(range(NCORES)))
    return res.results


build_l0a = _mk_builder("l0a")
build_l0f = _mk_builder("l0f")
build_l1 = _mk_builder("l1")
build_l2 = _mk_builder("l2")
build_l3 = _mk_builder("l3")


def build_l0b(mode):
    return _mk_builder("l0b_" + mode)()


def run_l0a(inputs):
    return _spmd("l0a", build_l0a, lambda: host_l0a_inputs(inputs))


def run_l0(inputs, upto="f"):
    r_a = run_l0a(inputs)
    alog_t = np.ascontiguousarray(
        np.asarray(inputs["L0_attn_A_log"]).reshape(CT, 128, NSTATE)
        .transpose(1, 0, 2).reshape(128, CT * NSTATE)).astype(np.float32)

    def maps_b1():
        return [dict(delta_i=r_a[c]["delta_o"], u_i=r_a[c]["u_o"],
                     bmcm_i=r_a[c]["bmcm_o"], alog=alog_t) for c in range(NCORES)]
    r_b1 = _spmd("l0b_carry", lambda: build_l0b("carry"), maps_b1)
    if upto == "b1":
        return r_a, r_b1
    dskip_t = np.ascontiguousarray(
        np.asarray(inputs["L0_attn_D_skip"]).reshape(CT, 128).T).astype(np.float32)
    wout = np.ascontiguousarray(np.asarray(inputs["L0_attn_W_out"], np.float32))
    zeros_c = np.zeros((128, CT * NSTATE), np.float32)

    def maps_b2():
        out = []
        for c in range(NCORES):
            b, h = c // 2, c % 2
            carry = zeros_c if h == 0 else np.asarray(r_b1[2 * b]["carry_o"], np.float32)
            out.append(dict(delta_i=r_a[c]["delta_o"], u_i=r_a[c]["u_o"],
                            bmcm_i=r_a[c]["bmcm_o"], alog=alog_t, carry_i=carry,
                            z_i=r_a[c]["z_o"], dskip=dskip_t, wout=wout))
        return out
    r_b2 = _spmd("l0b_full", lambda: build_l0b("full"), maps_b2)
    if upto == "b2":
        return r_a, r_b1, r_b2
    x = np.asarray(inputs["L0_attn_x"])
    b1_t = np.ascontiguousarray(np.asarray(inputs["L0_ffn_b1"]).reshape(8, 32, 128)
                                .transpose(2, 0, 1).reshape(128, 256)).astype(np.float32)
    b2_t = np.ascontiguousarray(np.asarray(inputs["L0_ffn_b2"]).reshape(8, 8, 128)
                                .transpose(2, 0, 1).reshape(128, 64)).astype(np.float32)
    wgred_t = np.ascontiguousarray(np.asarray(inputs["L0_ffn_wg_red"]).T).astype(np.float32)
    wg_t = np.ascontiguousarray(np.asarray(inputs["L0_ffn_wg"]).T).astype(np.float32)
    w1 = np.ascontiguousarray(np.asarray(inputs["L0_ffn_w1"], np.float32))
    w2 = np.ascontiguousarray(np.asarray(inputs["L0_ffn_w2"], np.float32))

    def maps_f():
        out = []
        for c in range(NCORES):
            out.append(dict(xn_i=r_a[c]["xn_o"], x_i=_fm(_percore_rows(x, c)),
                            s_i=r_b2[c]["s_o"], wgred_t=wgred_t, wg_t=wg_t,
                            w1=w1, b1_t=b1_t, w2=w2, b2_t=b2_t))
        return out
    r_f = _spmd("l0f", build_l0f, maps_f)
    return r_a, r_b1, r_b2, r_f


def host_l1_colq(c):
    b, h = c // 2, c % 2
    arr = np.zeros((128, 4 * 32), np.float32)
    p = np.arange(128)
    for qb in range(4):
        for kt in range(32):
            rot_base = (h * T + kt * 128) if kt < 16 else ((1 - h) * T + (kt - 16) * 128)
            arr[:, qb * 32 + kt] = rot_base + p - qb * 512 - h * T
    return arr


def run_l1(inputs, x1_percore):
    wq_t = np.ascontiguousarray(np.asarray(inputs["L1_attn_Wq"], np.float32).T)
    wk_t = np.ascontiguousarray(np.asarray(inputs["L1_attn_Wk"], np.float32).T)
    wv_t = np.ascontiguousarray(np.asarray(inputs["L1_attn_Wv"], np.float32).T)
    wo_t = np.ascontiguousarray(np.asarray(inputs["L1_attn_Wo"], np.float32).T)
    gw = np.ascontiguousarray(np.asarray(inputs["L1_ffn_gate_w"], np.float32))
    gb = np.ascontiguousarray(np.asarray(inputs["L1_ffn_gate_b"], np.float32).reshape(8, 1))
    ew = np.ascontiguousarray(np.asarray(inputs["L1_ffn_expert_w"], np.float32))
    eb = np.ascontiguousarray(np.asarray(inputs["L1_ffn_expert_b"], np.float32))

    def maps():
        out = []
        for c in range(NCORES):
            b, h = c // 2, c % 2
            own = np.asarray(x1_percore[c], np.float32)
            other = np.asarray(x1_percore[2 * b + (1 - h)], np.float32)
            xf = np.concatenate([own, other], axis=2)
            out.append(dict(xf_i=np.ascontiguousarray(xf), wq_t=wq_t, wk_t=wk_t,
                            wv_t=wv_t, wo_t=wo_t, colq=host_l1_colq(c), gw=gw, gb=gb,
                            ew=ew, eb=eb))
        return out
    return _spmd("l1", build_l1, maps)


def _rope_tables(pos):
    inv = 1.0 / 10000.0 ** (np.arange(0, 64, 2) / 64.0)
    ang = pos[:, None] * inv[None]
    sin = np.concatenate([np.sin(ang)] * 2, -1)
    cos = np.concatenate([np.cos(ang)] * 2, -1)
    return np.ascontiguousarray(cos.T).astype(np.float32), \
        np.ascontiguousarray(sin.T).astype(np.float32)


def run_l2(inputs, x2_percore):
    g = lambda k: np.asarray(inputs[k], np.float32)
    wq_t = np.ascontiguousarray(g("L2_attn_Wq").T)
    bq_c = np.ascontiguousarray(g("L2_attn_bq").reshape(8, 128).T)
    wk_t = np.ascontiguousarray(g("L2_attn_Wk").T)
    bk_c = np.ascontiguousarray(g("L2_attn_bk").reshape(2, 128).T)
    wv_t = np.ascontiguousarray(g("L2_attn_Wv").T)
    bv_r = np.ascontiguousarray(g("L2_attn_bv").reshape(1, 256))
    wo_t = np.ascontiguousarray(g("L2_attn_Wo").T)
    bo_r = np.ascontiguousarray(g("L2_attn_bo").reshape(1, D))
    wr_t = np.ascontiguousarray(g("L2_ffn_Wr").T)
    br_c = np.ascontiguousarray(g("L2_ffn_br").reshape(8, 1))
    we_t = np.ascontiguousarray(g("L2_ffn_We").transpose(0, 2, 1))
    be_i = np.ascontiguousarray(g("L2_ffn_be"))
    fwo_t = np.ascontiguousarray(g("L2_ffn_Wo").T)
    fbo_r = np.ascontiguousarray(g("L2_ffn_bo").reshape(1, D))

    def maps():
        out = []
        for c in range(NCORES):
            b, h = c // 2, c % 2
            own = np.asarray(x2_percore[c], np.float32)
            other = np.asarray(x2_percore[2 * b + (1 - h)], np.float32)
            xf = np.ascontiguousarray(np.concatenate([own, other], axis=2))
            own_pos = np.arange(h * T, (h + 1) * T, dtype=np.float64)
            rot_pos = np.concatenate([own_pos, np.arange((1 - h) * T, (2 - h) * T,
                                                         dtype=np.float64)])
            cosq, sinq = _rope_tables(own_pos)
            cosk, sink = _rope_tables(rot_pos)
            out.append(dict(xf_i=xf, wq_t=wq_t, bq_c=bq_c, wk_t=wk_t, bk_c=bk_c,
                            wv_t=wv_t, bv_r=bv_r, wo_t=wo_t, bo_r=bo_r,
                            cosq=cosq * 0.125, sinq=sinq * 0.125, cosk=cosk, sink=sink,
                            wr_t=wr_t, br_c=br_c, we_t=we_t, be_i=be_i,
                            fwo_t=fwo_t, fbo_r=fbo_r))
        return out
    return _spmd("l2", build_l2, maps)


def run_l3(inputs, x3_percore):
    g = lambda k: np.asarray(inputs[k], np.float32)
    wqkv = np.ascontiguousarray(g("L3_attn_w_qkv"))
    qs_c = np.ascontiguousarray(g("L3_attn_q_scale").reshape(64, 1))
    ks_c = np.ascontiguousarray(g("L3_attn_k_scale").reshape(64, 1))
    wp_t = np.ascontiguousarray(g("L3_attn_w_proj"))
    bp_r = np.ascontiguousarray(g("L3_attn_b_proj").reshape(1, D))
    bw_t = np.ascontiguousarray(g("L3_ffn_base_W").T)
    bb_r = np.ascontiguousarray(g("L3_ffn_base_b").reshape(1, D))
    shA_t = np.ascontiguousarray(g("L3_ffn_shared_A").T)
    shB_t = np.ascontiguousarray(g("L3_ffn_shared_B").T)
    rA_t = np.ascontiguousarray(g("L3_ffn_routed_A").transpose(0, 2, 1))
    rB_t = np.ascontiguousarray(g("L3_ffn_routed_B").transpose(0, 2, 1))
    rw_t = np.ascontiguousarray(g("L3_ffn_router_W").T)
    rb_c = np.ascontiguousarray(g("L3_ffn_router_b").reshape(6, 1))

    def maps():
        out = []
        for c in range(NCORES):
            b, h = c // 2, c % 2
            own = np.asarray(x3_percore[c], np.float32)
            other = np.asarray(x3_percore[2 * b + (1 - h)], np.float32)
            xf = np.ascontiguousarray(np.concatenate([own, other], axis=2))
            own_pos = np.arange(h * T, (h + 1) * T, dtype=np.float64)
            rot_pos = np.concatenate([own_pos, np.arange((1 - h) * T, (2 - h) * T,
                                                         dtype=np.float64)])
            cosq, sinq = _rope_tables(own_pos)
            cosk, sink = _rope_tables(rot_pos)
            out.append(dict(xf_i=xf, wqkv=wqkv, qs_c=qs_c, ks_c=ks_c, wp_t=wp_t, bp_r=bp_r,
                            cosq=cosq * 0.125, sinq=sinq * 0.125, cosk=cosk, sink=sink,
                            bw_t=bw_t, bb_r=bb_r, shA_t=shA_t, shB_t=shB_t, rA_t=rA_t,
                            rB_t=rB_t, rw_t=rw_t, rb_c=rb_c))
        return out
    return _spmd("l3", build_l3, maps)


def kernel(**inputs):
    return kernel_fused(inputs)


def kernel_multilaunch(**inputs):
    r = run_l0(inputs, upto="f")
    r_f = r[3]
    x1_pc = [np.asarray(r_f[c]["x1_o"]) for c in range(NCORES)]
    r1 = run_l1(inputs, x1_pc)
    x2_pc = [np.asarray(r1[c]["x2_o"]) for c in range(NCORES)]
    r2 = run_l2(inputs, x2_pc)
    x3_pc = [np.asarray(r2[c]["x3_o"]) for c in range(NCORES)]
    r3 = run_l3(inputs, x3_pc)
    out = np.empty((4, 4096, D), np.float32)
    for c in range(NCORES):
        b, h = c // 2, c % 2
        out[b, h * T:(h + 1) * T] = _unfm(np.asarray(r3[c]["x4_o"], np.float32))
    return out


# version 06
# speedup vs baseline: 1.0999x; 1.0999x over previous
import math
import numpy as np
import ml_dtypes
from contextlib import ExitStack
import concourse.bass as bass
import concourse.mybir as mybir
from concourse.bass_utils import run_bass_kernel_spmd

F32 = mybir.dt.float32
BF16 = mybir.dt.bfloat16
AF = mybir.ActivationFunctionType
ALU = mybir.AluOpType
AX = mybir.AxisListType

BF = ml_dtypes.bfloat16

NCORES = 8
T = 2048
TH = 2560
TF = 2 * T
D = 1024
NK = D // 128
CT = 16
NSTATE = 16
BLKS = [(i * 512, 512) for i in range(4)]
BLKS_H = [(i * 512, 512) for i in range(5)]

ENG_ATTR = {"pe": "tensor", "act": "scalar", "dve": "vector", "pool": "gpsimd", "sp": "sync"}


class Op:
    __slots__ = ("eng", "clockeng", "build", "tag", "idx",
                 "eng_idx", "waits", "needs_inc", "inc_val", "cc")

    def __init__(self):
        self.waits = []
        self.needs_inc = False
        self.inc_val = None


class Fw:
    def __init__(self, nc):
        self.nc = nc
        self.ops = []
        self.eng_ops = {e: [] for e in ENG_ATTR}
        self.tacc = {}
        self.eng_clock = {e: {} for e in ENG_ATTR}
        self.part_elems = {}
        self.cms = {}
        self.alloc_seq = {}
        self.free_evt = {}
        self.ctx = ExitStack()
        self.n_false_fallback = 0
        self.dma_domains = set()
        self.prefix = ""
        self.input_names = []
        self.onerow = None
        self.aux_ps = None

    def sb(self, name, shape, dtype=F32):
        name = self.prefix + name
        cm = self.nc.sbuf_tensor(name, shape, dtype)
        h = cm.__enter__()
        self.cms[name] = cm
        self.alloc_seq[name] = len(self.alloc_seq)
        self.ctx.callback(self._free_cm, name)
        self.part_elems[name] = int(np.prod(shape[1:]))
        return h

    def _free_cm(self, name):
        cm = self.cms.pop(name, None)
        if cm is not None:
            self.free_evt[name] = len(self.ops)
            cm.__exit__(None, None, None)

    def mark(self):
        return len(self.alloc_seq)

    def free_since(self, mark):
        names = [n for n, i in self.alloc_seq.items() if i >= mark and n in self.cms]
        for name in sorted(names, key=lambda n: -self.alloc_seq[n]):
            self._free_cm(name)

    def free_sb(self, *handles):
        names = [h.name if hasattr(h, "name") else self.prefix + h for h in handles]
        for name in sorted(names, key=lambda n: -self.alloc_seq.get(n, -1)):
            self._free_cm(name)

    def barrier(self, engines=None):
        deps = {}
        for ce, lst in self.eng_ops.items():
            if lst:
                deps[ce] = len(lst) - 1
        for eng in (engines or list(ENG_ATTR)):
            op = self._add(eng, eng, lambda h: h.nop(), [], [], tag="barrier")
            self._force_deps(op, {ce: k for ce, k in deps.items()
                                  if not (ce == eng and k >= op.eng_idx)})

    def ps(self, name, shape, dtype=F32):
        name = self.prefix + name
        h = self.ctx.enter_context(self.nc.psum_tensor(name, shape, dtype))
        self.part_elems[name] = int(np.prod(shape[1:]))
        return h

    def dram(self, name, shape, dtype=F32, kind="Internal"):
        name = self.prefix + name
        t = self.nc.dram_tensor(name, shape, dtype, kind=kind)
        self.part_elems[name] = None
        return t.ap()

    def inp(self, name, shape, dtype=F32):
        ap = self.dram(name, shape, dtype, kind="ExternalInput")
        self.input_names.append(self.prefix + name)
        return ap

    def outp(self, name, shape, dtype=F32):
        return self.dram(name, shape, dtype, kind="ExternalOutput")

    def _region(self, ap):
        th = ap.tensor
        name = th.name
        pe = self.part_elems.get(name, None)
        off = int(ap.offset)
        dims = ap.ap
        if pe is None:
            ext = 0
            for step, count in dims:
                if count > 1:
                    ext += abs(int(step)) * (int(count) - 1)
            return (name, 0, 1, off, off + ext + 1)
        p_lo, b_lo = divmod(off, pe)
        p_ext = 0
        b_ext = 0
        ok = True
        for step, count in dims:
            step = int(step); count = int(count)
            if count <= 1 or step == 0:
                continue
            if abs(step) % pe == 0:
                p_ext += (abs(step) // pe) * (count - 1)
            elif abs(step) < pe:
                b_ext += abs(step) * (count - 1)
            else:
                ok = False
                break
        if not ok or b_lo + b_ext >= pe:
            self.n_false_fallback += 1
            return (name, 0, 1 << 20, 0, 1 << 40)
        return (name, p_lo, p_lo + p_ext + 1, b_lo, b_lo + b_ext + 1)

    def _full_region(self, ap):
        return (ap.tensor.name, 0, 1 << 20, 0, 1 << 40)

    @staticmethod
    def _overlap(r1, r2):
        return (r1[1] < r2[2] and r2[1] < r1[2] and r1[3] < r2[4] and r2[3] < r1[4])

    @staticmethod
    def _covers(r1, r2):
        return r1[1] <= r2[1] and r1[2] >= r2[2] and r1[3] <= r2[3] and r1[4] >= r2[4]

    def comp(self, eng, build, reads=(), writes=(), tag=""):
        rregs = [self._region(ap) for ap in reads]
        wregs = [self._region(ap) for ap in writes]
        return self._add(eng, eng, build, rregs, wregs, tag)

    def dma(self, qeng, out, in_, tag="", **kw):
        def build(h):
            return h.dma_start(out=out, in_=in_, **kw)
        dom = "D." + out.tensor.name
        if dom not in self.eng_ops:
            self.eng_ops[dom] = []
            self.dma_domains.add(dom)
        rregs = [self._region(in_)]
        wregs = [self._full_region(out)]
        op = self._add(qeng, dom, build, rregs, wregs, tag)
        op.needs_inc = True
        return op

    def _add(self, eng, clockeng, build, rregs, wregs, tag):
        op = Op()
        op.eng = eng
        op.clockeng = clockeng
        op.build = build
        op.tag = tag
        op.idx = len(self.ops)
        op.eng_idx = len(self.eng_ops[clockeng])
        self.ops.append(op)
        self.eng_ops[clockeng].append(op)
        deps = {}

        def add_dep(dop):
            if dop is op:
                return
            if dop.clockeng == "pe" and op.clockeng == "pe":
                return
            ce = dop.clockeng
            if deps.get(ce, -1) < dop.eng_idx:
                deps[ce] = dop.eng_idx

        for reg in rregs:
            acc = self.tacc.get(reg[0])
            if acc:
                for dop, wreg in acc["w"]:
                    if self._overlap(reg, wreg):
                        add_dep(dop)
        for reg in wregs:
            acc = self.tacc.get(reg[0])
            if acc:
                for dop, wreg in acc["w"]:
                    if self._overlap(reg, wreg):
                        add_dep(dop)
                for dop, rreg in acc["r"]:
                    if self._overlap(reg, rreg):
                        add_dep(dop)
        self._install(op, deps)
        for reg in wregs:
            acc = self.tacc.setdefault(reg[0], {"w": [], "r": []})
            acc["w"] = [(o, r) for (o, r) in acc["w"] if not self._covers(reg, r)]
            acc["r"] = [(o, r) for (o, r) in acc["r"] if not self._covers(reg, r)]
            acc["w"].append((op, reg))
        for reg in rregs:
            acc = self.tacc.setdefault(reg[0], {"w": [], "r": []})
            acc["r"].append((op, reg))
        return op

    def _add_wait(self, op, ce, k):
        wait_eng = op.eng
        if self.eng_clock[wait_eng].get(ce, -1) >= k:
            return
        target = self.eng_ops[ce][k]
        if ce not in self.dma_domains:
            target.needs_inc = True
        op.waits.append((ce, k))
        clk = self.eng_clock[wait_eng]
        tc = target.cc
        if tc:
            for e2, v2 in tc.items():
                if clk.get(e2, -1) < v2:
                    clk[e2] = v2
        if clk.get(ce, -1) < k:
            clk[ce] = k

    def _install(self, op, deps):
        for ce in sorted(deps):
            self._add_wait(op, ce, deps[ce])
        cc = dict(self.eng_clock[op.eng])
        cc[op.clockeng] = op.eng_idx
        op.cc = cc

    def _force_deps(self, op, deps):
        for ce, k in deps.items():
            if ce == op.clockeng and k <= op.eng_idx:
                continue
            self._add_wait(op, ce, k)

    def finish_outputs(self):
        deps = {}
        for name, pe in self.part_elems.items():
            if pe is not None:
                continue
            acc = self.tacc.get(name)
            if not acc:
                continue
            for dop, _ in acc["w"]:
                ce = dop.clockeng
                if deps.get(ce, -1) < dop.eng_idx:
                    deps[ce] = dop.eng_idx
        op = self._add("sp", "sp", lambda h: h.nop(), [], [], tag="finish")
        self._force_deps(op, deps)

    def finalize(self, emit_stats=True):
        nc = self.nc
        val_of = {}
        for ce, lst in self.eng_ops.items():
            n = 0
            for o in lst:
                if o.needs_inc:
                    n += 1
                    o.inc_val = n
                val_of[(ce, o.eng_idx)] = n
        sems = {}
        base_val = {}
        n_phys = 0
        phys = []
        doms = []
        for ce in self.eng_ops:
            if ce in self.dma_domains and self.eng_ops[ce]:
                first = self.eng_ops[ce][0].idx
                last = self.eng_ops[ce][-1].idx
                tname = ce[2:]
                fidx = self.free_evt.get(tname)
                end = fidx if (fidx is not None and fidx > last) else None
                doms.append((first, end, ce))
        doms.sort()
        for first, end, ce in doms:
            slot = None
            for s in phys:
                if s["end"] is not None and s["end"] <= first:
                    slot = s
                    break
            if slot is None:
                slot = {"end": None, "cum": 0,
                        "handle": self.ctx.enter_context(nc.semaphore(f"dsem{len(phys)}"))}
                phys.append(slot)
            base_val[ce] = slot["cum"]
            slot["cum"] += 16 * len(self.eng_ops[ce])
            slot["end"] = end
            sems[ce] = slot["handle"]
        for ce, lst in self.eng_ops.items():
            if ce in self.dma_domains:
                continue
            if any(o.needs_inc for o in lst):
                sems[ce] = self.ctx.enter_context(nc.semaphore("sem_" + ce))
                base_val[ce] = 0
        per_eng = {e: [] for e in ENG_ATTR}
        for op in self.ops:
            per_eng[op.eng].append(op)
        mult = {ce: (16 if ce in self.dma_domains else 1) for ce in self.eng_ops}
        tmap = {}
        nc._tag_map = tmap
        block = self.ctx.enter_context(nc.Block())

        def make_body(ops):
            def body(h):
                for op in ops:
                    for (ce, k) in op.waits:
                        h.wait_ge(sems[ce], base_val[ce] + val_of[(ce, k)] * mult[ce])
                    inst = op.build(h)
                    try:
                        tmap[inst.ins.name] = op.tag or "?"
                    except Exception:
                        pass
                    if op.needs_inc:
                        inst.then_inc(sems[op.clockeng], mult[op.clockeng])
            return body

        for eng_name, ops in per_eng.items():
            if not ops:
                continue
            getattr(block, ENG_ATTR[eng_name])(make_body(ops))
        if emit_stats:
            n_w = sum(len(o.waits) for o in self.ops)
            n_i = sum(1 for o in self.ops if o.needs_inc)
            print(f"[fw] dsems={len(phys)} ", end="")
            print(f"[fw] ops={len(self.ops)} waits={n_w} incs={n_i} "
                  f"fallbacks={self.n_false_fallback} "
                  f"per_eng={{{', '.join(f'{e}:{len(v)}' for e, v in per_eng.items() if v)}}}",
                  flush=True)
        self.ctx.close()



def mm1(fw, psum_ap, l, r, first, last, tag=""):
    return fw.comp("pe",
                   lambda h: h.matmul(psum_ap, l, r, start=first, stop=last),
                   reads=[l, r], writes=[psum_ap], tag=tag)


def mm(fw, psum_ap, pairs, tag=""):
    n = len(pairs)
    for i, (l, r) in enumerate(pairs):
        mm1(fw, psum_ap, l, r, i == 0, i == n - 1, tag=tag)


def act(fw, out, in_, func, bias=0.0, scale=1.0, tag=""):
    reads = [in_] + [b for b in (bias, scale) if hasattr(b, "tensor")]
    return fw.comp("act", lambda h: h.activation(out=out, in_=in_, func=func, bias=bias, scale=scale),
                   reads=reads, writes=[out], tag=tag)


def tt(fw, eng, out, in0, in1, op, tag=""):
    return fw.comp(eng, lambda h: h.tensor_tensor(out=out, in0=in0, in1=in1, op=op),
                   reads=[in0, in1], writes=[out], tag=tag)


def ts(fw, eng, out, in0, s1, op0, s2=None, op1=None, tag=""):
    reads = [in0] + [s for s in (s1, s2) if hasattr(s, "tensor")]
    if op1 is None:
        return fw.comp(eng, lambda h: h.tensor_single_scalar(out=out, in_=in0, scalar=s1, op=op0),
                       reads=reads, writes=[out], tag=tag)
    return fw.comp(eng, lambda h: h.tensor_scalar(out=out, in0=in0, scalar1=s1, scalar2=s2,
                                                  op0=op0, op1=op1),
                   reads=reads, writes=[out], tag=tag)


def stt(fw, eng, out, in0, s, in1, op0, op1, tag=""):
    reads = [in0, in1] + ([s] if hasattr(s, "tensor") else [])
    return fw.comp(eng, lambda h: h.scalar_tensor_tensor(out=out, in0=in0, scalar=s, in1=in1,
                                                         op0=op0, op1=op1),
                   reads=reads, writes=[out], tag=tag)


def copy(fw, eng, out, in_, tag=""):
    if eng == "act":
        return fw.comp("act", lambda h: h.copy(out=out, in_=in_), reads=[in_], writes=[out], tag=tag)
    return fw.comp(eng, lambda h: h.tensor_copy(out=out, in_=in_), reads=[in_], writes=[out], tag=tag)


def scan(fw, eng, out, d0, d1, initial, tag=""):
    reads = [d0, d1] + ([initial] if hasattr(initial, "tensor") else [])
    return fw.comp(eng, lambda h: h.tensor_tensor_scan(out=out, data0=d0, data1=d1, initial=initial,
                                                       op0=ALU.mult, op1=ALU.add),
                   reads=reads, writes=[out], tag=tag)


def reduce(fw, eng, out, in_, op=ALU.add, axis=AX.X, negate=None, tag=""):
    return fw.comp(eng, lambda h: h.tensor_reduce(out=out, in_=in_, axis=axis, op=op, negate=negate),
                   reads=[in_], writes=[out], tag=tag)


def recip(fw, out, in_, tag=""):
    return fw.comp("dve", lambda h: h.reciprocal(out=out, in_=in_), reads=[in_], writes=[out], tag=tag)


def memset(fw, eng, ap_, v, tag=""):
    return fw.comp(eng, lambda h: h.memset(ap_, v), reads=[], writes=[ap_], tag=tag)


def transpose_pe(fw, out_psum, in_, identity, tag=""):
    return fw.comp("pe", lambda h: h.transpose(out_psum, in_, identity),
                   reads=[in_, identity], writes=[out_psum], tag=tag)


class PsumPool:
    def __init__(self, fw, n=6, prefix="pp"):
        self.tiles = [fw.ps(f"{prefix}{i}", [128, 512], F32) for i in range(n)]
        self.i = 0

    def take(self):
        t = self.tiles[self.i % len(self.tiles)]
        self.i += 1
        return t


class Consts:
    def __init__(self, fw):
        self.fw = fw
        self.arrays = {}
        self.handles = {}
        self.scalar_vals = []
        self.scalar_sb = None

    def scalar(self, val, p=128):
        if self.scalar_sb is None:
            self.scalar_sb = self.fw.sb("cs_scalars", [128, 16], F32)
            d_in = self.fw.inp("c_scalars", [128, 16], F32)
            self.fw.dma("sp", self.scalar_sb[:], d_in)
        if val not in self.scalar_vals:
            assert len(self.scalar_vals) < 16
            self.scalar_vals.append(val)
            arr = np.zeros((128, 16), np.float32)
            arr[:, :len(self.scalar_vals)] = np.asarray(self.scalar_vals, np.float32)
            self.arrays["c_scalars"] = arr
        i = self.scalar_vals.index(val)
        return self.scalar_sb[0:p, i:i + 1]

    def add(self, name, arr, dtype=None, qeng="sp"):
        if name in self.handles:
            return self.handles[name]
        fw = self.fw
        arr = np.asarray(arr)
        dt = dtype or (BF16 if arr.dtype == BF else F32)
        d_in = fw.inp("c_" + name, list(arr.shape), F32 if arr.dtype == np.float32 else dt)
        sb_h = fw.sb("cs_" + name, list(arr.shape), dt)
        fw.dma("pool" if (dt == BF16 and arr.dtype == np.float32) else qeng, sb_h[:], d_in)
        self.arrays["c_" + name] = arr.astype(np.float32) if arr.dtype != BF else arr
        self.handles[name] = sb_h
        return sb_h


class Pipe:
    def __init__(self):
        self.q = []

    def add(self, *stage_fns):
        self.q.append(stage_fns)

    def run(self):
        n = len(self.q)
        if not n:
            return
        ns = max(len(s) for s in self.q)
        for t in range(n + ns - 1):
            for s in range(ns):
                i = t - s
                if 0 <= i < n and s < len(self.q[i]):
                    self.q[i][s]()
        self.q = []


def mk_ln_tmp(fw, pre="t"):
    sq = fw.sb(pre + "_sq", [128, 512], F32)
    m = fw.sb(pre + "_ra", [1, 512], F32)
    q = fw.sb(pre + "_qq", [1, 512], F32)
    v = fw.sb(pre + "_vv", [1, 512], F32)
    r = fw.sb(pre + "_rb", [1, 512], F32)
    b = fw.sb(pre + "_bb", [1, 512], F32)
    rr = fw.sb(pre + "_rr", [128, 512], F32)
    br = fw.sb(pre + "_br", [128, 512], F32)
    return (sq, m[:], q[:], v[:], r[:], b[:], rr, br)


def ln_block(fw, P, cst, xtiles, src_off, W, out_tiles, dst_off, tmp, nd=1024, eps=1e-5,
             out_dtype_note="bf16"):
    onecol = cst.handles["onecol"]
    onerow = cst.handles["onerow"]
    sq, row_m, row_q, row_v, row_r, row_b, rrep, brep = tmp
    nk = len(xtiles)
    ps_s = P.take()
    ps_q = P.take()
    for i, x in enumerate(xtiles):
        mm1(fw, ps_s[0:1, :W], onecol[:, 0:1], x[:, src_off:src_off + W], i == 0, i == nk - 1, tag="ln_s")
    for i, x in enumerate(xtiles):
        act(fw, sq[:, :W], x[:, src_off:src_off + W], AF.Square, tag="ln_sq")
        mm1(fw, ps_q[0:1, :W], onecol[:, 0:1], sq[:, :W], i == 0, i == nk - 1, tag="ln_q")
    ts(fw, "dve", row_m[0:1, :W], ps_s[0:1, :W], 1.0 / nd, ALU.mult, tag="ln_m")
    ts(fw, "dve", row_q[0:1, :W], ps_q[0:1, :W], 1.0 / nd, ALU.mult, tag="ln_q2")
    stt(fw, "dve", row_v[0:1, :W], row_m[0:1, :W], -1.0, row_m[0:1, :W], ALU.mult, ALU.mult, tag="ln_mm")
    tt(fw, "dve", row_v[0:1, :W], row_q[0:1, :W], row_v[0:1, :W], ALU.add, tag="ln_v")
    act(fw, row_r[0:1, :W], row_v[0:1, :W], AF.Ln, bias=cst.scalar(eps, 1), tag="ln_ln")
    act(fw, row_r[0:1, :W], row_r[0:1, :W], AF.Exp, scale=cst.scalar(-0.5, 1), tag="ln_rs")
    stt(fw, "dve", row_b[0:1, :W], row_m[0:1, :W], -1.0, row_r[0:1, :W], ALU.mult, ALU.mult, tag="ln_b")
    ps_r = P.take()
    ps_b = P.take()
    mm1(fw, ps_r[:, :W], onerow[0:1, :], row_r[0:1, :W], True, True, tag="ln_repr")
    mm1(fw, ps_b[:, :W], onerow[0:1, :], row_b[0:1, :W], True, True, tag="ln_repb")
    copy(fw, "act", rrep[:, :W], ps_r[:, :W], tag="ln_er")
    copy(fw, "act", brep[:, :W], ps_b[:, :W], tag="ln_eb")
    for x, o in zip(xtiles, out_tiles):
        tt(fw, "dve", sq[:, :W], x[:, src_off:src_off + W], rrep[:, :W], ALU.mult, tag="ln_xr")
        tt(fw, "dve", o[:, dst_off:dst_off + W], sq[:, :W], brep[:, :W], ALU.add, tag="ln_xn")



CONST_SETS = {
    "l0a": {"onecol", "onerow"},
    "l0b_carry": {"sel16"},
    "l0b_full": {"sel16"},
    "l0f": {"onecol", "onerow", "idt", "sel8"},
    "l1": {"onecol", "onecol_bf", "onerow", "idt", "iotaq", "sel8"},
    "l2": {"onecol", "onerow", "ones_row512", "idt", "sel8", "perm"},
    "l3": {"onecol", "onerow", "ones_row512", "idt", "sel6", "perm3"},
}


def setup_consts(fw, cst, only=None):
    def add(name, arr, **kw):
        if only is None or name in only:
            cst.add(name, arr, **kw)
    add("onecol", np.ones((128, 1), np.float32))
    add("onecol_bf", np.ones((128, 1), np.float32), dtype=BF16)
    add("onerow", np.ones((96, 128), np.float32))
    add("ones_row512", np.ones((1, 512), np.float32), dtype=BF16)
    add("idt", np.eye(128, dtype=np.float32))
    add("iotaq", np.broadcast_to(np.arange(512, dtype=np.float32), (128, 512)).copy())
    sel16 = np.zeros((16, 16 * 128), np.float32)
    for n in range(16):
        sel16[n, n * 128:(n + 1) * 128] = 1.0
    add("sel16", sel16, dtype=BF16)
    sel8 = np.zeros((8, 8 * 128), np.float32)
    for e in range(8):
        sel8[e, e * 128:(e + 1) * 128] = 1.0
    add("sel8", sel8, dtype=BF16)
    sel6 = np.zeros((6, 6 * 128), np.float32)
    for e in range(6):
        sel6[e, e * 128:(e + 1) * 128] = 1.0
    add("sel6", sel6, dtype=BF16)
    perm = np.zeros((64, 64), np.float32)
    for i in range(32):
        perm[2 * i + 1, i] = -1.0
    for i in range(32, 64):
        perm[2 * (i - 32), i] = 1.0
    add("perm", perm, dtype=BF16)
    perm3 = np.zeros((64, 64), np.float32)
    for i in range(32):
        perm3[i + 32, i] = -1.0
        perm3[i, i + 32] = 1.0
    add("perm3", perm3, dtype=BF16)
    for v in (1e-5, -0.5, 1.0 / 32.0, 1e-6, 1.0 / 64.0):
        cst.scalar(v, 1)
    fw.onerow = cst.handles.get("onerow")
    if fw.aux_ps is None:
        fw.aux_ps = fw.ps("aux_ps", [1, 512], F32)


IO_SPECS = {
    "l0a": dict(
        ins=dict(xh=([NK, 128, TH], F32), w_in=([D, 4096], F32), convw=([128, CT * 4], F32),
                 convb=([128, CT], F32), wxp=([2048, 96], F32), wdt=([64, 2048], F32),
                 bdt=([128, CT], F32)),
        outs=dict(xn_o=([NK, 128, T], BF16), u_o=([CT, 128, T], BF16),
                  z_o=([CT, 128, T], BF16), delta_o=([CT, 128, T], F32),
                  bmcm_o=([32, T], F32))),
    "l0b_carry": dict(
        ins=dict(delta_i=([CT, 128, T], F32), u_i=([CT, 128, T], BF16),
                 bmcm_i=([32, T], F32), alog=([128, CT * NSTATE], F32)),
        outs=dict(carry_o=([128, CT * NSTATE], F32))),
    "l0b_full": dict(
        ins=dict(delta_i=([CT, 128, T], F32), u_i=([CT, 128, T], BF16),
                 bmcm_i=([32, T], F32), alog=([128, CT * NSTATE], F32),
                 carry_i=([128, CT * NSTATE], F32), z_i=([CT, 128, T], BF16),
                 dskip=([128, CT], F32), wout=([2048, D], F32)),
        outs=dict(s_o=([NK, 128, T], F32))),
    "l0f": dict(
        ins=dict(xn_i=([NK, 128, T], BF16), x_i=([NK, 128, T], F32), s_i=([NK, 128, T], F32),
                 wgred_t=([D, 16], F32), wg_t=([16, 8], F32), w1=([8, D, 4096], F32),
                 b1_t=([128, 8 * 32], F32), w2=([8, 4096, D], F32), b2_t=([128, 8 * 8], F32)),
        outs=dict(x1_o=([NK, 128, T], F32))),
    "l1": dict(
        ins=dict(xf_i=([NK, 128, TF], F32), wq_t=([D, D], F32), wk_t=([D, D], F32),
                 wv_t=([D, D], F32), wo_t=([D, D], F32), colq=([128, 4 * (TF // 128)], F32),
                 gw=([D, 8], F32), gb=([8, 1], F32), ew=([8, D, D], F32), eb=([8, D], F32)),
        outs=dict(x2_o=([NK, 128, T], F32))),
    "l2": dict(
        ins=dict(xf_i=([NK, 128, TF], F32), wq_t=([D, D], F32), bq_c=([128, 8], F32),
                 wk_t=([D, 256], F32), bk_c=([128, 2], F32), wv_t=([D, 256], F32),
                 bv_r=([1, 256], F32), wo_t=([D, D], F32), bo_r=([1, D], F32),
                 cosq=([64, T], F32), sinq=([64, T], F32), cosk=([64, TF], F32),
                 sink=([64, TF], F32), wr_t=([D, 8], F32), br_c=([8, 1], F32),
                 we_t=([8, D, D], F32), be_i=([8, D], F32), fwo_t=([D, D], F32),
                 fbo_r=([1, D], F32)),
        outs=dict(x3_o=([NK, 128, T], F32))),
    "l3": dict(
        ins=dict(xf_i=([NK, 128, TF], F32), wqkv=([D, 3 * D], F32), qs_c=([64, 1], F32),
                 ks_c=([64, 1], F32), wp_t=([D, D], F32), bp_r=([1, D], F32),
                 cosq=([64, T], F32), sinq=([64, T], F32), cosk=([64, TF], F32),
                 sink=([64, TF], F32), bw_t=([D, D], F32), bb_r=([1, D], F32),
                 shA_t=([D, 16], F32), shB_t=([16, D], F32), rA_t=([6, D, 16], F32),
                 rB_t=([6, 16, D], F32), rw_t=([D, 6], F32), rb_c=([6, 1], F32)),
        outs=dict(x4_o=([NK, 128, T], F32))),
}

EMIT_OF = {"l0a": "emit_l0a", "l0b_carry": "emit_l0b", "l0b_full": "emit_l0b",
           "l0f": "emit_l0f", "l1": "emit_l1", "l2": "emit_l2", "l3": "emit_l3"}


def _mk_builder(key):
    def build():
        nc = bass.Bass("TRN2", target_bir_lowering=False)
        fw = Fw(nc)
        cst = Consts(fw)
        setup_consts(fw, cst, CONST_SETS.get(key))
        P = PsumPool(fw, 5)
        spec = IO_SPECS[key]
        IO = {}
        for nm, (shape, dt) in spec["ins"].items():
            IO[nm] = fw.inp(nm, shape, dt)
        for nm, (shape, dt) in spec["outs"].items():
            IO[nm] = fw.outp(nm, shape, dt)
        fn = globals()[EMIT_OF[key]]
        if key.startswith("l0b"):
            fn(fw, cst, P, IO, mode=key.split("_")[1])
        else:
            fn(fw, cst, P, IO)
        fw.finish_outputs()
        fw.finalize()
        return nc, cst
    return build


def emit_l0a(fw, cst, P, IO):

    xh = IO["xh"]
    w_in = IO["w_in"]
    convw = IO["convw"]
    convb = IO["convb"]
    wxp = IO["wxp"]
    wdt = IO["wdt"]
    bdt = IO["bdt"]

    xn_o = IO["xn_o"]
    u_o = IO["u_o"]
    z_o = IO["z_o"]
    delta_o = IO["delta_o"]
    bmcm_o = IO["bmcm_o"]


    win_sb = [fw.sb(f"win{k}", [128, 4096], BF16) for k in range(NK)]
    for k in range(NK):
        fw.dma("pool", win_sb[k][:], w_in[k * 128:(k + 1) * 128, :], tag="ld_win")
    wxp_sb = fw.sb("s_wxp", [128, CT * 96], BF16)
    fw.dma("pool", wxp_sb[:].rearrange("p (c r) -> p c r", c=CT), wxp.rearrange("(c p) r -> p c r", p=128), tag="ld_wxp")
    wdt_sb = fw.sb("s_wdt", [64, 2048], BF16)
    fw.dma("pool", wdt_sb[:], wdt, tag="ld_wdt")
    convw_sb = fw.sb("s_convw", [128, CT * 4], F32)
    fw.dma("sp", convw_sb[:], convw)
    convb_sb = fw.sb("s_convb", [128, CT], F32)
    fw.dma("sp", convb_sb[:], convb)
    bdt_sb = fw.sb("s_bdt", [128, CT], F32)
    fw.dma("sp", bdt_sb[:], bdt)

    xblk = [fw.sb(f"xb{k}", [128, 512], F32) for k in range(NK)]
    xn_sb = [fw.sb(f"xn{k}", [128, TH], BF16) for k in range(NK)]
    ln_tmp = mk_ln_tmp(fw, "t")
    xin_buf = fw.sb("xin", [128, TH], F32)
    cacc_a = fw.sb("cacc_a", [128, T], F32)
    cacc_b = fw.sb("cacc_b", [128, T], F32)
    u_ct = fw.sb("u_ct", [128, T], BF16)
    z_ct = fw.sb("z_ct", [128, T], BF16)
    xdbl = fw.sb("xdbl", [96, T], F32)
    dtlr = fw.sb("dtlr", [64, T], BF16)
    delta_tmp = fw.sb("delta_tmp", [128, T], F32)
    d_abs = fw.sb("d_abs", [128, 512], F32)
    d_m = fw.sb("d_m", [128, 512], F32)

    for (off, W) in BLKS_H:
        for k in range(NK):
            fw.dma("sp", xblk[k][:], xh[k, :, off:off + W], tag="ld_x")
        ln_block(fw, P, cst, xblk, 0, W, xn_sb, off, ln_tmp)

    for k in range(NK):
        fw.dma("sp", xn_o[k], xn_sb[k][:, 512:TH], tag="st_xn")

    for ct in range(CT):
        for (off, W) in BLKS_H:
            p = P.take()
            mm(fw, p[:, :W],
               [(win_sb[k][:, ct * 128:(ct + 1) * 128], xn_sb[k][:, off:off + W]) for k in range(NK)],
               tag="mm_xin")
            copy(fw, "act", xin_buf[:, off:off + W], p[:, :W], tag="ev_xin")
        ts(fw, "dve", cacc_a[:], xin_buf[:, 509:509 + T], convw_sb[:, ct * 4:ct * 4 + 1],
           ALU.mult, tag="conv0")
        stt(fw, "dve", cacc_b[:], xin_buf[:, 510:510 + T], convw_sb[:, ct * 4 + 1:ct * 4 + 2],
            cacc_a[:], ALU.mult, ALU.add, tag="conv1")
        stt(fw, "dve", cacc_a[:], xin_buf[:, 511:511 + T], convw_sb[:, ct * 4 + 2:ct * 4 + 3],
            cacc_b[:], ALU.mult, ALU.add, tag="conv2")
        stt(fw, "dve", cacc_b[:], xin_buf[:, 512:512 + T], convw_sb[:, ct * 4 + 3:ct * 4 + 4],
            cacc_a[:], ALU.mult, ALU.add, tag="conv3")
        act(fw, u_ct[:], cacc_b[:], AF.Silu, bias=convb_sb[:, ct:ct + 1], tag="silu_u")
        fw.dma("sp", u_o[ct], u_ct[:], tag="st_u")
        for (off, W) in BLKS:
            p = P.take()
            mm(fw, p[:, :W],
               [(win_sb[k][:, 2048 + ct * 128:2048 + (ct + 1) * 128],
                 xn_sb[k][:, 512 + off:512 + off + W]) for k in range(NK)],
               tag="mm_z")
            copy(fw, "act", z_ct[:, off:off + W], p[:, :W], tag="ev_z")
        fw.dma("sp", z_o[ct], z_ct[:], tag="st_z")
        for (off, W) in BLKS:
            p = P.take()
            mm1(fw, p[0:96, :W], wxp_sb[:, ct * 96:(ct + 1) * 96], u_ct[:, off:off + W],
                True, True, tag="mm_xdbl")
            if ct == 0:
                copy(fw, "dve", xdbl[:, off:off + W], p[0:96, :W], tag="xdbl_init")
            else:
                tt(fw, "dve", xdbl[:, off:off + W], xdbl[:, off:off + W], p[0:96, :W],
                   ALU.add, tag="xdbl_acc")

    copy(fw, "dve", dtlr[:], xdbl[0:64, :], tag="dtlr")
    fw.dma("sp", bmcm_o, xdbl[64:96, :], tag="st_bmcm")

    for ct in range(CT):
        for (off, W) in BLKS:
            p = P.take()
            mm1(fw, p[:, :W], wdt_sb[:, ct * 128:(ct + 1) * 128], dtlr[:, off:off + W],
                True, True, tag="mm_dt")
            act(fw, d_abs[:, :W], p[:, :W], AF.Abs, bias=bdt_sb[:, ct:ct + 1], tag="dt_abs")
            act(fw, d_abs[:, :W], d_abs[:, :W], AF.Exp, scale=-1.0, tag="dt_exp")
            act(fw, d_abs[:, :W], d_abs[:, :W], AF.Ln, bias=1.0, tag="dt_ln")
            act(fw, d_m[:, :W], p[:, :W], AF.Relu, bias=bdt_sb[:, ct:ct + 1], tag="dt_relu")
            tt(fw, "dve", delta_tmp[:, off:off + W], d_abs[:, :W], d_m[:, :W], ALU.add, tag="dt_add")
        fw.dma("sp", delta_o[ct], delta_tmp[:], tag="st_delta")





def emit_l0b(fw, cst, P, IO, mode):
    full = (mode == "full")

    delta_i = IO["delta_i"]
    u_i = IO["u_i"]
    bmcm_i = IO["bmcm_i"]
    alog = IO["alog"]
    if full:
        carry_i = IO["carry_i"]
        z_i = IO["z_i"]
        dskip = IO["dskip"]
        wout = IO["wout"]
        s_o = IO["s_o"]
    else:
        carry_o = IO["carry_o"]

    sel_sb = cst.handles["sel16"]

    bm_sb = fw.sb("s_bm", [16, T], BF16)
    fw.dma("pool", bm_sb[:], bmcm_i[0:16, :])
    cm_sb = fw.sb("s_cm", [16, T], BF16)
    fw.dma("pool", cm_sb[:], bmcm_i[16:32, :])
    alog_sb = fw.sb("s_alog", [128, CT * NSTATE], F32)
    fw.dma("sp", alog_sb[:], alog)
    A_sb = fw.sb("s_A", [128, CT * NSTATE], F32)
    act(fw, A_sb[:], alog_sb[:], AF.Exp, tag="A_exp")
    ts(fw, "dve", A_sb[:], A_sb[:], -1.0, ALU.mult, tag="A_neg")

    hcarry = fw.sb("hcarry", [128, CT * NSTATE], F32)
    if full:
        fw.dma("sp", hcarry[:], carry_i)
        dskip_sb = fw.sb("s_dskip", [128, CT], F32)
        fw.dma("sp", dskip_sb[:], dskip)
    else:
        memset(fw, "dve", hcarry[:], 0.0)

    NG = 2
    GW = 8 * 512
    brep = fw.sb("brep", [128, NSTATE * 512], BF16)
    crep = fw.sb("crep", [128, NSTATE * 512], BF16) if full else None
    dAb = fw.sb("dAb", [128, GW], F32)
    dBb = fw.sb("dBb", [128, GW], F32)
    hb = fw.sb("hb", [128, GW], F32)
    du = fw.sb("du", [128, 512], F32)
    delta_ct = fw.sb("delta_ct", [128, 512], F32)
    u_blk = fw.sb("u_blk", [128, 512], BF16)
    if full:
        y_all = [fw.sb(f"y{ct}", [128, T], BF16) for ct in range(CT)]
        y_blk = fw.sb("y_blk", [128, 512], F32)
        y_tmp = fw.sb("y_tmp", [128, 512], F32)
        z_blk = fw.sb("z_blk", [128, 512], BF16)
        zs_blk = fw.sb("zs_blk", [128, 512], F32)

    for (off, W) in BLKS:
        for n in range(NSTATE):
            p = P.take()
            mm1(fw, p[:, :W], sel_sb[:, n * 128:(n + 1) * 128], bm_sb[:, off:off + W],
                True, True, tag="rep_b")
            copy(fw, "act", brep[:, n * 512:n * 512 + W], p[:, :W], tag="ev_repb")
            if full:
                p2 = P.take()
                mm1(fw, p2[:, :W], sel_sb[:, n * 128:(n + 1) * 128],
                    cm_sb[:, off:off + W], True, True, tag="rep_c")
                copy(fw, "act", crep[:, n * 512:n * 512 + W], p2[:, :W], tag="ev_repc")
        for ct in range(CT):
            fw.dma("sp", delta_ct[:, :W], delta_i[ct, :, off:off + W], tag="ld_d")
            fw.dma("sp", u_blk[:, :W], u_i[ct, :, off:off + W], tag="ld_u")
            tt(fw, "dve", du[:, :W], delta_ct[:, :W], u_blk[:, :W], ALU.mult, tag="du")
            for g in range(NG):
                for j in range(8):
                    n = g * 8 + j
                    act(fw, dAb[:, j * 512:j * 512 + W], delta_ct[:, :W], AF.Exp,
                        scale=A_sb[:, ct * NSTATE + n:ct * NSTATE + n + 1], tag="dA")
                    tt(fw, "dve", dBb[:, j * 512:j * 512 + W], du[:, :W],
                       brep[:, n * 512:n * 512 + W], ALU.mult, tag="dBu")
                    scan(fw, "dve", hb[:, j * 512:j * 512 + W], dAb[:, j * 512:j * 512 + W],
                         dBb[:, j * 512:j * 512 + W],
                         hcarry[:, ct * NSTATE + n:ct * NSTATE + n + 1], tag="scan")
                hlast = hb[:].rearrange("p (n w) -> p n w", n=8)[:, :, W - 1:W]
                copy(fw, "dve", hcarry[:, ct * NSTATE + g * 8:ct * NSTATE + g * 8 + 8],
                     hlast, tag="carry")
                if full:
                    tt(fw, "pool", dBb[:, :8 * W], hb[:, :8 * W],
                       crep[:, g * GW:g * GW + 8 * W], ALU.mult, tag="hC")
                    red_in = dBb[:].rearrange("p (n w) -> p w n", n=8)[:, :W, :]
                    if g == 0:
                        reduce(fw, "dve", y_blk[:, :W], red_in, tag="yred")
                    else:
                        reduce(fw, "dve", y_tmp[:, :W], red_in, tag="yred2")
                        tt(fw, "pool", y_blk[:, :W], y_blk[:, :W], y_tmp[:, :W], ALU.add, tag="yacc")
            if full:
                fw.dma("sp", z_blk[:, :W], z_i[ct, :, off:off + W], tag="ld_z")
                act(fw, zs_blk[:, :W], z_blk[:, :W], AF.Silu, tag="silu_z")
                stt(fw, "dve", y_blk[:, :W], u_blk[:, :W], dskip_sb[:, ct:ct + 1], y_blk[:, :W],
                    ALU.mult, ALU.add, tag="yD")
                tt(fw, "dve", y_all[ct][:, off:off + W], y_blk[:, :W], zs_blk[:, :W],
                   ALU.mult, tag="ygate")

    if not full:
        fw.dma("sp", carry_o, hcarry[:], tag="st_carry")
    else:
        wv = [dAb[:].bitcast(BF16), dBb[:].bitcast(BF16)]

        def wtile(ct):
            v = wv[ct // 8]
            return v[:, (ct % 8) * D:(ct % 8 + 1) * D]
        s_out_buf = fw.sb("s_out", [128, T], F32)
        for ct in range(CT):
            fw.dma("pool", wtile(ct), wout[ct * 128:(ct + 1) * 128, :], tag="ld_wout")
        for ot in range(NK):
            for (off, W) in BLKS:
                p = P.take()
                mm(fw, p[:, :W],
                   [(wtile(ct)[:, ot * 128:(ot + 1) * 128], y_all[ct][:, off:off + W])
                    for ct in range(CT)], tag="mm_s")
                copy(fw, "act", s_out_buf[:, off:off + W], p[:, :W], tag="ev_s")
            fw.dma("sp", s_o[ot], s_out_buf[:], tag="st_s")





def emit_l0f(fw, cst, P, IO):

    xn_i = IO["xn_i"]
    x_i = IO["x_i"]
    s_i = IO["s_i"]
    wgred_t = IO["wgred_t"]
    wg_t = IO["wg_t"]
    w1 = IO["w1"]
    b1_t = IO["b1_t"]
    w2 = IO["w2"]
    b2_t = IO["b2_t"]
    x1_o = IO["x1_o"]

    onecol = cst.handles["onecol"]
    onerow = cst.handles["onerow"]
    idt = cst.handles["idt"]
    sel_sb = cst.handles["sel8"]

    xn_sb = [fw.sb(f"s_xn{k}", [128, T], BF16) for k in range(NK)]
    for k in range(NK):
        fw.dma("sp", xn_sb[k][:], xn_i[k], tag="ld_xn")
    b1_sb = fw.sb("s_b1", [128, 8 * 32], F32)
    fw.dma("sp", b1_sb[:], b1_t)
    b2_sb = fw.sb("s_b2", [128, 8 * 8], F32)
    fw.dma("sp", b2_sb[:], b2_t)

    wgred_sb = fw.sb("s_wgred", [128, NK * 16], BF16)
    fw.dma("pool", wgred_sb[:].rearrange("p (k g) -> p k g", k=NK),
           wgred_t.rearrange("(k p) g -> p k g", p=128), tag="ld_wgred")
    wgt_sb = fw.sb("s_wgt", [16, 8], F32)
    fw.dma("sp", wgt_sb[:], wg_t)
    xr_sb = fw.sb("s_xr", [16, T], BF16)
    for (off, W) in BLKS:
        p = P.take()
        mm(fw, p[0:16, :W], [(wgred_sb[:, k * 16:(k + 1) * 16], xn_sb[k][:, off:off + W])
                             for k in range(NK)], tag="mm_xr")
        copy(fw, "act", xr_sb[:, off:off + W], p[0:16, :W], tag="ev_xr")
    wg_sq = fw.sb("wg_sq", [16, 8], F32)
    nrm = fw.sb("nrm", [1, 8], F32)
    nrm_rep = fw.sb("nrm_rep", [16, 8], F32)
    wgu_bf = fw.sb("wgu_bf", [16, 8], BF16)
    act(fw, wg_sq[:], wgt_sb[:], AF.Square, tag="wg_sq")
    p = P.take()
    mm1(fw, p[0:1, 0:8], onecol[0:16, 0:1], wg_sq[:], True, True, tag="wg_ss")
    act(fw, nrm[:], p[0:1, 0:8], AF.Sqrt, tag="wg_sqrt")
    ts(fw, "dve", nrm[:], nrm[:], 1e-4, ALU.max, tag="wg_max")
    recip(fw, nrm[:], nrm[:], tag="wg_rec")
    p = P.take()
    mm1(fw, p[0:16, 0:8], onerow[0:1, 0:16], nrm[:], True, True, tag="wg_rep")
    copy(fw, "act", nrm_rep[:], p[0:16, 0:8], tag="wg_evr")
    tt(fw, "dve", wgu_bf[:], wgt_sb[:], nrm_rep[:], ALU.mult, tag="wg_div")
    lg_sb = fw.sb("lg", [8, T], F32)
    for (off, W) in BLKS:
        p = P.take()
        mm1(fw, p[0:8, :W], wgu_bf[:], xr_sb[:, off:off + W], True, True, tag="mm_lg")
        copy(fw, "act", lg_sb[:, off:off + W], p[0:8, :W], tag="ev_lg")
    ms_fm = fw.sb("ms_fm", [8, T], BF16)
    tm_g = fw.sb("tm_g", [128, 8], F32)
    tm_e = fw.sb("tm_e", [128, 8], F32)
    tm_r = fw.sb("tm_r", [128, 1], F32)
    for i in range(T // 128):
        p = P.take()
        transpose_pe(fw, p[:, 0:8], lg_sb[:, i * 128:(i + 1) * 128], idt[0:8, 0:8], tag="tr_lg")
        copy(fw, "dve", tm_g[:], p[:, 0:8], tag="ev_tg")
        reduce(fw, "dve", tm_r[:], tm_g[:], op=ALU.max, negate=True, tag="g_max")
        act(fw, tm_e[:], tm_g[:], AF.Exp, bias=tm_r[:, 0:1], tag="g_exp")
        reduce(fw, "dve", tm_r[:], tm_e[:], op=ALU.add, tag="g_sum")
        recip(fw, tm_r[:], tm_r[:], tag="g_rec")
        ts(fw, "dve", tm_e[:], tm_e[:], tm_r[:, 0:1], ALU.mult, tag="g_nrm")
        reduce(fw, "dve", tm_r[:], tm_e[:], op=ALU.max, tag="g_m")
        ts(fw, "dve", tm_g[:], tm_e[:], tm_r[:, 0:1], ALU.is_equal, tag="g_msk")
        ts(fw, "dve", tm_g[:], tm_g[:], tm_r[:, 0:1], ALU.mult, tag="g_msc")
        p = P.take()
        transpose_pe(fw, p[0:8, 0:128], tm_g[:], idt[:], tag="tr_ms")
        copy(fw, "dve", ms_fm[:, i * 128:(i + 1) * 128], p[0:8, 0:128], tag="ev_ms")

    TG = 1024
    msrep = fw.sb("msrep", [128, T], BF16)
    h_buf = [fw.sb(f"hb{s}", [128, 32 * 1024], BF16) for s in range(1)]
    out_acc = [fw.sb(f"oacc{o}", [128, T], BF16) for o in range(NK)]
    w1buf = [fw.sb(f"w1b{i}", [128, NK * 128], BF16) for i in range(3)]
    w2buf = [fw.sb(f"w2b{i}", [128, 32 * 128], BF16) for i in range(2)]
    t_eo = [fw.sb(f"t_eo{s}", [128, 512], BF16) for s in range(2)]
    wslot = [0]
    w2slot = [0]

    for e in range(8):
        for (off, W) in BLKS:
            p = P.take()
            mm1(fw, p[:, :W], sel_sb[:, e * 128:(e + 1) * 128], ms_fm[:, off:off + W],
                True, True, tag="rep_ms")
            copy(fw, "act", msrep[:, off:off + W], p[:, :W], tag="ev_msr")
        for g in range(T // TG):
            g0 = g * TG
            hb = h_buf[0]
            for ht in range(32):
                wb = w1buf[wslot[0] % 3]; wslot[0] += 1
                fw.dma("pool", wb[:].rearrange("p (k h) -> p k h", k=NK),
                       w1.rearrange("e (k p) h -> e p k h", p=128)[e, :, :, ht * 128:(ht + 1) * 128],
                       tag="ld_w1")
                for sb_ in range(2):
                    o2 = g0 + sb_ * 512
                    p = P.take()
                    mm(fw, p[:],
                       [(wb[:, k * 128:(k + 1) * 128], xn_sb[k][:, o2:o2 + 512]) for k in range(NK)],
                       tag="mm_h")
                    act(fw, hb[:, ht * 1024 + sb_ * 512:ht * 1024 + (sb_ + 1) * 512], p[:], AF.Gelu,
                        bias=b1_sb[:, e * 32 + ht:e * 32 + ht + 1], tag="gelu")
            for ot in range(NK):
                wb = w2buf[w2slot[0] % 2]; w2slot[0] += 1
                fw.dma("pool", wb[:].rearrange("p (c o) -> p c o", c=32),
                       w2.rearrange("e (c p) o -> e p c o", p=128)[e, :, :, ot * 128:(ot + 1) * 128],
                       tag="ld_w2")
                for sb_ in range(2):
                    o2 = g0 + sb_ * 512
                    p = P.take()
                    mm(fw, p[:],
                       [(wb[:, c * 128:(c + 1) * 128], hb[:, c * 1024 + sb_ * 512:c * 1024 + (sb_ + 1) * 512])
                        for c in range(32)], tag="mm_o")
                    if e == 0:
                        stt(fw, "dve", out_acc[ot][:, o2:o2 + 512], p[:],
                            b2_sb[:, ot:ot + 1], msrep[:, o2:o2 + 512],
                            ALU.add, ALU.mult, tag="eo0")
                    else:
                        te = t_eo[ot % 2]
                        stt(fw, "dve", te[:], p[:],
                            b2_sb[:, e * 8 + ot:e * 8 + ot + 1], msrep[:, o2:o2 + 512],
                            ALU.add, ALU.mult, tag="eo")
                        tt(fw, "dve", out_acc[ot][:, o2:o2 + 512], out_acc[ot][:, o2:o2 + 512],
                           te[:], ALU.add, tag="eacc")

    g_sl = fw.sb("g_sl", [128, 512], F32)
    x_sl = fw.sb("x_sl", [128, 512], F32)
    s_sl = fw.sb("s_sl", [128, 512], F32)
    x1_blk = fw.sb("x1_blk", [128, 512], F32)
    for ot in range(NK):
        for (off, W) in BLKS:
            act(fw, g_sl[:, :W], out_acc[ot][:, off:off + W], AF.Silu, tag="silu_g")
            fw.dma("sp", x_sl[:, :W], x_i[ot, :, off:off + W], tag="ld_x")
            fw.dma("sp", s_sl[:, :W], s_i[ot, :, off:off + W], tag="ld_s")
            tt(fw, "dve", g_sl[:, :W], s_sl[:, :W], g_sl[:, :W], ALU.mult, tag="sg")
            tt(fw, "dve", x1_blk[:, :W], x_sl[:, :W], g_sl[:, :W], ALU.add, tag="x1")
            fw.dma("sp", x1_o[ot, :, off:off + W], x1_blk[:, :W], tag="st_x1")




def emit_l1(fw, cst, P, IO):

    xf_i = IO["xf_i"]
    wq_t = IO["wq_t"]
    wk_t = IO["wk_t"]
    wv_t = IO["wv_t"]
    wo_t = IO["wo_t"]
    colq = IO["colq"]
    gw = IO["gw"]
    gb = IO["gb"]
    ew = IO["ew"]
    eb = IO["eb"]
    x2_o = IO["x2_o"]

    v_d = fw.dram("v_spill", [TF // 128, 128, D], BF16)
    p_d = fw.dram("p_spill", [4, 128, (TF // 128) * 512], BF16)
    s1_d = fw.dram("s1_spill", [NK, 128, T], F32)

    onecol = cst.handles["onecol"]
    onecol_bf = cst.handles["onecol_bf"]
    onerow = cst.handles["onerow"]
    iotaq = cst.handles["iotaq"]
    NKT = TF // 128

    ln_tmp = mk_ln_tmp(fw, "t")
    xblk = [fw.sb(f"xb{k}", [128, 512], F32) for k in range(NK)]
    xn_blk = [fw.sb(f"xnb{k}", [128, 512], BF16) for k in range(NK)]
    sqt = fw.sb("sqt", [128, 512], F32)
    colq_sb = fw.sb("s_colq", [128, 4 * NKT], F32)
    fw.dma("sp", colq_sb[:], colq)

    def ln_into(cols_off, W=512):
        for k in range(NK):
            fw.dma("sp", xblk[k][:, :W], xf_i[k, :, cols_off:cols_off + W], tag="ld_x")
        ln_block(fw, P, cst, xblk, 0, W, xn_blk, 0, ln_tmp)

    k2max = fw.sb("k2max", [1, 1], F32)
    rden = fw.sb("rden", [1, 4 * 512], F32)

    K_sb = [fw.sb(f"K{i}", [128, TF], BF16) for i in range(NK)]
    wkT = [fw.sb(f"s_wk{k}", [128, D], BF16) for k in range(NK)]
    wvT = [fw.sb(f"s_wv{k}", [128, D], BF16) for k in range(NK)]
    for k in range(NK):
        fw.dma("pool", wkT[k][:], wk_t[k * 128:(k + 1) * 128, :], tag="ld_wk")
        fw.dma("pool", wvT[k][:], wv_t[k * 128:(k + 1) * 128, :], tag="ld_wv")
    v_tile = fw.sb("v_tile", [128, D], BF16)
    k2psum = fw.aux_ps
    k2row = fw.sb("k2row", [1, TF], F32)
    for bi in range(TF // 512):
        off = bi * 512
        ln_into(off)
        for ot in range(NK):
            p = P.take()
            mm(fw, p[:], [(wkT[k][:, ot * 128:(ot + 1) * 128], xn_blk[k][:]) for k in range(NK)],
               tag="mm_k")
            copy(fw, "act", K_sb[ot][:, off:off + 512], p[:], tag="ev_k")
        for st_ in range(4):
            to = st_ * 128
            for half in range(2):
                p = P.take()
                mm(fw, p[:], [(xn_blk[k][:, to:to + 128], wvT[k][:, half * 512:(half + 1) * 512])
                              for k in range(NK)], tag="mm_v")
                copy(fw, "act", v_tile[:, half * 512:(half + 1) * 512], p[:], tag="ev_v")
            fw.dma("sp", v_d[(off + to) // 128], v_tile[:], tag="st_v")
    for bi in range(TF // 512):
        off = bi * 512
        for i in range(NK):
            act(fw, sqt[:], K_sb[i][:, off:off + 512], AF.Square, tag="k2sq")
            mm1(fw, k2psum[:], onecol[:, 0:1], sqt[:], i == 0, i == NK - 1, tag="k2mm")
        copy(fw, "dve", k2row[0:1, off:off + 512], k2psum[:], tag="k2ev")
    reduce(fw, "dve", k2max[:], k2row[0:1, :], op=ALU.max, tag="k2max")

    fw.barrier()
    fw.free_sb(*[f"s_wk{k}" for k in range(NK)], *[f"s_wv{k}" for k in range(NK)],
               "v_tile", "k2row")

    wqT = [fw.sb(f"s_wq{k}", [128, D], BF16) for k in range(NK)]
    for k in range(NK):
        fw.dma("pool", wqT[k][:], wq_t[k * 128:(k + 1) * 128, :], tag="ld_wq")
    q_blk = [fw.sb(f"qt{k}", [128, 512], BF16) for k in range(NK)]
    Psb = fw.sb("Psb", [128, NKT * 512], BF16)
    negc = fw.sb("negc", [1, 512], F32)
    q2 = fw.sb("q2", [1, 512], F32)
    etmp = fw.sb("etmp", [128, 512], BF16)
    mtmp = fw.sb("mtmp", [128, 512], BF16)
    den = fw.sb("den", [1, 4 * 512], F32)
    denp = fw.aux_ps
    for qb in range(4):
        qoff = qb * 512
        ln_into(qoff)
        for k in range(NK):
            p = P.take()
            mm(fw, p[:], [(wqT[i][:, k * 128:(k + 1) * 128], xn_blk[i][:]) for i in range(NK)],
               tag="mm_q")
            act(fw, q_blk[k][:], p[:], AF.Identity, scale=cst.scalar(1.0 / 32.0, 128), tag="ev_q")
        p = P.take()
        for k in range(NK):
            act(fw, sqt[:], q_blk[k][:], AF.Square, tag="q2sq")
            mm1(fw, p[0:1, :], onecol[:, 0:1], sqt[:], k == 0, k == NK - 1, tag="q2mm")
        copy(fw, "dve", q2[:], p[0:1, :], tag="q2ev")
        act(fw, negc[:], q2[:], AF.Sqrt, scale=k2max[0:1, 0:1], tag="csqrt")
        ts(fw, "dve", negc[:], negc[:], -1.0, ALU.mult, tag="cneg")
        for kt in range(NKT):
            p = P.take()
            for i in range(NK):
                mm1(fw, p[:], K_sb[i][:, kt * 128:(kt + 1) * 128], q_blk[i][:],
                    i == 0, False, tag="mm_s")
            mm1(fw, p[:], onerow[0:1, :], negc[0:1, :], False, True, tag="mm_c")
            act(fw, etmp[:], p[:], AF.Exp, tag="p_exp")
            ts(fw, "dve", mtmp[:], iotaq[:], colq_sb[:, (qb * NKT + kt):(qb * NKT + kt) + 1],
               ALU.is_ge, tag="p_msk")
            tt(fw, "dve", Psb[:, kt * 512:(kt + 1) * 512], etmp[:], mtmp[:], ALU.mult, tag="p_m")
            mm1(fw, denp[:], onecol_bf[:, 0:1], Psb[:, kt * 512:(kt + 1) * 512],
                kt == 0, kt == NKT - 1, tag="mm_den")
        copy(fw, "dve", den[0:1, qoff:qoff + 512], denp[:], tag="ev_den")
        fw.dma("sp", p_d[qb], Psb[:], tag="st_p")
    recip(fw, rden[:], den[:], tag="rden")

    fw.barrier()
    fw.free_sb(*[f"K{i}" for i in range(NK)], *[f"s_wq{k}" for k in range(NK)],
               *[f"qt{k}" for k in range(NK)], "negc", "q2", "den", "Psb", "etmp", "mtmp")

    V_sb = [fw.sb(f"V{i}", [128, D], BF16) for i in range(NKT)]
    for i in range(NKT):
        fw.dma("sp", V_sb[i][:], v_d[i], tag="ld_v")
    woT = [fw.sb(f"s_wo{k}", [128, D], BF16) for k in range(NK)]
    for k in range(NK):
        fw.dma("pool", woT[k][:], wo_t[k * 128:(k + 1) * 128, :], tag="ld_wo")
    attn = [fw.sb(f"at{k}", [128, 512], BF16) for k in range(NK)]
    rrep = fw.sb("rrep", [128, 512], F32)
    s1_blk = fw.sb("s1_blk", [128, 512], F32)
    Psb = fw.sb("Psb2", [128, NKT * 512], BF16)
    for qb in range(4):
        fw.dma("sp", Psb[:], p_d[qb], tag="ld_p")
        p = P.take()
        mm1(fw, p[:], onerow[0:1, :], rden[0:1, qb * 512:(qb + 1) * 512], True, True, tag="mm_rr")
        copy(fw, "act", rrep[:], p[:], tag="ev_rr")
        for dt in range(NK):
            p = P.take()
            mm(fw, p[:], [(V_sb[kt][:, dt * 128:(dt + 1) * 128], Psb[:, kt * 512:(kt + 1) * 512])
                          for kt in range(NKT)], tag="mm_pv")
            tt(fw, "dve", attn[dt][:], p[:], rrep[:], ALU.mult, tag="pv_nrm")
        for ot in range(NK):
            p = P.take()
            mm(fw, p[:], [(woT[k][:, ot * 128:(ot + 1) * 128], attn[k][:]) for k in range(NK)],
               tag="mm_wo")
            copy(fw, "act", s1_blk[:], p[:], tag="ev_s1")
            fw.dma("sp", s1_d[ot, :, qb * 512:(qb + 1) * 512], s1_blk[:], tag="st_s1")

    fw.barrier()
    fw.free_sb(*[f"V{i}" for i in range(NKT)], *[f"s_wo{k}" for k in range(NK)],
               *[f"at{k}" for k in range(NK)], "Psb2", "rrep", "s1_blk", "rden")

    idt = cst.handles["idt"]
    sel_sb = cst.handles["sel8"]
    gw_sb = fw.sb("s_gw", [128, NK * 8], BF16)
    fw.dma("pool", gw_sb[:].rearrange("p (k g) -> p k g", k=NK),
           gw.rearrange("(k p) g -> p k g", p=128), tag="ld_gw")
    gb_sb = fw.sb("s_gb", [8, 1], F32)
    fw.dma("sp", gb_sb[:], gb)
    eb_sb = fw.sb("s_eb", [8, D], BF16)
    fw.dma("pool", eb_sb[:], eb)
    lgb = fw.sb("lgb", [8, 512], F32)
    tm_g = fw.sb("tm_g", [128, 8], F32)
    tm_e = fw.sb("tm_e", [128, 8], F32)
    tm_r = fw.sb("tm_r", [128, 1], F32)
    wfm = fw.sb("wfm", [8, 512], BF16)
    wrep_all = fw.sb("wrep_all", [128, 8 * 512], BF16)
    xnw = [fw.sb(f"xnw{k}", [128, 512], BF16) for k in range(NK)]
    wexp = [fw.sb(f"wexp{i}", [128, NK * 128], BF16) for i in range(2)]
    out_acc = [fw.sb(f"oacc{o}", [128, 512], F32) for o in range(NK)]
    g_sl = fw.sb("g_sl", [128, 512], F32)
    x_sl = fw.sb("x_sl", [128, 512], F32)
    s_sl = fw.sb("s_sl", [128, 512], F32)
    x2_blk = fw.sb("x2_blk", [128, 512], F32)
    wslot = [0]
    for qb in range(4):
        qoff = qb * 512
        ln_into(qoff)
        p = P.take()
        mm(fw, p[0:8, :], [(gw_sb[:, k * 8:(k + 1) * 8], xn_blk[k][:]) for k in range(NK)],
           tag="mm_lg")
        act(fw, lgb[:], p[0:8, :], AF.Identity, bias=gb_sb[0:8, 0:1], tag="ev_lg")
        for i in range(4):
            p = P.take()
            transpose_pe(fw, p[:, 0:8], lgb[:, i * 128:(i + 1) * 128], idt[0:8, 0:8], tag="tr_lg")
            copy(fw, "dve", tm_g[:], p[:, 0:8], tag="ev_tg")
            reduce(fw, "dve", tm_r[:], tm_g[:], op=ALU.max, negate=True, tag="f_max")
            act(fw, tm_e[:], tm_g[:], AF.Exp, bias=tm_r[:, 0:1], tag="f_exp")
            reduce(fw, "dve", tm_r[:], tm_e[:], op=ALU.add, tag="f_sum")
            recip(fw, tm_r[:], tm_r[:], tag="f_rec")
            ts(fw, "dve", tm_e[:], tm_e[:], tm_r[:, 0:1], ALU.mult, tag="f_nrm")
            ts(fw, "dve", tm_g[:], tm_e[:], 0.1, ALU.is_ge, tag="f_sel")
            tt(fw, "dve", tm_e[:], tm_e[:], tm_g[:], ALU.mult, tag="f_wm")
            reduce(fw, "dve", tm_r[:], tm_e[:], op=ALU.add, tag="f_ws")
            recip(fw, tm_r[:], tm_r[:], tag="f_wr")
            ts(fw, "dve", tm_e[:], tm_e[:], tm_r[:, 0:1], ALU.mult, tag="f_wn")
            p = P.take()
            transpose_pe(fw, p[0:8, 0:128], tm_e[:], idt[:], tag="tr_w")
            copy(fw, "dve", wfm[:, i * 128:(i + 1) * 128], p[0:8, 0:128], tag="ev_w")
        for e in range(8):
            p = P.take()
            mm1(fw, p[:], sel_sb[:, e * 128:(e + 1) * 128], wfm[:], True, True, tag="mm_wr")
            copy(fw, "act", wrep_all[:, e * 512:(e + 1) * 512], p[:], tag="ev_wr")
        for e in range(8):
            for k in range(NK):
                tt(fw, "dve", xnw[k][:], xn_blk[k][:], wrep_all[:, e * 512:(e + 1) * 512],
                   ALU.mult, tag="xnw")
            for ot in range(NK):
                wb = wexp[wslot[0] % 2]; wslot[0] += 1
                fw.dma("pool", wb[:].rearrange("p (k o) -> p k o", k=NK),
                       ew.rearrange("e (k p) o -> e p k o", p=128)[e, :, :, ot * 128:(ot + 1) * 128],
                       tag="ld_ew")
                p = P.take()
                pairs = [(wb[:, k * 128:(k + 1) * 128], xnw[k][:]) for k in range(NK)]
                for i, (l, r) in enumerate(pairs):
                    mm1(fw, p[:], l, r, i == 0, (e != 0 and i == len(pairs) - 1), tag="mm_ex")
                if e == 0:
                    mm1(fw, p[:], eb_sb[:, ot * 128:(ot + 1) * 128], wfm[:],
                        False, True, tag="mm_eb")
                if e == 0:
                    copy(fw, "dve", out_acc[ot][:], p[:], tag="eo0")
                else:
                    tt(fw, "dve", out_acc[ot][:], out_acc[ot][:], p[:], ALU.add, tag="eo")
        for ot in range(NK):
            act(fw, g_sl[:], out_acc[ot][:], AF.Silu, tag="silu_g")
            fw.dma("sp", x_sl[:], xf_i[ot, :, qoff:qoff + 512], tag="ld_xr")
            fw.dma("sp", s_sl[:], s1_d[ot, :, qoff:qoff + 512], tag="ld_s1")
            tt(fw, "dve", g_sl[:], s_sl[:], g_sl[:], ALU.mult, tag="sg")
            tt(fw, "dve", x2_blk[:], x_sl[:], g_sl[:], ALU.add, tag="x2")
            fw.dma("sp", x2_o[ot, :, qoff:qoff + 512], x2_blk[:], tag="st_x2")





def attn_core(fw, P, q_heads, k_heads, kv_of, V_tm, attn_tiles, Psb, rrep, r_row,
              NKT, n_heads, tag=""):
    for h in range(n_heads):
        hv = kv_of(h)
        kh = k_heads[hv]
        for qb in range(4):
            q_ap = q_heads[h][:, qb * 512:(qb + 1) * 512]
            for kt in range(NKT):
                p = P.take()
                mm1(fw, p[:], kh[:, kt * 128:(kt + 1) * 128], q_ap, True, True, tag=tag + "mm_s")
                act(fw, Psb[:, kt * 512:(kt + 1) * 512], p[:], AF.Exp, tag=tag + "p_exp")
            p = P.take()
            mm(fw, p[0:65, :], [(V_tm[kt][:, hv * 65:(hv + 1) * 65],
                                 Psb[:, kt * 512:(kt + 1) * 512]) for kt in range(NKT)],
               tag=tag + "mm_pv")
            recip(fw, r_row[0:1, :], p[64:65, :], tag=tag + "rden")
            p2 = P.take()
            mm1(fw, p2[0:64, :], fw.onerow[0:1, 0:64], r_row[0:1, :], True, True, tag=tag + "mm_rr")
            copy(fw, "act", rrep[0:64, :], p2[0:64, :], tag=tag + "ev_rr")
            dst = attn_tiles[(h // 2, qb)][(h % 2) * 64:(h % 2) * 64 + 64, :]
            fw.comp("dve", lambda hh, dst=dst, p=p: hh.tensor_tensor(
                out=dst, in0=p[0:64, :], in1=rrep[0:64, :], op=ALU.mult),
                reads=[p[0:64, :], rrep[0:64, :]], writes=[dst], tag=tag + "pv_nrm")



def emit_l2(fw, cst, P, IO):

    xf_i = IO["xf_i"]
    wq_t = IO["wq_t"]
    bq_c = IO["bq_c"]
    wk_t = IO["wk_t"]
    bk_c = IO["bk_c"]
    wv_t = IO["wv_t"]
    bv_r = IO["bv_r"]
    wo_t = IO["wo_t"]
    bo_r = IO["bo_r"]
    cosq = IO["cosq"]
    sinq = IO["sinq"]
    cosk = IO["cosk"]
    sink = IO["sink"]
    wr_t = IO["wr_t"]
    br_c = IO["br_c"]
    we_t = IO["we_t"]
    be_i = IO["be_i"]
    fwo_t = IO["fwo_t"]
    fbo_r = IO["fbo_r"]
    x3_o = IO["x3_o"]

    s2_d = fw.dram("s2_spill", [NK, 128, T], F32)
    onecol = cst.handles["onecol"]
    onerow = cst.handles["onerow"]
    ones512 = cst.handles["ones_row512"]
    perm_sb = cst.handles["perm"]
    NKT = TF // 128

    attn_tiles = {}
    for pair in range(8):
        for qb in range(4):
            attn_tiles[(pair, qb)] = fw.sb(f"atn{pair}_{qb}", [128, 512], BF16)
    k_heads = [fw.sb(f"kh{v}", [65, TF], BF16) for v in range(4)]
    V_tm = [fw.sb(f"vt{i}", [128, 4 * 65], BF16) for i in range(NKT)]
    k2m = fw.sb("k2m", [1, 4], F32)
    q_d = fw.dram("q2_spill", [16, 64, T], BF16)

    ln_tmp = mk_ln_tmp(fw, "t")
    xblk = [fw.sb(f"xb{k}", [128, 512], F32) for k in range(NK)]
    xn_blk = [fw.sb(f"xnb{k}", [128, 512], BF16) for k in range(NK)]
    sqt = ln_tmp[0]

    def ln_into(cols_off, W=512):
        for k in range(NK):
            fw.dma("sp", xblk[k][:, :W], xf_i[k, :, cols_off:cols_off + W], tag="ld_x")
        ln_block(fw, P, cst, xblk, 0, W, xn_blk, 0, ln_tmp)

    wqs = [fw.sb(f"s_wqs{i}", [128, NK * 128], BF16) for i in range(2)]
    wqslot = [0]
    wkT = fw.sb("s_wkT", [128, 8 * 256], BF16)
    wvT = fw.sb("s_wvT", [128, 8 * 256], BF16)
    fw.dma("pool", wkT[:].rearrange("p (k o) -> p k o", k=NK),
           wk_t.rearrange("(k p) o -> p k o", p=128), tag="ld_wk")
    fw.dma("pool", wvT[:].rearrange("p (k o) -> p k o", k=NK),
           wv_t.rearrange("(k p) o -> p k o", p=128), tag="ld_wv")
    bq_sb = fw.sb("s_bq", [128, 8], F32)
    fw.dma("sp", bq_sb[:], bq_c)
    bk_sb = fw.sb("s_bk", [128, 2], F32)
    fw.dma("sp", bk_sb[:], bk_c)
    bv_sb = fw.sb("s_bv", [1, 256], BF16)
    fw.dma("pool", bv_sb[:], bv_r)
    tb_cos = fw.sb("tb_cos", [64, 512], BF16)
    tb_sin = fw.sb("tb_sin", [64, 512], BF16)
    NSET = 4
    rsets = [dict(pre_h=fw.sb(f"pre_h{s}", [64, 512], BF16),
                  rop1=fw.sb(f"rop1{s}", [64, 512], F32),
                  rop2=fw.sb(f"rop2{s}", [64, 512], F32),
                  qout=fw.sb(f"qout{s}", [64, 512], BF16)) for s in range(NSET)]
    rturn = [0]
    vtile = [fw.sb(f"vtile{s}", [128, 256], BF16) for s in range(2)]
    vturn = [0]

    for hv in range(4):
        memset(fw, "dve", k_heads[hv][64:65, :], 1.0, tag="kones")
    for bi in range(NKT // 4):
        off = bi * 512
        own = off < T
        ln_into(off)
        fw.dma("pool", tb_cos[:], cosk[:, off:off + 512], tag="ld_ck")
        fw.dma("pool", tb_sin[:], sink[:, off:off + 512], tag="ld_sk")
        for mt in range(2):
            p = P.take()
            mm(fw, p[:], [(wkT[:, k * 256 + mt * 128:k * 256 + (mt + 1) * 128], xn_blk[k][:])
                          for k in range(NK)], tag="mm_kv")
            for sub in range(2):
                hv = mt * 2 + sub
                B = rsets[rturn[0] % NSET]; rturn[0] += 1
                pre_h, rop1, rop2 = B["pre_h"], B["rop1"], B["rop2"]
                act(fw, pre_h[:], p[sub * 64:(sub + 1) * 64, :], AF.Identity,
                    bias=bk_sb[sub * 64:(sub + 1) * 64, mt:mt + 1], tag="ev_kh")
                p2 = P.take()
                mm1(fw, p2[0:64, :], perm_sb[:], pre_h[:], True, True, tag="mm_rot")
                tt(fw, "dve", rop1[:], pre_h[:], tb_cos[:], ALU.mult, tag="rp1")
                fw.comp("dve", lambda h, p2=p2, rop2=rop2: h.tensor_tensor(
                    out=rop2[:], in0=p2[0:64, :], in1=tb_sin[:], op=ALU.mult),
                    reads=[p2[0:64, :], tb_sin[:]], writes=[rop2[:]], tag="rp2")
                tt(fw, "dve", k_heads[hv][0:64, off:off + 512], rop1[:], rop2[:],
                   ALU.add, tag="rp3")
        for st_ in range(4):
            to = st_ * 128
            p = P.take()
            pairs = [(xn_blk[k][:, to:to + 128], wvT[:, k * 256:(k + 1) * 256]) for k in range(NK)]
            for i, (l, r) in enumerate(pairs):
                mm1(fw, p[:, 0:256], l, r, i == 0, False, tag="mm_v")
            mm1(fw, p[:, 0:256], ones512[0:1, 0:128], bv_sb[0:1, :], False, True, tag="mm_vb")
            vt = vtile[vturn[0] % 2]; vturn[0] += 1
            copy(fw, "act", vt[:], p[:, 0:256], tag="ev_v")
            kt = (off + to) // 128
            for hv in range(4):
                copy(fw, "dve", V_tm[kt][:, hv * 65:hv * 65 + 64], vt[:, hv * 64:(hv + 1) * 64],
                     tag="v_sp")
            memset(fw, "dve", V_tm[kt][:].rearrange("p (g c) -> p g c", g=4)[:, :, 64:65],
                   1.0, tag="v_ones")
        if own:
            fw.dma("pool", tb_cos[:], cosq[:, off:off + 512], tag="ld_cq")
            fw.dma("pool", tb_sin[:], sinq[:, off:off + 512], tag="ld_sq")
            for mt in range(NK):
                wb = wqs[wqslot[0] % 2]; wqslot[0] += 1
                fw.dma("pool", wb[:].rearrange("p (k o) -> p k o", k=NK),
                       wq_t.rearrange("(k p) o -> p k o", p=128)[:, :, mt * 128:(mt + 1) * 128],
                       tag="ld_wq")
                p = P.take()
                mm(fw, p[:], [(wb[:, k * 128:(k + 1) * 128], xn_blk[k][:])
                              for k in range(NK)], tag="mm_q")
                for sub in range(2):
                    h_ = mt * 2 + sub
                    B = rsets[rturn[0] % NSET]; rturn[0] += 1
                    pre_h, rop1, rop2, qout = B["pre_h"], B["rop1"], B["rop2"], B["qout"]
                    act(fw, pre_h[:], p[sub * 64:(sub + 1) * 64, :], AF.Identity,
                        bias=bq_sb[sub * 64:(sub + 1) * 64, mt:mt + 1], tag="ev_qh")
                    p2 = P.take()
                    mm1(fw, p2[0:64, :], perm_sb[:], pre_h[:], True, True, tag="mm_rotq")
                    tt(fw, "dve", rop1[:], pre_h[:], tb_cos[:], ALU.mult, tag="rq1")
                    fw.comp("dve", lambda h, p2=p2, rop2=rop2: h.tensor_tensor(
                        out=rop2[:], in0=p2[0:64, :], in1=tb_sin[:], op=ALU.mult),
                        reads=[p2[0:64, :], tb_sin[:]], writes=[rop2[:]], tag="rq2")
                    tt(fw, "dve", qout[:], rop1[:], rop2[:], ALU.add, tag="rq3")
                    fw.dma("sp", q_d[h_, :, off:off + 512], qout[:], tag="st_q2")
    k2ps = fw.aux_ps
    k2one = fw.sb("k2one", [1, 1], F32)
    for hv in range(4):
        for bi in range(NKT // 4):
            off = bi * 512
            act(fw, sqt[0:64, :], k_heads[hv][0:64, off:off + 512], AF.Square, tag="k2sq")
            mm1(fw, k2ps[:], onecol[0:64, 0:1], sqt[0:64, :], True, True, tag="k2mm")
            if bi == 0:
                reduce(fw, "dve", k2m[0:1, hv:hv + 1], k2ps[:], op=ALU.max, tag="k2max")
            else:
                reduce(fw, "dve", k2one[:], k2ps[:], op=ALU.max, tag="k2max2")
                tt(fw, "dve", k2m[0:1, hv:hv + 1], k2m[0:1, hv:hv + 1], k2one[:],
                   ALU.max, tag="k2acc")
    fw.barrier()
    fw.free_sb("k2one", "vtile0", "vtile1",
               *[f"{nm}{s}" for s in range(4) for nm in ("pre_h", "rop1", "rop2", "qout")],
               "tb_sin", "tb_cos",
               "s_bv", "s_bk", "s_bq", "s_wvT", "s_wkT", "s_wqs1", "s_wqs0",
               *[f"xnb{k}" for k in range(NK)], *[f"xb{k}" for k in range(NK)],
               "t_br", "t_rr", "t_bb", "t_rb", "t_vv", "t_qq", "t_ra", "t_sq")

    Psb = fw.sb("Psb", [128, NKT * 512], BF16)
    rrep = fw.sb("rrep", [128, 512], F32)
    r_row = fw.sb("r_row", [1, 512], F32)
    qbuf = [fw.sb(f"qb2_{i}", [65, T], BF16) for i in range(2)]
    qsq = fw.sb("qsq", [64, 512], F32)
    q2r = fw.sb("q2r", [1, 512], F32)
    onerow = fw.onerow
    for h_ in range(16):
        hv = h_ // 4
        qh = qbuf[h_ % 2]
        fw.dma("sp", qh[0:64, :], q_d[h_], tag="ld_q2")
        for qb in range(4):
            off = qb * 512
            act(fw, qsq[:], qh[0:64, off:off + 512], AF.Square, tag="q2sq")
            p = P.take()
            mm1(fw, p[0:1, :], onecol[0:64, 0:1], qsq[:], True, True, tag="q2mm")
            copy(fw, "dve", q2r[:], p[0:1, :], tag="q2ev")
            act(fw, q2r[:], q2r[:], AF.Sqrt, scale=k2m[0:1, hv:hv + 1], tag="csq")
            fw.comp("dve", lambda h, qh=qh, off=off: h.tensor_single_scalar(
                out=qh[64:65, off:off + 512], in_=q2r[0:1, :], scalar=-1.0, op=ALU.mult),
                reads=[q2r[0:1, :]], writes=[qh[64:65, off:off + 512]], tag="cneg")
        for qb in range(4):
            q_ap = qh[:, qb * 512:(qb + 1) * 512]
            for kt in range(NKT):
                p = P.take()
                mm1(fw, p[:], k_heads[hv][:, kt * 128:(kt + 1) * 128], q_ap, True, True,
                    tag="mm_s2")
                act(fw, Psb[:, kt * 512:(kt + 1) * 512], p[:], AF.Exp, tag="p2_exp")
            p = P.take()
            mm(fw, p[0:65, :], [(V_tm[kt][:, hv * 65:(hv + 1) * 65],
                                 Psb[:, kt * 512:(kt + 1) * 512]) for kt in range(NKT)],
               tag="mm_pv2")
            recip(fw, r_row[0:1, :], p[64:65, :], tag="rden2")
            p2 = P.take()
            mm1(fw, p2[0:64, :], onerow[0:1, 0:64], r_row[0:1, :], True, True, tag="mm_rr2")
            copy(fw, "act", rrep[0:64, :], p2[0:64, :], tag="ev_rr2")
            dst = attn_tiles[(h_ // 2, qb)][(h_ % 2) * 64:(h_ % 2) * 64 + 64, :]
            fw.comp("dve", lambda hh, dst=dst, p=p: hh.tensor_tensor(
                out=dst, in0=p[0:64, :], in1=rrep[0:64, :], op=ALU.mult),
                reads=[p[0:64, :], rrep[0:64, :]], writes=[dst], tag="pv2_nrm")
    fw.barrier()
    fw.free_sb("q2r", "qsq", "qb2_1", "qb2_0", "r_row", "rrep", "Psb", "k2m",
               *[f"kh{v}" for v in range(4)], *[f"vt{i}" for i in range(NKT)])
    woT = [fw.sb(f"s_wo{k}", [128, D], BF16) for k in range(NK)]
    for k in range(NK):
        fw.dma("pool", woT[k][:], wo_t[k * 128:(k + 1) * 128, :], tag="ld_wo")
    bo_sb = fw.sb("s_bo", [1, D], BF16)
    fw.dma("pool", bo_sb[:], bo_r)
    s2_blk = fw.sb("s2_blk", [128, 512], F32)
    for qb in range(4):
        for ot in range(NK):
            p = P.take()
            pairs = [(woT[k][:, ot * 128:(ot + 1) * 128], attn_tiles[(k, qb)][:]) for k in range(NK)]
            for i, (l, r) in enumerate(pairs):
                mm1(fw, p[:], l, r, i == 0, False, tag="mm_wo")
            mm1(fw, p[:], bo_sb[0:1, ot * 128:(ot + 1) * 128], ones512[0:1, :], False, True,
                tag="mm_bo")
            copy(fw, "act", s2_blk[:], p[:], tag="ev_s2")
            fw.dma("sp", s2_d[ot, :, qb * 512:(qb + 1) * 512], s2_blk[:], tag="st_s2")

    fw.barrier()
    fw.free_sb("s2_blk", "s_bo", *[f"s_wo{k}" for k in range(NK)],
               *[f"atn{pr}_{qb}" for pr in range(8) for qb in range(4)])

    idt = cst.handles["idt"]
    sel_sb = cst.handles["sel8"]
    wrT = fw.sb("s_wrT", [128, NK * 8], BF16)
    fw.dma("pool", wrT[:].rearrange("p (k g) -> p k g", k=NK),
           wr_t.rearrange("(k p) g -> p k g", p=128), tag="ld_wr")
    br_sb = fw.sb("s_br", [8, 1], F32)
    fw.dma("sp", br_sb[:], br_c)
    be_sb = fw.sb("s_be", [8, D], BF16)
    fw.dma("pool", be_sb[:], be_i)
    fwoT = [fw.sb(f"s_fwo{k}", [128, D], BF16) for k in range(NK)]
    for k in range(NK):
        fw.dma("pool", fwoT[k][:], fwo_t[k * 128:(k + 1) * 128, :], tag="ld_fwo")
    fbo_sb = fw.sb("s_fbo", [1, D], BF16)
    fw.dma("pool", fbo_sb[:], fbo_r)
    ln_tmp = mk_ln_tmp(fw, "c")
    xblk = [fw.sb(f"cxb{k}", [128, 512], F32) for k in range(NK)]
    xn_blk = [fw.sb(f"cxnb{k}", [128, 512], BF16) for k in range(NK)]

    def ln_into(cols_off, W=512, xblk=xblk, xn_blk=xn_blk, ln_tmp=ln_tmp):
        for k in range(NK):
            fw.dma("sp", xblk[k][:, :W], xf_i[k, :, cols_off:cols_off + W], tag="ld_x")
        ln_block(fw, P, cst, xblk, 0, W, xn_blk, 0, ln_tmp)

    lgb = fw.sb("lgb", [8, 512], F32)
    tm_g = fw.sb("tm_g", [128, 8], F32)
    tm_e = fw.sb("tm_e", [128, 8], F32)
    tm_m = fw.sb("tm_m", [128, 8], F32)
    tm_m2 = fw.sb("tm_m2", [128, 8], F32)
    tm_r = fw.sb("tm_r", [128, 1], F32)
    wfm = fw.sb("wfm", [8, 512], BF16)
    wrep_all = fw.sb("wrep_all", [128, 8 * 512], BF16)
    xnw = [fw.sb(f"xnw{k}", [128, 512], BF16) for k in range(NK)]
    wexp = [fw.sb(f"wexp{i}", [128, NK * 128], BF16) for i in range(2)]
    comb = [fw.sb(f"comb{k}", [128, 512], BF16) for k in range(NK)]
    out_acc = [fw.sb(f"oacc{o}", [128, 512], F32) for o in range(NK)]
    g_sl = fw.sb("g_sl", [128, 512], F32)
    x_sl = fw.sb("x_sl", [128, 512], F32)
    s_sl = fw.sb("s_sl", [128, 512], F32)
    x3_blk = fw.sb("x3_blk", [128, 512], F32)
    wslot = [0]
    for qb in range(4):
        qoff = qb * 512
        ln_into(qoff)
        p = P.take()
        mm(fw, p[0:8, :], [(wrT[:, k * 8:(k + 1) * 8], xn_blk[k][:]) for k in range(NK)],
           tag="mm_lg")
        act(fw, lgb[:], p[0:8, :], AF.Identity, bias=br_sb[0:8, 0:1], tag="ev_lg")
        for i in range(4):
            p = P.take()
            transpose_pe(fw, p[:, 0:8], lgb[:, i * 128:(i + 1) * 128], idt[0:8, 0:8], tag="tr_lg")
            copy(fw, "dve", tm_g[:], p[:, 0:8], tag="ev_tg")
            reduce(fw, "dve", tm_r[:], tm_g[:], op=ALU.max, negate=True, tag="f_max")
            act(fw, tm_e[:], tm_g[:], AF.Exp, bias=tm_r[:, 0:1], tag="f_exp")
            reduce(fw, "dve", tm_r[:], tm_e[:], op=ALU.add, tag="f_sum")
            recip(fw, tm_r[:], tm_r[:], tag="f_rec")
            ts(fw, "dve", tm_e[:], tm_e[:], tm_r[:, 0:1], ALU.mult, tag="f_nrm")
            reduce(fw, "dve", tm_r[:], tm_e[:], op=ALU.max, tag="t_m1")
            ts(fw, "dve", tm_m[:], tm_e[:], tm_r[:, 0:1], ALU.is_equal, tag="t_k1")
            stt(fw, "dve", tm_g[:], tm_m[:], -1e30, tm_e[:], ALU.mult, ALU.add, tag="t_g2")
            reduce(fw, "dve", tm_r[:], tm_g[:], op=ALU.max, tag="t_m2")
            ts(fw, "dve", tm_m2[:], tm_g[:], tm_r[:, 0:1], ALU.is_equal, tag="t_k2")
            tt(fw, "dve", tm_m[:], tm_m[:], tm_m2[:], ALU.add, tag="t_ms")
            tt(fw, "dve", tm_e[:], tm_e[:], tm_m[:], ALU.mult, tag="t_w")
            p = P.take()
            transpose_pe(fw, p[0:8, 0:128], tm_e[:], idt[:], tag="tr_w")
            copy(fw, "dve", wfm[:, i * 128:(i + 1) * 128], p[0:8, 0:128], tag="ev_w")
        for e in range(8):
            p = P.take()
            mm1(fw, p[:], sel_sb[:, e * 128:(e + 1) * 128], wfm[:], True, True, tag="mm_wr")
            copy(fw, "act", wrep_all[:, e * 512:(e + 1) * 512], p[:], tag="ev_wr")
        for e in range(8):
            for k in range(NK):
                tt(fw, "dve", xnw[k][:], xn_blk[k][:], wrep_all[:, e * 512:(e + 1) * 512],
                   ALU.mult, tag="xnw")
            for ot in range(NK):
                wb = wexp[wslot[0] % 2]; wslot[0] += 1
                fw.dma("pool", wb[:].rearrange("p (k o) -> p k o", k=NK),
                       we_t.rearrange("e (k p) o -> e p k o", p=128)[e, :, :, ot * 128:(ot + 1) * 128],
                       tag="ld_we")
                p = P.take()
                pairs = [(wb[:, k * 128:(k + 1) * 128], xnw[k][:]) for k in range(NK)]
                for i, (l, r) in enumerate(pairs):
                    mm1(fw, p[:], l, r, i == 0, (e != 0 and i == len(pairs) - 1), tag="mm_ex")
                if e == 0:
                    mm1(fw, p[:], be_sb[:, ot * 128:(ot + 1) * 128], wfm[:], False, True,
                        tag="mm_be")
                if e == 0:
                    copy(fw, "dve", out_acc[ot][:], p[:], tag="eo0")
                elif e < 7:
                    tt(fw, "dve", out_acc[ot][:], out_acc[ot][:], p[:], ALU.add, tag="eo")
                else:
                    tt(fw, "dve", comb[ot][:], out_acc[ot][:], p[:], ALU.add, tag="comb_c")
        for ot in range(NK):
            p = P.take()
            pairs = [(fwoT[k][:, ot * 128:(ot + 1) * 128], comb[k][:]) for k in range(NK)]
            for i, (l, r) in enumerate(pairs):
                mm1(fw, p[:], l, r, i == 0, False, tag="mm_fwo")
            mm1(fw, p[:], fbo_sb[0:1, ot * 128:(ot + 1) * 128], ones512[0:1, :], False, True,
                tag="mm_fbo")
            act(fw, g_sl[:], p[:], AF.Silu, tag="silu_g")
            fw.dma("sp", x_sl[:], xf_i[ot, :, qoff:qoff + 512], tag="ld_xr")
            fw.dma("sp", s_sl[:], s2_d[ot, :, qoff:qoff + 512], tag="ld_s2")
            tt(fw, "dve", g_sl[:], s_sl[:], g_sl[:], ALU.mult, tag="sg")
            tt(fw, "dve", x3_blk[:], x_sl[:], g_sl[:], ALU.add, tag="x3")
            fw.dma("sp", x3_o[ot, :, qoff:qoff + 512], x3_blk[:], tag="st_x3")





def emit_l3(fw, cst, P, IO):

    xf_i = IO["xf_i"]
    wqkv = IO["wqkv"]
    qs_c = IO["qs_c"]
    ks_c = IO["ks_c"]
    wp_t = IO["wp_t"]
    bp_r = IO["bp_r"]
    cosq = IO["cosq"]
    sinq = IO["sinq"]
    cosk = IO["cosk"]
    sink = IO["sink"]
    bw_t = IO["bw_t"]
    bb_r = IO["bb_r"]
    shA_t = IO["shA_t"]
    shB_t = IO["shB_t"]
    rA_t = IO["rA_t"]
    rB_t = IO["rB_t"]
    rw_t = IO["rw_t"]
    rb_c = IO["rb_c"]
    x4_o = IO["x4_o"]

    k_d = fw.dram("k3_spill", [16, 64, TF], BF16)
    v_d = fw.dram("v3_spill", [16, TF // 128, 128, 65], BF16)
    s3_d = fw.dram("s3_spill", [NK, 128, T], F32)

    onecol = cst.handles["onecol"]
    onerow = cst.handles["onerow"]
    ones512 = cst.handles["ones_row512"]
    perm_sb = cst.handles["perm3"]
    NKT = TF // 128

    k2m = fw.sb("k2m", [1, 16], F32)
    q_d = fw.dram("q3_spill", [16, 64, T], BF16)

    ln_tmp = mk_ln_tmp(fw, "t")
    xblk = [fw.sb(f"xb{k}", [128, 512], F32) for k in range(NK)]
    xn_blk = [fw.sb(f"xnb{k}", [128, 512], BF16) for k in range(NK)]

    def ln_into(cols_off, W=512):
        for k in range(NK):
            fw.dma("sp", xblk[k][:, :W], xf_i[k, :, cols_off:cols_off + W], tag="ld_x")
        ln_block(fw, P, cst, xblk, 0, W, xn_blk, 0, ln_tmp)

    wqkv_sb = [fw.sb(f"s_wqkv{k}", [128, 3 * D], BF16) for k in range(NK)]
    for k in range(NK):
        fw.dma("pool", wqkv_sb[k][:], wqkv[k * 128:(k + 1) * 128, :], tag="ld_wqkv")
    qs_sb = fw.sb("s_qs", [64, 1], F32)
    fw.dma("sp", qs_sb[:], qs_c)
    ks_sb = fw.sb("s_ks", [64, 1], F32)
    fw.dma("sp", ks_sb[:], ks_c)
    tb_cos = fw.sb("tb_cos", [64, 512], BF16)
    tb_sin = fw.sb("tb_sin", [64, 512], BF16)
    NSET = 4
    rsets = []
    for s in range(NSET):
        rsets.append(dict(
            pre_h=fw.sb(f"pre_h{s}", [64, 512], F32),
            sq_h=fw.sb(f"sq_h{s}", [64, 512], F32),
            row_a=fw.sb(f"row_a{s}", [1, 512], F32),
            rrep_h=fw.sb(f"rrep_h{s}", [64, 512], F32),
            qn_h=fw.sb(f"qn_h{s}", [64, 512], BF16),
            rop1=fw.sb(f"rop1{s}", [64, 512], F32),
            rop2=fw.sb(f"rop2{s}", [64, 512], F32),
            kout=fw.sb(f"kout{s}", [64, 512], BF16)))
    rturn = [0]
    vtile = fw.sb("vtile", [128, 512], BF16)
    k2one = fw.sb("k2one", [1, 1], F32)
    k2ps = fw.aux_ps
    pk_ps = [fw.ps(f"pk{s}", [128, 512], F32) for s in range(2)]

    def rms_rope_stages(sub_psum, scale_col, cos_ap, sin_ap, eps_s, tag, post):
        B = rsets[rturn[0] % NSET]; rturn[0] += 1
        pre_h, sq_h, row_a = B["pre_h"], B["sq_h"], B["row_a"]
        rrep_h, qn_h, rop1, rop2, kout = (B["rrep_h"], B["qn_h"], B["rop1"],
                                          B["rop2"], B["kout"])

        def s1():
            copy(fw, "act", pre_h[:], sub_psum, tag=tag + "pre")
            act(fw, sq_h[:], pre_h[:], AF.Square, tag=tag + "sq")
            p2 = P.take()
            mm1(fw, p2[0:1, :], onecol[0:64, 0:1], sq_h[:], True, True, tag=tag + "ssum")
            ts(fw, "dve", row_a[:], p2[0:1, :], 1.0 / 64.0, ALU.mult, tag=tag + "var")
            act(fw, row_a[:], row_a[:], AF.Ln, bias=eps_s, tag=tag + "ln")
            act(fw, row_a[:], row_a[:], AF.Exp, scale=cst.scalar(-0.5, 1), tag=tag + "rs")

        def s2():
            p2 = P.take()
            mm1(fw, p2[0:64, :], onerow[0:1, 0:64], row_a[0:1, :], True, True, tag=tag + "rep")
            copy(fw, "act", rrep_h[:], p2[0:64, :], tag=tag + "evr")
            stt(fw, "dve", qn_h[:], pre_h[:], scale_col, rrep_h[:], ALU.mult, ALU.mult,
                tag=tag + "n")

        def s3():
            p2 = P.take()
            mm1(fw, p2[0:64, :], perm_sb[:], qn_h[:], True, True, tag=tag + "rot")
            tt(fw, "dve", rop1[:], qn_h[:], cos_ap, ALU.mult, tag=tag + "r1")
            fw.comp("dve", lambda h, p2=p2, sin_ap=sin_ap, rop2=rop2: h.tensor_tensor(
                out=rop2[:], in0=p2[0:64, :], in1=sin_ap, op=ALU.mult),
                reads=[p2[0:64, :], sin_ap], writes=[rop2[:]], tag=tag + "r2")
            tt(fw, "dve", kout[:], rop1[:], rop2[:], ALU.add, tag=tag + "r3")
            post(B)
        return s1, s2, s3

    eps6 = cst.scalar(1e-6, 1)
    for bi in range(NKT // 4):
        off = bi * 512
        own = off < T
        ln_into(off)
        fw.dma("pool", tb_cos[:], cosk[:, off:off + 512], tag="ld_ck")
        fw.dma("pool", tb_sin[:], sink[:, off:off + 512], tag="ld_sk")
        pipe = Pipe()
        for mt in range(NK):
            def proj(mt=mt):
                p = pk_ps[mt % 2]
                mm(fw, p[:], [(wqkv_sb[k][:, D + mt * 128:D + (mt + 1) * 128], xn_blk[k][:])
                              for k in range(NK)], tag="mm_k3")
            pipe.add(proj)
            for sub in range(2):
                h_ = mt * 2 + sub

                def post(B, h_=h_, bi=bi, off=off):
                    kout = B["kout"]
                    fw.dma("sp", k_d[h_, :, off:off + 512], kout[:], tag="st_k3")
                    act(fw, B["sq_h"][:], kout[:], AF.Square, tag="k2sq")
                    mm1(fw, k2ps[:], onecol[0:64, 0:1], B["sq_h"][:], True, True, tag="k2mm")
                    if bi == 0:
                        reduce(fw, "dve", k2m[0:1, h_:h_ + 1], k2ps[:], op=ALU.max, tag="k2max")
                    else:
                        reduce(fw, "dve", k2one[:], k2ps[:], op=ALU.max, tag="k2max2")
                        tt(fw, "dve", k2m[0:1, h_:h_ + 1], k2m[0:1, h_:h_ + 1], k2one[:],
                           ALU.max, tag="k2acc")
                s1, s2, s3 = rms_rope_stages(pk_ps[mt % 2][sub * 64:(sub + 1) * 64, :],
                                             ks_sb[:, 0:1], tb_cos[:], tb_sin[:], eps6,
                                             "k3", post)
                pipe.add(s1, s2, s3)
        pipe.run()
        for st_ in range(4):
            to = st_ * 128
            kt = (off + to) // 128
            for half in range(2):
                p = P.take()
                mm(fw, p[:], [(xn_blk[k][:, to:to + 128],
                               wqkv_sb[k][:, 2 * D + half * 512:2 * D + (half + 1) * 512])
                              for k in range(NK)], tag="mm_v3")
                copy(fw, "act", vtile[:], p[:], tag="ev_v3")
                for hv in range(8):
                    h_ = half * 8 + hv
                    fw.dma("sp", v_d[h_, kt, :, 0:64], vtile[:, hv * 64:(hv + 1) * 64],
                           tag="st_v3")
        if own:
            fw.dma("pool", tb_cos[:], cosq[:, off:off + 512], tag="ld_cq")
            fw.dma("pool", tb_sin[:], sinq[:, off:off + 512], tag="ld_sq")
            pipe = Pipe()
            for mt in range(NK):
                def proj(mt=mt):
                    p = pk_ps[mt % 2]
                    mm(fw, p[:], [(wqkv_sb[k][:, mt * 128:(mt + 1) * 128], xn_blk[k][:])
                                  for k in range(NK)], tag="mm_q3")
                pipe.add(proj)
                for sub in range(2):
                    h_ = mt * 2 + sub

                    def post(B, h_=h_, off=off):
                        fw.dma("sp", q_d[h_, :, off:off + 512], B["kout"][:], tag="st_q3")
                    s1, s2, s3 = rms_rope_stages(pk_ps[mt % 2][sub * 64:(sub + 1) * 64, :],
                                                 qs_sb[:, 0:1], tb_cos[:], tb_sin[:], eps6,
                                                 "q3", post)
                    pipe.add(s1, s2, s3)
            pipe.run()
    fw.barrier()
    fw.free_sb("k2one", "vtile",
               *[f"{nm}{s}" for s in range(4) for nm in
                 ("pre_h", "sq_h", "row_a", "rrep_h", "qn_h", "rop1", "rop2", "kout")],
               "tb_sin", "tb_cos",
               "s_ks", "s_qs", *[f"s_wqkv{k}" for k in range(NK)],
               *[f"xnb{k}" for k in range(NK)], *[f"xb{k}" for k in range(NK)],
               "t_br", "t_rr", "t_bb", "t_rb", "t_vv", "t_qq", "t_ra", "t_sq")

    kbuf = [fw.sb(f"kb{i}", [65, TF], BF16) for i in range(2)]
    vbuf_t = [fw.sb(f"vb{i}", [128, NKT * 65], BF16) for i in range(2)]
    vbuf = [[vbuf_t[i][:, kt * 65:(kt + 1) * 65] for kt in range(NKT)] for i in range(2)]
    attn_tiles = {}
    for pair in range(8):
        for qb in range(4):
            attn_tiles[(pair, qb)] = fw.sb(f"atn{pair}_{qb}", [128, 512], BF16)
    Psb = fw.sb("Psb", [128, NKT * 512], BF16)
    rrep = fw.sb("rrep", [128, 512], F32)
    r_row = fw.sb("r_row", [1, 512], F32)
    qbuf = [fw.sb(f"qb3_{i}", [65, T], BF16) for i in range(2)]
    qsq = fw.sb("qsq3", [64, 512], F32)
    q2r = fw.sb("q2r3", [1, 512], F32)
    for h_ in range(16):
        sl = h_ % 2
        kb = kbuf[sl]
        fw.dma("sp", kb[0:64, :], k_d[h_], tag="ld_k3")
        memset(fw, "dve", kb[64:65, :], 1.0, tag="k3ones")
        fw.dma("sp", vbuf_t[sl][:].rearrange("p (kt c) -> p kt c", kt=NKT)[:, :, 0:64],
               v_d[h_, :, :, 0:64].rearrange("kt p c -> p kt c"), tag="ld_v3")
        memset(fw, "dve", vbuf_t[sl][:].rearrange("p (kt c) -> p kt c", kt=NKT)[:, :, 64:65],
               1.0, tag="v3ones")
        qh = qbuf[sl]
        fw.dma("sp", qh[0:64, :], q_d[h_], tag="ld_q3")
        for qb in range(4):
            off = qb * 512
            act(fw, qsq[:], qh[0:64, off:off + 512], AF.Square, tag="q3sq")
            p = P.take()
            mm1(fw, p[0:1, :], onecol[0:64, 0:1], qsq[:], True, True, tag="q3mm")
            copy(fw, "dve", q2r[:], p[0:1, :], tag="q3ev")
            act(fw, q2r[:], q2r[:], AF.Sqrt, scale=k2m[0:1, h_:h_ + 1], tag="csq3")
            fw.comp("dve", lambda h, qh=qh, off=off: h.tensor_single_scalar(
                out=qh[64:65, off:off + 512], in_=q2r[0:1, :], scalar=-1.0, op=ALU.mult),
                reads=[q2r[0:1, :]], writes=[qh[64:65, off:off + 512]], tag="cneg3")
        for qb in range(4):
            q_ap = qh[:, qb * 512:(qb + 1) * 512]
            for kt in range(NKT):
                p = P.take()
                mm1(fw, p[:], kb[:, kt * 128:(kt + 1) * 128], q_ap, True, True, tag="mm_s3")
                act(fw, Psb[:, kt * 512:(kt + 1) * 512], p[:], AF.Exp, tag="p3_exp")
            p = P.take()
            mm(fw, p[0:65, :], [(vbuf[sl][kt], Psb[:, kt * 512:(kt + 1) * 512])
                                for kt in range(NKT)], tag="mm_pv3")
            recip(fw, r_row[0:1, :], p[64:65, :], tag="rden3")
            p2 = P.take()
            mm1(fw, p2[0:64, :], onerow[0:1, 0:64], r_row[0:1, :], True, True, tag="mm_rr3")
            copy(fw, "act", rrep[0:64, :], p2[0:64, :], tag="ev_rr3")
            dst = attn_tiles[(h_ // 2, qb)][(h_ % 2) * 64:(h_ % 2) * 64 + 64, :]
            fw.comp("dve", lambda hh, dst=dst, p=p: hh.tensor_tensor(
                out=dst, in0=p[0:64, :], in1=rrep[0:64, :], op=ALU.mult),
                reads=[p[0:64, :], rrep[0:64, :]], writes=[dst], tag="pv3_nrm")
    wpT = [fw.sb(f"s_wp{k}", [128, D], BF16) for k in range(NK)]
    for k in range(NK):
        fw.dma("pool", wpT[k][:], wp_t[k * 128:(k + 1) * 128, :], tag="ld_wp")
    bp_sb = fw.sb("s_bp", [1, D], BF16)
    fw.dma("pool", bp_sb[:], bp_r)
    s3_blk = fw.sb("s3_blk", [128, 512], F32)
    for qb in range(4):
        for ot in range(NK):
            p = P.take()
            pairs = [(wpT[k][:, ot * 128:(ot + 1) * 128], attn_tiles[(k, qb)][:]) for k in range(NK)]
            for i, (l, r) in enumerate(pairs):
                mm1(fw, p[:], l, r, i == 0, False, tag="mm_wp")
            mm1(fw, p[:], bp_sb[0:1, ot * 128:(ot + 1) * 128], ones512[0:1, :], False, True,
                tag="mm_bp")
            copy(fw, "act", s3_blk[:], p[:], tag="ev_s3")
            fw.dma("sp", s3_d[ot, :, qb * 512:(qb + 1) * 512], s3_blk[:], tag="st_s3")

    fw.barrier()
    fw.free_sb("s3_blk", "s_bp", *[f"s_wp{k}" for k in range(NK)],
               "r_row", "rrep", "Psb", "q2r3", "qsq3", "qb3_1", "qb3_0",
               *[f"atn{pr}_{qb}" for pr in range(8) for qb in range(4)],
               "vb0", "vb1", "kb0", "kb1", "k2m")

    idt = cst.handles["idt"]
    sel_sb = cst.handles["sel6"]
    bwT = [fw.sb(f"s_bw{k}", [128, D], BF16) for k in range(NK)]
    for k in range(NK):
        fw.dma("pool", bwT[k][:], bw_t[k * 128:(k + 1) * 128, :], tag="ld_bw")
    bb_sb = fw.sb("s_bb", [1, D], BF16)
    fw.dma("pool", bb_sb[:], bb_r)
    shA_sb = fw.sb("s_shA", [128, NK * 16], BF16)
    fw.dma("pool", shA_sb[:].rearrange("p (k r) -> p k r", k=NK),
           shA_t.rearrange("(k p) r -> p k r", p=128), tag="ld_shA")
    shB_sb = fw.sb("s_shB", [16, D], BF16)
    fw.dma("pool", shB_sb[:], shB_t)
    rA_sb = fw.sb("s_rA", [128, 6 * NK * 16], BF16)
    fw.dma("pool", rA_sb[:].rearrange("p (e k r) -> p e k r", e=6, k=NK),
           rA_t.rearrange("e (k p) r -> p e k r", p=128), tag="ld_rA")
    rB_sb = fw.sb("s_rB", [16, 6 * D], BF16)
    fw.dma("pool", rB_sb[:].rearrange("p (e d) -> p e d", e=6),
           rB_t.rearrange("e r d -> r e d"), tag="ld_rB")
    rwT = fw.sb("s_rwT", [128, NK * 6], BF16)
    fw.dma("pool", rwT[:].rearrange("p (k g) -> p k g", k=NK),
           rw_t.rearrange("(k p) g -> p k g", p=128), tag="ld_rw")
    rb_sb = fw.sb("s_rb", [6, 1], F32)
    fw.dma("sp", rb_sb[:], rb_c)
    ln_tmp = mk_ln_tmp(fw, "c")
    xblk = [fw.sb(f"cxb{k}", [128, 512], F32) for k in range(NK)]
    xn_blk = [fw.sb(f"cxnb{k}", [128, 512], BF16) for k in range(NK)]

    def ln_into(cols_off, W=512, xblk=xblk, xn_blk=xn_blk, ln_tmp=ln_tmp):
        for k in range(NK):
            fw.dma("sp", xblk[k][:, :W], xf_i[k, :, cols_off:cols_off + W], tag="ld_x")
        ln_block(fw, P, cst, xblk, 0, W, xn_blk, 0, ln_tmp)

    lgb = fw.sb("lgb", [6, 512], F32)
    tm_g = fw.sb("tm_g", [128, 6], F32)
    tm_e = fw.sb("tm_e", [128, 6], F32)
    tm_m = fw.sb("tm_m", [128, 6], F32)
    tm_m2 = fw.sb("tm_m2", [128, 6], F32)
    tm_r = fw.sb("tm_r", [128, 1], F32)
    wfm = fw.sb("wfm", [6, 512], BF16)
    wrep = fw.sb("wrep", [128, 512], BF16)
    xnw = [fw.sb(f"xnw{k}", [128, 512], BF16) for k in range(NK)]
    tsh = fw.sb("tsh", [16, 512], BF16)
    he_sb = fw.sb("he_sb", [16, 6 * 512], BF16)
    g_sl = fw.sb("g_sl", [128, 512], F32)
    x_sl = fw.sb("x_sl", [128, 512], F32)
    s_sl = fw.sb("s_sl", [128, 512], F32)
    x4_blk = fw.sb("x4_blk", [128, 512], F32)
    for qb in range(4):
        qoff = qb * 512
        ln_into(qoff)
        p = P.take()
        mm(fw, p[0:6, :], [(rwT[:, k * 6:(k + 1) * 6], xn_blk[k][:]) for k in range(NK)],
           tag="mm_rt")
        act(fw, lgb[:], p[0:6, :], AF.Identity, bias=rb_sb[0:6, 0:1], tag="ev_rt")
        for i in range(4):
            p = P.take()
            transpose_pe(fw, p[:, 0:6], lgb[:, i * 128:(i + 1) * 128], idt[0:6, 0:6], tag="tr_rt")
            copy(fw, "dve", tm_g[:], p[:, 0:6], tag="ev_tg")
            reduce(fw, "dve", tm_r[:], tm_g[:], op=ALU.max, negate=True, tag="r_max")
            act(fw, tm_e[:], tm_g[:], AF.Exp, bias=tm_r[:, 0:1], tag="r_exp")
            reduce(fw, "dve", tm_r[:], tm_e[:], op=ALU.add, tag="r_sum")
            recip(fw, tm_r[:], tm_r[:], tag="r_rec")
            ts(fw, "dve", tm_e[:], tm_e[:], tm_r[:, 0:1], ALU.mult, tag="r_nrm")
            reduce(fw, "dve", tm_r[:], tm_e[:], op=ALU.max, tag="r_m1")
            ts(fw, "dve", tm_m[:], tm_e[:], tm_r[:, 0:1], ALU.is_equal, tag="r_k1")
            stt(fw, "dve", tm_g[:], tm_m[:], -1e30, tm_e[:], ALU.mult, ALU.add, tag="r_g2")
            reduce(fw, "dve", tm_r[:], tm_g[:], op=ALU.max, tag="r_m2")
            ts(fw, "dve", tm_m2[:], tm_g[:], tm_r[:, 0:1], ALU.is_equal, tag="r_k2")
            tt(fw, "dve", tm_m[:], tm_m[:], tm_m2[:], ALU.add, tag="r_ms")
            tt(fw, "dve", tm_e[:], tm_e[:], tm_m[:], ALU.mult, tag="r_w")
            reduce(fw, "dve", tm_r[:], tm_e[:], op=ALU.add, tag="r_ws")
            ts(fw, "dve", tm_r[:], tm_r[:], 1e-6, ALU.add, tag="r_eps")
            recip(fw, tm_r[:], tm_r[:], tag="r_wr")
            ts(fw, "dve", tm_e[:], tm_e[:], tm_r[:, 0:1], ALU.mult, tag="r_wn")
            p = P.take()
            transpose_pe(fw, p[0:6, 0:128], tm_e[:], idt[:], tag="tr_w")
            copy(fw, "dve", wfm[:, i * 128:(i + 1) * 128], p[0:6, 0:128], tag="ev_w")
        p = P.take()
        mm(fw, p[0:16, :], [(shA_sb[:, k * 16:(k + 1) * 16], xn_blk[k][:]) for k in range(NK)],
           tag="mm_shA")
        copy(fw, "act", tsh[:], p[0:16, :], tag="ev_shA")
        for e in range(6):
            p = P.take()
            mm1(fw, p[:], sel_sb[:, e * 128:(e + 1) * 128], wfm[:], True, True, tag="mm_wr")
            copy(fw, "act", wrep[:], p[:], tag="ev_wr")
            for k in range(NK):
                tt(fw, "dve", xnw[k][:], xn_blk[k][:], wrep[:], ALU.mult, tag="xnw")
            p = P.take()
            mm(fw, p[0:16, :], [(rA_sb[:, (e * NK + k) * 16:(e * NK + k + 1) * 16], xnw[k][:])
                                for k in range(NK)], tag="mm_rA")
            copy(fw, "act", he_sb[:, e * 512:(e + 1) * 512], p[0:16, :], tag="ev_he")
        for ot in range(NK):
            p = P.take()
            for i, k in enumerate(range(NK)):
                mm1(fw, p[:], bwT[k][:, ot * 128:(ot + 1) * 128], xn_blk[k][:],
                    i == 0, False, tag="mm_base")
            mm1(fw, p[:], bb_sb[0:1, ot * 128:(ot + 1) * 128], ones512[0:1, :], False, False,
                tag="mm_bb")
            mm1(fw, p[:], shB_sb[:, ot * 128:(ot + 1) * 128], tsh[:], False, False, tag="mm_shB")
            for e in range(6):
                mm1(fw, p[:], rB_sb[:, e * D + ot * 128:e * D + (ot + 1) * 128],
                    he_sb[:, e * 512:(e + 1) * 512], False, e == 5, tag="mm_rB")
            act(fw, g_sl[:], p[:], AF.Silu, tag="silu_g")
            fw.dma("sp", x_sl[:], xf_i[ot, :, qoff:qoff + 512], tag="ld_xr")
            fw.dma("sp", s_sl[:], s3_d[ot, :, qoff:qoff + 512], tag="ld_s3")
            tt(fw, "dve", g_sl[:], s_sl[:], g_sl[:], ALU.mult, tag="sg")
            tt(fw, "dve", x4_blk[:], x_sl[:], g_sl[:], ALU.add, tag="x4")
            fw.dma("sp", x4_o[ot, :, qoff:qoff + 512], x4_blk[:], tag="st_x4")




RG_PAIRS = [[0, 1], [2, 3], [4, 5], [6, 7]]


def emit_collective(fw, kind, src_ap, dst_ap, tag="cc"):
    def build(h):
        return h.collective_compute(kind, ALU.bypass, replica_groups=RG_PAIRS,
                                    ins=[src_ap.opt()], outs=[dst_ap.opt()])
    return fw.comp("pool", build, reads=[src_ap], writes=[dst_ap], tag=tag)


def emit_rotate(fw, P, gathered_ap, xf_ap, mrot_sb, bufs, tag="rot"):
    bufA, bufB, bufT = bufs
    g3 = gathered_ap.rearrange("(r k p) t -> r k p t", r=2, p=128)
    for k in range(NK):
        for half in range(2):
            ma = mrot_sb[:, half:half + 1]
            mb = mrot_sb[:, 1 - half:2 - half]
            for (off, W) in BLKS:
                fw.dma("sp", bufA[:, :W], g3[0, k, :, off:off + W], tag=tag + "g0")
                fw.dma("sp", bufB[:, :W], g3[1, k, :, off:off + W], tag=tag + "g1")
                ts(fw, "dve", bufT[:, :W], bufA[:, :W], ma, ALU.mult, tag=tag + "m0")
                stt(fw, "dve", bufA[:, :W], bufB[:, :W], mb, bufT[:, :W],
                    ALU.mult, ALU.add, tag=tag + "m1")
                fw.dma("sp", xf_ap[k, :, half * T + off:half * T + off + W], bufA[:, :W],
                       tag=tag + "st")


def build_fused():
    nc = bass.Bass("TRN2", target_bir_lowering=False)
    fw = Fw(nc)
    cst = Consts(fw)
    setup_consts(fw, cst, only={"onecol", "onerow"})
    P = PsumPool(fw, 5)

    def ext(pre, names, spec):
        fw.prefix = pre
        out = {}
        for nm in names:
            shape, dt = spec["ins"][nm]
            out[nm] = fw.inp(nm, shape, dt)
        fw.prefix = ""
        return out

    E0a = ext("l0a_", ["xh", "w_in", "convw", "convb", "wxp", "wdt", "bdt"], IO_SPECS["l0a"])
    E0b = ext("l0b_", ["alog", "dskip", "wout"], IO_SPECS["l0b_full"])
    E0f = ext("l0f_", ["wgred_t", "wg_t", "w1", "b1_t", "w2", "b2_t"], IO_SPECS["l0f"])
    E1 = ext("l1_", ["wq_t", "wk_t", "wv_t", "wo_t", "colq", "gw", "gb", "ew", "eb"],
             IO_SPECS["l1"])
    E2 = ext("l2_", ["wq_t", "bq_c", "wk_t", "bk_c", "wv_t", "bv_r", "wo_t", "bo_r",
                     "cosq", "sinq", "cosk", "sink", "wr_t", "br_c", "we_t", "be_i",
                     "fwo_t", "fbo_r"], IO_SPECS["l2"])
    E3 = ext("l3_", ["wqkv", "qs_c", "ks_c", "wp_t", "bp_r", "cosq", "sinq", "cosk",
                     "sink", "bw_t", "bb_r", "shA_t", "shB_t", "rA_t", "rB_t", "rw_t",
                     "rb_c"], IO_SPECS["l3"])
    mrot_i = fw.inp("mrot", [128, 2], F32)
    x4_ext = fw.outp("x4", [NK, 128, T], F32)

    ia = {}
    for nm, (shape, dt) in IO_SPECS["l0a"]["outs"].items():
        ia[nm] = fw.dram("ia_" + nm, shape, dt)
    carry_own = fw.dram("carry_own", [128, CT * NSTATE], F32)
    carry_g = fw.dram("carry_g", [2 * 128, CT * NSTATE], F32)
    carry_in = fw.dram("carry_in", [128, CT * NSTATE], F32)
    s0_d = fw.dram("s0_d", [NK, 128, T], F32)
    ccx = [fw.dram(f"ccx{l}_src", [NK * 128, T], F32) for l in (1, 2, 3)]
    ccg = [fw.dram(f"ccx{l}_dst", [2 * NK * 128, T], F32) for l in (1, 2, 3)]
    xf = [fw.dram(f"xf{l}", [NK, 128, TF], F32) for l in (1, 2, 3)]

    fw.prefix = "l0a_"
    m = fw.mark()
    IO = dict(E0a)
    IO.update({k: ia[k] for k in ("xn_o", "u_o", "z_o", "delta_o", "bmcm_o")})
    emit_l0a(fw, cst, P, IO)
    fw.barrier(); fw.free_since(m); fw.prefix = ""

    setup_consts(fw, cst)
    mrot_sb = fw.sb("g_mrot", [128, 2], F32)
    fw.dma("sp", mrot_sb[:], mrot_i)
    rot_bufs = (fw.sb("g_rA", [128, 512], F32), fw.sb("g_rB", [128, 512], F32),
                fw.sb("g_rT", [128, 512], F32))
    cbuf = fw.sb("g_cb", [128, CT * NSTATE], F32)

    fw.prefix = "l0c_"
    m = fw.mark()
    IO = dict(delta_i=ia["delta_o"], u_i=ia["u_o"], bmcm_i=ia["bmcm_o"],
              alog=E0b["alog"], carry_o=carry_own)
    emit_l0b(fw, cst, P, IO, mode="carry")
    fw.barrier(); fw.free_since(m); fw.prefix = ""

    emit_collective(fw, "AllGather", carry_own, carry_g, tag="cc_carry")
    fw.dma("sp", cbuf[:], carry_g[0:128, :], tag="cc_ld")
    ts(fw, "dve", cbuf[:], cbuf[:], mrot_sb[:, 1:2], ALU.mult, tag="cc_sel")
    fw.dma("sp", carry_in, cbuf[:], tag="cc_st")

    fw.prefix = "l0d_"
    m = fw.mark()
    IO = dict(delta_i=ia["delta_o"], u_i=ia["u_o"], bmcm_i=ia["bmcm_o"],
              alog=E0b["alog"], carry_i=carry_in, z_i=ia["z_o"], dskip=E0b["dskip"],
              wout=E0b["wout"], s_o=s0_d)
    emit_l0b(fw, cst, P, IO, mode="full")
    fw.barrier(); fw.free_since(m); fw.prefix = ""

    fw.prefix = "l0f_"
    m = fw.mark()
    IO = dict(E0f)
    IO.update(xn_i=ia["xn_o"], x_i=E0a["xh"][:, :, 512:TH], s_i=s0_d,
              x1_o=ccx[0].rearrange("(k p) t -> k p t", p=128))
    emit_l0f(fw, cst, P, IO)
    fw.barrier(); fw.free_since(m); fw.prefix = ""

    emits = [emit_l1, emit_l2, emit_l3]
    exts = [E1, E2, E3]
    outs_nm = ["x2_o", "x3_o", "x4_o"]
    for l in range(3):
        emit_collective(fw, "AllGather", ccx[l], ccg[l], tag=f"cc_x{l + 1}")
        fw.prefix = f"g{l + 1}_"
        emit_rotate(fw, P, ccg[l], xf[l], mrot_sb, rot_bufs, tag=f"rot{l + 1}")
        fw.prefix = ""
        fw.barrier()
        fw.prefix = f"l{l + 1}_"
        m = fw.mark()
        IO = dict(exts[l])
        IO["xf_i"] = xf[l]
        if l < 2:
            IO[outs_nm[l]] = ccx[l + 1].rearrange("(k p) t -> k p t", p=128)
        else:
            IO[outs_nm[l]] = x4_ext
        emits[l](fw, cst, P, IO)
        fw.barrier(); fw.free_since(m); fw.prefix = ""

    fw.finish_outputs()
    fw.finalize()
    return nc, cst


def host_fused_maps(inputs):
    g = lambda k: np.asarray(inputs[k], np.float32)
    x = np.asarray(inputs["L0_attn_x"])
    shared = {}
    conv_w = g("L0_attn_conv_w").reshape(2048, 4)
    shared.update({
        "l0a_w_in": np.ascontiguousarray(g("L0_attn_W_in")),
        "l0a_convw": np.ascontiguousarray(conv_w.reshape(CT, 128, 4).transpose(1, 0, 2)
                                          .reshape(128, CT * 4)),
        "l0a_convb": np.ascontiguousarray(g("L0_attn_conv_b").reshape(CT, 128).T),
        "l0a_wxp": np.ascontiguousarray(g("L0_attn_W_xproj")),
        "l0a_wdt": np.ascontiguousarray(g("L0_attn_W_dt")),
        "l0a_bdt": np.ascontiguousarray(g("L0_attn_b_dt").reshape(CT, 128).T),
        "l0b_alog": np.ascontiguousarray(g("L0_attn_A_log").reshape(CT, 128, NSTATE)
                                         .transpose(1, 0, 2).reshape(128, CT * NSTATE)),
        "l0b_dskip": np.ascontiguousarray(g("L0_attn_D_skip").reshape(CT, 128).T),
        "l0b_wout": np.ascontiguousarray(g("L0_attn_W_out")),
        "l0f_wgred_t": np.ascontiguousarray(g("L0_ffn_wg_red").T),
        "l0f_wg_t": np.ascontiguousarray(g("L0_ffn_wg").T),
        "l0f_w1": np.ascontiguousarray(g("L0_ffn_w1")),
        "l0f_b1_t": np.ascontiguousarray(g("L0_ffn_b1").reshape(8, 32, 128)
                                         .transpose(2, 0, 1).reshape(128, 256)),
        "l0f_w2": np.ascontiguousarray(g("L0_ffn_w2")),
        "l0f_b2_t": np.ascontiguousarray(g("L0_ffn_b2").reshape(8, 8, 128)
                                         .transpose(2, 0, 1).reshape(128, 64)),
        "l1_wq_t": np.ascontiguousarray(g("L1_attn_Wq").T),
        "l1_wk_t": np.ascontiguousarray(g("L1_attn_Wk").T),
        "l1_wv_t": np.ascontiguousarray(g("L1_attn_Wv").T),
        "l1_wo_t": np.ascontiguousarray(g("L1_attn_Wo").T),
        "l1_gw": np.ascontiguousarray(g("L1_ffn_gate_w")),
        "l1_gb": np.ascontiguousarray(g("L1_ffn_gate_b").reshape(8, 1)),
        "l1_ew": np.ascontiguousarray(g("L1_ffn_expert_w")),
        "l1_eb": np.ascontiguousarray(g("L1_ffn_expert_b")),
        "l2_wq_t": np.ascontiguousarray(g("L2_attn_Wq").T),
        "l2_bq_c": np.ascontiguousarray(g("L2_attn_bq").reshape(8, 128).T),
        "l2_wk_t": np.ascontiguousarray(g("L2_attn_Wk").T),
        "l2_bk_c": np.ascontiguousarray(g("L2_attn_bk").reshape(2, 128).T),
        "l2_wv_t": np.ascontiguousarray(g("L2_attn_Wv").T),
        "l2_bv_r": np.ascontiguousarray(g("L2_attn_bv").reshape(1, 256)),
        "l2_wo_t": np.ascontiguousarray(g("L2_attn_Wo").T),
        "l2_bo_r": np.ascontiguousarray(g("L2_attn_bo").reshape(1, D)),
        "l2_wr_t": np.ascontiguousarray(g("L2_ffn_Wr").T),
        "l2_br_c": np.ascontiguousarray(g("L2_ffn_br").reshape(8, 1)),
        "l2_we_t": np.ascontiguousarray(g("L2_ffn_We").transpose(0, 2, 1)),
        "l2_be_i": np.ascontiguousarray(g("L2_ffn_be")),
        "l2_fwo_t": np.ascontiguousarray(g("L2_ffn_Wo").T),
        "l2_fbo_r": np.ascontiguousarray(g("L2_ffn_bo").reshape(1, D)),
        "l3_wqkv": np.ascontiguousarray(g("L3_attn_w_qkv")),
        "l3_qs_c": np.ascontiguousarray(g("L3_attn_q_scale").reshape(64, 1)),
        "l3_ks_c": np.ascontiguousarray(g("L3_attn_k_scale").reshape(64, 1)),
        "l3_wp_t": np.ascontiguousarray(g("L3_attn_w_proj")),
        "l3_bp_r": np.ascontiguousarray(g("L3_attn_b_proj").reshape(1, D)),
        "l3_bw_t": np.ascontiguousarray(g("L3_ffn_base_W").T),
        "l3_bb_r": np.ascontiguousarray(g("L3_ffn_base_b").reshape(1, D)),
        "l3_shA_t": np.ascontiguousarray(g("L3_ffn_shared_A").T),
        "l3_shB_t": np.ascontiguousarray(g("L3_ffn_shared_B").T),
        "l3_rA_t": np.ascontiguousarray(g("L3_ffn_routed_A").transpose(0, 2, 1)),
        "l3_rB_t": np.ascontiguousarray(g("L3_ffn_routed_B").transpose(0, 2, 1)),
        "l3_rw_t": np.ascontiguousarray(g("L3_ffn_router_W").T),
        "l3_rb_c": np.ascontiguousarray(g("L3_ffn_router_b").reshape(6, 1)),
    })
    maps = []
    for c in range(NCORES):
        b, h = c // 2, c % 2
        rows = np.zeros((TH, D), np.float32)
        rows[512:] = x[b, h * T:(h + 1) * T]
        if h == 1:
            rows[:512] = x[b, T - 512:T]
        mrot = np.zeros((128, 2), np.float32)
        mrot[:, h] = 1.0
        own_pos = np.arange(h * T, (h + 1) * T, dtype=np.float64)
        rot_pos = np.concatenate([own_pos, np.arange((1 - h) * T, (2 - h) * T,
                                                     dtype=np.float64)])
        cosq, sinq = _rope_tables(own_pos)
        cosk, sink = _rope_tables(rot_pos)
        maps.append(dict(shared, l0a_xh=_fm(rows), mrot=mrot,
                         l1_colq=host_l1_colq(c),
                         l2_cosq=cosq * 0.125, l2_sinq=sinq * 0.125,
                         l2_cosk=cosk, l2_sink=sink,
                         l3_cosq=cosq * 0.125, l3_sinq=sinq * 0.125,
                         l3_cosk=cosk, l3_sink=sink))
    return maps


def kernel_fused(inputs):
    nc, cst = _get_prog("fused", build_fused)
    maps = host_fused_maps(inputs)
    in_maps = [dict(m, **cst.arrays) for m in maps]
    res = run_bass_kernel_spmd(nc, in_maps, core_ids=list(range(NCORES)))
    out = np.empty((4, 4096, D), np.float32)
    for c in range(NCORES):
        b, h = c // 2, c % 2
        out[b, h * T:(h + 1) * T] = _unfm(np.asarray(res.results[c]["x4"], np.float32))
    return out



def _fm(a2d):
    Tt, D2 = a2d.shape
    return np.ascontiguousarray(a2d.T.reshape(D2 // 128, 128, Tt))


def _unfm(a3d):
    nt, p, Tt = a3d.shape
    return np.ascontiguousarray(a3d.reshape(nt * p, Tt).T)


def _percore_rows(x, c):
    b, h = c // 2, c % 2
    return x[b, h * T:(h + 1) * T]


def host_l0a_inputs(inputs):
    x = np.asarray(inputs["L0_attn_x"])
    conv_w = np.asarray(inputs["L0_attn_conv_w"]).reshape(2048, 4)
    per_core = []
    shared = {
        "w_in": np.ascontiguousarray(inputs["L0_attn_W_in"]),
        "convw": np.ascontiguousarray(conv_w.reshape(CT, 128, 4).transpose(1, 0, 2).reshape(128, CT * 4)),
        "convb": np.ascontiguousarray(np.asarray(inputs["L0_attn_conv_b"]).reshape(CT, 128).T),
        "wxp": np.ascontiguousarray(inputs["L0_attn_W_xproj"]),
        "wdt": np.ascontiguousarray(inputs["L0_attn_W_dt"]),
        "bdt": np.ascontiguousarray(np.asarray(inputs["L0_attn_b_dt"]).reshape(CT, 128).T),
    }
    for c in range(NCORES):
        b, h = c // 2, c % 2
        rows = np.zeros((TH, D), np.float32)
        rows[512:] = x[b, h * T:(h + 1) * T]
        if h == 1:
            rows[:512] = x[b, T - 512:T]
        per_core.append(dict(shared, xh=_fm(rows)))
    return per_core


_BUILt = {}


def _get_prog(name, builder):
    if name not in _BUILt:
        _BUILt[name] = builder()
    return _BUILt[name]


def _spmd(name, builder, in_maps_fn):
    nc, cst = _get_prog(name, builder)
    in_maps = [dict(pc, **cst.arrays) for pc in in_maps_fn()]
    res = run_bass_kernel_spmd(nc, in_maps, core_ids=list(range(NCORES)))
    return res.results


build_l0a = _mk_builder("l0a")
build_l0f = _mk_builder("l0f")
build_l1 = _mk_builder("l1")
build_l2 = _mk_builder("l2")
build_l3 = _mk_builder("l3")


def build_l0b(mode):
    return _mk_builder("l0b_" + mode)()


def run_l0a(inputs):
    return _spmd("l0a", build_l0a, lambda: host_l0a_inputs(inputs))


def run_l0(inputs, upto="f"):
    r_a = run_l0a(inputs)
    alog_t = np.ascontiguousarray(
        np.asarray(inputs["L0_attn_A_log"]).reshape(CT, 128, NSTATE)
        .transpose(1, 0, 2).reshape(128, CT * NSTATE)).astype(np.float32)

    def maps_b1():
        return [dict(delta_i=r_a[c]["delta_o"], u_i=r_a[c]["u_o"],
                     bmcm_i=r_a[c]["bmcm_o"], alog=alog_t) for c in range(NCORES)]
    r_b1 = _spmd("l0b_carry", lambda: build_l0b("carry"), maps_b1)
    if upto == "b1":
        return r_a, r_b1
    dskip_t = np.ascontiguousarray(
        np.asarray(inputs["L0_attn_D_skip"]).reshape(CT, 128).T).astype(np.float32)
    wout = np.ascontiguousarray(np.asarray(inputs["L0_attn_W_out"], np.float32))
    zeros_c = np.zeros((128, CT * NSTATE), np.float32)

    def maps_b2():
        out = []
        for c in range(NCORES):
            b, h = c // 2, c % 2
            carry = zeros_c if h == 0 else np.asarray(r_b1[2 * b]["carry_o"], np.float32)
            out.append(dict(delta_i=r_a[c]["delta_o"], u_i=r_a[c]["u_o"],
                            bmcm_i=r_a[c]["bmcm_o"], alog=alog_t, carry_i=carry,
                            z_i=r_a[c]["z_o"], dskip=dskip_t, wout=wout))
        return out
    r_b2 = _spmd("l0b_full", lambda: build_l0b("full"), maps_b2)
    if upto == "b2":
        return r_a, r_b1, r_b2
    x = np.asarray(inputs["L0_attn_x"])
    b1_t = np.ascontiguousarray(np.asarray(inputs["L0_ffn_b1"]).reshape(8, 32, 128)
                                .transpose(2, 0, 1).reshape(128, 256)).astype(np.float32)
    b2_t = np.ascontiguousarray(np.asarray(inputs["L0_ffn_b2"]).reshape(8, 8, 128)
                                .transpose(2, 0, 1).reshape(128, 64)).astype(np.float32)
    wgred_t = np.ascontiguousarray(np.asarray(inputs["L0_ffn_wg_red"]).T).astype(np.float32)
    wg_t = np.ascontiguousarray(np.asarray(inputs["L0_ffn_wg"]).T).astype(np.float32)
    w1 = np.ascontiguousarray(np.asarray(inputs["L0_ffn_w1"], np.float32))
    w2 = np.ascontiguousarray(np.asarray(inputs["L0_ffn_w2"], np.float32))

    def maps_f():
        out = []
        for c in range(NCORES):
            out.append(dict(xn_i=r_a[c]["xn_o"], x_i=_fm(_percore_rows(x, c)),
                            s_i=r_b2[c]["s_o"], wgred_t=wgred_t, wg_t=wg_t,
                            w1=w1, b1_t=b1_t, w2=w2, b2_t=b2_t))
        return out
    r_f = _spmd("l0f", build_l0f, maps_f)
    return r_a, r_b1, r_b2, r_f


def host_l1_colq(c):
    b, h = c // 2, c % 2
    arr = np.zeros((128, 4 * 32), np.float32)
    p = np.arange(128)
    for qb in range(4):
        for kt in range(32):
            rot_base = (h * T + kt * 128) if kt < 16 else ((1 - h) * T + (kt - 16) * 128)
            arr[:, qb * 32 + kt] = rot_base + p - qb * 512 - h * T
    return arr


def run_l1(inputs, x1_percore):
    wq_t = np.ascontiguousarray(np.asarray(inputs["L1_attn_Wq"], np.float32).T)
    wk_t = np.ascontiguousarray(np.asarray(inputs["L1_attn_Wk"], np.float32).T)
    wv_t = np.ascontiguousarray(np.asarray(inputs["L1_attn_Wv"], np.float32).T)
    wo_t = np.ascontiguousarray(np.asarray(inputs["L1_attn_Wo"], np.float32).T)
    gw = np.ascontiguousarray(np.asarray(inputs["L1_ffn_gate_w"], np.float32))
    gb = np.ascontiguousarray(np.asarray(inputs["L1_ffn_gate_b"], np.float32).reshape(8, 1))
    ew = np.ascontiguousarray(np.asarray(inputs["L1_ffn_expert_w"], np.float32))
    eb = np.ascontiguousarray(np.asarray(inputs["L1_ffn_expert_b"], np.float32))

    def maps():
        out = []
        for c in range(NCORES):
            b, h = c // 2, c % 2
            own = np.asarray(x1_percore[c], np.float32)
            other = np.asarray(x1_percore[2 * b + (1 - h)], np.float32)
            xf = np.concatenate([own, other], axis=2)
            out.append(dict(xf_i=np.ascontiguousarray(xf), wq_t=wq_t, wk_t=wk_t,
                            wv_t=wv_t, wo_t=wo_t, colq=host_l1_colq(c), gw=gw, gb=gb,
                            ew=ew, eb=eb))
        return out
    return _spmd("l1", build_l1, maps)


def _rope_tables(pos):
    inv = 1.0 / 10000.0 ** (np.arange(0, 64, 2) / 64.0)
    ang = pos[:, None] * inv[None]
    sin = np.concatenate([np.sin(ang)] * 2, -1)
    cos = np.concatenate([np.cos(ang)] * 2, -1)
    return np.ascontiguousarray(cos.T).astype(np.float32), \
        np.ascontiguousarray(sin.T).astype(np.float32)


def run_l2(inputs, x2_percore):
    g = lambda k: np.asarray(inputs[k], np.float32)
    wq_t = np.ascontiguousarray(g("L2_attn_Wq").T)
    bq_c = np.ascontiguousarray(g("L2_attn_bq").reshape(8, 128).T)
    wk_t = np.ascontiguousarray(g("L2_attn_Wk").T)
    bk_c = np.ascontiguousarray(g("L2_attn_bk").reshape(2, 128).T)
    wv_t = np.ascontiguousarray(g("L2_attn_Wv").T)
    bv_r = np.ascontiguousarray(g("L2_attn_bv").reshape(1, 256))
    wo_t = np.ascontiguousarray(g("L2_attn_Wo").T)
    bo_r = np.ascontiguousarray(g("L2_attn_bo").reshape(1, D))
    wr_t = np.ascontiguousarray(g("L2_ffn_Wr").T)
    br_c = np.ascontiguousarray(g("L2_ffn_br").reshape(8, 1))
    we_t = np.ascontiguousarray(g("L2_ffn_We").transpose(0, 2, 1))
    be_i = np.ascontiguousarray(g("L2_ffn_be"))
    fwo_t = np.ascontiguousarray(g("L2_ffn_Wo").T)
    fbo_r = np.ascontiguousarray(g("L2_ffn_bo").reshape(1, D))

    def maps():
        out = []
        for c in range(NCORES):
            b, h = c // 2, c % 2
            own = np.asarray(x2_percore[c], np.float32)
            other = np.asarray(x2_percore[2 * b + (1 - h)], np.float32)
            xf = np.ascontiguousarray(np.concatenate([own, other], axis=2))
            own_pos = np.arange(h * T, (h + 1) * T, dtype=np.float64)
            rot_pos = np.concatenate([own_pos, np.arange((1 - h) * T, (2 - h) * T,
                                                         dtype=np.float64)])
            cosq, sinq = _rope_tables(own_pos)
            cosk, sink = _rope_tables(rot_pos)
            out.append(dict(xf_i=xf, wq_t=wq_t, bq_c=bq_c, wk_t=wk_t, bk_c=bk_c,
                            wv_t=wv_t, bv_r=bv_r, wo_t=wo_t, bo_r=bo_r,
                            cosq=cosq * 0.125, sinq=sinq * 0.125, cosk=cosk, sink=sink,
                            wr_t=wr_t, br_c=br_c, we_t=we_t, be_i=be_i,
                            fwo_t=fwo_t, fbo_r=fbo_r))
        return out
    return _spmd("l2", build_l2, maps)


def run_l3(inputs, x3_percore):
    g = lambda k: np.asarray(inputs[k], np.float32)
    wqkv = np.ascontiguousarray(g("L3_attn_w_qkv"))
    qs_c = np.ascontiguousarray(g("L3_attn_q_scale").reshape(64, 1))
    ks_c = np.ascontiguousarray(g("L3_attn_k_scale").reshape(64, 1))
    wp_t = np.ascontiguousarray(g("L3_attn_w_proj"))
    bp_r = np.ascontiguousarray(g("L3_attn_b_proj").reshape(1, D))
    bw_t = np.ascontiguousarray(g("L3_ffn_base_W").T)
    bb_r = np.ascontiguousarray(g("L3_ffn_base_b").reshape(1, D))
    shA_t = np.ascontiguousarray(g("L3_ffn_shared_A").T)
    shB_t = np.ascontiguousarray(g("L3_ffn_shared_B").T)
    rA_t = np.ascontiguousarray(g("L3_ffn_routed_A").transpose(0, 2, 1))
    rB_t = np.ascontiguousarray(g("L3_ffn_routed_B").transpose(0, 2, 1))
    rw_t = np.ascontiguousarray(g("L3_ffn_router_W").T)
    rb_c = np.ascontiguousarray(g("L3_ffn_router_b").reshape(6, 1))

    def maps():
        out = []
        for c in range(NCORES):
            b, h = c // 2, c % 2
            own = np.asarray(x3_percore[c], np.float32)
            other = np.asarray(x3_percore[2 * b + (1 - h)], np.float32)
            xf = np.ascontiguousarray(np.concatenate([own, other], axis=2))
            own_pos = np.arange(h * T, (h + 1) * T, dtype=np.float64)
            rot_pos = np.concatenate([own_pos, np.arange((1 - h) * T, (2 - h) * T,
                                                         dtype=np.float64)])
            cosq, sinq = _rope_tables(own_pos)
            cosk, sink = _rope_tables(rot_pos)
            out.append(dict(xf_i=xf, wqkv=wqkv, qs_c=qs_c, ks_c=ks_c, wp_t=wp_t, bp_r=bp_r,
                            cosq=cosq * 0.125, sinq=sinq * 0.125, cosk=cosk, sink=sink,
                            bw_t=bw_t, bb_r=bb_r, shA_t=shA_t, shB_t=shB_t, rA_t=rA_t,
                            rB_t=rB_t, rw_t=rw_t, rb_c=rb_c))
        return out
    return _spmd("l3", build_l3, maps)


def kernel(**inputs):
    return kernel_fused(inputs)


def kernel_multilaunch(**inputs):
    r = run_l0(inputs, upto="f")
    r_f = r[3]
    x1_pc = [np.asarray(r_f[c]["x1_o"]) for c in range(NCORES)]
    r1 = run_l1(inputs, x1_pc)
    x2_pc = [np.asarray(r1[c]["x2_o"]) for c in range(NCORES)]
    r2 = run_l2(inputs, x2_pc)
    x3_pc = [np.asarray(r2[c]["x3_o"]) for c in range(NCORES)]
    r3 = run_l3(inputs, x3_pc)
    out = np.empty((4, 4096, D), np.float32)
    for c in range(NCORES):
        b, h = c // 2, c % 2
        out[b, h * T:(h + 1) * T] = _unfm(np.asarray(r3[c]["x4_o"], np.float32))
    return out


# version 07
# speedup vs baseline: 1.1475x; 1.0433x over previous
import math
import numpy as np
import ml_dtypes
from contextlib import ExitStack
import concourse.bass as bass
import concourse.mybir as mybir
from concourse.bass_utils import run_bass_kernel_spmd

F32 = mybir.dt.float32
BF16 = mybir.dt.bfloat16
AF = mybir.ActivationFunctionType
ALU = mybir.AluOpType
AX = mybir.AxisListType

BF = ml_dtypes.bfloat16

NCORES = 8
T = 2048
TH = 2560
TF = 2 * T
D = 1024
NK = D // 128
CT = 16
NSTATE = 16
BLKS = [(i * 512, 512) for i in range(4)]
BLKS_H = [(i * 512, 512) for i in range(5)]

ENG_ATTR = {"pe": "tensor", "act": "scalar", "dve": "vector", "pool": "gpsimd", "sp": "sync"}


class Op:
    __slots__ = ("eng", "clockeng", "build", "tag", "idx",
                 "eng_idx", "waits", "needs_inc", "inc_val", "cc")

    def __init__(self):
        self.waits = []
        self.needs_inc = False
        self.inc_val = None


class Fw:
    def __init__(self, nc):
        self.nc = nc
        self.ops = []
        self.eng_ops = {e: [] for e in ENG_ATTR}
        self.tacc = {}
        self.eng_clock = {e: {} for e in ENG_ATTR}
        self.part_elems = {}
        self.cms = {}
        self.alloc_seq = {}
        self.free_evt = {}
        self.ctx = ExitStack()
        self.n_false_fallback = 0
        self.dma_domains = set()
        self.prefix = ""
        self.input_names = []
        self.onerow = None
        self.aux_ps = None

    def sb(self, name, shape, dtype=F32):
        name = self.prefix + name
        cm = self.nc.sbuf_tensor(name, shape, dtype)
        h = cm.__enter__()
        self.cms[name] = cm
        self.alloc_seq[name] = len(self.alloc_seq)
        self.ctx.callback(self._free_cm, name)
        self.part_elems[name] = int(np.prod(shape[1:]))
        return h

    def _free_cm(self, name):
        cm = self.cms.pop(name, None)
        if cm is not None:
            self.free_evt[name] = len(self.ops)
            cm.__exit__(None, None, None)

    def mark(self):
        return len(self.alloc_seq)

    def free_since(self, mark):
        names = [n for n, i in self.alloc_seq.items() if i >= mark and n in self.cms]
        for name in sorted(names, key=lambda n: -self.alloc_seq[n]):
            self._free_cm(name)

    def free_sb(self, *handles):
        names = [h.name if hasattr(h, "name") else self.prefix + h for h in handles]
        for name in sorted(names, key=lambda n: -self.alloc_seq.get(n, -1)):
            self._free_cm(name)

    def barrier(self, engines=None):
        deps = {}
        for ce, lst in self.eng_ops.items():
            if lst:
                deps[ce] = len(lst) - 1
        for eng in (engines or list(ENG_ATTR)):
            op = self._add(eng, eng, lambda h: h.nop(), [], [], tag="barrier")
            self._force_deps(op, {ce: k for ce, k in deps.items()
                                  if not (ce == eng and k >= op.eng_idx)})

    def ps(self, name, shape, dtype=F32):
        name = self.prefix + name
        cm = self.nc.psum_tensor(name, shape, dtype)
        h = cm.__enter__()
        self.cms[name] = cm
        self.alloc_seq[name] = len(self.alloc_seq)
        self.ctx.callback(self._free_cm, name)
        self.part_elems[name] = int(np.prod(shape[1:]))
        return h

    def dram(self, name, shape, dtype=F32, kind="Internal"):
        name = self.prefix + name
        t = self.nc.dram_tensor(name, shape, dtype, kind=kind)
        self.part_elems[name] = None
        return t.ap()

    def inp(self, name, shape, dtype=F32):
        ap = self.dram(name, shape, dtype, kind="ExternalInput")
        self.input_names.append(self.prefix + name)
        return ap

    def outp(self, name, shape, dtype=F32):
        return self.dram(name, shape, dtype, kind="ExternalOutput")

    def _region(self, ap):
        th = ap.tensor
        name = th.name
        pe = self.part_elems.get(name, None)
        off = int(ap.offset)
        dims = ap.ap
        if pe is None:
            ext = 0
            for step, count in dims:
                if count > 1:
                    ext += abs(int(step)) * (int(count) - 1)
            return (name, 0, 1, off, off + ext + 1)
        p_lo, b_lo = divmod(off, pe)
        p_ext = 0
        b_ext = 0
        ok = True
        for step, count in dims:
            step = int(step); count = int(count)
            if count <= 1 or step == 0:
                continue
            if abs(step) % pe == 0:
                p_ext += (abs(step) // pe) * (count - 1)
            elif abs(step) < pe:
                b_ext += abs(step) * (count - 1)
            else:
                ok = False
                break
        if not ok or b_lo + b_ext >= pe:
            self.n_false_fallback += 1
            return (name, 0, 1 << 20, 0, 1 << 40)
        return (name, p_lo, p_lo + p_ext + 1, b_lo, b_lo + b_ext + 1)

    def _full_region(self, ap):
        return (ap.tensor.name, 0, 1 << 20, 0, 1 << 40)

    @staticmethod
    def _overlap(r1, r2):
        return (r1[1] < r2[2] and r2[1] < r1[2] and r1[3] < r2[4] and r2[3] < r1[4])

    @staticmethod
    def _covers(r1, r2):
        return r1[1] <= r2[1] and r1[2] >= r2[2] and r1[3] <= r2[3] and r1[4] >= r2[4]

    def comp(self, eng, build, reads=(), writes=(), tag=""):
        rregs = [self._region(ap) for ap in reads]
        wregs = [self._region(ap) for ap in writes]
        return self._add(eng, eng, build, rregs, wregs, tag)

    def dma(self, qeng, out, in_, tag="", **kw):
        def build(h):
            return h.dma_start(out=out, in_=in_, **kw)
        dom = "D." + out.tensor.name
        if dom not in self.eng_ops:
            self.eng_ops[dom] = []
            self.dma_domains.add(dom)
        rregs = [self._region(in_)]
        wregs = [self._full_region(out)]
        op = self._add(qeng, dom, build, rregs, wregs, tag)
        op.needs_inc = True
        return op

    def _add(self, eng, clockeng, build, rregs, wregs, tag):
        op = Op()
        op.eng = eng
        op.clockeng = clockeng
        op.build = build
        op.tag = tag
        op.idx = len(self.ops)
        op.eng_idx = len(self.eng_ops[clockeng])
        self.ops.append(op)
        self.eng_ops[clockeng].append(op)
        deps = {}

        def add_dep(dop):
            if dop is op:
                return
            if dop.clockeng == "pe" and op.clockeng == "pe":
                return
            ce = dop.clockeng
            if deps.get(ce, -1) < dop.eng_idx:
                deps[ce] = dop.eng_idx

        for reg in rregs:
            acc = self.tacc.get(reg[0])
            if acc:
                for dop, wreg in acc["w"]:
                    if self._overlap(reg, wreg):
                        add_dep(dop)
        for reg in wregs:
            acc = self.tacc.get(reg[0])
            if acc:
                for dop, wreg in acc["w"]:
                    if self._overlap(reg, wreg):
                        add_dep(dop)
                for dop, rreg in acc["r"]:
                    if self._overlap(reg, rreg):
                        add_dep(dop)
        self._install(op, deps)
        for reg in wregs:
            acc = self.tacc.setdefault(reg[0], {"w": [], "r": []})
            acc["w"] = [(o, r) for (o, r) in acc["w"] if not self._covers(reg, r)]
            acc["r"] = [(o, r) for (o, r) in acc["r"] if not self._covers(reg, r)]
            acc["w"].append((op, reg))
        for reg in rregs:
            acc = self.tacc.setdefault(reg[0], {"w": [], "r": []})
            acc["r"].append((op, reg))
        return op

    def _add_wait(self, op, ce, k):
        wait_eng = op.eng
        if self.eng_clock[wait_eng].get(ce, -1) >= k:
            return
        target = self.eng_ops[ce][k]
        if ce not in self.dma_domains:
            target.needs_inc = True
        op.waits.append((ce, k))
        clk = self.eng_clock[wait_eng]
        tc = target.cc
        if tc:
            for e2, v2 in tc.items():
                if clk.get(e2, -1) < v2:
                    clk[e2] = v2
        if clk.get(ce, -1) < k:
            clk[ce] = k

    def _install(self, op, deps):
        for ce in sorted(deps):
            self._add_wait(op, ce, deps[ce])
        cc = dict(self.eng_clock[op.eng])
        cc[op.clockeng] = op.eng_idx
        op.cc = cc

    def _force_deps(self, op, deps):
        for ce, k in deps.items():
            if ce == op.clockeng and k <= op.eng_idx:
                continue
            self._add_wait(op, ce, k)

    def finish_outputs(self):
        deps = {}
        for name, pe in self.part_elems.items():
            if pe is not None:
                continue
            acc = self.tacc.get(name)
            if not acc:
                continue
            for dop, _ in acc["w"]:
                ce = dop.clockeng
                if deps.get(ce, -1) < dop.eng_idx:
                    deps[ce] = dop.eng_idx
        op = self._add("sp", "sp", lambda h: h.nop(), [], [], tag="finish")
        self._force_deps(op, deps)

    def finalize(self, emit_stats=True):
        nc = self.nc
        val_of = {}
        for ce, lst in self.eng_ops.items():
            n = 0
            for o in lst:
                if o.needs_inc:
                    n += 1
                    o.inc_val = n
                val_of[(ce, o.eng_idx)] = n
        sems = {}
        base_val = {}
        n_phys = 0
        phys = []
        doms = []
        for ce in self.eng_ops:
            if ce in self.dma_domains and self.eng_ops[ce]:
                first = self.eng_ops[ce][0].idx
                last = self.eng_ops[ce][-1].idx
                tname = ce[2:]
                fidx = self.free_evt.get(tname)
                end = fidx if (fidx is not None and fidx > last) else None
                doms.append((first, end, ce))
        doms.sort()
        for first, end, ce in doms:
            slot = None
            for s in phys:
                if s["end"] is not None and s["end"] <= first:
                    slot = s
                    break
            if slot is None:
                slot = {"end": None, "cum": 0,
                        "handle": self.ctx.enter_context(nc.semaphore(f"dsem{len(phys)}"))}
                phys.append(slot)
            base_val[ce] = slot["cum"]
            slot["cum"] += 16 * len(self.eng_ops[ce])
            slot["end"] = end
            sems[ce] = slot["handle"]
        for ce, lst in self.eng_ops.items():
            if ce in self.dma_domains:
                continue
            if any(o.needs_inc for o in lst):
                sems[ce] = self.ctx.enter_context(nc.semaphore("sem_" + ce))
                base_val[ce] = 0
        per_eng = {e: [] for e in ENG_ATTR}
        for op in self.ops:
            per_eng[op.eng].append(op)
        mult = {ce: (16 if ce in self.dma_domains else 1) for ce in self.eng_ops}
        tmap = {}
        nc._tag_map = tmap
        block = self.ctx.enter_context(nc.Block())

        def make_body(ops):
            def body(h):
                for op in ops:
                    for (ce, k) in op.waits:
                        h.wait_ge(sems[ce], base_val[ce] + val_of[(ce, k)] * mult[ce])
                    inst = op.build(h)
                    try:
                        tmap[inst.ins.name] = op.tag or "?"
                    except Exception:
                        pass
                    if op.needs_inc:
                        inst.then_inc(sems[op.clockeng], mult[op.clockeng])
            return body

        for eng_name, ops in per_eng.items():
            if not ops:
                continue
            getattr(block, ENG_ATTR[eng_name])(make_body(ops))
        if emit_stats:
            n_w = sum(len(o.waits) for o in self.ops)
            n_i = sum(1 for o in self.ops if o.needs_inc)
            print(f"[fw] dsems={len(phys)} ", end="")
            print(f"[fw] ops={len(self.ops)} waits={n_w} incs={n_i} "
                  f"fallbacks={self.n_false_fallback} "
                  f"per_eng={{{', '.join(f'{e}:{len(v)}' for e, v in per_eng.items() if v)}}}",
                  flush=True)
        self.ctx.close()



def mm1(fw, psum_ap, l, r, first, last, tag=""):
    return fw.comp("pe",
                   lambda h: h.matmul(psum_ap, l, r, start=first, stop=last),
                   reads=[l, r], writes=[psum_ap], tag=tag)


def mm(fw, psum_ap, pairs, tag=""):
    n = len(pairs)
    for i, (l, r) in enumerate(pairs):
        mm1(fw, psum_ap, l, r, i == 0, i == n - 1, tag=tag)


def act(fw, out, in_, func, bias=0.0, scale=1.0, tag=""):
    reads = [in_] + [b for b in (bias, scale) if hasattr(b, "tensor")]
    return fw.comp("act", lambda h: h.activation(out=out, in_=in_, func=func, bias=bias, scale=scale),
                   reads=reads, writes=[out], tag=tag)


def tt(fw, eng, out, in0, in1, op, tag=""):
    return fw.comp(eng, lambda h: h.tensor_tensor(out=out, in0=in0, in1=in1, op=op),
                   reads=[in0, in1], writes=[out], tag=tag)


def ts(fw, eng, out, in0, s1, op0, s2=None, op1=None, tag=""):
    reads = [in0] + [s for s in (s1, s2) if hasattr(s, "tensor")]
    if op1 is None:
        return fw.comp(eng, lambda h: h.tensor_single_scalar(out=out, in_=in0, scalar=s1, op=op0),
                       reads=reads, writes=[out], tag=tag)
    return fw.comp(eng, lambda h: h.tensor_scalar(out=out, in0=in0, scalar1=s1, scalar2=s2,
                                                  op0=op0, op1=op1),
                   reads=reads, writes=[out], tag=tag)


def stt(fw, eng, out, in0, s, in1, op0, op1, tag=""):
    reads = [in0, in1] + ([s] if hasattr(s, "tensor") else [])
    return fw.comp(eng, lambda h: h.scalar_tensor_tensor(out=out, in0=in0, scalar=s, in1=in1,
                                                         op0=op0, op1=op1),
                   reads=reads, writes=[out], tag=tag)


def copy(fw, eng, out, in_, tag=""):
    if eng == "act":
        return fw.comp("act", lambda h: h.copy(out=out, in_=in_), reads=[in_], writes=[out], tag=tag)
    return fw.comp(eng, lambda h: h.tensor_copy(out=out, in_=in_), reads=[in_], writes=[out], tag=tag)


def scan(fw, eng, out, d0, d1, initial, tag=""):
    reads = [d0, d1] + ([initial] if hasattr(initial, "tensor") else [])
    return fw.comp(eng, lambda h: h.tensor_tensor_scan(out=out, data0=d0, data1=d1, initial=initial,
                                                       op0=ALU.mult, op1=ALU.add),
                   reads=reads, writes=[out], tag=tag)


def reduce(fw, eng, out, in_, op=ALU.add, axis=AX.X, negate=None, tag=""):
    return fw.comp(eng, lambda h: h.tensor_reduce(out=out, in_=in_, axis=axis, op=op, negate=negate),
                   reads=[in_], writes=[out], tag=tag)


def recip(fw, out, in_, tag=""):
    return fw.comp("dve", lambda h: h.reciprocal(out=out, in_=in_), reads=[in_], writes=[out], tag=tag)


def memset(fw, eng, ap_, v, tag=""):
    return fw.comp(eng, lambda h: h.memset(ap_, v), reads=[], writes=[ap_], tag=tag)


def transpose_pe(fw, out_psum, in_, identity, tag=""):
    return fw.comp("pe", lambda h: h.transpose(out_psum, in_, identity),
                   reads=[in_, identity], writes=[out_psum], tag=tag)


class PsumPool:
    def __init__(self, fw, n=6, prefix="pp"):
        self.tiles = [fw.ps(f"{prefix}{i}", [128, 512], F32) for i in range(n)]
        self.i = 0

    def take(self):
        t = self.tiles[self.i % len(self.tiles)]
        self.i += 1
        return t


class Consts:
    def __init__(self, fw):
        self.fw = fw
        self.arrays = {}
        self.handles = {}
        self.scalar_vals = []
        self.scalar_sb = None

    def scalar(self, val, p=128):
        if self.scalar_sb is None:
            self.scalar_sb = self.fw.sb("cs_scalars", [128, 16], F32)
            d_in = self.fw.inp("c_scalars", [128, 16], F32)
            self.fw.dma("sp", self.scalar_sb[:], d_in)
        if val not in self.scalar_vals:
            assert len(self.scalar_vals) < 16
            self.scalar_vals.append(val)
            arr = np.zeros((128, 16), np.float32)
            arr[:, :len(self.scalar_vals)] = np.asarray(self.scalar_vals, np.float32)
            self.arrays["c_scalars"] = arr
        i = self.scalar_vals.index(val)
        return self.scalar_sb[0:p, i:i + 1]

    def add(self, name, arr, dtype=None, qeng="sp"):
        if name in self.handles:
            return self.handles[name]
        fw = self.fw
        arr = np.asarray(arr)
        dt = dtype or (BF16 if arr.dtype == BF else F32)
        d_in = fw.inp("c_" + name, list(arr.shape), F32 if arr.dtype == np.float32 else dt)
        sb_h = fw.sb("cs_" + name, list(arr.shape), dt)
        fw.dma("pool" if (dt == BF16 and arr.dtype == np.float32) else qeng, sb_h[:], d_in)
        self.arrays["c_" + name] = arr.astype(np.float32) if arr.dtype != BF else arr
        self.handles[name] = sb_h
        return sb_h


class Pipe:
    def __init__(self):
        self.q = []

    def add(self, *stage_fns):
        self.q.append(stage_fns)

    def run(self):
        n = len(self.q)
        if not n:
            return
        ns = max(len(s) for s in self.q)
        for t in range(n + ns - 1):
            for s in range(ns):
                i = t - s
                if 0 <= i < n and s < len(self.q[i]):
                    self.q[i][s]()
        self.q = []


def mk_ln_tmp(fw, pre="t"):
    sq = fw.sb(pre + "_sq", [128, 512], F32)
    m = fw.sb(pre + "_ra", [1, 512], F32)
    q = fw.sb(pre + "_qq", [1, 512], F32)
    v = fw.sb(pre + "_vv", [1, 512], F32)
    r = fw.sb(pre + "_rb", [1, 512], F32)
    b = fw.sb(pre + "_bb", [1, 512], F32)
    rr = fw.sb(pre + "_rr", [128, 512], F32)
    br = fw.sb(pre + "_br", [128, 512], F32)
    return (sq, m[:], q[:], v[:], r[:], b[:], rr, br)


def ln_block(fw, P, cst, xtiles, src_off, W, out_tiles, dst_off, tmp, nd=1024, eps=1e-5,
             out_dtype_note="bf16"):
    onecol = cst.handles["onecol"]
    onerow = cst.handles["onerow"]
    sq, row_m, row_q, row_v, row_r, row_b, rrep, brep = tmp
    nk = len(xtiles)
    ps_s = P.take()
    ps_q = P.take()
    for i, x in enumerate(xtiles):
        mm1(fw, ps_s[0:1, :W], onecol[:, 0:1], x[:, src_off:src_off + W], i == 0, i == nk - 1, tag="ln_s")
    for i, x in enumerate(xtiles):
        act(fw, sq[:, :W], x[:, src_off:src_off + W], AF.Square, tag="ln_sq")
        mm1(fw, ps_q[0:1, :W], onecol[:, 0:1], sq[:, :W], i == 0, i == nk - 1, tag="ln_q")
    ts(fw, "dve", row_m[0:1, :W], ps_s[0:1, :W], 1.0 / nd, ALU.mult, tag="ln_m")
    ts(fw, "dve", row_q[0:1, :W], ps_q[0:1, :W], 1.0 / nd, ALU.mult, tag="ln_q2")
    stt(fw, "dve", row_v[0:1, :W], row_m[0:1, :W], -1.0, row_m[0:1, :W], ALU.mult, ALU.mult, tag="ln_mm")
    tt(fw, "dve", row_v[0:1, :W], row_q[0:1, :W], row_v[0:1, :W], ALU.add, tag="ln_v")
    act(fw, row_r[0:1, :W], row_v[0:1, :W], AF.Ln, bias=cst.scalar(eps, 1), tag="ln_ln")
    act(fw, row_r[0:1, :W], row_r[0:1, :W], AF.Exp, scale=cst.scalar(-0.5, 1), tag="ln_rs")
    stt(fw, "dve", row_b[0:1, :W], row_m[0:1, :W], -1.0, row_r[0:1, :W], ALU.mult, ALU.mult, tag="ln_b")
    ps_r = P.take()
    ps_b = P.take()
    mm1(fw, ps_r[:, :W], onerow[0:1, :], row_r[0:1, :W], True, True, tag="ln_repr")
    mm1(fw, ps_b[:, :W], onerow[0:1, :], row_b[0:1, :W], True, True, tag="ln_repb")
    copy(fw, "act", rrep[:, :W], ps_r[:, :W], tag="ln_er")
    copy(fw, "act", brep[:, :W], ps_b[:, :W], tag="ln_eb")
    for x, o in zip(xtiles, out_tiles):
        tt(fw, "dve", sq[:, :W], x[:, src_off:src_off + W], rrep[:, :W], ALU.mult, tag="ln_xr")
        tt(fw, "dve", o[:, dst_off:dst_off + W], sq[:, :W], brep[:, :W], ALU.add, tag="ln_xn")



CONST_SETS = {
    "l0a": {"onecol", "onerow"},
    "l0b_carry": {"sel16"},
    "l0b_full": {"sel16"},
    "l0f": {"onecol", "onerow", "idt", "sel8"},
    "l1": {"onecol", "onecol_bf", "onerow", "idt", "iotaq", "sel8"},
    "l2": {"onecol", "onerow", "ones_row512", "idt", "sel8", "perm"},
    "l3": {"onecol", "onerow", "ones_row512", "idt", "sel6", "perm3"},
}


def setup_consts(fw, cst, only=None):
    def add(name, arr, **kw):
        if only is None or name in only:
            cst.add(name, arr, **kw)
    add("onecol", np.ones((128, 1), np.float32))
    add("onecol_bf", np.ones((128, 1), np.float32), dtype=BF16)
    add("onerow", np.ones((96, 128), np.float32))
    add("ones_row512", np.ones((1, 512), np.float32), dtype=BF16)
    add("idt", np.eye(128, dtype=np.float32))
    add("iotaq", np.broadcast_to(np.arange(512, dtype=np.float32), (128, 512)).copy())
    sel16 = np.zeros((16, 16 * 128), np.float32)
    for n in range(16):
        sel16[n, n * 128:(n + 1) * 128] = 1.0
    add("sel16", sel16, dtype=BF16)
    sel8 = np.zeros((8, 8 * 128), np.float32)
    for e in range(8):
        sel8[e, e * 128:(e + 1) * 128] = 1.0
    add("sel8", sel8, dtype=BF16)
    sel6 = np.zeros((6, 6 * 128), np.float32)
    for e in range(6):
        sel6[e, e * 128:(e + 1) * 128] = 1.0
    add("sel6", sel6, dtype=BF16)
    perm = np.zeros((64, 64), np.float32)
    for i in range(32):
        perm[2 * i + 1, i] = -1.0
    for i in range(32, 64):
        perm[2 * (i - 32), i] = 1.0
    add("perm", perm, dtype=BF16)
    perm3 = np.zeros((64, 64), np.float32)
    for i in range(32):
        perm3[i + 32, i] = -1.0
        perm3[i, i + 32] = 1.0
    add("perm3", perm3, dtype=BF16)
    for v in (1e-5, -0.5, 1.0 / 32.0, 1e-6, 1.0 / 64.0):
        cst.scalar(v, 1)
    fw.onerow = cst.handles.get("onerow")
    if fw.aux_ps is None:
        fw.aux_ps = fw.ps("aux_ps", [1, 512], F32)


IO_SPECS = {
    "l0a": dict(
        ins=dict(xh=([NK, 128, TH], F32), w_in=([D, 4096], F32), convw=([128, CT * 4], F32),
                 convb=([128, CT], F32), wxp=([2048, 96], F32), wdt=([64, 2048], F32),
                 bdt=([128, CT], F32)),
        outs=dict(xn_o=([NK, 128, T], BF16), u_o=([CT, 128, T], BF16),
                  z_o=([CT, 128, T], BF16), delta_o=([CT, 128, T], F32),
                  bmcm_o=([32, T], F32))),
    "l0b_carry": dict(
        ins=dict(delta_i=([CT, 128, T], F32), u_i=([CT, 128, T], BF16),
                 bmcm_i=([32, T], F32), alog=([128, CT * NSTATE], F32)),
        outs=dict(carry_o=([128, CT * NSTATE], F32))),
    "l0b_full": dict(
        ins=dict(delta_i=([CT, 128, T], F32), u_i=([CT, 128, T], BF16),
                 bmcm_i=([32, T], F32), alog=([128, CT * NSTATE], F32),
                 carry_i=([128, CT * NSTATE], F32), z_i=([CT, 128, T], BF16),
                 dskip=([128, CT], F32), wout=([2048, D], F32)),
        outs=dict(s_o=([NK, 128, T], F32))),
    "l0f": dict(
        ins=dict(xn_i=([NK, 128, T], BF16), x_i=([NK, 128, T], F32), s_i=([NK, 128, T], F32),
                 wgred_t=([D, 16], F32), wg_t=([16, 8], F32), w1=([8, D, 4096], F32),
                 b1_t=([128, 8 * 32], F32), w2=([8, 4096, D], F32), b2_t=([128, 8 * 8], F32)),
        outs=dict(x1_o=([NK, 128, T], F32))),
    "l1": dict(
        ins=dict(xf_i=([NK, 128, TF], F32), wq_t=([D, D], F32), wk_t=([D, D], F32),
                 wv_t=([D, D], F32), wo_t=([D, D], F32), colq=([128, 4 * (TF // 128)], F32),
                 gw=([D, 8], F32), gb=([8, 1], F32), ew=([8, D, D], F32), eb=([8, D], F32)),
        outs=dict(x2_o=([NK, 128, T], F32))),
    "l2": dict(
        ins=dict(xf_i=([NK, 128, TF], F32), wq_t=([D, D], F32), bq_c=([128, 8], F32),
                 wk_t=([D, 256], F32), bk_c=([128, 2], F32), wv_t=([D, 256], F32),
                 bv_r=([1, 256], F32), wo_t=([D, D], F32), bo_r=([1, D], F32),
                 cosq=([64, T], F32), sinq=([64, T], F32), cosk=([64, TF], F32),
                 sink=([64, TF], F32), wr_t=([D, 8], F32), br_c=([8, 1], F32),
                 we_t=([8, D, D], F32), be_i=([8, D], F32), fwo_t=([D, D], F32),
                 fbo_r=([1, D], F32)),
        outs=dict(x3_o=([NK, 128, T], F32))),
    "l3": dict(
        ins=dict(xf_i=([NK, 128, TF], F32), wqkv=([D, 3 * D], F32), qs_c=([64, 1], F32),
                 ks_c=([64, 1], F32), wp_t=([D, D], F32), bp_r=([1, D], F32),
                 cosq=([64, T], F32), sinq=([64, T], F32), cosk=([64, TF], F32),
                 sink=([64, TF], F32), bw_t=([D, D], F32), bb_r=([1, D], F32),
                 shA_t=([D, 16], F32), shB_t=([16, D], F32), rA_t=([6, D, 16], F32),
                 rB_t=([6, 16, D], F32), rw_t=([D, 6], F32), rb_c=([6, 1], F32)),
        outs=dict(x4_o=([NK, 128, T], F32))),
}

EMIT_OF = {"l0a": "emit_l0a", "l0b_carry": "emit_l0b", "l0b_full": "emit_l0b",
           "l0f": "emit_l0f", "l1": "emit_l1", "l2": "emit_l2", "l3": "emit_l3"}


def _mk_builder(key):
    def build():
        nc = bass.Bass("TRN2", target_bir_lowering=False)
        fw = Fw(nc)
        cst = Consts(fw)
        setup_consts(fw, cst, CONST_SETS.get(key))
        P = PsumPool(fw, 5)
        spec = IO_SPECS[key]
        IO = {}
        for nm, (shape, dt) in spec["ins"].items():
            IO[nm] = fw.inp(nm, shape, dt)
        for nm, (shape, dt) in spec["outs"].items():
            IO[nm] = fw.outp(nm, shape, dt)
        fn = globals()[EMIT_OF[key]]
        if key.startswith("l0b"):
            fn(fw, cst, P, IO, mode=key.split("_")[1])
        else:
            fn(fw, cst, P, IO)
        fw.finish_outputs()
        fw.finalize()
        return nc, cst
    return build


def emit_l0a(fw, cst, P, IO):

    xh = IO["xh"]
    w_in = IO["w_in"]
    convw = IO["convw"]
    convb = IO["convb"]
    wxp = IO["wxp"]
    wdt = IO["wdt"]
    bdt = IO["bdt"]

    xn_o = IO["xn_o"]
    u_o = IO["u_o"]
    z_o = IO["z_o"]
    delta_o = IO["delta_o"]
    bmcm_o = IO["bmcm_o"]


    win_sb = [fw.sb(f"win{k}", [128, 4096], BF16) for k in range(NK)]
    for k in range(NK):
        fw.dma("pool", win_sb[k][:], w_in[k * 128:(k + 1) * 128, :], tag="ld_win")
    wxp_sb = fw.sb("s_wxp", [128, CT * 96], BF16)
    fw.dma("pool", wxp_sb[:].rearrange("p (c r) -> p c r", c=CT), wxp.rearrange("(c p) r -> p c r", p=128), tag="ld_wxp")
    wdt_sb = fw.sb("s_wdt", [64, 2048], BF16)
    fw.dma("pool", wdt_sb[:], wdt, tag="ld_wdt")
    convw_sb = fw.sb("s_convw", [128, CT * 4], F32)
    fw.dma("sp", convw_sb[:], convw)
    convb_sb = fw.sb("s_convb", [128, CT], F32)
    fw.dma("sp", convb_sb[:], convb)
    bdt_sb = fw.sb("s_bdt", [128, CT], F32)
    fw.dma("sp", bdt_sb[:], bdt)

    xblk = [fw.sb(f"xb{k}", [128, 512], F32) for k in range(NK)]
    xn_sb = [fw.sb(f"xn{k}", [128, TH], BF16) for k in range(NK)]
    ln_tmp = mk_ln_tmp(fw, "t")
    xin_buf = fw.sb("xin", [128, TH], F32)
    cacc_a = fw.sb("cacc_a", [128, T], F32)
    cacc_b = fw.sb("cacc_b", [128, T], F32)
    u_ct = fw.sb("u_ct", [128, T], BF16)
    z_ct = fw.sb("z_ct", [128, T], BF16)
    xdbl = fw.sb("xdbl", [96, T], F32)
    dtlr = fw.sb("dtlr", [64, T], BF16)
    delta_tmp = fw.sb("delta_tmp", [128, T], F32)
    d_abs = fw.sb("d_abs", [128, 512], F32)
    d_m = fw.sb("d_m", [128, 512], F32)

    for (off, W) in BLKS_H:
        for k in range(NK):
            fw.dma("sp", xblk[k][:], xh[k, :, off:off + W], tag="ld_x")
        ln_block(fw, P, cst, xblk, 0, W, xn_sb, off, ln_tmp)

    for k in range(NK):
        fw.dma("sp", xn_o[k], xn_sb[k][:, 512:TH], tag="st_xn")

    for ct in range(CT):
        for (off, W) in BLKS_H:
            p = P.take()
            mm(fw, p[:, :W],
               [(win_sb[k][:, ct * 128:(ct + 1) * 128], xn_sb[k][:, off:off + W]) for k in range(NK)],
               tag="mm_xin")
            copy(fw, "act", xin_buf[:, off:off + W], p[:, :W], tag="ev_xin")
        ts(fw, "dve", cacc_a[:], xin_buf[:, 509:509 + T], convw_sb[:, ct * 4:ct * 4 + 1],
           ALU.mult, tag="conv0")
        stt(fw, "dve", cacc_b[:], xin_buf[:, 510:510 + T], convw_sb[:, ct * 4 + 1:ct * 4 + 2],
            cacc_a[:], ALU.mult, ALU.add, tag="conv1")
        stt(fw, "dve", cacc_a[:], xin_buf[:, 511:511 + T], convw_sb[:, ct * 4 + 2:ct * 4 + 3],
            cacc_b[:], ALU.mult, ALU.add, tag="conv2")
        stt(fw, "dve", cacc_b[:], xin_buf[:, 512:512 + T], convw_sb[:, ct * 4 + 3:ct * 4 + 4],
            cacc_a[:], ALU.mult, ALU.add, tag="conv3")
        act(fw, u_ct[:], cacc_b[:], AF.Silu, bias=convb_sb[:, ct:ct + 1], tag="silu_u")
        fw.dma("sp", u_o[ct], u_ct[:], tag="st_u")
        for (off, W) in BLKS:
            p = P.take()
            mm(fw, p[:, :W],
               [(win_sb[k][:, 2048 + ct * 128:2048 + (ct + 1) * 128],
                 xn_sb[k][:, 512 + off:512 + off + W]) for k in range(NK)],
               tag="mm_z")
            copy(fw, "act", z_ct[:, off:off + W], p[:, :W], tag="ev_z")
        fw.dma("sp", z_o[ct], z_ct[:], tag="st_z")
        for (off, W) in BLKS:
            p = P.take()
            mm1(fw, p[0:96, :W], wxp_sb[:, ct * 96:(ct + 1) * 96], u_ct[:, off:off + W],
                True, True, tag="mm_xdbl")
            if ct == 0:
                copy(fw, "dve", xdbl[:, off:off + W], p[0:96, :W], tag="xdbl_init")
            else:
                tt(fw, "dve", xdbl[:, off:off + W], xdbl[:, off:off + W], p[0:96, :W],
                   ALU.add, tag="xdbl_acc")

    copy(fw, "dve", dtlr[:], xdbl[0:64, :], tag="dtlr")
    fw.dma("sp", bmcm_o, xdbl[64:96, :], tag="st_bmcm")

    for ct in range(CT):
        for (off, W) in BLKS:
            p = P.take()
            mm1(fw, p[:, :W], wdt_sb[:, ct * 128:(ct + 1) * 128], dtlr[:, off:off + W],
                True, True, tag="mm_dt")
            act(fw, d_abs[:, :W], p[:, :W], AF.Abs, bias=bdt_sb[:, ct:ct + 1], tag="dt_abs")
            act(fw, d_abs[:, :W], d_abs[:, :W], AF.Exp, scale=-1.0, tag="dt_exp")
            act(fw, d_abs[:, :W], d_abs[:, :W], AF.Ln, bias=1.0, tag="dt_ln")
            act(fw, d_m[:, :W], p[:, :W], AF.Relu, bias=bdt_sb[:, ct:ct + 1], tag="dt_relu")
            tt(fw, "dve", delta_tmp[:, off:off + W], d_abs[:, :W], d_m[:, :W], ALU.add, tag="dt_add")
        fw.dma("sp", delta_o[ct], delta_tmp[:], tag="st_delta")





def emit_l0b(fw, cst, P, IO, mode):
    full = (mode == "full")

    delta_i = IO["delta_i"]
    u_i = IO["u_i"]
    bmcm_i = IO["bmcm_i"]
    alog = IO["alog"]
    if full:
        carry_i = IO["carry_i"]
        z_i = IO["z_i"]
        dskip = IO["dskip"]
        wout = IO["wout"]
        s_o = IO["s_o"]
    else:
        carry_o = IO["carry_o"]

    sel_sb = cst.handles["sel16"]

    bm_sb = fw.sb("s_bm", [16, T], BF16)
    fw.dma("pool", bm_sb[:], bmcm_i[0:16, :])
    cm_sb = fw.sb("s_cm", [16, T], BF16)
    fw.dma("pool", cm_sb[:], bmcm_i[16:32, :])
    alog_sb = fw.sb("s_alog", [128, CT * NSTATE], F32)
    fw.dma("sp", alog_sb[:], alog)
    A_sb = fw.sb("s_A", [128, CT * NSTATE], F32)
    act(fw, A_sb[:], alog_sb[:], AF.Exp, tag="A_exp")
    ts(fw, "dve", A_sb[:], A_sb[:], -1.0, ALU.mult, tag="A_neg")

    hcarry = fw.sb("hcarry", [128, CT * NSTATE], F32)
    if full:
        fw.dma("sp", hcarry[:], carry_i)
        dskip_sb = fw.sb("s_dskip", [128, CT], F32)
        fw.dma("sp", dskip_sb[:], dskip)
    else:
        memset(fw, "dve", hcarry[:], 0.0)

    NG = 2
    GW = 8 * 512
    brep = fw.sb("brep", [128, NSTATE * 512], BF16)
    crep = fw.sb("crep", [128, NSTATE * 512], BF16) if full else None
    dAb = fw.sb("dAb", [128, GW], F32)
    dBb = fw.sb("dBb", [128, GW], F32)
    hb = fw.sb("hb", [128, GW], F32)
    du = fw.sb("du", [128, 512], F32)
    delta_ct = fw.sb("delta_ct", [128, 512], F32)
    u_blk = fw.sb("u_blk", [128, 512], BF16)
    if full:
        y_all = [fw.sb(f"y{ct}", [128, T], BF16) for ct in range(CT)]
        y_blk = fw.sb("y_blk", [128, 512], F32)
        y_tmp = fw.sb("y_tmp", [128, 512], F32)
        z_blk = fw.sb("z_blk", [128, 512], BF16)
        zs_blk = fw.sb("zs_blk", [128, 512], F32)

    for (off, W) in BLKS:
        for n in range(NSTATE):
            p = P.take()
            mm1(fw, p[:, :W], sel_sb[:, n * 128:(n + 1) * 128], bm_sb[:, off:off + W],
                True, True, tag="rep_b")
            copy(fw, "act", brep[:, n * 512:n * 512 + W], p[:, :W], tag="ev_repb")
            if full:
                p2 = P.take()
                mm1(fw, p2[:, :W], sel_sb[:, n * 128:(n + 1) * 128],
                    cm_sb[:, off:off + W], True, True, tag="rep_c")
                copy(fw, "act", crep[:, n * 512:n * 512 + W], p2[:, :W], tag="ev_repc")
        for ct in range(CT):
            fw.dma("sp", delta_ct[:, :W], delta_i[ct, :, off:off + W], tag="ld_d")
            fw.dma("sp", u_blk[:, :W], u_i[ct, :, off:off + W], tag="ld_u")
            tt(fw, "dve", du[:, :W], delta_ct[:, :W], u_blk[:, :W], ALU.mult, tag="du")
            for g in range(NG):
                for j in range(8):
                    n = g * 8 + j
                    act(fw, dAb[:, j * 512:j * 512 + W], delta_ct[:, :W], AF.Exp,
                        scale=A_sb[:, ct * NSTATE + n:ct * NSTATE + n + 1], tag="dA")
                    tt(fw, "dve", dBb[:, j * 512:j * 512 + W], du[:, :W],
                       brep[:, n * 512:n * 512 + W], ALU.mult, tag="dBu")
                    scan(fw, "dve", hb[:, j * 512:j * 512 + W], dAb[:, j * 512:j * 512 + W],
                         dBb[:, j * 512:j * 512 + W],
                         hcarry[:, ct * NSTATE + n:ct * NSTATE + n + 1], tag="scan")
                hlast = hb[:].rearrange("p (n w) -> p n w", n=8)[:, :, W - 1:W]
                copy(fw, "dve", hcarry[:, ct * NSTATE + g * 8:ct * NSTATE + g * 8 + 8],
                     hlast, tag="carry")
                if full:
                    tt(fw, "pool", dBb[:, :8 * W], hb[:, :8 * W],
                       crep[:, g * GW:g * GW + 8 * W], ALU.mult, tag="hC")
                    red_in = dBb[:].rearrange("p (n w) -> p w n", n=8)[:, :W, :]
                    if g == 0:
                        reduce(fw, "dve", y_blk[:, :W], red_in, tag="yred")
                    else:
                        reduce(fw, "dve", y_tmp[:, :W], red_in, tag="yred2")
                        tt(fw, "pool", y_blk[:, :W], y_blk[:, :W], y_tmp[:, :W], ALU.add, tag="yacc")
            if full:
                fw.dma("sp", z_blk[:, :W], z_i[ct, :, off:off + W], tag="ld_z")
                act(fw, zs_blk[:, :W], z_blk[:, :W], AF.Silu, tag="silu_z")
                stt(fw, "dve", y_blk[:, :W], u_blk[:, :W], dskip_sb[:, ct:ct + 1], y_blk[:, :W],
                    ALU.mult, ALU.add, tag="yD")
                tt(fw, "dve", y_all[ct][:, off:off + W], y_blk[:, :W], zs_blk[:, :W],
                   ALU.mult, tag="ygate")

    if not full:
        fw.dma("sp", carry_o, hcarry[:], tag="st_carry")
    else:
        wv = [dAb[:].bitcast(BF16), dBb[:].bitcast(BF16)]

        def wtile(ct):
            v = wv[ct // 8]
            return v[:, (ct % 8) * D:(ct % 8 + 1) * D]
        s_out_buf = fw.sb("s_out", [128, T], F32)
        for ct in range(CT):
            fw.dma("pool", wtile(ct), wout[ct * 128:(ct + 1) * 128, :], tag="ld_wout")
        for ot in range(NK):
            for (off, W) in BLKS:
                p = P.take()
                mm(fw, p[:, :W],
                   [(wtile(ct)[:, ot * 128:(ot + 1) * 128], y_all[ct][:, off:off + W])
                    for ct in range(CT)], tag="mm_s")
                copy(fw, "act", s_out_buf[:, off:off + W], p[:, :W], tag="ev_s")
            fw.dma("sp", s_o[ot], s_out_buf[:], tag="st_s")





def emit_l0f(fw, cst, P, IO):

    xn_i = IO["xn_i"]
    x_i = IO["x_i"]
    s_i = IO["s_i"]
    wgred_t = IO["wgred_t"]
    wg_t = IO["wg_t"]
    w1 = IO["w1"]
    b1_t = IO["b1_t"]
    w2 = IO["w2"]
    b2_t = IO["b2_t"]
    x1_o = IO["x1_o"]

    onecol = cst.handles["onecol"]
    onerow = cst.handles["onerow"]
    idt = cst.handles["idt"]
    sel_sb = cst.handles["sel8"]

    xn_sb = [fw.sb(f"s_xn{k}", [128, T], BF16) for k in range(NK)]
    for k in range(NK):
        fw.dma("sp", xn_sb[k][:], xn_i[k], tag="ld_xn")
    b1_sb = fw.sb("s_b1", [128, 8 * 32], F32)
    fw.dma("sp", b1_sb[:], b1_t)
    b2_sb = fw.sb("s_b2", [128, 8 * 8], F32)
    fw.dma("sp", b2_sb[:], b2_t)

    wgred_sb = fw.sb("s_wgred", [128, NK * 16], BF16)
    fw.dma("pool", wgred_sb[:].rearrange("p (k g) -> p k g", k=NK),
           wgred_t.rearrange("(k p) g -> p k g", p=128), tag="ld_wgred")
    wgt_sb = fw.sb("s_wgt", [16, 8], F32)
    fw.dma("sp", wgt_sb[:], wg_t)
    xr_sb = fw.sb("s_xr", [16, T], BF16)
    for (off, W) in BLKS:
        p = P.take()
        mm(fw, p[0:16, :W], [(wgred_sb[:, k * 16:(k + 1) * 16], xn_sb[k][:, off:off + W])
                             for k in range(NK)], tag="mm_xr")
        copy(fw, "act", xr_sb[:, off:off + W], p[0:16, :W], tag="ev_xr")
    wg_sq = fw.sb("wg_sq", [16, 8], F32)
    nrm = fw.sb("nrm", [1, 8], F32)
    nrm_rep = fw.sb("nrm_rep", [16, 8], F32)
    wgu_bf = fw.sb("wgu_bf", [16, 8], BF16)
    act(fw, wg_sq[:], wgt_sb[:], AF.Square, tag="wg_sq")
    p = P.take()
    mm1(fw, p[0:1, 0:8], onecol[0:16, 0:1], wg_sq[:], True, True, tag="wg_ss")
    act(fw, nrm[:], p[0:1, 0:8], AF.Sqrt, tag="wg_sqrt")
    ts(fw, "dve", nrm[:], nrm[:], 1e-4, ALU.max, tag="wg_max")
    recip(fw, nrm[:], nrm[:], tag="wg_rec")
    p = P.take()
    mm1(fw, p[0:16, 0:8], onerow[0:1, 0:16], nrm[:], True, True, tag="wg_rep")
    copy(fw, "act", nrm_rep[:], p[0:16, 0:8], tag="wg_evr")
    tt(fw, "dve", wgu_bf[:], wgt_sb[:], nrm_rep[:], ALU.mult, tag="wg_div")
    lg_sb = fw.sb("lg", [8, T], F32)
    for (off, W) in BLKS:
        p = P.take()
        mm1(fw, p[0:8, :W], wgu_bf[:], xr_sb[:, off:off + W], True, True, tag="mm_lg")
        copy(fw, "act", lg_sb[:, off:off + W], p[0:8, :W], tag="ev_lg")
    ms_fm = fw.sb("ms_fm", [8, T], BF16)
    tm_g = fw.sb("tm_g", [128, 8], F32)
    tm_e = fw.sb("tm_e", [128, 8], F32)
    tm_r = fw.sb("tm_r", [128, 1], F32)
    for i in range(T // 128):
        p = P.take()
        transpose_pe(fw, p[:, 0:8], lg_sb[:, i * 128:(i + 1) * 128], idt[0:8, 0:8], tag="tr_lg")
        copy(fw, "dve", tm_g[:], p[:, 0:8], tag="ev_tg")
        reduce(fw, "dve", tm_r[:], tm_g[:], op=ALU.max, negate=True, tag="g_max")
        act(fw, tm_e[:], tm_g[:], AF.Exp, bias=tm_r[:, 0:1], tag="g_exp")
        reduce(fw, "dve", tm_r[:], tm_e[:], op=ALU.add, tag="g_sum")
        recip(fw, tm_r[:], tm_r[:], tag="g_rec")
        ts(fw, "dve", tm_e[:], tm_e[:], tm_r[:, 0:1], ALU.mult, tag="g_nrm")
        reduce(fw, "dve", tm_r[:], tm_e[:], op=ALU.max, tag="g_m")
        ts(fw, "dve", tm_g[:], tm_e[:], tm_r[:, 0:1], ALU.is_equal, tag="g_msk")
        ts(fw, "dve", tm_g[:], tm_g[:], tm_r[:, 0:1], ALU.mult, tag="g_msc")
        p = P.take()
        transpose_pe(fw, p[0:8, 0:128], tm_g[:], idt[:], tag="tr_ms")
        copy(fw, "dve", ms_fm[:, i * 128:(i + 1) * 128], p[0:8, 0:128], tag="ev_ms")

    TG = 1024
    msrep = fw.sb("msrep", [128, T], BF16)
    h_buf = [fw.sb(f"hb{s}", [128, 32 * 1024], BF16) for s in range(1)]
    out_acc = [fw.sb(f"oacc{o}", [128, T], BF16) for o in range(NK)]
    w1buf = [fw.sb(f"w1b{i}", [128, NK * 128], BF16) for i in range(3)]
    w2buf = [fw.sb(f"w2b{i}", [128, 32 * 128], BF16) for i in range(2)]
    t_eo = [fw.sb(f"t_eo{s}", [128, 512], BF16) for s in range(2)]
    wslot = [0]
    w2slot = [0]

    for e in range(8):
        for (off, W) in BLKS:
            p = P.take()
            mm1(fw, p[:, :W], sel_sb[:, e * 128:(e + 1) * 128], ms_fm[:, off:off + W],
                True, True, tag="rep_ms")
            copy(fw, "act", msrep[:, off:off + W], p[:, :W], tag="ev_msr")
        for g in range(T // TG):
            g0 = g * TG
            hb = h_buf[0]
            for ht in range(32):
                wb = w1buf[wslot[0] % 3]; wslot[0] += 1
                fw.dma("pool", wb[:].rearrange("p (k h) -> p k h", k=NK),
                       w1.rearrange("e (k p) h -> e p k h", p=128)[e, :, :, ht * 128:(ht + 1) * 128],
                       tag="ld_w1")
                for sb_ in range(2):
                    o2 = g0 + sb_ * 512
                    p = P.take()
                    mm(fw, p[:],
                       [(wb[:, k * 128:(k + 1) * 128], xn_sb[k][:, o2:o2 + 512]) for k in range(NK)],
                       tag="mm_h")
                    act(fw, hb[:, ht * 1024 + sb_ * 512:ht * 1024 + (sb_ + 1) * 512], p[:], AF.Gelu,
                        bias=b1_sb[:, e * 32 + ht:e * 32 + ht + 1], tag="gelu")
            for ot in range(NK):
                wb = w2buf[w2slot[0] % 2]; w2slot[0] += 1
                fw.dma("pool", wb[:].rearrange("p (c o) -> p c o", c=32),
                       w2.rearrange("e (c p) o -> e p c o", p=128)[e, :, :, ot * 128:(ot + 1) * 128],
                       tag="ld_w2")
                for sb_ in range(2):
                    o2 = g0 + sb_ * 512
                    p = P.take()
                    mm(fw, p[:],
                       [(wb[:, c * 128:(c + 1) * 128], hb[:, c * 1024 + sb_ * 512:c * 1024 + (sb_ + 1) * 512])
                        for c in range(32)], tag="mm_o")
                    if e == 0:
                        stt(fw, "dve", out_acc[ot][:, o2:o2 + 512], p[:],
                            b2_sb[:, ot:ot + 1], msrep[:, o2:o2 + 512],
                            ALU.add, ALU.mult, tag="eo0")
                    else:
                        te = t_eo[ot % 2]
                        stt(fw, "dve", te[:], p[:],
                            b2_sb[:, e * 8 + ot:e * 8 + ot + 1], msrep[:, o2:o2 + 512],
                            ALU.add, ALU.mult, tag="eo")
                        tt(fw, "dve", out_acc[ot][:, o2:o2 + 512], out_acc[ot][:, o2:o2 + 512],
                           te[:], ALU.add, tag="eacc")

    g_sl = fw.sb("g_sl", [128, 512], F32)
    x_sl = fw.sb("x_sl", [128, 512], F32)
    s_sl = fw.sb("s_sl", [128, 512], F32)
    x1_blk = fw.sb("x1_blk", [128, 512], F32)
    for ot in range(NK):
        for (off, W) in BLKS:
            act(fw, g_sl[:, :W], out_acc[ot][:, off:off + W], AF.Silu, tag="silu_g")
            fw.dma("sp", x_sl[:, :W], x_i[ot, :, off:off + W], tag="ld_x")
            fw.dma("sp", s_sl[:, :W], s_i[ot, :, off:off + W], tag="ld_s")
            tt(fw, "dve", g_sl[:, :W], s_sl[:, :W], g_sl[:, :W], ALU.mult, tag="sg")
            tt(fw, "dve", x1_blk[:, :W], x_sl[:, :W], g_sl[:, :W], ALU.add, tag="x1")
            fw.dma("sp", x1_o[ot, :, off:off + W], x1_blk[:, :W], tag="st_x1")




def emit_l1(fw, cst, P, IO):

    xf_i = IO["xf_i"]
    wq_t = IO["wq_t"]
    wk_t = IO["wk_t"]
    wv_t = IO["wv_t"]
    wo_t = IO["wo_t"]
    colq = IO["colq"]
    gw = IO["gw"]
    gb = IO["gb"]
    ew = IO["ew"]
    eb = IO["eb"]
    x2_o = IO["x2_o"]

    v_d = fw.dram("v_spill", [TF // 128, 128, D], BF16)
    p_d = fw.dram("p_spill", [4, 128, (TF // 128) * 512], BF16)
    s1_d = fw.dram("s1_spill", [NK, 128, T], F32)

    onecol = cst.handles["onecol"]
    onecol_bf = cst.handles["onecol_bf"]
    onerow = cst.handles["onerow"]
    iotaq = cst.handles["iotaq"]
    NKT = TF // 128

    ln_tmp = mk_ln_tmp(fw, "t")
    xblk = [fw.sb(f"xb{k}", [128, 512], F32) for k in range(NK)]
    xn_blk = [fw.sb(f"xnb{k}", [128, 512], BF16) for k in range(NK)]
    sqt = fw.sb("sqt", [128, 512], F32)
    colq_sb = fw.sb("s_colq", [128, 4 * NKT], F32)
    fw.dma("sp", colq_sb[:], colq)

    def ln_into(cols_off, W=512):
        for k in range(NK):
            fw.dma("sp", xblk[k][:, :W], xf_i[k, :, cols_off:cols_off + W], tag="ld_x")
        ln_block(fw, P, cst, xblk, 0, W, xn_blk, 0, ln_tmp)

    k2max = fw.sb("k2max", [1, 1], F32)
    rden = fw.sb("rden", [1, 4 * 512], F32)

    K_sb = [fw.sb(f"K{i}", [128, TF], BF16) for i in range(NK)]
    wkT = [fw.sb(f"s_wk{k}", [128, D], BF16) for k in range(NK)]
    wvT = [fw.sb(f"s_wv{k}", [128, D], BF16) for k in range(NK)]
    for k in range(NK):
        fw.dma("pool", wkT[k][:], wk_t[k * 128:(k + 1) * 128, :], tag="ld_wk")
        fw.dma("pool", wvT[k][:], wv_t[k * 128:(k + 1) * 128, :], tag="ld_wv")
    v_tile = fw.sb("v_tile", [128, D], BF16)
    k2psum = fw.aux_ps
    k2row = fw.sb("k2row", [1, TF], F32)
    for bi in range(TF // 512):
        off = bi * 512
        ln_into(off)
        for ot in range(NK):
            p = P.take()
            mm(fw, p[:], [(wkT[k][:, ot * 128:(ot + 1) * 128], xn_blk[k][:]) for k in range(NK)],
               tag="mm_k")
            copy(fw, "act", K_sb[ot][:, off:off + 512], p[:], tag="ev_k")
        for st_ in range(4):
            to = st_ * 128
            for half in range(2):
                p = P.take()
                mm(fw, p[:], [(xn_blk[k][:, to:to + 128], wvT[k][:, half * 512:(half + 1) * 512])
                              for k in range(NK)], tag="mm_v")
                copy(fw, "act", v_tile[:, half * 512:(half + 1) * 512], p[:], tag="ev_v")
            fw.dma("sp", v_d[(off + to) // 128], v_tile[:], tag="st_v")
    for bi in range(TF // 512):
        off = bi * 512
        for i in range(NK):
            act(fw, sqt[:], K_sb[i][:, off:off + 512], AF.Square, tag="k2sq")
            mm1(fw, k2psum[:], onecol[:, 0:1], sqt[:], i == 0, i == NK - 1, tag="k2mm")
        copy(fw, "dve", k2row[0:1, off:off + 512], k2psum[:], tag="k2ev")
    reduce(fw, "dve", k2max[:], k2row[0:1, :], op=ALU.max, tag="k2max")

    fw.barrier()
    fw.free_sb(*[f"s_wk{k}" for k in range(NK)], *[f"s_wv{k}" for k in range(NK)],
               "v_tile", "k2row")

    wqT = [fw.sb(f"s_wq{k}", [128, D], BF16) for k in range(NK)]
    for k in range(NK):
        fw.dma("pool", wqT[k][:], wq_t[k * 128:(k + 1) * 128, :], tag="ld_wq")
    q_blk = [fw.sb(f"qt{k}", [128, 512], BF16) for k in range(NK)]
    Psb = fw.sb("Psb", [128, NKT * 512], BF16)
    negc = fw.sb("negc", [1, 512], F32)
    q2 = fw.sb("q2", [1, 512], F32)
    etmp = fw.sb("etmp", [128, 512], BF16)
    mtmp = fw.sb("mtmp", [128, 512], BF16)
    den = fw.sb("den", [1, 4 * 512], F32)
    denp = fw.aux_ps
    for qb in range(4):
        qoff = qb * 512
        ln_into(qoff)
        for k in range(NK):
            p = P.take()
            mm(fw, p[:], [(wqT[i][:, k * 128:(k + 1) * 128], xn_blk[i][:]) for i in range(NK)],
               tag="mm_q")
            act(fw, q_blk[k][:], p[:], AF.Identity, scale=cst.scalar(1.0 / 32.0, 128), tag="ev_q")
        p = P.take()
        for k in range(NK):
            act(fw, sqt[:], q_blk[k][:], AF.Square, tag="q2sq")
            mm1(fw, p[0:1, :], onecol[:, 0:1], sqt[:], k == 0, k == NK - 1, tag="q2mm")
        copy(fw, "dve", q2[:], p[0:1, :], tag="q2ev")
        act(fw, negc[:], q2[:], AF.Sqrt, scale=k2max[0:1, 0:1], tag="csqrt")
        ts(fw, "dve", negc[:], negc[:], -1.0, ALU.mult, tag="cneg")
        for kt in range(NKT):
            p = P.take()
            for i in range(NK):
                mm1(fw, p[:], K_sb[i][:, kt * 128:(kt + 1) * 128], q_blk[i][:],
                    i == 0, False, tag="mm_s")
            mm1(fw, p[:], onerow[0:1, :], negc[0:1, :], False, True, tag="mm_c")
            act(fw, etmp[:], p[:], AF.Exp, tag="p_exp")
            ts(fw, "dve", mtmp[:], iotaq[:], colq_sb[:, (qb * NKT + kt):(qb * NKT + kt) + 1],
               ALU.is_ge, tag="p_msk")
            tt(fw, "dve", Psb[:, kt * 512:(kt + 1) * 512], etmp[:], mtmp[:], ALU.mult, tag="p_m")
            mm1(fw, denp[:], onecol_bf[:, 0:1], Psb[:, kt * 512:(kt + 1) * 512],
                kt == 0, kt == NKT - 1, tag="mm_den")
        copy(fw, "dve", den[0:1, qoff:qoff + 512], denp[:], tag="ev_den")
        fw.dma("sp", p_d[qb], Psb[:], tag="st_p")
    recip(fw, rden[:], den[:], tag="rden")

    fw.barrier()
    fw.free_sb(*[f"K{i}" for i in range(NK)], *[f"s_wq{k}" for k in range(NK)],
               *[f"qt{k}" for k in range(NK)], "negc", "q2", "den", "Psb", "etmp", "mtmp")

    V_sb = [fw.sb(f"V{i}", [128, D], BF16) for i in range(NKT)]
    for i in range(NKT):
        fw.dma("sp", V_sb[i][:], v_d[i], tag="ld_v")
    woT = [fw.sb(f"s_wo{k}", [128, D], BF16) for k in range(NK)]
    for k in range(NK):
        fw.dma("pool", woT[k][:], wo_t[k * 128:(k + 1) * 128, :], tag="ld_wo")
    attn = [fw.sb(f"at{k}", [128, 512], BF16) for k in range(NK)]
    rrep = fw.sb("rrep", [128, 512], F32)
    s1_blk = fw.sb("s1_blk", [128, 512], F32)
    Psb = fw.sb("Psb2", [128, NKT * 512], BF16)
    for qb in range(4):
        fw.dma("sp", Psb[:], p_d[qb], tag="ld_p")
        p = P.take()
        mm1(fw, p[:], onerow[0:1, :], rden[0:1, qb * 512:(qb + 1) * 512], True, True, tag="mm_rr")
        copy(fw, "act", rrep[:], p[:], tag="ev_rr")
        for dt in range(NK):
            p = P.take()
            mm(fw, p[:], [(V_sb[kt][:, dt * 128:(dt + 1) * 128], Psb[:, kt * 512:(kt + 1) * 512])
                          for kt in range(NKT)], tag="mm_pv")
            tt(fw, "dve", attn[dt][:], p[:], rrep[:], ALU.mult, tag="pv_nrm")
        for ot in range(NK):
            p = P.take()
            mm(fw, p[:], [(woT[k][:, ot * 128:(ot + 1) * 128], attn[k][:]) for k in range(NK)],
               tag="mm_wo")
            copy(fw, "act", s1_blk[:], p[:], tag="ev_s1")
            fw.dma("sp", s1_d[ot, :, qb * 512:(qb + 1) * 512], s1_blk[:], tag="st_s1")

    fw.barrier()
    fw.free_sb(*[f"V{i}" for i in range(NKT)], *[f"s_wo{k}" for k in range(NK)],
               *[f"at{k}" for k in range(NK)], "Psb2", "rrep", "s1_blk", "rden")

    idt = cst.handles["idt"]
    sel_sb = cst.handles["sel8"]
    gw_sb = fw.sb("s_gw", [128, NK * 8], BF16)
    fw.dma("pool", gw_sb[:].rearrange("p (k g) -> p k g", k=NK),
           gw.rearrange("(k p) g -> p k g", p=128), tag="ld_gw")
    gb_sb = fw.sb("s_gb", [8, 1], F32)
    fw.dma("sp", gb_sb[:], gb)
    eb_sb = fw.sb("s_eb", [8, D], BF16)
    fw.dma("pool", eb_sb[:], eb)
    lgb = fw.sb("lgb", [8, 512], F32)
    tm_g = fw.sb("tm_g", [128, 8], F32)
    tm_e = fw.sb("tm_e", [128, 8], F32)
    tm_r = fw.sb("tm_r", [128, 1], F32)
    wfm = fw.sb("wfm", [8, 512], BF16)
    wrep_all = fw.sb("wrep_all", [128, 8 * 512], BF16)
    xnw = [fw.sb(f"xnw{k}", [128, 512], BF16) for k in range(NK)]
    wexp = [fw.sb(f"wexp{i}", [128, NK * 128], BF16) for i in range(2)]
    out_acc = [fw.sb(f"oacc{o}", [128, 512], F32) for o in range(NK)]
    g_sl = fw.sb("g_sl", [128, 512], F32)
    x_sl = fw.sb("x_sl", [128, 512], F32)
    s_sl = fw.sb("s_sl", [128, 512], F32)
    x2_blk = fw.sb("x2_blk", [128, 512], F32)
    wslot = [0]
    for qb in range(4):
        qoff = qb * 512
        ln_into(qoff)
        p = P.take()
        mm(fw, p[0:8, :], [(gw_sb[:, k * 8:(k + 1) * 8], xn_blk[k][:]) for k in range(NK)],
           tag="mm_lg")
        act(fw, lgb[:], p[0:8, :], AF.Identity, bias=gb_sb[0:8, 0:1], tag="ev_lg")
        for i in range(4):
            p = P.take()
            transpose_pe(fw, p[:, 0:8], lgb[:, i * 128:(i + 1) * 128], idt[0:8, 0:8], tag="tr_lg")
            copy(fw, "dve", tm_g[:], p[:, 0:8], tag="ev_tg")
            reduce(fw, "dve", tm_r[:], tm_g[:], op=ALU.max, negate=True, tag="f_max")
            act(fw, tm_e[:], tm_g[:], AF.Exp, bias=tm_r[:, 0:1], tag="f_exp")
            reduce(fw, "dve", tm_r[:], tm_e[:], op=ALU.add, tag="f_sum")
            recip(fw, tm_r[:], tm_r[:], tag="f_rec")
            ts(fw, "dve", tm_e[:], tm_e[:], tm_r[:, 0:1], ALU.mult, tag="f_nrm")
            ts(fw, "dve", tm_g[:], tm_e[:], 0.1, ALU.is_ge, tag="f_sel")
            tt(fw, "dve", tm_e[:], tm_e[:], tm_g[:], ALU.mult, tag="f_wm")
            reduce(fw, "dve", tm_r[:], tm_e[:], op=ALU.add, tag="f_ws")
            recip(fw, tm_r[:], tm_r[:], tag="f_wr")
            ts(fw, "dve", tm_e[:], tm_e[:], tm_r[:, 0:1], ALU.mult, tag="f_wn")
            p = P.take()
            transpose_pe(fw, p[0:8, 0:128], tm_e[:], idt[:], tag="tr_w")
            copy(fw, "dve", wfm[:, i * 128:(i + 1) * 128], p[0:8, 0:128], tag="ev_w")
        for e in range(8):
            p = P.take()
            mm1(fw, p[:], sel_sb[:, e * 128:(e + 1) * 128], wfm[:], True, True, tag="mm_wr")
            copy(fw, "act", wrep_all[:, e * 512:(e + 1) * 512], p[:], tag="ev_wr")
        for e in range(8):
            for k in range(NK):
                tt(fw, "dve", xnw[k][:], xn_blk[k][:], wrep_all[:, e * 512:(e + 1) * 512],
                   ALU.mult, tag="xnw")
            for ot in range(NK):
                wb = wexp[wslot[0] % 2]; wslot[0] += 1
                fw.dma("pool", wb[:].rearrange("p (k o) -> p k o", k=NK),
                       ew.rearrange("e (k p) o -> e p k o", p=128)[e, :, :, ot * 128:(ot + 1) * 128],
                       tag="ld_ew")
                p = P.take()
                pairs = [(wb[:, k * 128:(k + 1) * 128], xnw[k][:]) for k in range(NK)]
                for i, (l, r) in enumerate(pairs):
                    mm1(fw, p[:], l, r, i == 0, (e != 0 and i == len(pairs) - 1), tag="mm_ex")
                if e == 0:
                    mm1(fw, p[:], eb_sb[:, ot * 128:(ot + 1) * 128], wfm[:],
                        False, True, tag="mm_eb")
                if e == 0:
                    copy(fw, "dve", out_acc[ot][:], p[:], tag="eo0")
                else:
                    tt(fw, "dve", out_acc[ot][:], out_acc[ot][:], p[:], ALU.add, tag="eo")
        for ot in range(NK):
            act(fw, g_sl[:], out_acc[ot][:], AF.Silu, tag="silu_g")
            fw.dma("sp", x_sl[:], xf_i[ot, :, qoff:qoff + 512], tag="ld_xr")
            fw.dma("sp", s_sl[:], s1_d[ot, :, qoff:qoff + 512], tag="ld_s1")
            tt(fw, "dve", g_sl[:], s_sl[:], g_sl[:], ALU.mult, tag="sg")
            tt(fw, "dve", x2_blk[:], x_sl[:], g_sl[:], ALU.add, tag="x2")
            fw.dma("sp", x2_o[ot, :, qoff:qoff + 512], x2_blk[:], tag="st_x2")





def attn_core(fw, P, q_heads, k_heads, kv_of, V_tm, attn_tiles, Psb, rrep, r_row,
              NKT, n_heads, tag=""):
    for h in range(n_heads):
        hv = kv_of(h)
        kh = k_heads[hv]
        for qb in range(4):
            q_ap = q_heads[h][:, qb * 512:(qb + 1) * 512]
            for kt in range(NKT):
                p = P.take()
                mm1(fw, p[:], kh[:, kt * 128:(kt + 1) * 128], q_ap, True, True, tag=tag + "mm_s")
                act(fw, Psb[:, kt * 512:(kt + 1) * 512], p[:], AF.Exp, tag=tag + "p_exp")
            p = P.take()
            mm(fw, p[0:65, :], [(V_tm[kt][:, hv * 65:(hv + 1) * 65],
                                 Psb[:, kt * 512:(kt + 1) * 512]) for kt in range(NKT)],
               tag=tag + "mm_pv")
            recip(fw, r_row[0:1, :], p[64:65, :], tag=tag + "rden")
            p2 = P.take()
            mm1(fw, p2[0:64, :], fw.onerow[0:1, 0:64], r_row[0:1, :], True, True, tag=tag + "mm_rr")
            copy(fw, "act", rrep[0:64, :], p2[0:64, :], tag=tag + "ev_rr")
            dst = attn_tiles[(h // 2, qb)][(h % 2) * 64:(h % 2) * 64 + 64, :]
            fw.comp("dve", lambda hh, dst=dst, p=p: hh.tensor_tensor(
                out=dst, in0=p[0:64, :], in1=rrep[0:64, :], op=ALU.mult),
                reads=[p[0:64, :], rrep[0:64, :]], writes=[dst], tag=tag + "pv_nrm")



def emit_l2(fw, cst, P, IO):

    xf_i = IO["xf_i"]
    wq_t = IO["wq_t"]
    bq_c = IO["bq_c"]
    wk_t = IO["wk_t"]
    bk_c = IO["bk_c"]
    wv_t = IO["wv_t"]
    bv_r = IO["bv_r"]
    wo_t = IO["wo_t"]
    bo_r = IO["bo_r"]
    cosq = IO["cosq"]
    sinq = IO["sinq"]
    cosk = IO["cosk"]
    sink = IO["sink"]
    wr_t = IO["wr_t"]
    br_c = IO["br_c"]
    we_t = IO["we_t"]
    be_i = IO["be_i"]
    fwo_t = IO["fwo_t"]
    fbo_r = IO["fbo_r"]
    x3_o = IO["x3_o"]

    s2_d = fw.dram("s2_spill", [NK, 128, T], F32)
    onecol = cst.handles["onecol"]
    onerow = cst.handles["onerow"]
    ones512 = cst.handles["ones_row512"]
    perm_sb = cst.handles["perm"]
    NKT = TF // 128

    attn_tiles = {}
    for pair in range(8):
        for qb in range(4):
            attn_tiles[(pair, qb)] = fw.sb(f"atn{pair}_{qb}", [128, 512], BF16)
    k_heads = [fw.sb(f"kh{v}", [65, TF], BF16) for v in range(4)]
    V_tm = [fw.sb(f"vt{i}", [128, 4 * 65], BF16) for i in range(NKT)]
    k2m = fw.sb("k2m", [1, 4], F32)
    q_d = fw.dram("q2_spill", [16, 64, T], BF16)

    ln_tmp = mk_ln_tmp(fw, "t")
    xblk = [fw.sb(f"xb{k}", [128, 512], F32) for k in range(NK)]
    xn_blk = [fw.sb(f"xnb{k}", [128, 512], BF16) for k in range(NK)]
    sqt = ln_tmp[0]

    def ln_into(cols_off, W=512):
        for k in range(NK):
            fw.dma("sp", xblk[k][:, :W], xf_i[k, :, cols_off:cols_off + W], tag="ld_x")
        ln_block(fw, P, cst, xblk, 0, W, xn_blk, 0, ln_tmp)

    wqs = [fw.sb(f"s_wqs{i}", [128, NK * 128], BF16) for i in range(2)]
    wqslot = [0]
    wkT = fw.sb("s_wkT", [128, 8 * 256], BF16)
    wvT = fw.sb("s_wvT", [128, 8 * 256], BF16)
    fw.dma("pool", wkT[:].rearrange("p (k o) -> p k o", k=NK),
           wk_t.rearrange("(k p) o -> p k o", p=128), tag="ld_wk")
    fw.dma("pool", wvT[:].rearrange("p (k o) -> p k o", k=NK),
           wv_t.rearrange("(k p) o -> p k o", p=128), tag="ld_wv")
    bq_sb = fw.sb("s_bq", [128, 8], F32)
    fw.dma("sp", bq_sb[:], bq_c)
    bk_sb = fw.sb("s_bk", [128, 2], F32)
    fw.dma("sp", bk_sb[:], bk_c)
    bv_sb = fw.sb("s_bv", [1, 256], BF16)
    fw.dma("pool", bv_sb[:], bv_r)
    tb_cos = fw.sb("tb_cos", [64, 512], BF16)
    tb_sin = fw.sb("tb_sin", [64, 512], BF16)
    NSET = 4
    rsets = [dict(pre_h=fw.sb(f"pre_h{s}", [64, 512], BF16),
                  rop1=fw.sb(f"rop1{s}", [64, 512], F32),
                  rop2=fw.sb(f"rop2{s}", [64, 512], F32),
                  qout=fw.sb(f"qout{s}", [64, 512], BF16)) for s in range(NSET)]
    rturn = [0]
    vtile = [fw.sb(f"vtile{s}", [128, 256], BF16) for s in range(2)]
    vturn = [0]

    for hv in range(4):
        memset(fw, "dve", k_heads[hv][64:65, :], 1.0, tag="kones")
    for bi in range(NKT // 4):
        off = bi * 512
        own = off < T
        ln_into(off)
        fw.dma("pool", tb_cos[:], cosk[:, off:off + 512], tag="ld_ck")
        fw.dma("pool", tb_sin[:], sink[:, off:off + 512], tag="ld_sk")
        for mt in range(2):
            p = P.take()
            mm(fw, p[:], [(wkT[:, k * 256 + mt * 128:k * 256 + (mt + 1) * 128], xn_blk[k][:])
                          for k in range(NK)], tag="mm_kv")
            for sub in range(2):
                hv = mt * 2 + sub
                B = rsets[rturn[0] % NSET]; rturn[0] += 1
                pre_h, rop1, rop2 = B["pre_h"], B["rop1"], B["rop2"]
                act(fw, pre_h[:], p[sub * 64:(sub + 1) * 64, :], AF.Identity,
                    bias=bk_sb[sub * 64:(sub + 1) * 64, mt:mt + 1], tag="ev_kh")
                p2 = P.take()
                mm1(fw, p2[0:64, :], perm_sb[:], pre_h[:], True, True, tag="mm_rot")
                tt(fw, "dve", rop1[:], pre_h[:], tb_cos[:], ALU.mult, tag="rp1")
                fw.comp("dve", lambda h, p2=p2, rop2=rop2: h.tensor_tensor(
                    out=rop2[:], in0=p2[0:64, :], in1=tb_sin[:], op=ALU.mult),
                    reads=[p2[0:64, :], tb_sin[:]], writes=[rop2[:]], tag="rp2")
                tt(fw, "dve", k_heads[hv][0:64, off:off + 512], rop1[:], rop2[:],
                   ALU.add, tag="rp3")
        for st_ in range(4):
            to = st_ * 128
            p = P.take()
            pairs = [(xn_blk[k][:, to:to + 128], wvT[:, k * 256:(k + 1) * 256]) for k in range(NK)]
            for i, (l, r) in enumerate(pairs):
                mm1(fw, p[:, 0:256], l, r, i == 0, False, tag="mm_v")
            mm1(fw, p[:, 0:256], ones512[0:1, 0:128], bv_sb[0:1, :], False, True, tag="mm_vb")
            vt = vtile[vturn[0] % 2]; vturn[0] += 1
            copy(fw, "act", vt[:], p[:, 0:256], tag="ev_v")
            kt = (off + to) // 128
            for hv in range(4):
                copy(fw, "dve", V_tm[kt][:, hv * 65:hv * 65 + 64], vt[:, hv * 64:(hv + 1) * 64],
                     tag="v_sp")
            memset(fw, "dve", V_tm[kt][:].rearrange("p (g c) -> p g c", g=4)[:, :, 64:65],
                   1.0, tag="v_ones")
        if own:
            fw.dma("pool", tb_cos[:], cosq[:, off:off + 512], tag="ld_cq")
            fw.dma("pool", tb_sin[:], sinq[:, off:off + 512], tag="ld_sq")
            for mt in range(NK):
                wb = wqs[wqslot[0] % 2]; wqslot[0] += 1
                fw.dma("pool", wb[:].rearrange("p (k o) -> p k o", k=NK),
                       wq_t.rearrange("(k p) o -> p k o", p=128)[:, :, mt * 128:(mt + 1) * 128],
                       tag="ld_wq")
                p = P.take()
                mm(fw, p[:], [(wb[:, k * 128:(k + 1) * 128], xn_blk[k][:])
                              for k in range(NK)], tag="mm_q")
                for sub in range(2):
                    h_ = mt * 2 + sub
                    B = rsets[rturn[0] % NSET]; rturn[0] += 1
                    pre_h, rop1, rop2, qout = B["pre_h"], B["rop1"], B["rop2"], B["qout"]
                    act(fw, pre_h[:], p[sub * 64:(sub + 1) * 64, :], AF.Identity,
                        bias=bq_sb[sub * 64:(sub + 1) * 64, mt:mt + 1], tag="ev_qh")
                    p2 = P.take()
                    mm1(fw, p2[0:64, :], perm_sb[:], pre_h[:], True, True, tag="mm_rotq")
                    tt(fw, "dve", rop1[:], pre_h[:], tb_cos[:], ALU.mult, tag="rq1")
                    fw.comp("dve", lambda h, p2=p2, rop2=rop2: h.tensor_tensor(
                        out=rop2[:], in0=p2[0:64, :], in1=tb_sin[:], op=ALU.mult),
                        reads=[p2[0:64, :], tb_sin[:]], writes=[rop2[:]], tag="rq2")
                    tt(fw, "dve", qout[:], rop1[:], rop2[:], ALU.add, tag="rq3")
                    fw.dma("sp", q_d[h_, :, off:off + 512], qout[:], tag="st_q2")
    k2ps = fw.aux_ps
    k2one = fw.sb("k2one", [1, 1], F32)
    for hv in range(4):
        for bi in range(NKT // 4):
            off = bi * 512
            act(fw, sqt[0:64, :], k_heads[hv][0:64, off:off + 512], AF.Square, tag="k2sq")
            mm1(fw, k2ps[:], onecol[0:64, 0:1], sqt[0:64, :], True, True, tag="k2mm")
            if bi == 0:
                reduce(fw, "dve", k2m[0:1, hv:hv + 1], k2ps[:], op=ALU.max, tag="k2max")
            else:
                reduce(fw, "dve", k2one[:], k2ps[:], op=ALU.max, tag="k2max2")
                tt(fw, "dve", k2m[0:1, hv:hv + 1], k2m[0:1, hv:hv + 1], k2one[:],
                   ALU.max, tag="k2acc")
    fw.barrier()
    fw.free_sb("k2one", "vtile0", "vtile1",
               *[f"{nm}{s}" for s in range(4) for nm in ("pre_h", "rop1", "rop2", "qout")],
               "tb_sin", "tb_cos",
               "s_bv", "s_bk", "s_bq", "s_wvT", "s_wkT", "s_wqs1", "s_wqs0",
               *[f"xnb{k}" for k in range(NK)], *[f"xb{k}" for k in range(NK)],
               "t_br", "t_rr", "t_bb", "t_rb", "t_vv", "t_qq", "t_ra", "t_sq")

    Psb2 = [fw.sb(f"Psb{s}", [128, NKT * 512], BF16) for s in range(2)]
    rrep2 = [fw.sb(f"rrep{s}", [128, 512], F32) for s in range(2)]
    r_row2 = [fw.sb(f"r_row{s}", [1, 512], F32) for s in range(2)]
    pvps = [fw.ps(f"pvps{s}", [128, 512], F32) for s in range(2)]
    it2 = [0]
    qbuf = [fw.sb(f"qb2_{i}", [65, T], BF16) for i in range(2)]
    qsq = fw.sb("qsq", [64, 512], F32)
    q2r = fw.sb("q2r", [1, 512], F32)
    onerow = fw.onerow
    for h_ in range(16):
        hv = h_ // 4
        qh = qbuf[h_ % 2]
        fw.dma("sp", qh[0:64, :], q_d[h_], tag="ld_q2")
        for qb in range(4):
            off = qb * 512
            act(fw, qsq[:], qh[0:64, off:off + 512], AF.Square, tag="q2sq")
            p = P.take()
            mm1(fw, p[0:1, :], onecol[0:64, 0:1], qsq[:], True, True, tag="q2mm")
            copy(fw, "dve", q2r[:], p[0:1, :], tag="q2ev")
            act(fw, q2r[:], q2r[:], AF.Sqrt, scale=k2m[0:1, hv:hv + 1], tag="csq")
            fw.comp("dve", lambda h, qh=qh, off=off: h.tensor_single_scalar(
                out=qh[64:65, off:off + 512], in_=q2r[0:1, :], scalar=-1.0, op=ALU.mult),
                reads=[q2r[0:1, :]], writes=[qh[64:65, off:off + 512]], tag="cneg")
        pipe = Pipe()
        for qb in range(4):
            q_ap = qh[:, qb * 512:(qb + 1) * 512]
            me = it2[0]; it2[0] += 1
            Psb = Psb2[me % 2]
            rrep = rrep2[me % 2]
            r_row = r_row2[me % 2]
            pv = pvps[me % 2]

            def s_sc(q_ap=q_ap, Psb=Psb, hv=hv):
                for kt in range(NKT):
                    p = P.take()
                    mm1(fw, p[:], k_heads[hv][:, kt * 128:(kt + 1) * 128], q_ap, True, True,
                        tag="mm_s2")
                    act(fw, Psb[:, kt * 512:(kt + 1) * 512], p[:], AF.Exp, tag="p2_exp")

            def s_pv(Psb=Psb, hv=hv, pv=pv, r_row=r_row):
                mm(fw, pv[0:65, :], [(V_tm[kt][:, hv * 65:(hv + 1) * 65],
                                      Psb[:, kt * 512:(kt + 1) * 512]) for kt in range(NKT)],
                   tag="mm_pv2")
                recip(fw, r_row[0:1, :], pv[64:65, :], tag="rden2")

            def s_nm(h_=h_, qb=qb, pv=pv, rrep=rrep, r_row=r_row):
                p2 = P.take()
                mm1(fw, p2[0:64, :], onerow[0:1, 0:64], r_row[0:1, :], True, True, tag="mm_rr2")
                copy(fw, "act", rrep[0:64, :], p2[0:64, :], tag="ev_rr2")
                dst = attn_tiles[(h_ // 2, qb)][(h_ % 2) * 64:(h_ % 2) * 64 + 64, :]
                fw.comp("dve", lambda hh, dst=dst, pv=pv: hh.tensor_tensor(
                    out=dst, in0=pv[0:64, :], in1=rrep[0:64, :], op=ALU.mult),
                    reads=[pv[0:64, :], rrep[0:64, :]], writes=[dst], tag="pv2_nrm")
            pipe.add(s_sc, s_pv, s_nm)
        pipe.run()
    fw.barrier()
    fw.free_sb("q2r", "qsq", "qb2_1", "qb2_0", "r_row1", "r_row0", "rrep1", "rrep0",
               "Psb1", "Psb0", "pvps0", "pvps1", "k2m",
               *[f"kh{v}" for v in range(4)], *[f"vt{i}" for i in range(NKT)])
    woT = [fw.sb(f"s_wo{k}", [128, D], BF16) for k in range(NK)]
    for k in range(NK):
        fw.dma("pool", woT[k][:], wo_t[k * 128:(k + 1) * 128, :], tag="ld_wo")
    bo_sb = fw.sb("s_bo", [1, D], BF16)
    fw.dma("pool", bo_sb[:], bo_r)
    s2_blk = fw.sb("s2_blk", [128, 512], F32)
    for qb in range(4):
        for ot in range(NK):
            p = P.take()
            pairs = [(woT[k][:, ot * 128:(ot + 1) * 128], attn_tiles[(k, qb)][:]) for k in range(NK)]
            for i, (l, r) in enumerate(pairs):
                mm1(fw, p[:], l, r, i == 0, False, tag="mm_wo")
            mm1(fw, p[:], bo_sb[0:1, ot * 128:(ot + 1) * 128], ones512[0:1, :], False, True,
                tag="mm_bo")
            copy(fw, "act", s2_blk[:], p[:], tag="ev_s2")
            fw.dma("sp", s2_d[ot, :, qb * 512:(qb + 1) * 512], s2_blk[:], tag="st_s2")

    fw.barrier()
    fw.free_sb("s2_blk", "s_bo", *[f"s_wo{k}" for k in range(NK)],
               *[f"atn{pr}_{qb}" for pr in range(8) for qb in range(4)])

    idt = cst.handles["idt"]
    sel_sb = cst.handles["sel8"]
    wrT = fw.sb("s_wrT", [128, NK * 8], BF16)
    fw.dma("pool", wrT[:].rearrange("p (k g) -> p k g", k=NK),
           wr_t.rearrange("(k p) g -> p k g", p=128), tag="ld_wr")
    br_sb = fw.sb("s_br", [8, 1], F32)
    fw.dma("sp", br_sb[:], br_c)
    be_sb = fw.sb("s_be", [8, D], BF16)
    fw.dma("pool", be_sb[:], be_i)
    fwoT = [fw.sb(f"s_fwo{k}", [128, D], BF16) for k in range(NK)]
    for k in range(NK):
        fw.dma("pool", fwoT[k][:], fwo_t[k * 128:(k + 1) * 128, :], tag="ld_fwo")
    fbo_sb = fw.sb("s_fbo", [1, D], BF16)
    fw.dma("pool", fbo_sb[:], fbo_r)
    ln_tmp = mk_ln_tmp(fw, "c")
    xblk = [fw.sb(f"cxb{k}", [128, 512], F32) for k in range(NK)]
    xn_blk = [fw.sb(f"cxnb{k}", [128, 512], BF16) for k in range(NK)]

    def ln_into(cols_off, W=512, xblk=xblk, xn_blk=xn_blk, ln_tmp=ln_tmp):
        for k in range(NK):
            fw.dma("sp", xblk[k][:, :W], xf_i[k, :, cols_off:cols_off + W], tag="ld_x")
        ln_block(fw, P, cst, xblk, 0, W, xn_blk, 0, ln_tmp)

    lgb = fw.sb("lgb", [8, 512], F32)
    tm_g = fw.sb("tm_g", [128, 8], F32)
    tm_e = fw.sb("tm_e", [128, 8], F32)
    tm_m = fw.sb("tm_m", [128, 8], F32)
    tm_m2 = fw.sb("tm_m2", [128, 8], F32)
    tm_r = fw.sb("tm_r", [128, 1], F32)
    wfm = fw.sb("wfm", [8, 512], BF16)
    wrep_all = fw.sb("wrep_all", [128, 8 * 512], BF16)
    xnw = [fw.sb(f"xnw{k}", [128, 512], BF16) for k in range(NK)]
    wexp = [fw.sb(f"wexp{i}", [128, NK * 128], BF16) for i in range(2)]
    comb = [fw.sb(f"comb{k}", [128, 512], BF16) for k in range(NK)]
    out_acc = [fw.sb(f"oacc{o}", [128, 512], F32) for o in range(NK)]
    g_sl = fw.sb("g_sl", [128, 512], F32)
    x_sl = fw.sb("x_sl", [128, 512], F32)
    s_sl = fw.sb("s_sl", [128, 512], F32)
    x3_blk = fw.sb("x3_blk", [128, 512], F32)
    wslot = [0]
    for qb in range(4):
        qoff = qb * 512
        ln_into(qoff)
        p = P.take()
        mm(fw, p[0:8, :], [(wrT[:, k * 8:(k + 1) * 8], xn_blk[k][:]) for k in range(NK)],
           tag="mm_lg")
        act(fw, lgb[:], p[0:8, :], AF.Identity, bias=br_sb[0:8, 0:1], tag="ev_lg")
        for i in range(4):
            p = P.take()
            transpose_pe(fw, p[:, 0:8], lgb[:, i * 128:(i + 1) * 128], idt[0:8, 0:8], tag="tr_lg")
            copy(fw, "dve", tm_g[:], p[:, 0:8], tag="ev_tg")
            reduce(fw, "dve", tm_r[:], tm_g[:], op=ALU.max, negate=True, tag="f_max")
            act(fw, tm_e[:], tm_g[:], AF.Exp, bias=tm_r[:, 0:1], tag="f_exp")
            reduce(fw, "dve", tm_r[:], tm_e[:], op=ALU.add, tag="f_sum")
            recip(fw, tm_r[:], tm_r[:], tag="f_rec")
            ts(fw, "dve", tm_e[:], tm_e[:], tm_r[:, 0:1], ALU.mult, tag="f_nrm")
            reduce(fw, "dve", tm_r[:], tm_e[:], op=ALU.max, tag="t_m1")
            ts(fw, "dve", tm_m[:], tm_e[:], tm_r[:, 0:1], ALU.is_equal, tag="t_k1")
            stt(fw, "dve", tm_g[:], tm_m[:], -1e30, tm_e[:], ALU.mult, ALU.add, tag="t_g2")
            reduce(fw, "dve", tm_r[:], tm_g[:], op=ALU.max, tag="t_m2")
            ts(fw, "dve", tm_m2[:], tm_g[:], tm_r[:, 0:1], ALU.is_equal, tag="t_k2")
            tt(fw, "dve", tm_m[:], tm_m[:], tm_m2[:], ALU.add, tag="t_ms")
            tt(fw, "dve", tm_e[:], tm_e[:], tm_m[:], ALU.mult, tag="t_w")
            p = P.take()
            transpose_pe(fw, p[0:8, 0:128], tm_e[:], idt[:], tag="tr_w")
            copy(fw, "dve", wfm[:, i * 128:(i + 1) * 128], p[0:8, 0:128], tag="ev_w")
        for e in range(8):
            p = P.take()
            mm1(fw, p[:], sel_sb[:, e * 128:(e + 1) * 128], wfm[:], True, True, tag="mm_wr")
            copy(fw, "act", wrep_all[:, e * 512:(e + 1) * 512], p[:], tag="ev_wr")
        for e in range(8):
            for k in range(NK):
                tt(fw, "dve", xnw[k][:], xn_blk[k][:], wrep_all[:, e * 512:(e + 1) * 512],
                   ALU.mult, tag="xnw")
            for ot in range(NK):
                wb = wexp[wslot[0] % 2]; wslot[0] += 1
                fw.dma("pool", wb[:].rearrange("p (k o) -> p k o", k=NK),
                       we_t.rearrange("e (k p) o -> e p k o", p=128)[e, :, :, ot * 128:(ot + 1) * 128],
                       tag="ld_we")
                p = P.take()
                pairs = [(wb[:, k * 128:(k + 1) * 128], xnw[k][:]) for k in range(NK)]
                for i, (l, r) in enumerate(pairs):
                    mm1(fw, p[:], l, r, i == 0, (e != 0 and i == len(pairs) - 1), tag="mm_ex")
                if e == 0:
                    mm1(fw, p[:], be_sb[:, ot * 128:(ot + 1) * 128], wfm[:], False, True,
                        tag="mm_be")
                if e == 0:
                    copy(fw, "dve", out_acc[ot][:], p[:], tag="eo0")
                elif e < 7:
                    tt(fw, "dve", out_acc[ot][:], out_acc[ot][:], p[:], ALU.add, tag="eo")
                else:
                    tt(fw, "dve", comb[ot][:], out_acc[ot][:], p[:], ALU.add, tag="comb_c")
        for ot in range(NK):
            p = P.take()
            pairs = [(fwoT[k][:, ot * 128:(ot + 1) * 128], comb[k][:]) for k in range(NK)]
            for i, (l, r) in enumerate(pairs):
                mm1(fw, p[:], l, r, i == 0, False, tag="mm_fwo")
            mm1(fw, p[:], fbo_sb[0:1, ot * 128:(ot + 1) * 128], ones512[0:1, :], False, True,
                tag="mm_fbo")
            act(fw, g_sl[:], p[:], AF.Silu, tag="silu_g")
            fw.dma("sp", x_sl[:], xf_i[ot, :, qoff:qoff + 512], tag="ld_xr")
            fw.dma("sp", s_sl[:], s2_d[ot, :, qoff:qoff + 512], tag="ld_s2")
            tt(fw, "dve", g_sl[:], s_sl[:], g_sl[:], ALU.mult, tag="sg")
            tt(fw, "dve", x3_blk[:], x_sl[:], g_sl[:], ALU.add, tag="x3")
            fw.dma("sp", x3_o[ot, :, qoff:qoff + 512], x3_blk[:], tag="st_x3")





def emit_l3(fw, cst, P, IO):

    xf_i = IO["xf_i"]
    wqkv = IO["wqkv"]
    qs_c = IO["qs_c"]
    ks_c = IO["ks_c"]
    wp_t = IO["wp_t"]
    bp_r = IO["bp_r"]
    cosq = IO["cosq"]
    sinq = IO["sinq"]
    cosk = IO["cosk"]
    sink = IO["sink"]
    bw_t = IO["bw_t"]
    bb_r = IO["bb_r"]
    shA_t = IO["shA_t"]
    shB_t = IO["shB_t"]
    rA_t = IO["rA_t"]
    rB_t = IO["rB_t"]
    rw_t = IO["rw_t"]
    rb_c = IO["rb_c"]
    x4_o = IO["x4_o"]

    k_d = fw.dram("k3_spill", [16, 64, TF], BF16)
    v_d = fw.dram("v3_spill", [16, TF // 128, 128, 65], BF16)
    s3_d = fw.dram("s3_spill", [NK, 128, T], F32)

    onecol = cst.handles["onecol"]
    onerow = cst.handles["onerow"]
    ones512 = cst.handles["ones_row512"]
    perm_sb = cst.handles["perm3"]
    NKT = TF // 128

    k2m = fw.sb("k2m", [1, 16], F32)
    q_d = fw.dram("q3_spill", [16, 64, T], BF16)

    ln_tmp = mk_ln_tmp(fw, "t")
    xblk = [fw.sb(f"xb{k}", [128, 512], F32) for k in range(NK)]
    xn_blk = [fw.sb(f"xnb{k}", [128, 512], BF16) for k in range(NK)]

    def ln_into(cols_off, W=512):
        for k in range(NK):
            fw.dma("sp", xblk[k][:, :W], xf_i[k, :, cols_off:cols_off + W], tag="ld_x")
        ln_block(fw, P, cst, xblk, 0, W, xn_blk, 0, ln_tmp)

    wqkv_sb = [fw.sb(f"s_wqkv{k}", [128, 3 * D], BF16) for k in range(NK)]
    for k in range(NK):
        fw.dma("pool", wqkv_sb[k][:], wqkv[k * 128:(k + 1) * 128, :], tag="ld_wqkv")
    qs_sb = fw.sb("s_qs", [64, 1], F32)
    fw.dma("sp", qs_sb[:], qs_c)
    ks_sb = fw.sb("s_ks", [64, 1], F32)
    fw.dma("sp", ks_sb[:], ks_c)
    tb_cos = fw.sb("tb_cos", [64, 512], BF16)
    tb_sin = fw.sb("tb_sin", [64, 512], BF16)
    NSET = 4
    rsets = []
    for s in range(NSET):
        rsets.append(dict(
            pre_h=fw.sb(f"pre_h{s}", [64, 512], F32),
            sq_h=fw.sb(f"sq_h{s}", [64, 512], F32),
            row_a=fw.sb(f"row_a{s}", [1, 512], F32),
            rrep_h=fw.sb(f"rrep_h{s}", [64, 512], F32),
            qn_h=fw.sb(f"qn_h{s}", [64, 512], BF16),
            rop1=fw.sb(f"rop1{s}", [64, 512], F32),
            rop2=fw.sb(f"rop2{s}", [64, 512], F32),
            kout=fw.sb(f"kout{s}", [64, 512], BF16)))
    rturn = [0]
    vtile = fw.sb("vtile", [128, 512], BF16)
    k2one = fw.sb("k2one", [1, 1], F32)
    k2ps = fw.aux_ps
    pk_ps = [fw.ps(f"pk{s}", [128, 512], F32) for s in range(2)]

    def rms_rope_stages(sub_psum, scale_col, cos_ap, sin_ap, eps_s, tag, post):
        B = rsets[rturn[0] % NSET]; rturn[0] += 1
        pre_h, sq_h, row_a = B["pre_h"], B["sq_h"], B["row_a"]
        rrep_h, qn_h, rop1, rop2, kout = (B["rrep_h"], B["qn_h"], B["rop1"],
                                          B["rop2"], B["kout"])

        def s1():
            copy(fw, "act", pre_h[:], sub_psum, tag=tag + "pre")
            act(fw, sq_h[:], pre_h[:], AF.Square, tag=tag + "sq")
            p2 = P.take()
            mm1(fw, p2[0:1, :], onecol[0:64, 0:1], sq_h[:], True, True, tag=tag + "ssum")
            ts(fw, "dve", row_a[:], p2[0:1, :], 1.0 / 64.0, ALU.mult, tag=tag + "var")
            act(fw, row_a[:], row_a[:], AF.Ln, bias=eps_s, tag=tag + "ln")
            act(fw, row_a[:], row_a[:], AF.Exp, scale=cst.scalar(-0.5, 1), tag=tag + "rs")

        def s2():
            p2 = P.take()
            mm1(fw, p2[0:64, :], onerow[0:1, 0:64], row_a[0:1, :], True, True, tag=tag + "rep")
            copy(fw, "act", rrep_h[:], p2[0:64, :], tag=tag + "evr")
            stt(fw, "dve", qn_h[:], pre_h[:], scale_col, rrep_h[:], ALU.mult, ALU.mult,
                tag=tag + "n")

        def s3():
            p2 = P.take()
            mm1(fw, p2[0:64, :], perm_sb[:], qn_h[:], True, True, tag=tag + "rot")
            tt(fw, "dve", rop1[:], qn_h[:], cos_ap, ALU.mult, tag=tag + "r1")
            fw.comp("dve", lambda h, p2=p2, sin_ap=sin_ap, rop2=rop2: h.tensor_tensor(
                out=rop2[:], in0=p2[0:64, :], in1=sin_ap, op=ALU.mult),
                reads=[p2[0:64, :], sin_ap], writes=[rop2[:]], tag=tag + "r2")
            tt(fw, "dve", kout[:], rop1[:], rop2[:], ALU.add, tag=tag + "r3")
            post(B)
        return s1, s2, s3

    eps6 = cst.scalar(1e-6, 1)
    for bi in range(NKT // 4):
        off = bi * 512
        own = off < T
        ln_into(off)
        fw.dma("pool", tb_cos[:], cosk[:, off:off + 512], tag="ld_ck")
        fw.dma("pool", tb_sin[:], sink[:, off:off + 512], tag="ld_sk")
        pipe = Pipe()
        for mt in range(NK):
            def proj(mt=mt):
                p = pk_ps[mt % 2]
                mm(fw, p[:], [(wqkv_sb[k][:, D + mt * 128:D + (mt + 1) * 128], xn_blk[k][:])
                              for k in range(NK)], tag="mm_k3")
            pipe.add(proj)
            for sub in range(2):
                h_ = mt * 2 + sub

                def post(B, h_=h_, bi=bi, off=off):
                    kout = B["kout"]
                    fw.dma("sp", k_d[h_, :, off:off + 512], kout[:], tag="st_k3")
                    act(fw, B["sq_h"][:], kout[:], AF.Square, tag="k2sq")
                    mm1(fw, k2ps[:], onecol[0:64, 0:1], B["sq_h"][:], True, True, tag="k2mm")
                    if bi == 0:
                        reduce(fw, "dve", k2m[0:1, h_:h_ + 1], k2ps[:], op=ALU.max, tag="k2max")
                    else:
                        reduce(fw, "dve", k2one[:], k2ps[:], op=ALU.max, tag="k2max2")
                        tt(fw, "dve", k2m[0:1, h_:h_ + 1], k2m[0:1, h_:h_ + 1], k2one[:],
                           ALU.max, tag="k2acc")
                s1, s2, s3 = rms_rope_stages(pk_ps[mt % 2][sub * 64:(sub + 1) * 64, :],
                                             ks_sb[:, 0:1], tb_cos[:], tb_sin[:], eps6,
                                             "k3", post)
                pipe.add(s1, s2, s3)
        pipe.run()
        for st_ in range(4):
            to = st_ * 128
            kt = (off + to) // 128
            for half in range(2):
                p = P.take()
                mm(fw, p[:], [(xn_blk[k][:, to:to + 128],
                               wqkv_sb[k][:, 2 * D + half * 512:2 * D + (half + 1) * 512])
                              for k in range(NK)], tag="mm_v3")
                copy(fw, "act", vtile[:], p[:], tag="ev_v3")
                for hv in range(8):
                    h_ = half * 8 + hv
                    fw.dma("sp", v_d[h_, kt, :, 0:64], vtile[:, hv * 64:(hv + 1) * 64],
                           tag="st_v3")
        if own:
            fw.dma("pool", tb_cos[:], cosq[:, off:off + 512], tag="ld_cq")
            fw.dma("pool", tb_sin[:], sinq[:, off:off + 512], tag="ld_sq")
            pipe = Pipe()
            for mt in range(NK):
                def proj(mt=mt):
                    p = pk_ps[mt % 2]
                    mm(fw, p[:], [(wqkv_sb[k][:, mt * 128:(mt + 1) * 128], xn_blk[k][:])
                                  for k in range(NK)], tag="mm_q3")
                pipe.add(proj)
                for sub in range(2):
                    h_ = mt * 2 + sub

                    def post(B, h_=h_, off=off):
                        fw.dma("sp", q_d[h_, :, off:off + 512], B["kout"][:], tag="st_q3")
                    s1, s2, s3 = rms_rope_stages(pk_ps[mt % 2][sub * 64:(sub + 1) * 64, :],
                                                 qs_sb[:, 0:1], tb_cos[:], tb_sin[:], eps6,
                                                 "q3", post)
                    pipe.add(s1, s2, s3)
            pipe.run()
    fw.barrier()
    fw.free_sb("k2one", "vtile", "pk0", "pk1",
               *[f"{nm}{s}" for s in range(4) for nm in
                 ("pre_h", "sq_h", "row_a", "rrep_h", "qn_h", "rop1", "rop2", "kout")],
               "tb_sin", "tb_cos",
               "s_ks", "s_qs", *[f"s_wqkv{k}" for k in range(NK)],
               *[f"xnb{k}" for k in range(NK)], *[f"xb{k}" for k in range(NK)],
               "t_br", "t_rr", "t_bb", "t_rb", "t_vv", "t_qq", "t_ra", "t_sq")

    kbuf = [fw.sb(f"kb{i}", [65, TF], BF16) for i in range(2)]
    vbuf_t = [fw.sb(f"vb{i}", [128, NKT * 65], BF16) for i in range(2)]
    vbuf = [[vbuf_t[i][:, kt * 65:(kt + 1) * 65] for kt in range(NKT)] for i in range(2)]
    attn_tiles = {}
    for pair in range(8):
        for qb in range(4):
            attn_tiles[(pair, qb)] = fw.sb(f"atn{pair}_{qb}", [128, 512], BF16)
    Psb2 = [fw.sb(f"Psb{s}", [128, NKT * 512], BF16) for s in range(2)]
    rrep2 = [fw.sb(f"rrep{s}", [128, 512], F32) for s in range(2)]
    r_row2 = [fw.sb(f"r_row{s}", [1, 512], F32) for s in range(2)]
    pvps = [fw.ps(f"pvps{s}", [128, 512], F32) for s in range(2)]
    it3 = [0]
    qbuf = [fw.sb(f"qb3_{i}", [65, T], BF16) for i in range(2)]
    qsq = fw.sb("qsq3", [64, 512], F32)
    q2r = fw.sb("q2r3", [1, 512], F32)
    for h_ in range(16):
        sl = h_ % 2
        kb = kbuf[sl]
        fw.dma("sp", kb[0:64, :], k_d[h_], tag="ld_k3")
        memset(fw, "dve", kb[64:65, :], 1.0, tag="k3ones")
        fw.dma("sp", vbuf_t[sl][:].rearrange("p (kt c) -> p kt c", kt=NKT)[:, :, 0:64],
               v_d[h_, :, :, 0:64].rearrange("kt p c -> p kt c"), tag="ld_v3")
        memset(fw, "dve", vbuf_t[sl][:].rearrange("p (kt c) -> p kt c", kt=NKT)[:, :, 64:65],
               1.0, tag="v3ones")
        qh = qbuf[sl]
        fw.dma("sp", qh[0:64, :], q_d[h_], tag="ld_q3")
        for qb in range(4):
            off = qb * 512
            act(fw, qsq[:], qh[0:64, off:off + 512], AF.Square, tag="q3sq")
            p = P.take()
            mm1(fw, p[0:1, :], onecol[0:64, 0:1], qsq[:], True, True, tag="q3mm")
            copy(fw, "dve", q2r[:], p[0:1, :], tag="q3ev")
            act(fw, q2r[:], q2r[:], AF.Sqrt, scale=k2m[0:1, h_:h_ + 1], tag="csq3")
            fw.comp("dve", lambda h, qh=qh, off=off: h.tensor_single_scalar(
                out=qh[64:65, off:off + 512], in_=q2r[0:1, :], scalar=-1.0, op=ALU.mult),
                reads=[q2r[0:1, :]], writes=[qh[64:65, off:off + 512]], tag="cneg3")
        pipe = Pipe()
        for qb in range(4):
            q_ap = qh[:, qb * 512:(qb + 1) * 512]
            me = it3[0]; it3[0] += 1
            Psb = Psb2[me % 2]
            rrep = rrep2[me % 2]
            r_row = r_row2[me % 2]
            pv = pvps[me % 2]

            def s_sc(q_ap=q_ap, Psb=Psb, sl=sl, kb=kb):
                for kt in range(NKT):
                    p = P.take()
                    mm1(fw, p[:], kb[:, kt * 128:(kt + 1) * 128], q_ap, True, True, tag="mm_s3")
                    act(fw, Psb[:, kt * 512:(kt + 1) * 512], p[:], AF.Exp, tag="p3_exp")

            def s_pv(Psb=Psb, sl=sl, pv=pv, r_row=r_row):
                mm(fw, pv[0:65, :], [(vbuf[sl][kt], Psb[:, kt * 512:(kt + 1) * 512])
                                     for kt in range(NKT)], tag="mm_pv3")
                recip(fw, r_row[0:1, :], pv[64:65, :], tag="rden3")

            def s_nm(h_=h_, qb=qb, pv=pv, rrep=rrep, r_row=r_row):
                p2 = P.take()
                mm1(fw, p2[0:64, :], onerow[0:1, 0:64], r_row[0:1, :], True, True, tag="mm_rr3")
                copy(fw, "act", rrep[0:64, :], p2[0:64, :], tag="ev_rr3")
                dst = attn_tiles[(h_ // 2, qb)][(h_ % 2) * 64:(h_ % 2) * 64 + 64, :]
                fw.comp("dve", lambda hh, dst=dst, pv=pv: hh.tensor_tensor(
                    out=dst, in0=pv[0:64, :], in1=rrep[0:64, :], op=ALU.mult),
                    reads=[pv[0:64, :], rrep[0:64, :]], writes=[dst], tag="pv3_nrm")
            pipe.add(s_sc, s_pv, s_nm)
        pipe.run()
    wpT = [fw.sb(f"s_wp{k}", [128, D], BF16) for k in range(NK)]
    for k in range(NK):
        fw.dma("pool", wpT[k][:], wp_t[k * 128:(k + 1) * 128, :], tag="ld_wp")
    bp_sb = fw.sb("s_bp", [1, D], BF16)
    fw.dma("pool", bp_sb[:], bp_r)
    s3_blk = fw.sb("s3_blk", [128, 512], F32)
    for qb in range(4):
        for ot in range(NK):
            p = P.take()
            pairs = [(wpT[k][:, ot * 128:(ot + 1) * 128], attn_tiles[(k, qb)][:]) for k in range(NK)]
            for i, (l, r) in enumerate(pairs):
                mm1(fw, p[:], l, r, i == 0, False, tag="mm_wp")
            mm1(fw, p[:], bp_sb[0:1, ot * 128:(ot + 1) * 128], ones512[0:1, :], False, True,
                tag="mm_bp")
            copy(fw, "act", s3_blk[:], p[:], tag="ev_s3")
            fw.dma("sp", s3_d[ot, :, qb * 512:(qb + 1) * 512], s3_blk[:], tag="st_s3")

    fw.barrier()
    fw.free_sb("s3_blk", "s_bp", *[f"s_wp{k}" for k in range(NK)],
               "r_row1", "r_row0", "rrep1", "rrep0", "Psb1", "Psb0", "pvps0", "pvps1",
               "q2r3", "qsq3", "qb3_1", "qb3_0",
               *[f"atn{pr}_{qb}" for pr in range(8) for qb in range(4)],
               "vb0", "vb1", "kb0", "kb1", "k2m")

    idt = cst.handles["idt"]
    sel_sb = cst.handles["sel6"]
    bwT = [fw.sb(f"s_bw{k}", [128, D], BF16) for k in range(NK)]
    for k in range(NK):
        fw.dma("pool", bwT[k][:], bw_t[k * 128:(k + 1) * 128, :], tag="ld_bw")
    bb_sb = fw.sb("s_bb", [1, D], BF16)
    fw.dma("pool", bb_sb[:], bb_r)
    shA_sb = fw.sb("s_shA", [128, NK * 16], BF16)
    fw.dma("pool", shA_sb[:].rearrange("p (k r) -> p k r", k=NK),
           shA_t.rearrange("(k p) r -> p k r", p=128), tag="ld_shA")
    shB_sb = fw.sb("s_shB", [16, D], BF16)
    fw.dma("pool", shB_sb[:], shB_t)
    rA_sb = fw.sb("s_rA", [128, 6 * NK * 16], BF16)
    fw.dma("pool", rA_sb[:].rearrange("p (e k r) -> p e k r", e=6, k=NK),
           rA_t.rearrange("e (k p) r -> p e k r", p=128), tag="ld_rA")
    rB_sb = fw.sb("s_rB", [16, 6 * D], BF16)
    fw.dma("pool", rB_sb[:].rearrange("p (e d) -> p e d", e=6),
           rB_t.rearrange("e r d -> r e d"), tag="ld_rB")
    rwT = fw.sb("s_rwT", [128, NK * 6], BF16)
    fw.dma("pool", rwT[:].rearrange("p (k g) -> p k g", k=NK),
           rw_t.rearrange("(k p) g -> p k g", p=128), tag="ld_rw")
    rb_sb = fw.sb("s_rb", [6, 1], F32)
    fw.dma("sp", rb_sb[:], rb_c)
    ln_tmp = mk_ln_tmp(fw, "c")
    xblk = [fw.sb(f"cxb{k}", [128, 512], F32) for k in range(NK)]
    xn_blk = [fw.sb(f"cxnb{k}", [128, 512], BF16) for k in range(NK)]

    def ln_into(cols_off, W=512, xblk=xblk, xn_blk=xn_blk, ln_tmp=ln_tmp):
        for k in range(NK):
            fw.dma("sp", xblk[k][:, :W], xf_i[k, :, cols_off:cols_off + W], tag="ld_x")
        ln_block(fw, P, cst, xblk, 0, W, xn_blk, 0, ln_tmp)

    lgb = fw.sb("lgb", [6, 512], F32)
    tm_g = fw.sb("tm_g", [128, 6], F32)
    tm_e = fw.sb("tm_e", [128, 6], F32)
    tm_m = fw.sb("tm_m", [128, 6], F32)
    tm_m2 = fw.sb("tm_m2", [128, 6], F32)
    tm_r = fw.sb("tm_r", [128, 1], F32)
    wfm = fw.sb("wfm", [6, 512], BF16)
    wrep = fw.sb("wrep", [128, 512], BF16)
    xnw = [fw.sb(f"xnw{k}", [128, 512], BF16) for k in range(NK)]
    tsh = fw.sb("tsh", [16, 512], BF16)
    he_sb = fw.sb("he_sb", [16, 6 * 512], BF16)
    g_sl = fw.sb("g_sl", [128, 512], F32)
    x_sl = fw.sb("x_sl", [128, 512], F32)
    s_sl = fw.sb("s_sl", [128, 512], F32)
    x4_blk = fw.sb("x4_blk", [128, 512], F32)
    for qb in range(4):
        qoff = qb * 512
        ln_into(qoff)
        p = P.take()
        mm(fw, p[0:6, :], [(rwT[:, k * 6:(k + 1) * 6], xn_blk[k][:]) for k in range(NK)],
           tag="mm_rt")
        act(fw, lgb[:], p[0:6, :], AF.Identity, bias=rb_sb[0:6, 0:1], tag="ev_rt")
        for i in range(4):
            p = P.take()
            transpose_pe(fw, p[:, 0:6], lgb[:, i * 128:(i + 1) * 128], idt[0:6, 0:6], tag="tr_rt")
            copy(fw, "dve", tm_g[:], p[:, 0:6], tag="ev_tg")
            reduce(fw, "dve", tm_r[:], tm_g[:], op=ALU.max, negate=True, tag="r_max")
            act(fw, tm_e[:], tm_g[:], AF.Exp, bias=tm_r[:, 0:1], tag="r_exp")
            reduce(fw, "dve", tm_r[:], tm_e[:], op=ALU.add, tag="r_sum")
            recip(fw, tm_r[:], tm_r[:], tag="r_rec")
            ts(fw, "dve", tm_e[:], tm_e[:], tm_r[:, 0:1], ALU.mult, tag="r_nrm")
            reduce(fw, "dve", tm_r[:], tm_e[:], op=ALU.max, tag="r_m1")
            ts(fw, "dve", tm_m[:], tm_e[:], tm_r[:, 0:1], ALU.is_equal, tag="r_k1")
            stt(fw, "dve", tm_g[:], tm_m[:], -1e30, tm_e[:], ALU.mult, ALU.add, tag="r_g2")
            reduce(fw, "dve", tm_r[:], tm_g[:], op=ALU.max, tag="r_m2")
            ts(fw, "dve", tm_m2[:], tm_g[:], tm_r[:, 0:1], ALU.is_equal, tag="r_k2")
            tt(fw, "dve", tm_m[:], tm_m[:], tm_m2[:], ALU.add, tag="r_ms")
            tt(fw, "dve", tm_e[:], tm_e[:], tm_m[:], ALU.mult, tag="r_w")
            reduce(fw, "dve", tm_r[:], tm_e[:], op=ALU.add, tag="r_ws")
            ts(fw, "dve", tm_r[:], tm_r[:], 1e-6, ALU.add, tag="r_eps")
            recip(fw, tm_r[:], tm_r[:], tag="r_wr")
            ts(fw, "dve", tm_e[:], tm_e[:], tm_r[:, 0:1], ALU.mult, tag="r_wn")
            p = P.take()
            transpose_pe(fw, p[0:6, 0:128], tm_e[:], idt[:], tag="tr_w")
            copy(fw, "dve", wfm[:, i * 128:(i + 1) * 128], p[0:6, 0:128], tag="ev_w")
        p = P.take()
        mm(fw, p[0:16, :], [(shA_sb[:, k * 16:(k + 1) * 16], xn_blk[k][:]) for k in range(NK)],
           tag="mm_shA")
        copy(fw, "act", tsh[:], p[0:16, :], tag="ev_shA")
        for e in range(6):
            p = P.take()
            mm1(fw, p[:], sel_sb[:, e * 128:(e + 1) * 128], wfm[:], True, True, tag="mm_wr")
            copy(fw, "act", wrep[:], p[:], tag="ev_wr")
            for k in range(NK):
                tt(fw, "dve", xnw[k][:], xn_blk[k][:], wrep[:], ALU.mult, tag="xnw")
            p = P.take()
            mm(fw, p[0:16, :], [(rA_sb[:, (e * NK + k) * 16:(e * NK + k + 1) * 16], xnw[k][:])
                                for k in range(NK)], tag="mm_rA")
            copy(fw, "act", he_sb[:, e * 512:(e + 1) * 512], p[0:16, :], tag="ev_he")
        for ot in range(NK):
            p = P.take()
            for i, k in enumerate(range(NK)):
                mm1(fw, p[:], bwT[k][:, ot * 128:(ot + 1) * 128], xn_blk[k][:],
                    i == 0, False, tag="mm_base")
            mm1(fw, p[:], bb_sb[0:1, ot * 128:(ot + 1) * 128], ones512[0:1, :], False, False,
                tag="mm_bb")
            mm1(fw, p[:], shB_sb[:, ot * 128:(ot + 1) * 128], tsh[:], False, False, tag="mm_shB")
            for e in range(6):
                mm1(fw, p[:], rB_sb[:, e * D + ot * 128:e * D + (ot + 1) * 128],
                    he_sb[:, e * 512:(e + 1) * 512], False, e == 5, tag="mm_rB")
            act(fw, g_sl[:], p[:], AF.Silu, tag="silu_g")
            fw.dma("sp", x_sl[:], xf_i[ot, :, qoff:qoff + 512], tag="ld_xr")
            fw.dma("sp", s_sl[:], s3_d[ot, :, qoff:qoff + 512], tag="ld_s3")
            tt(fw, "dve", g_sl[:], s_sl[:], g_sl[:], ALU.mult, tag="sg")
            tt(fw, "dve", x4_blk[:], x_sl[:], g_sl[:], ALU.add, tag="x4")
            fw.dma("sp", x4_o[ot, :, qoff:qoff + 512], x4_blk[:], tag="st_x4")




RG_PAIRS = [[0, 1], [2, 3], [4, 5], [6, 7]]


def emit_collective(fw, kind, src_ap, dst_ap, tag="cc"):
    def build(h):
        return h.collective_compute(kind, ALU.bypass, replica_groups=RG_PAIRS,
                                    ins=[src_ap.opt()], outs=[dst_ap.opt()])
    return fw.comp("pool", build, reads=[src_ap], writes=[dst_ap], tag=tag)


def emit_rotate(fw, P, gathered_ap, xf_ap, mrot_sb, bufs, tag="rot"):
    bufA = [fw.sb(f"rA{s}", [128, T], F32) for s in range(2)]
    bufB = [fw.sb(f"rB{s}", [128, T], F32) for s in range(2)]
    bufT = [fw.sb(f"rT{s}", [128, T], F32) for s in range(2)]
    g3 = gathered_ap.rearrange("(r k p) t -> r k p t", r=2, p=128)
    W = T
    for k in range(NK):
        for half in range(2):
            ma = mrot_sb[:, half:half + 1]
            mb = mrot_sb[:, 1 - half:2 - half]
            s = (k * 2 + half) % 2
            fw.dma("sp", bufA[s][:, :W], g3[0, k, :, 0:W], tag=tag + "g0")
            fw.dma("pool", bufB[s][:, :W], g3[1, k, :, 0:W], tag=tag + "g1")
            ts(fw, "dve", bufT[s][:, :W], bufA[s][:, :W], ma, ALU.mult, tag=tag + "m0")
            stt(fw, "dve", bufA[s][:, :W], bufB[s][:, :W], mb, bufT[s][:, :W],
                ALU.mult, ALU.add, tag=tag + "m1")
            fw.dma("sp", xf_ap[k, :, half * T:half * T + W], bufA[s][:, :W],
                   tag=tag + "st")


def build_fused():
    nc = bass.Bass("TRN2", target_bir_lowering=False)
    fw = Fw(nc)
    cst = Consts(fw)
    setup_consts(fw, cst, only={"onecol", "onerow"})
    P = PsumPool(fw, 5)

    def ext(pre, names, spec):
        fw.prefix = pre
        out = {}
        for nm in names:
            shape, dt = spec["ins"][nm]
            out[nm] = fw.inp(nm, shape, dt)
        fw.prefix = ""
        return out

    E0a = ext("l0a_", ["xh", "w_in", "convw", "convb", "wxp", "wdt", "bdt"], IO_SPECS["l0a"])
    E0b = ext("l0b_", ["alog", "dskip", "wout"], IO_SPECS["l0b_full"])
    E0f = ext("l0f_", ["wgred_t", "wg_t", "w1", "b1_t", "w2", "b2_t"], IO_SPECS["l0f"])
    E1 = ext("l1_", ["wq_t", "wk_t", "wv_t", "wo_t", "colq", "gw", "gb", "ew", "eb"],
             IO_SPECS["l1"])
    E2 = ext("l2_", ["wq_t", "bq_c", "wk_t", "bk_c", "wv_t", "bv_r", "wo_t", "bo_r",
                     "cosq", "sinq", "cosk", "sink", "wr_t", "br_c", "we_t", "be_i",
                     "fwo_t", "fbo_r"], IO_SPECS["l2"])
    E3 = ext("l3_", ["wqkv", "qs_c", "ks_c", "wp_t", "bp_r", "cosq", "sinq", "cosk",
                     "sink", "bw_t", "bb_r", "shA_t", "shB_t", "rA_t", "rB_t", "rw_t",
                     "rb_c"], IO_SPECS["l3"])
    mrot_i = fw.inp("mrot", [128, 2], F32)
    x4_ext = fw.outp("x4", [NK, 128, T], F32)

    ia = {}
    for nm, (shape, dt) in IO_SPECS["l0a"]["outs"].items():
        ia[nm] = fw.dram("ia_" + nm, shape, dt)
    carry_own = fw.dram("carry_own", [128, CT * NSTATE], F32)
    carry_g = fw.dram("carry_g", [2 * 128, CT * NSTATE], F32)
    carry_in = fw.dram("carry_in", [128, CT * NSTATE], F32)
    s0_d = fw.dram("s0_d", [NK, 128, T], F32)
    ccx = [fw.dram(f"ccx{l}_src", [NK * 128, T], F32) for l in (1, 2, 3)]
    ccg = [fw.dram(f"ccx{l}_dst", [2 * NK * 128, T], F32) for l in (1, 2, 3)]
    xf = [fw.dram(f"xf{l}", [NK, 128, TF], F32) for l in (1, 2, 3)]

    fw.prefix = "l0a_"
    m = fw.mark()
    IO = dict(E0a)
    IO.update({k: ia[k] for k in ("xn_o", "u_o", "z_o", "delta_o", "bmcm_o")})
    emit_l0a(fw, cst, P, IO)
    fw.barrier(); fw.free_since(m); fw.prefix = ""

    setup_consts(fw, cst)
    mrot_sb = fw.sb("g_mrot", [128, 2], F32)
    fw.dma("sp", mrot_sb[:], mrot_i)
    rot_bufs = None
    cbuf = fw.sb("g_cb", [128, CT * NSTATE], F32)

    fw.prefix = "l0c_"
    m = fw.mark()
    IO = dict(delta_i=ia["delta_o"], u_i=ia["u_o"], bmcm_i=ia["bmcm_o"],
              alog=E0b["alog"], carry_o=carry_own)
    emit_l0b(fw, cst, P, IO, mode="carry")
    fw.barrier(); fw.free_since(m); fw.prefix = ""

    emit_collective(fw, "AllGather", carry_own, carry_g, tag="cc_carry")
    fw.dma("sp", cbuf[:], carry_g[0:128, :], tag="cc_ld")
    ts(fw, "dve", cbuf[:], cbuf[:], mrot_sb[:, 1:2], ALU.mult, tag="cc_sel")
    fw.dma("sp", carry_in, cbuf[:], tag="cc_st")

    fw.prefix = "l0d_"
    m = fw.mark()
    IO = dict(delta_i=ia["delta_o"], u_i=ia["u_o"], bmcm_i=ia["bmcm_o"],
              alog=E0b["alog"], carry_i=carry_in, z_i=ia["z_o"], dskip=E0b["dskip"],
              wout=E0b["wout"], s_o=s0_d)
    emit_l0b(fw, cst, P, IO, mode="full")
    fw.barrier(); fw.free_since(m); fw.prefix = ""

    fw.prefix = "l0f_"
    m = fw.mark()
    IO = dict(E0f)
    IO.update(xn_i=ia["xn_o"], x_i=E0a["xh"][:, :, 512:TH], s_i=s0_d,
              x1_o=ccx[0].rearrange("(k p) t -> k p t", p=128))
    emit_l0f(fw, cst, P, IO)
    fw.barrier(); fw.free_since(m); fw.prefix = ""

    emits = [emit_l1, emit_l2, emit_l3]
    exts = [E1, E2, E3]
    outs_nm = ["x2_o", "x3_o", "x4_o"]
    for l in range(3):
        emit_collective(fw, "AllGather", ccx[l], ccg[l], tag=f"cc_x{l + 1}")
        fw.prefix = f"g{l + 1}_"
        emit_rotate(fw, P, ccg[l], xf[l], mrot_sb, rot_bufs, tag=f"rot{l + 1}")
        fw.barrier()
        fw.free_sb("rA0", "rA1", "rB0", "rB1", "rT0", "rT1")
        fw.prefix = ""
        fw.prefix = f"l{l + 1}_"
        m = fw.mark()
        IO = dict(exts[l])
        IO["xf_i"] = xf[l]
        if l < 2:
            IO[outs_nm[l]] = ccx[l + 1].rearrange("(k p) t -> k p t", p=128)
        else:
            IO[outs_nm[l]] = x4_ext
        emits[l](fw, cst, P, IO)
        fw.barrier(); fw.free_since(m); fw.prefix = ""

    fw.finish_outputs()
    fw.finalize()
    return nc, cst


def host_fused_maps(inputs):
    g = lambda k: np.asarray(inputs[k], np.float32)
    x = np.asarray(inputs["L0_attn_x"])
    shared = {}
    conv_w = g("L0_attn_conv_w").reshape(2048, 4)
    shared.update({
        "l0a_w_in": np.ascontiguousarray(g("L0_attn_W_in")),
        "l0a_convw": np.ascontiguousarray(conv_w.reshape(CT, 128, 4).transpose(1, 0, 2)
                                          .reshape(128, CT * 4)),
        "l0a_convb": np.ascontiguousarray(g("L0_attn_conv_b").reshape(CT, 128).T),
        "l0a_wxp": np.ascontiguousarray(g("L0_attn_W_xproj")),
        "l0a_wdt": np.ascontiguousarray(g("L0_attn_W_dt")),
        "l0a_bdt": np.ascontiguousarray(g("L0_attn_b_dt").reshape(CT, 128).T),
        "l0b_alog": np.ascontiguousarray(g("L0_attn_A_log").reshape(CT, 128, NSTATE)
                                         .transpose(1, 0, 2).reshape(128, CT * NSTATE)),
        "l0b_dskip": np.ascontiguousarray(g("L0_attn_D_skip").reshape(CT, 128).T),
        "l0b_wout": np.ascontiguousarray(g("L0_attn_W_out")),
        "l0f_wgred_t": np.ascontiguousarray(g("L0_ffn_wg_red").T),
        "l0f_wg_t": np.ascontiguousarray(g("L0_ffn_wg").T),
        "l0f_w1": np.ascontiguousarray(g("L0_ffn_w1")),
        "l0f_b1_t": np.ascontiguousarray(g("L0_ffn_b1").reshape(8, 32, 128)
                                         .transpose(2, 0, 1).reshape(128, 256)),
        "l0f_w2": np.ascontiguousarray(g("L0_ffn_w2")),
        "l0f_b2_t": np.ascontiguousarray(g("L0_ffn_b2").reshape(8, 8, 128)
                                         .transpose(2, 0, 1).reshape(128, 64)),
        "l1_wq_t": np.ascontiguousarray(g("L1_attn_Wq").T),
        "l1_wk_t": np.ascontiguousarray(g("L1_attn_Wk").T),
        "l1_wv_t": np.ascontiguousarray(g("L1_attn_Wv").T),
        "l1_wo_t": np.ascontiguousarray(g("L1_attn_Wo").T),
        "l1_gw": np.ascontiguousarray(g("L1_ffn_gate_w")),
        "l1_gb": np.ascontiguousarray(g("L1_ffn_gate_b").reshape(8, 1)),
        "l1_ew": np.ascontiguousarray(g("L1_ffn_expert_w")),
        "l1_eb": np.ascontiguousarray(g("L1_ffn_expert_b")),
        "l2_wq_t": np.ascontiguousarray(g("L2_attn_Wq").T),
        "l2_bq_c": np.ascontiguousarray(g("L2_attn_bq").reshape(8, 128).T),
        "l2_wk_t": np.ascontiguousarray(g("L2_attn_Wk").T),
        "l2_bk_c": np.ascontiguousarray(g("L2_attn_bk").reshape(2, 128).T),
        "l2_wv_t": np.ascontiguousarray(g("L2_attn_Wv").T),
        "l2_bv_r": np.ascontiguousarray(g("L2_attn_bv").reshape(1, 256)),
        "l2_wo_t": np.ascontiguousarray(g("L2_attn_Wo").T),
        "l2_bo_r": np.ascontiguousarray(g("L2_attn_bo").reshape(1, D)),
        "l2_wr_t": np.ascontiguousarray(g("L2_ffn_Wr").T),
        "l2_br_c": np.ascontiguousarray(g("L2_ffn_br").reshape(8, 1)),
        "l2_we_t": np.ascontiguousarray(g("L2_ffn_We").transpose(0, 2, 1)),
        "l2_be_i": np.ascontiguousarray(g("L2_ffn_be")),
        "l2_fwo_t": np.ascontiguousarray(g("L2_ffn_Wo").T),
        "l2_fbo_r": np.ascontiguousarray(g("L2_ffn_bo").reshape(1, D)),
        "l3_wqkv": np.ascontiguousarray(g("L3_attn_w_qkv")),
        "l3_qs_c": np.ascontiguousarray(g("L3_attn_q_scale").reshape(64, 1)),
        "l3_ks_c": np.ascontiguousarray(g("L3_attn_k_scale").reshape(64, 1)),
        "l3_wp_t": np.ascontiguousarray(g("L3_attn_w_proj")),
        "l3_bp_r": np.ascontiguousarray(g("L3_attn_b_proj").reshape(1, D)),
        "l3_bw_t": np.ascontiguousarray(g("L3_ffn_base_W").T),
        "l3_bb_r": np.ascontiguousarray(g("L3_ffn_base_b").reshape(1, D)),
        "l3_shA_t": np.ascontiguousarray(g("L3_ffn_shared_A").T),
        "l3_shB_t": np.ascontiguousarray(g("L3_ffn_shared_B").T),
        "l3_rA_t": np.ascontiguousarray(g("L3_ffn_routed_A").transpose(0, 2, 1)),
        "l3_rB_t": np.ascontiguousarray(g("L3_ffn_routed_B").transpose(0, 2, 1)),
        "l3_rw_t": np.ascontiguousarray(g("L3_ffn_router_W").T),
        "l3_rb_c": np.ascontiguousarray(g("L3_ffn_router_b").reshape(6, 1)),
    })
    maps = []
    for c in range(NCORES):
        b, h = c // 2, c % 2
        rows = np.zeros((TH, D), np.float32)
        rows[512:] = x[b, h * T:(h + 1) * T]
        if h == 1:
            rows[:512] = x[b, T - 512:T]
        mrot = np.zeros((128, 2), np.float32)
        mrot[:, h] = 1.0
        own_pos = np.arange(h * T, (h + 1) * T, dtype=np.float64)
        rot_pos = np.concatenate([own_pos, np.arange((1 - h) * T, (2 - h) * T,
                                                     dtype=np.float64)])
        cosq, sinq = _rope_tables(own_pos)
        cosk, sink = _rope_tables(rot_pos)
        maps.append(dict(shared, l0a_xh=_fm(rows), mrot=mrot,
                         l1_colq=host_l1_colq(c),
                         l2_cosq=cosq * 0.125, l2_sinq=sinq * 0.125,
                         l2_cosk=cosk, l2_sink=sink,
                         l3_cosq=cosq * 0.125, l3_sinq=sinq * 0.125,
                         l3_cosk=cosk, l3_sink=sink))
    return maps


def kernel_fused(inputs):
    nc, cst = _get_prog("fused", build_fused)
    maps = host_fused_maps(inputs)
    in_maps = [dict(m, **cst.arrays) for m in maps]
    res = run_bass_kernel_spmd(nc, in_maps, core_ids=list(range(NCORES)))
    out = np.empty((4, 4096, D), np.float32)
    for c in range(NCORES):
        b, h = c // 2, c % 2
        out[b, h * T:(h + 1) * T] = _unfm(np.asarray(res.results[c]["x4"], np.float32))
    return out



def _fm(a2d):
    Tt, D2 = a2d.shape
    return np.ascontiguousarray(a2d.T.reshape(D2 // 128, 128, Tt))


def _unfm(a3d):
    nt, p, Tt = a3d.shape
    return np.ascontiguousarray(a3d.reshape(nt * p, Tt).T)


def _percore_rows(x, c):
    b, h = c // 2, c % 2
    return x[b, h * T:(h + 1) * T]


def host_l0a_inputs(inputs):
    x = np.asarray(inputs["L0_attn_x"])
    conv_w = np.asarray(inputs["L0_attn_conv_w"]).reshape(2048, 4)
    per_core = []
    shared = {
        "w_in": np.ascontiguousarray(inputs["L0_attn_W_in"]),
        "convw": np.ascontiguousarray(conv_w.reshape(CT, 128, 4).transpose(1, 0, 2).reshape(128, CT * 4)),
        "convb": np.ascontiguousarray(np.asarray(inputs["L0_attn_conv_b"]).reshape(CT, 128).T),
        "wxp": np.ascontiguousarray(inputs["L0_attn_W_xproj"]),
        "wdt": np.ascontiguousarray(inputs["L0_attn_W_dt"]),
        "bdt": np.ascontiguousarray(np.asarray(inputs["L0_attn_b_dt"]).reshape(CT, 128).T),
    }
    for c in range(NCORES):
        b, h = c // 2, c % 2
        rows = np.zeros((TH, D), np.float32)
        rows[512:] = x[b, h * T:(h + 1) * T]
        if h == 1:
            rows[:512] = x[b, T - 512:T]
        per_core.append(dict(shared, xh=_fm(rows)))
    return per_core


_BUILt = {}


def _get_prog(name, builder):
    if name not in _BUILt:
        _BUILt[name] = builder()
    return _BUILt[name]


def _spmd(name, builder, in_maps_fn):
    nc, cst = _get_prog(name, builder)
    in_maps = [dict(pc, **cst.arrays) for pc in in_maps_fn()]
    res = run_bass_kernel_spmd(nc, in_maps, core_ids=list(range(NCORES)))
    return res.results


build_l0a = _mk_builder("l0a")
build_l0f = _mk_builder("l0f")
build_l1 = _mk_builder("l1")
build_l2 = _mk_builder("l2")
build_l3 = _mk_builder("l3")


def build_l0b(mode):
    return _mk_builder("l0b_" + mode)()


def run_l0a(inputs):
    return _spmd("l0a", build_l0a, lambda: host_l0a_inputs(inputs))


def run_l0(inputs, upto="f"):
    r_a = run_l0a(inputs)
    alog_t = np.ascontiguousarray(
        np.asarray(inputs["L0_attn_A_log"]).reshape(CT, 128, NSTATE)
        .transpose(1, 0, 2).reshape(128, CT * NSTATE)).astype(np.float32)

    def maps_b1():
        return [dict(delta_i=r_a[c]["delta_o"], u_i=r_a[c]["u_o"],
                     bmcm_i=r_a[c]["bmcm_o"], alog=alog_t) for c in range(NCORES)]
    r_b1 = _spmd("l0b_carry", lambda: build_l0b("carry"), maps_b1)
    if upto == "b1":
        return r_a, r_b1
    dskip_t = np.ascontiguousarray(
        np.asarray(inputs["L0_attn_D_skip"]).reshape(CT, 128).T).astype(np.float32)
    wout = np.ascontiguousarray(np.asarray(inputs["L0_attn_W_out"], np.float32))
    zeros_c = np.zeros((128, CT * NSTATE), np.float32)

    def maps_b2():
        out = []
        for c in range(NCORES):
            b, h = c // 2, c % 2
            carry = zeros_c if h == 0 else np.asarray(r_b1[2 * b]["carry_o"], np.float32)
            out.append(dict(delta_i=r_a[c]["delta_o"], u_i=r_a[c]["u_o"],
                            bmcm_i=r_a[c]["bmcm_o"], alog=alog_t, carry_i=carry,
                            z_i=r_a[c]["z_o"], dskip=dskip_t, wout=wout))
        return out
    r_b2 = _spmd("l0b_full", lambda: build_l0b("full"), maps_b2)
    if upto == "b2":
        return r_a, r_b1, r_b2
    x = np.asarray(inputs["L0_attn_x"])
    b1_t = np.ascontiguousarray(np.asarray(inputs["L0_ffn_b1"]).reshape(8, 32, 128)
                                .transpose(2, 0, 1).reshape(128, 256)).astype(np.float32)
    b2_t = np.ascontiguousarray(np.asarray(inputs["L0_ffn_b2"]).reshape(8, 8, 128)
                                .transpose(2, 0, 1).reshape(128, 64)).astype(np.float32)
    wgred_t = np.ascontiguousarray(np.asarray(inputs["L0_ffn_wg_red"]).T).astype(np.float32)
    wg_t = np.ascontiguousarray(np.asarray(inputs["L0_ffn_wg"]).T).astype(np.float32)
    w1 = np.ascontiguousarray(np.asarray(inputs["L0_ffn_w1"], np.float32))
    w2 = np.ascontiguousarray(np.asarray(inputs["L0_ffn_w2"], np.float32))

    def maps_f():
        out = []
        for c in range(NCORES):
            out.append(dict(xn_i=r_a[c]["xn_o"], x_i=_fm(_percore_rows(x, c)),
                            s_i=r_b2[c]["s_o"], wgred_t=wgred_t, wg_t=wg_t,
                            w1=w1, b1_t=b1_t, w2=w2, b2_t=b2_t))
        return out
    r_f = _spmd("l0f", build_l0f, maps_f)
    return r_a, r_b1, r_b2, r_f


def host_l1_colq(c):
    b, h = c // 2, c % 2
    arr = np.zeros((128, 4 * 32), np.float32)
    p = np.arange(128)
    for qb in range(4):
        for kt in range(32):
            rot_base = (h * T + kt * 128) if kt < 16 else ((1 - h) * T + (kt - 16) * 128)
            arr[:, qb * 32 + kt] = rot_base + p - qb * 512 - h * T
    return arr


def run_l1(inputs, x1_percore):
    wq_t = np.ascontiguousarray(np.asarray(inputs["L1_attn_Wq"], np.float32).T)
    wk_t = np.ascontiguousarray(np.asarray(inputs["L1_attn_Wk"], np.float32).T)
    wv_t = np.ascontiguousarray(np.asarray(inputs["L1_attn_Wv"], np.float32).T)
    wo_t = np.ascontiguousarray(np.asarray(inputs["L1_attn_Wo"], np.float32).T)
    gw = np.ascontiguousarray(np.asarray(inputs["L1_ffn_gate_w"], np.float32))
    gb = np.ascontiguousarray(np.asarray(inputs["L1_ffn_gate_b"], np.float32).reshape(8, 1))
    ew = np.ascontiguousarray(np.asarray(inputs["L1_ffn_expert_w"], np.float32))
    eb = np.ascontiguousarray(np.asarray(inputs["L1_ffn_expert_b"], np.float32))

    def maps():
        out = []
        for c in range(NCORES):
            b, h = c // 2, c % 2
            own = np.asarray(x1_percore[c], np.float32)
            other = np.asarray(x1_percore[2 * b + (1 - h)], np.float32)
            xf = np.concatenate([own, other], axis=2)
            out.append(dict(xf_i=np.ascontiguousarray(xf), wq_t=wq_t, wk_t=wk_t,
                            wv_t=wv_t, wo_t=wo_t, colq=host_l1_colq(c), gw=gw, gb=gb,
                            ew=ew, eb=eb))
        return out
    return _spmd("l1", build_l1, maps)


def _rope_tables(pos):
    inv = 1.0 / 10000.0 ** (np.arange(0, 64, 2) / 64.0)
    ang = pos[:, None] * inv[None]
    sin = np.concatenate([np.sin(ang)] * 2, -1)
    cos = np.concatenate([np.cos(ang)] * 2, -1)
    return np.ascontiguousarray(cos.T).astype(np.float32), \
        np.ascontiguousarray(sin.T).astype(np.float32)


def run_l2(inputs, x2_percore):
    g = lambda k: np.asarray(inputs[k], np.float32)
    wq_t = np.ascontiguousarray(g("L2_attn_Wq").T)
    bq_c = np.ascontiguousarray(g("L2_attn_bq").reshape(8, 128).T)
    wk_t = np.ascontiguousarray(g("L2_attn_Wk").T)
    bk_c = np.ascontiguousarray(g("L2_attn_bk").reshape(2, 128).T)
    wv_t = np.ascontiguousarray(g("L2_attn_Wv").T)
    bv_r = np.ascontiguousarray(g("L2_attn_bv").reshape(1, 256))
    wo_t = np.ascontiguousarray(g("L2_attn_Wo").T)
    bo_r = np.ascontiguousarray(g("L2_attn_bo").reshape(1, D))
    wr_t = np.ascontiguousarray(g("L2_ffn_Wr").T)
    br_c = np.ascontiguousarray(g("L2_ffn_br").reshape(8, 1))
    we_t = np.ascontiguousarray(g("L2_ffn_We").transpose(0, 2, 1))
    be_i = np.ascontiguousarray(g("L2_ffn_be"))
    fwo_t = np.ascontiguousarray(g("L2_ffn_Wo").T)
    fbo_r = np.ascontiguousarray(g("L2_ffn_bo").reshape(1, D))

    def maps():
        out = []
        for c in range(NCORES):
            b, h = c // 2, c % 2
            own = np.asarray(x2_percore[c], np.float32)
            other = np.asarray(x2_percore[2 * b + (1 - h)], np.float32)
            xf = np.ascontiguousarray(np.concatenate([own, other], axis=2))
            own_pos = np.arange(h * T, (h + 1) * T, dtype=np.float64)
            rot_pos = np.concatenate([own_pos, np.arange((1 - h) * T, (2 - h) * T,
                                                         dtype=np.float64)])
            cosq, sinq = _rope_tables(own_pos)
            cosk, sink = _rope_tables(rot_pos)
            out.append(dict(xf_i=xf, wq_t=wq_t, bq_c=bq_c, wk_t=wk_t, bk_c=bk_c,
                            wv_t=wv_t, bv_r=bv_r, wo_t=wo_t, bo_r=bo_r,
                            cosq=cosq * 0.125, sinq=sinq * 0.125, cosk=cosk, sink=sink,
                            wr_t=wr_t, br_c=br_c, we_t=we_t, be_i=be_i,
                            fwo_t=fwo_t, fbo_r=fbo_r))
        return out
    return _spmd("l2", build_l2, maps)


def run_l3(inputs, x3_percore):
    g = lambda k: np.asarray(inputs[k], np.float32)
    wqkv = np.ascontiguousarray(g("L3_attn_w_qkv"))
    qs_c = np.ascontiguousarray(g("L3_attn_q_scale").reshape(64, 1))
    ks_c = np.ascontiguousarray(g("L3_attn_k_scale").reshape(64, 1))
    wp_t = np.ascontiguousarray(g("L3_attn_w_proj"))
    bp_r = np.ascontiguousarray(g("L3_attn_b_proj").reshape(1, D))
    bw_t = np.ascontiguousarray(g("L3_ffn_base_W").T)
    bb_r = np.ascontiguousarray(g("L3_ffn_base_b").reshape(1, D))
    shA_t = np.ascontiguousarray(g("L3_ffn_shared_A").T)
    shB_t = np.ascontiguousarray(g("L3_ffn_shared_B").T)
    rA_t = np.ascontiguousarray(g("L3_ffn_routed_A").transpose(0, 2, 1))
    rB_t = np.ascontiguousarray(g("L3_ffn_routed_B").transpose(0, 2, 1))
    rw_t = np.ascontiguousarray(g("L3_ffn_router_W").T)
    rb_c = np.ascontiguousarray(g("L3_ffn_router_b").reshape(6, 1))

    def maps():
        out = []
        for c in range(NCORES):
            b, h = c // 2, c % 2
            own = np.asarray(x3_percore[c], np.float32)
            other = np.asarray(x3_percore[2 * b + (1 - h)], np.float32)
            xf = np.ascontiguousarray(np.concatenate([own, other], axis=2))
            own_pos = np.arange(h * T, (h + 1) * T, dtype=np.float64)
            rot_pos = np.concatenate([own_pos, np.arange((1 - h) * T, (2 - h) * T,
                                                         dtype=np.float64)])
            cosq, sinq = _rope_tables(own_pos)
            cosk, sink = _rope_tables(rot_pos)
            out.append(dict(xf_i=xf, wqkv=wqkv, qs_c=qs_c, ks_c=ks_c, wp_t=wp_t, bp_r=bp_r,
                            cosq=cosq * 0.125, sinq=sinq * 0.125, cosk=cosk, sink=sink,
                            bw_t=bw_t, bb_r=bb_r, shA_t=shA_t, shB_t=shB_t, rA_t=rA_t,
                            rB_t=rB_t, rw_t=rw_t, rb_c=rb_c))
        return out
    return _spmd("l3", build_l3, maps)


def kernel(**inputs):
    return kernel_fused(inputs)


def kernel_multilaunch(**inputs):
    r = run_l0(inputs, upto="f")
    r_f = r[3]
    x1_pc = [np.asarray(r_f[c]["x1_o"]) for c in range(NCORES)]
    r1 = run_l1(inputs, x1_pc)
    x2_pc = [np.asarray(r1[c]["x2_o"]) for c in range(NCORES)]
    r2 = run_l2(inputs, x2_pc)
    x3_pc = [np.asarray(r2[c]["x3_o"]) for c in range(NCORES)]
    r3 = run_l3(inputs, x3_pc)
    out = np.empty((4, 4096, D), np.float32)
    for c in range(NCORES):
        b, h = c // 2, c % 2
        out[b, h * T:(h + 1) * T] = _unfm(np.asarray(r3[c]["x4_o"], np.float32))
    return out
